# version 03
# baseline (speedup 1.0000x reference)
import contextlib
import math

import numpy as np
import ml_dtypes
import concourse.bass as bass
import concourse.mybir as mybir
from concourse.bass_utils import run_bass_kernel_spmd

F32 = mybir.dt.float32
BF16 = mybir.dt.bfloat16
AF = mybir.ActivationFunctionType
ALU = mybir.AluOpType
AX = mybir.AxisListType

D = 1024
T = 2048
NC = 8
NQ = 4
SEQ = 8192
LN_EPS = 1e-5


class Sem:
    __slots__ = ("h", "n")

    def __init__(self, h):
        self.h = h
        self.n = 0


class Prog:
    ENGINES = ("sync", "scalar", "tensor", "vector", "gpsimd")

    def __init__(self, nc):
        self.nc = nc
        self.ops = {e: [] for e in self.ENGINES}
        self.stack = contextlib.ExitStack()
        self.bar = self.sem("bar")
        self.done = {e: self.sem("done_" + e) for e in self.ENGINES}

    def sem(self, name):
        return Sem(self.stack.enter_context(self.nc.semaphore(name)))

    def psem(self, name):
        d = getattr(self, "_psems", None)
        if d is None:
            d = self._psems = {}
        if name not in d:
            d[name] = self.sem(name)
        return d[name]

    def sb(self, name, shape, dtype):
        return self.stack.enter_context(self.nc.sbuf_tensor(name, shape, dtype))

    def psum(self, name, shape, dtype=F32):
        return self.stack.enter_context(self.nc.psum_tensor(name, shape, dtype))

    def emit(self, eng, fn, sem=None, k=1, serial=False):
        if serial:
            self.wait(eng, self.done[eng])
            sem = sem or self.done[eng]
        if sem is not None:
            sem.n += k
            self.ops[eng].append(lambda e, fn=fn, s=sem.h, k=k: fn(e).then_inc(s, k))
        else:
            self.ops[eng].append(fn)

    def wait(self, eng, sem, val=None):
        v = sem.n if val is None else val
        if v <= 0:
            return
        self.ops[eng].append(lambda e, s=sem.h, v=v: e.wait_ge(s, v))

    def dma(self, eng, out, in_, sem, **kw):
        self.emit(eng, lambda e, o=out, i=in_, kw=kw: e.dma_start(out=o, in_=i, **kw),
                  sem, 16)

    def mark(self, eng):
        self.emit(eng, lambda e: e.wait_ge(self.bar.h, 0), self.done[eng], 1)

    def mark_mm(self, tag=""):
        ops = self.ops["tensor"]
        fn = ops.pop()
        self.emit("tensor", fn, self.done["tensor"], 1)

    def fence(self, engines=None):
        engines = tuple(engines or self.ENGINES)
        base = self.bar.n
        for eng in engines:
            self.emit(eng, lambda e: e.wait_ge(self.bar.h, 0), self.bar, 1)
        tgt = base + len(engines)
        for eng in engines:
            self.wait(eng, self.bar, tgt)

    def build(self):
        with self.nc.Block() as block:
            for eng in self.ENGINES:
                ops = self.ops[eng]
                if ops:
                    getattr(block, eng)(lambda e, ops=ops: [f(e) for f in ops])
        return self.nc


def bf16(a):
    return np.asarray(a).astype(ml_dtypes.bfloat16).astype(np.float32)


AG_GROUPS = [[0, 1, 2, 3], [4, 5, 6, 7]]


def emit_ag(P, pairs, wait_sems=()):
    cc = P.psem("cc")
    for sm in wait_sems:
        P.wait("gpsimd", sm)
    for in_ap, out_ap in pairs:
        P.emit("gpsimd", lambda e, i=in_ap, o=out_ap: e.collective_compute(
            "AllGather", ALU.bypass, replica_groups=AG_GROUPS,
            ins=[i.opt()], outs=[o.opt()]), cc)
    P.wait("gpsimd", cc)


class Common:
    def __init__(self, P, Tn=T, with_ln_xb=True):
        self.Tn = Tn
        self.ones128 = P.sb("c_ones128", [128, 1], BF16)
        self.ones1 = P.sb("c_ones1", [1, 128], BF16)
        self.ones1f = P.sb("c_ones1f", [1, 128], F32)
        self.onesP = P.sb("c_onesP", [128, 128], F32)
        self.onesP_bf = P.sb("c_onesPb", [128, 128], BF16)
        P.emit("vector", lambda e, t=self.ones128: e.memset(t[:], 1.0))
        P.emit("vector", lambda e, t=self.ones1: e.memset(t[:], 1.0))
        P.emit("vector", lambda e, t=self.ones1f: e.memset(t[:], 1.0))
        P.emit("vector", lambda e, t=self.onesP: e.memset(t[:], 1.0))
        P.emit("vector", lambda e, t=self.onesP_bf: e.memset(t[:], 1.0),
               P.done["vector"], 1)
        self._P = P
        self._cols = {}
        self.xsq = P.sb("ln_xsq", [128, 2 * Tn], BF16)
        self.m_row = P.sb("ln_m", [1, Tn], F32)
        self.w_row = P.sb("ln_w", [1, Tn], F32)
        self.rows_bf = P.sb("ln_rows_bf", [1, 2 * Tn], BF16)
        self.tmp = P.sb("ln_tmp", [128, Tn], F32)
        self.ln_xb = P.sb("ln_xb", [128, 8 * 512], BF16) if with_ln_xb else None
        self.col(LN_EPS)

    def col(self, val):
        if val not in self._cols:
            t = self._P.sb("c_col_%d" % len(self._cols), [128, 1], F32)
            self._P.emit("vector", lambda e, t=t, v=val: e.memset(t[:], v),
                         self._P.done["vector"], 1)
            self._cols[val] = t
        return self._cols[val][:, 0:1]


def emit_ln(P, com, x, Tn, mode, out=None, xold=None, xnew=None, tag="ln",
            waits=(), x_bf16=False, xsl=None, out_sl=None, xold_sl=None,
            xnew_sl=None):
    nc = P.nc
    NG = Tn // 512
    assert Tn <= com.Tn
    if xsl is None:
        xsl = [x[:, c * Tn:(c + 1) * Tn] for c in range(8)]
    if out_sl is None and out is not None:
        out_sl = [out[:, c * Tn:(c + 1) * Tn] for c in range(8)]
    if xold_sl is None and xold is not None:
        xold_sl = [xold[:, c * Tn:(c + 1) * Tn] for c in range(8)]
    if xnew_sl is None and xnew is not None:
        xnew_sl = [xnew[:, c * Tn:(c + 1) * Tn] for c in range(8)]
    xb = x if x_bf16 else (out if mode == "bf16" else com.ln_xb)
    if mode == "residual" and not x_bf16:
        assert Tn <= 512, "ln_xb scratch sized for Tn<=512"
    xsq, tmp = com.xsq, com.tmp
    m_row, w_row = com.m_row[:, 0:Tn], com.w_row[:, 0:Tn]
    rstd_bf = com.rows_bf[:, 0:Tn]
    mr_bf = com.rows_bf[:, Tn:2 * Tn]
    epsb = com.col(LN_EPS)[0:1, :]

    for eng, s, v in waits:
        P.wait(eng, s, v)

    if not x_bf16:
        xbsl = (out_sl if mode == "bf16" else
                [xb[:, c * Tn:(c + 1) * Tn] for c in range(8)])
        for c in range(8):
            P.emit("vector", lambda e, c=c: e.tensor_copy(xbsl[c], xsl[c]),
                   P.done["vector"] if c == 7 else None)

    with contextlib.ExitStack() as sp:
        ps_s = sp.enter_context(nc.psum_tensor(tag + "_ps_s", [1, Tn], F32))
        ps_q = sp.enter_context(nc.psum_tensor(tag + "_ps_q", [1, Tn], F32))
        P.wait("scalar", P.done["vector"])
        P.wait("tensor", P.done["vector"])
        pe_base = P.done["tensor"].n
        for c in range(8):
            P.wait("scalar", P.done["tensor"],
                   pe_base if c < 2 else pe_base + c - 1)
            xbc = (xsl[c] if x_bf16 else
                   (out_sl[c] if mode == "bf16"
                    else xb[:, c * Tn:(c + 1) * Tn]))
            P.emit("scalar", lambda e, c=c, xbc=xbc: e.activation(
                out=xsq[:, (c % 2) * Tn:(c % 2) * Tn + Tn],
                in_=xbc, func=AF.Square),
                P.done["scalar"])
            P.wait("tensor", P.done["scalar"])
            for g in range(NG):
                xbg = (xsl[c][:, g * 512:(g + 1) * 512] if x_bf16
                       else (out_sl[c][:, g * 512:(g + 1) * 512]
                             if mode == "bf16" else
                             xb[:, c * Tn + g * 512:c * Tn + (g + 1) * 512]))
                P.emit("tensor", lambda e, c=c, g=g, xbg=xbg: e.matmul(
                    ps_s[:, g * 512:(g + 1) * 512], com.ones128[:],
                    xbg, start=(c == 0), stop=(c == 7)))
                P.emit("tensor", lambda e, c=c, g=g: e.matmul(
                    ps_q[:, g * 512:(g + 1) * 512], com.ones128[:],
                    xsq[:, (c % 2) * Tn + g * 512:(c % 2) * Tn + (g + 1) * 512],
                    start=(c == 0), stop=(c == 7)),
                    P.done["tensor"] if g == NG - 1 else None)

        P.wait("scalar", P.done["tensor"])
        P.emit("scalar", lambda e: e.activation(out=m_row, in_=ps_s[:],
                                                func=AF.Copy, scale=1.0 / D),
               serial=True)
        P.emit("scalar", lambda e: e.activation(out=w_row, in_=m_row,
                                                func=AF.Square), P.done["scalar"],
               serial=True)
        P.wait("vector", P.done["scalar"])
        P.emit("vector", lambda e: e.scalar_tensor_tensor(
            w_row, ps_q[:], 1.0 / D, w_row, ALU.mult, ALU.subtract),
            P.done["vector"])
        P.wait("scalar", P.done["vector"])
        P.emit("scalar", lambda e: e.activation(out=w_row, in_=w_row,
                                                func=AF.Ln, bias=epsb),
               serial=True)
        P.emit("scalar", lambda e: e.activation(out=w_row, in_=w_row,
                                                func=AF.Exp, scale=-0.5),
               P.done["scalar"], serial=True)
        P.wait("vector", P.done["scalar"])
        P.emit("vector", lambda e: e.tensor_copy(rstd_bf, w_row), serial=True)
        P.emit("vector", lambda e: e.tensor_mul(mr_bf, m_row, w_row),
               P.done["vector"], serial=True)

    with contextlib.ExitStack() as sp:
        rb = sp.enter_context(nc.psum_tensor(tag + "_rb", [128, Tn], F32))
        mb = sp.enter_context(nc.psum_tensor(tag + "_mb", [128, Tn], F32))
        P.wait("tensor", P.done["vector"])
        for g in range(NG):
            P.emit("tensor", lambda e, g=g: e.matmul(
                rb[:, g * 512:(g + 1) * 512], com.ones1[:],
                rstd_bf[:, g * 512:(g + 1) * 512], start=True, stop=True))
            P.emit("tensor", lambda e, g=g: e.matmul(
                mb[:, g * 512:(g + 1) * 512], com.ones1[:],
                mr_bf[:, g * 512:(g + 1) * 512], start=True, stop=True),
                P.done["tensor"] if g == NG - 1 else None)

        P.wait("vector", P.done["tensor"])
        tp = tmp[:, 0:Tn]
        for c in range(8):
            P.emit("vector", lambda e, c=c: e.tensor_mul(
                tp, xsl[c], rb[:]), serial=True)
            if mode == "bf16":
                P.emit("vector", lambda e, c=c: e.tensor_sub(
                    out_sl[c], tp, mb[:]), serial=True)
            else:
                P.emit("vector", lambda e, c=c: e.tensor_sub(tp, tp, mb[:]),
                       serial=True)
                P.emit("vector", lambda e, c=c: e.tensor_add(
                    xnew_sl[c], xold_sl[c], tp), serial=True)


def emit_ln_big(P, com, x, Tfull, mode, out=None, xold=None, xnew=None,
                tag="lnb", waits=(), x_bf16=False):
    for g in range(Tfull // 512):
        def sl(t):
            return (None if t is None else
                    [t[:, c * Tfull + g * 512:c * Tfull + (g + 1) * 512]
                     for c in range(8)])
        emit_ln(P, com, None, 512, mode, out=out, xold=xold, xnew=xnew,
                tag="%s%d" % (tag, g), waits=(waits if g == 0 else ()),
                x_bf16=x_bf16, xsl=sl(x), out_sl=sl(out), xold_sl=sl(xold),
                xnew_sl=sl(xnew))


def load_wslab(P, name, w_dram, IN, OUT, sem):
    slab = P.sb(name, [128, (IN // 128) * OUT], BF16)
    for kc in range(IN // 128):
        P.dma("gpsimd", slab[:, kc * OUT:(kc + 1) * OUT],
              w_dram[kc * 128:(kc + 1) * 128, 0:OUT], sem)
    return slab


def load_wslab_sc(P, sc, name, w_dram, IN, OUT, sem, col0=0):
    slab = sc.enter_context(P.nc.sbuf_tensor(name, [128, (IN // 128) * OUT], BF16))
    for kc in range(IN // 128):
        P.dma("gpsimd", slab[:, kc * OUT:(kc + 1) * OUT],
              w_dram[kc * 128:(kc + 1) * 128, col0:col0 + OUT], sem)
    return slab


def emit_proj_fm(P, com, xb, slab, OUT, oc0, n_oc, Tn, evac, evac_eng,
                 nkc=8, tag="proj", extra_mm=None):
    nc = P.nc
    NG = Tn // 512
    with contextlib.ExitStack() as sp:
        ps = [sp.enter_context(nc.psum_tensor(tag + "_ps%d" % i, [128, 512], F32))
              for i in range(2)]
        idx = 0
        ev = P.done[evac_eng]
        ev_counts = []
        P.wait("tensor", P.done["vector"])
        for oc in range(oc0, oc0 + n_oc):
            for g in range(NG):
                b = idx % 2
                if idx >= 2:
                    P.wait("tensor", ev, ev_counts[idx - 2])
                for kc in range(nkc):
                    P.emit("tensor", lambda e, oc=oc, g=g, kc=kc, b=b: e.matmul(
                        ps[b][:], slab[:, kc * OUT + oc * 128:kc * OUT + (oc + 1) * 128],
                        xb[:, kc * Tn + g * 512:kc * Tn + (g + 1) * 512],
                        start=(kc == 0), stop=(kc == nkc - 1 and extra_mm is None)))
                if extra_mm is not None:
                    P.emit("tensor", lambda e, oc=oc, g=g, b=b: extra_mm(e, oc, g, ps[b]))
                P.mark_mm(tag)
                P.wait(evac_eng, P.done["tensor"])
                evac(oc, g, ps[b][:])
                ev_counts.append(ev.n)
                idx += 1
        P.wait("tensor", ev)


def emit_proj_tm(P, com, xb, slab, OUT, ob0, n_ob, Tn, evac, evac_eng,
                 nkc=8, tag="ptm", extra_mm=None):
    nc = P.nc
    with contextlib.ExitStack() as sp:
        ps = [sp.enter_context(nc.psum_tensor(tag + "_ps%d" % i, [128, 512], F32))
              for i in range(2)]
        idx = 0
        ev = P.done[evac_eng]
        ev_counts = []
        P.wait("tensor", P.done["vector"])
        for tt in range(Tn // 128):
            for ob in range(ob0, ob0 + n_ob):
                b = idx % 2
                if idx >= 2:
                    P.wait("tensor", ev, ev_counts[idx - 2])
                for kc in range(nkc):
                    P.emit("tensor", lambda e, tt=tt, ob=ob, kc=kc, b=b: e.matmul(
                        ps[b][:], xb[:, kc * Tn + tt * 128:kc * Tn + (tt + 1) * 128],
                        slab[:, kc * OUT + ob * 512:kc * OUT + (ob + 1) * 512],
                        start=(kc == 0),
                        stop=(kc == nkc - 1 and extra_mm is None)))
                if extra_mm is not None:
                    P.emit("tensor", lambda e, tt=tt, ob=ob, b=b: extra_mm(e, tt, ob, ps[b]))
                P.mark_mm(tag)
                P.wait(evac_eng, P.done["tensor"])
                evac(tt, ob, ps[b][:])
                ev_counts.append(ev.n)
                idx += 1
        P.wait("tensor", ev)


def emit_flash(P, com, *, Q_fm, o_fm, kv_loader, HDv, scale, tag):
    nc = P.nc
    real = HDv - 1
    n_kb = 64
    ngr = n_kb // 2
    with contextlib.ExitStack() as sc:
        K_sb = sc.enter_context(nc.sbuf_tensor(tag + "_K", [128, 2 * SEQ], BF16))
        V_sb = sc.enter_context(nc.sbuf_tensor(tag + "_V", [128, n_kb * 4 * HDv], BF16))
        pt = sc.enter_context(nc.sbuf_tensor(tag + "_pt", [128, 2 * 1024], BF16))
        rc = sc.enter_context(nc.sbuf_tensor(tag + "_rc", [128, 512], F32))
        rcb = sc.enter_context(nc.sbuf_tensor(tag + "_rcb", [128, 2 * 512], BF16))
        bcb = sc.enter_context(nc.sbuf_tensor(tag + "_bcb", [64, 512], BF16))
        ps_s = sc.enter_context(nc.psum_tensor(tag + "_s", [128, 2 * 1024], F32))
        ps_o = [sc.enter_context(nc.psum_tensor(tag + "_o%d" % i, [HDv, 512], F32))
                for i in range(2)]
        ps_bc = sc.enter_context(nc.psum_tensor(tag + "_bc", [real, 512], F32))
        ld = P.sem(tag + "_ld")
        oi = 0
        epi_v = {}
        for hg in range(4):
            if hg > 0:
                P.wait("gpsimd", P.done["tensor"])
            kv_loader(hg, K_sb, V_sb, ld)
            for eng in ("tensor", "scalar"):
                P.wait(eng, ld)
            for h4 in range(4):
                h = hg * 4 + h4
                kch, krow = h4 // 2, (h4 % 2) * 64
                qch, qrow = h // 2, (h % 2) * 64
                for qb in range(4):
                    b_o = oi % 2
                    if oi >= 2:
                        P.wait("tensor", P.done["vector"], epi_v[oi - 2])
                    s_cnt = [0] * ngr
                    e_cnt = [0] * ngr
                    a_cnt = [0] * ngr

                    def s_group(g, krow=krow, kch=kch, qrow=qrow, qch=qch, qb=qb):
                        bs = g % 2
                        if g >= 2:
                            P.wait("tensor", P.done["scalar"], e_cnt[g - 2])
                        for j in range(2):
                            kb = 2 * g + j
                            P.emit("tensor", lambda e, kb=kb, bs=bs, j=j, krow=krow,
                                   kch=kch, qrow=qrow, qch=qch, qb=qb: e.matmul(
                                ps_s[:, bs * 1024 + j * 512: bs * 1024 + (j + 1) * 512],
                                K_sb[krow:krow + 64,
                                     kch * SEQ + kb * 128: kch * SEQ + (kb + 1) * 128],
                                Q_fm[qrow:qrow + 64,
                                     qch * T + qb * 512: qch * T + (qb + 1) * 512],
                                start=True, stop=True))
                        P.mark_mm()
                        s_cnt[g] = P.done["tensor"].n

                    def av_group(g, h4=h4, b_o=b_o):
                        bs = g % 2
                        P.wait("tensor", P.done["scalar"], e_cnt[g])
                        for j in range(2):
                            kb = 2 * g + j
                            P.emit("tensor", lambda e, kb=kb, bs=bs, j=j, b_o=b_o,
                                   h4=h4: e.matmul(
                                ps_o[b_o][:],
                                V_sb[:, kb * 4 * HDv + h4 * HDv: kb * 4 * HDv + (h4 + 1) * HDv],
                                pt[:, bs * 1024 + j * 512: bs * 1024 + (j + 1) * 512],
                                start=(kb == 0), stop=(kb == n_kb - 1)))
                        P.mark_mm()
                        a_cnt[g] = P.done["tensor"].n

                    P.wait("tensor", P.done["scalar"])
                    for g in range(ngr):
                        s_group(g)
                        if g >= 1:
                            av_group(g - 1)
                        P.wait("scalar", P.done["tensor"], s_cnt[g])
                        if g >= 2:
                            P.wait("scalar", P.done["tensor"], a_cnt[g - 2])
                        for j in range(2):
                            P.emit("scalar", lambda e, bs=g % 2, j=j: e.activation(
                                out=pt[:, bs * 1024 + j * 512:bs * 1024 + (j + 1) * 512],
                                in_=ps_s[:, bs * 1024 + j * 512:bs * 1024 + (j + 1) * 512],
                                func=AF.Exp, scale=scale),
                                P.done["scalar"] if j == 1 else None)
                        e_cnt[g] = P.done["scalar"].n
                    av_group(ngr - 1)

                    P.wait("vector", P.done["tensor"], a_cnt[ngr - 1])
                    P.emit("vector", lambda e, b_o=b_o: e.reciprocal(
                        rc[real:real + 1, :], ps_o[b_o][real:real + 1, :]),
                        serial=True)
                    P.emit("vector", lambda e: e.tensor_copy(
                        rcb[real:real + 1, 0:512], rc[real:real + 1, :]),
                        serial=True)
                    P.emit("vector", lambda e: e.tensor_sub(
                        rc[real:real + 1, :], rc[real:real + 1, :],
                        rcb[real:real + 1, 0:512]), serial=True)
                    P.emit("vector", lambda e: e.tensor_copy(
                        rcb[real:real + 1, 512:1024], rc[real:real + 1, :]),
                        P.done["vector"], serial=True)
                    P.wait("tensor", P.done["vector"])
                    for j in range(2):
                        P.emit("tensor", lambda e, j=j: e.matmul(
                            ps_bc[:], com.onesP_bf[real:real + 1, 0:real],
                            rcb[real:real + 1, j * 512:(j + 1) * 512],
                            start=(j == 0), stop=(j == 1)))
                    P.mark_mm()
                    P.wait("vector", P.done["tensor"])
                    P.emit("vector", lambda e: e.tensor_copy(bcb[0:real, :], ps_bc[:]),
                           serial=True)
                    P.emit("vector", lambda e, b_o=b_o, qrow=qrow, qch=qch,
                           qb=qb: e.tensor_mul(
                        o_fm[qrow:qrow + real,
                             qch * T + qb * 512: qch * T + (qb + 1) * 512],
                        ps_o[b_o][0:real, :], bcb[0:real, :]),
                        P.done["vector"], serial=True)
                    epi_v[oi] = P.done["vector"].n
                    oi += 1
        P.wait("gpsimd", P.done["vector"])



def emit_l0a_slice(P, com, slab, xsrc, t0g, kts, vs, Q_fm, q_tloc, st, tag,
                   OUT=3072, HDv=65, nvh=2, qn_oc=8, kn_oc=8, do_kv=True,
                   qbias=None, kbias=None, vb_row=None):
    nc = P.nc
    Tn = 1024
    with contextlib.ExitStack() as sc:
        xh = sc.enter_context(nc.sbuf_tensor(tag + "_x", [128, 8 * Tn], BF16))
        xln = sc.enter_context(nc.sbuf_tensor(tag + "_xln", [128, 8 * Tn], BF16))
        ktmp = sc.enter_context(nc.sbuf_tensor(tag + "_kt", [128, 2 * 512], BF16))
        vt = [sc.enter_context(nc.sbuf_tensor(tag + "_vt%d" % i, [128, 16 * HDv], BF16))
              for i in range(2)]
        ld = P.psem("l0a_ld")
        kst = [P.psem("l0a_kst0"), P.psem("l0a_kst1")]
        vst = [P.psem("l0a_vst0"), P.psem("l0a_vst1")]
        P.wait("gpsimd", P.done["vector"])
        for c in range(8):
            P.dma("gpsimd", xh[:, c * Tn:(c + 1) * Tn],
                  xsrc[c * 128:(c + 1) * 128, t0g:t0g + Tn], ld)
        for sm in kst + vst:
            P.wait("vector", sm, sm.n)
        for i in range(2):
            P.emit("vector", lambda e, i=i: e.memset(vt[i][:], 1.0),
                   P.done["vector"], serial=True)
        w = [(e, ld, ld.n) for e in ("vector", "scalar", "tensor")]
        emit_ln_big(P, com, xh[:], Tn, "bf16", out=xln, tag=tag + "_ln",
                    waits=w, x_bf16=True)

        kd = {"i": 0}

        def kevac(oc, g, ps):
            i = kd["i"]
            b = i % 2
            keng = "vector"
            P.wait(keng, kst[b], kst[b].n)
            if kbias is None:
                P.emit("vector", lambda e, b=b: e.tensor_copy(
                    ktmp[:, b * 512:(b + 1) * 512], ps), P.done["vector"],
                    serial=True)
            else:
                P.emit("vector", lambda e, b=b, oc=oc: e.tensor_scalar(
                    ktmp[:, b * 512:(b + 1) * 512], ps, kbias[oc - qn_oc],
                    None, ALU.add), P.done["vector"], serial=True)
            P.wait("gpsimd", P.done[keng])
            P.dma("gpsimd", kts[(oc - qn_oc) * 128:(oc - qn_oc + 1) * 128,
                                t0g + g * 512:t0g + (g + 1) * 512],
                  ktmp[:, b * 512:(b + 1) * 512], kst[b])
            kd["i"] += 1

        if do_kv:
            emit_proj_fm(P, com, xln[:], slab, OUT, qn_oc, kn_oc, Tn, kevac,
                         "vector", tag=tag + "_kp")

        vb0 = (qn_oc + kn_oc) * 128 // 512
        vd = {"i": 0}
        nvt = Tn // 128

        def vevac(tt, ob, ps):
            i = vd["i"]
            bt = tt % 2
            half = ob - vb0
            if half == 0:
                P.wait("vector", vst[bt], vst[bt].n)
            nh = 512 // (HDv - 1)
            P.emit("vector", lambda e, bt=bt, half=half, nh=nh: e.tensor_copy(
                vt[bt][:].rearrange("p (h w) -> p h w", h=16)
                [:, half * nh:(half + 1) * nh, 0:HDv - 1],
                ps.rearrange("p (h c) -> p h c", h=nh)), P.done["vector"],
                serial=True)
            if half == nvh - 1:
                P.wait("gpsimd", P.done["vector"])
                P.dma("gpsimd", vs[t0g + tt * 128:t0g + (tt + 1) * 128, :],
                      vt[bt][:], vst[bt])
            vd["i"] += 1

        if do_kv:
            vxm = None
            if vb_row is not None:
                def vxm(e, tt, ob, ps, vb_row=vb_row, vb0=vb0):
                    return e.matmul(ps[:], com.ones1[:],
                                    vb_row[:, (ob - vb0) * 512:(ob - vb0 + 1) * 512],
                                    start=False, stop=True)
            emit_proj_tm(P, com, xln[:], slab, OUT, vb0, nvh, Tn, vevac,
                         "vector", tag=tag + "_vp", extra_mm=vxm)

        if q_tloc is not None:
            def qevac(oc, g, ps):
                if qbias is None:
                    P.emit("vector", lambda e, oc=oc, g=g: e.tensor_copy(
                        Q_fm[:, oc * T + q_tloc + g * 512:
                             oc * T + q_tloc + (g + 1) * 512],
                        ps), P.done["vector"], serial=True)
                else:
                    P.emit("vector", lambda e, oc=oc, g=g: e.tensor_scalar(
                        Q_fm[:, oc * T + q_tloc + g * 512:
                             oc * T + q_tloc + (g + 1) * 512],
                        ps, qbias[oc], None, ALU.add),
                        P.done["vector"], serial=True)

            emit_proj_fm(P, com, xln[:], slab, OUT, 0, qn_oc, Tn, qevac,
                         "vector", tag=tag + "_qp")
        for sm in kst + vst:
            P.wait("gpsimd", sm)
        st.append((kst, vst))


def l0a_kv_loader(P, kts, vs, HDv=65):
    W = 4 * HDv

    def loader(hg, K_sb, V_sb, sem):
        for cc in range(2):
            P.dma("gpsimd", K_sb[:, cc * SEQ:(cc + 1) * SEQ],
                  kts[hg * 256 + cc * 128:hg * 256 + (cc + 1) * 128, :], sem)
        vsr = vs.rearrange("(kb p) w -> p kb w", p=128)
        vdr = V_sb[:].rearrange("p (kb w) -> p kb w", w=W)
        for i in range(8):
            P.dma("gpsimd", vdr[:, i * 8:(i + 1) * 8, :],
                  vsr[:, i * 8:(i + 1) * 8, hg * W:(hg + 1) * W], sem)
    return loader


def emit_l0a(P, com, A):
    nc = P.nc
    st = []
    with contextlib.ExitStack() as sa:
        wsem = P.psem("l0a_w")
        slab = load_wslab_sc(P, sa, A.tag + "wqkv_sb", A.wqkv, D, 3072, wsem)
        P.wait("tensor", wsem)
        for si, (xsrc, t0g) in enumerate(A.kv_slices):
            emit_l0a_slice(P, com, slab, xsrc, t0g, A.kts, A.vs, None, None,
                           st, "%sA_%d" % (A.tag, si))
        for si, (xsrc, t0s, t0d) in enumerate(A.q_slices):
            emit_l0a_slice(P, com, slab, xsrc, t0s, A.kts, A.vs, A.Q_fm,
                           t0d, st, "%sQ_%d" % (A.tag, si), do_kv=False)
    P.fence()
    if A.ag_groups is not None:
        cc = P.psem("cc")
        P.wait("gpsimd", P.psem("l0a_kst0"))
        P.wait("gpsimd", P.psem("l0a_kst1"))
        P.wait("gpsimd", P.psem("l0a_vst0"))
        P.wait("gpsimd", P.psem("l0a_vst1"))
        P.emit("gpsimd", lambda e: e.collective_compute(
            "AllGather", ALU.bypass, replica_groups=A.ag_groups,
            ins=[A.kts.opt()], outs=[A.ktsg.opt()]), cc)
        P.emit("gpsimd", lambda e: e.collective_compute(
            "AllGather", ALU.bypass, replica_groups=A.ag_groups,
            ins=[A.vs.opt()], outs=[A.vsg.opt()]), cc)
        P.wait("gpsimd", cc)
    if A.stage == "A":
        return
    emit_flash(P, com, Q_fm=A.Q_fm, o_fm=A.o_fm,
               kv_loader=A.kv_loader, HDv=65, scale=0.125, tag=A.tag + "F")
    P.fence()
    with contextlib.ExitStack() as sc:
        xo = sc.enter_context(nc.sbuf_tensor(A.tag + "xo", [128, 8 * T], F32))
        ld = P.psem("c_ld")
        sto = P.psem("c_st")
        P.wait("sync", P.done["vector"])
        for c in range(8):
            P.dma("sync", xo[:, c * T:(c + 1) * T],
                  A.x_resid[c * 128:(c + 1) * 128, :], ld)
        w = [(e, ld, ld.n) for e in ("vector", "scalar", "tensor")]
        emit_ln_big(P, com, A.o_fm[:], T, "residual", xold=xo, xnew=xo,
                    tag=A.tag + "postln", waits=w, x_bf16=True)
        P.wait("sync", P.done["vector"])
        for c in range(8):
            P.dma("sync", A.x_out[c * 128:(c + 1) * 128, :],
                  xo[:, c * T:(c + 1) * T], sto)
        P.wait("sync", sto)


def build_l0a_ckpt(stage="full"):
    import types
    nc = bass.Bass("TRN2", target_bir_lowering=False, debug=False)
    xfull = nc.dram_tensor("xfull", [D, SEQ], F32, kind="ExternalInput").ap()
    xown = nc.dram_tensor("xown", [D, T], F32, kind="ExternalInput").ap()
    wqkv = nc.dram_tensor("wqkv", [D, 3072], F32, kind="ExternalInput").ap()
    kvkind = "ExternalOutput" if stage == "A" else "Internal"
    x1 = nc.dram_tensor("x1", [D, T], F32, kind="ExternalOutput").ap()
    kts = nc.dram_tensor("kts", [D, SEQ], BF16, kind=kvkind).ap()
    vs = nc.dram_tensor("vs", [SEQ, 16 * 65], BF16, kind=kvkind).ap()
    P = Prog(nc)
    com = Common(P, Tn=512, with_ln_xb=False)
    A = types.SimpleNamespace(
        tag="l0a", wqkv=wqkv, kts=kts, vs=vs, ktsg=None, vsg=None,
        ag_groups=None, stage=stage,
        kv_slices=[(xfull, s * T + h * 1024) for s in range(4)
                   for h in range(2)],
        q_slices=[(xown, h * 1024, h * 1024) for h in range(2)],
        Q_fm=P.sb("Q_fm", [128, 8 * T], BF16),
        o_fm=P.sb("o_fm", [128, 8 * T], BF16),
        kv_loader=l0a_kv_loader(P, kts, vs),
        x_resid=xown, x_out=x1)
    emit_l0a(P, com, A)
    P.build()
    return nc


def build_l0b_ckpt(ctx=None):
    if ctx is None:
        nc = bass.Bass("TRN2", target_bir_lowering=False, debug=False)
        xin = nc.dram_tensor("xin", [D, T], F32, kind="ExternalInput").ap()
        xfull1 = nc.dram_tensor("xfull1", [D, SEQ], F32, kind="ExternalInput").ap()
        wqkvb = nc.dram_tensor("wqkvb", [D, 2560], F32, kind="ExternalInput").ap()
        wob = nc.dram_tensor("wob", [D, D], F32, kind="ExternalInput").ap()
        wrt = nc.dram_tensor("wrt", [D, 8], F32, kind="ExternalInput").ap()
        w1 = nc.dram_tensor("w1", [8, D, 2048], F32, kind="ExternalInput").ap()
        w2 = nc.dram_tensor("w2", [8, 2048, D], F32, kind="ExternalInput").ap()
        bpack = nc.dram_tensor("bpack", [128, 216], F32, kind="ExternalInput").ap()
        vbr = nc.dram_tensor("vbr", [1, 512], F32, kind="ExternalInput").ap()
        rbr = nc.dram_tensor("rbr", [1, 8], F32, kind="ExternalInput").ap()
        ident = nc.dram_tensor("ident", [128, 128], F32, kind="ExternalInput").ap()
        x2 = nc.dram_tensor("x2", [D, T], F32, kind="ExternalOutput").ap()
        kts = nc.dram_tensor("kts2", [D, SEQ], BF16).ap()
        vs = nc.dram_tensor("vs2", [SEQ, 16 * 33], BF16).ap()
        P = Prog(nc)
        com = Common(P, Tn=512, with_ln_xb=False)
        com.ln_xb = P.sb("lnxb512", [128, 8 * 512], BF16)
        sc0 = P.stack
        kv_slices = [(xfull1, s_ * T + h_ * 1024) for s_ in range(4)
                     for h_ in range(2)]
        kvld = l0a_kv_loader(P, kts, vs, HDv=33)
        ag = None
    else:
        nc, P, com, sc0 = ctx["nc"], ctx["P"], ctx["com"], ctx["sc"]
        io = ctx["io"]
        (xin, wqkvb, wob, wrt, w1, w2, bpack, vbr, rbr, ident, x2, kts,
         vs) = (io["xin"], io["wqkvb"], io["wob"], io["wrt"], io["w1"],
                io["w2"], io["bpack"], io["vbr"], io["rbr"], io["ident"],
                io["x2"], io["kts"], io["vs"])
        kv_slices = ctx["kv_slices"]
        kvld = ctx["kv_loader"]
        ag = ctx.get("ag")
    sb0 = lambda n_, sh_, dt_: sc0.enter_context(nc.sbuf_tensor(n_, sh_, dt_))
    mla = sb0("l0b_mla", [128, 8 * T], BF16)
    bp = sb0("l0b_bp", [128, 216], F32)
    vb_row = sb0("vb_row", [1, 512], BF16)
    rb_row = sb0("rb_row", [1, 8], BF16)
    idt = sb0("l0b_idt", [128, 128], BF16)
    eye8 = sb0("eye8", [8, 8], BF16)
    ld0 = P.psem("ld0")
    P.dma("gpsimd", bp[:], bpack[:, :], ld0)
    P.dma("gpsimd", vb_row[:], vbr[:, :], ld0)
    P.dma("gpsimd", rb_row[:], rbr[:, :], ld0)
    P.dma("gpsimd", idt[:], ident[:, :], ld0)
    P.dma("gpsimd", eye8[:], ident[0:8, 0:8], ld0)
    qbias = [bp[:, i:i + 1] for i in range(8)]
    kbias = [bp[:, 8 + i:8 + i + 1] for i in range(8)]
    obias = [bp[:, 16 + i:16 + i + 1] for i in range(8)]
    b1col = lambda e_, hc: bp[:, 24 + e_ * 16 + hc:24 + e_ * 16 + hc + 1]
    b2col = lambda e_, oc: bp[:, 152 + e_ * 8 + oc:152 + e_ * 8 + oc + 1]
    for eng in ("scalar", "tensor", "vector", "gpsimd"):
        P.wait(eng, ld0)
    st = []

    with contextlib.ExitStack() as sa:
        wsem = P.psem("l0b_w")
        slab = load_wslab_sc(P, sa, "wqkvb_sb", wqkvb, D, 2560, wsem)
        P.wait("tensor", wsem)
        for si, (xsrc_, t0g) in enumerate(kv_slices):
            emit_l0a_slice(P, com, slab, xsrc_, t0g, kts, vs, None, None,
                           st, "l0bA_%d" % si, OUT=2560, HDv=33,
                           nvh=1, qn_oc=8, kn_oc=8, kbias=kbias,
                           vb_row=vb_row)
    P.fence()
    if ag is not None:
        emit_ag(P, ag, wait_sems=[P.psem(nm) for nm in
                                  ("l0a_kst0", "l0a_kst1", "l0a_vst0",
                                   "l0a_vst1")])
    with contextlib.ExitStack() as sb:
        Q_fm = sb.enter_context(nc.sbuf_tensor("Q_fm", [128, 8 * T], BF16))
        o_fm = sb.enter_context(nc.sbuf_tensor("o_fm", [128, 8 * T], BF16))
        P.emit("vector", lambda e: e.memset(o_fm[:], 0.0), P.done["vector"])
        with contextlib.ExitStack() as sq:
            wsem = P.psem("l0b_wq")
            qslab = load_wslab_sc(P, sq, "wq_sb", wqkvb, D, 1024, wsem, col0=0)
            P.wait("tensor", wsem)
            for half in range(2):
                emit_l0a_slice(P, com, qslab, xin, half * 1024, kts, vs, Q_fm,
                               half * 1024, st, "l0bQ_%d" % half, OUT=1024,
                               HDv=33, nvh=1, do_kv=False, qbias=qbias)
        P.fence()
        emit_flash(P, com, Q_fm=Q_fm, o_fm=o_fm,
                   kv_loader=kvld,
                   HDv=33, scale=1.0 / math.sqrt(32.0), tag="l0bF")
        P.fence()

        with contextlib.ExitStack() as sc:
            wsem = P.psem("l0b_wo")
            woslab = load_wslab_sc(P, sc, "wo_sb", wob, D, D, wsem)
            P.wait("tensor", wsem)

            def oevac(oc, g, ps):
                P.emit("vector", lambda e, oc=oc, g=g: e.tensor_scalar(
                    mla[:, oc * T + g * 512:oc * T + (g + 1) * 512],
                    ps, obias[oc], None, ALU.add),
                    P.done["vector"], serial=True)

            emit_proj_fm(P, com, o_fm[:], woslab, D, 0, 8, T, oevac, "vector",
                         tag="l0b_wo")
        P.fence()

    cb_sb = sb0("cb_sb", [8, T], BF16)
    with contextlib.ExitStack() as sc:
        wsem = P.psem("l0b_wr")
        rslab = load_wslab_sc(P, sc, "wr_sb", wrt, D, 8, wsem)
        lg = sc.enter_context(nc.sbuf_tensor("rt_lg", [128, 8], F32))
        sc1 = sc.enter_context(nc.sbuf_tensor("rt_s", [128, 8], F32))
        cols = sc.enter_context(nc.sbuf_tensor("rt_c", [128, 8], F32))
        cbt = sc.enter_context(nc.sbuf_tensor("rt_cb", [128, 8], BF16))
        rbb = sc.enter_context(nc.sbuf_tensor("rt_rbb", [128, 8], F32))
        ps_l = sc.enter_context(nc.psum_tensor("rt_ps", [128, 8], F32))
        ps_t = sc.enter_context(nc.psum_tensor("rt_pt", [8, 128], BF16))
        P.wait("tensor", wsem)
        P.wait("tensor", P.done["vector"])
        P.emit("tensor", lambda e: e.matmul(ps_l[:], com.ones1[:], rb_row[:],
                                            start=True, stop=True))
        P.mark_mm()
        P.wait("vector", P.done["tensor"])
        P.emit("vector", lambda e: e.tensor_copy(rbb[:], ps_l[:]),
               P.done["vector"], serial=True)
        for tt in range(T // 128):
            P.wait("tensor", P.done["vector"])
            for kc in range(8):
                P.emit("tensor", lambda e, tt=tt, kc=kc: e.matmul(
                    ps_l[:], mla[:, kc * T + tt * 128:kc * T + (tt + 1) * 128],
                    rslab[:, kc * 8:(kc + 1) * 8],
                    start=(kc == 0), stop=(kc == 7)))
            P.mark_mm()
            P.wait("vector", P.done["tensor"])
            V = "vector"
            P.emit(V, lambda e: e.tensor_add(lg[:], ps_l[:], rbb[:]), serial=True)
            P.emit(V, lambda e: e.reduce_max(sc1[:, 0:1], lg[:], AX.X), serial=True)
            P.emit(V, lambda e: e.tensor_scalar(lg[:], lg[:], sc1[:, 0:1], None,
                                                ALU.subtract), serial=True)
            P.wait("scalar", P.done["vector"])
            P.emit("scalar", lambda e: e.activation(out=lg[:], in_=lg[:],
                                                    func=AF.Exp),
                   P.done["scalar"], serial=True)
            P.wait(V, P.done["scalar"])
            P.emit(V, lambda e: e.reduce_sum(sc1[:, 1:2], lg[:], AX.X), serial=True)
            P.emit(V, lambda e: e.reciprocal(sc1[:, 2:3], sc1[:, 1:2]), serial=True)
            P.emit(V, lambda e: e.tensor_scalar(lg[:], lg[:], sc1[:, 2:3], None,
                                                ALU.mult), serial=True)
            P.emit(V, lambda e: e.reduce_max(sc1[:, 3:4], lg[:], AX.X), serial=True)
            P.emit(V, lambda e: e.tensor_scalar(cols[:], lg[:], sc1[:, 3:4], None,
                                                ALU.is_ge), serial=True)
            P.emit(V, lambda e: e.scalar_tensor_tensor(
                cols[:], cols[:], -1e30, lg[:], ALU.mult, ALU.add), serial=True)
            P.emit(V, lambda e: e.reduce_max(sc1[:, 4:5], cols[:], AX.X),
                   serial=True)
            P.emit(V, lambda e: e.tensor_scalar(cols[:], lg[:], sc1[:, 4:5], None,
                                                ALU.is_ge), serial=True)
            P.emit(V, lambda e: e.tensor_add(sc1[:, 5:6], sc1[:, 3:4],
                                             sc1[:, 4:5]), serial=True)
            P.emit(V, lambda e: e.tensor_scalar(sc1[:, 6:7], sc1[:, 5:6], 1e-8,
                                                None, ALU.add), serial=True)
            P.emit(V, lambda e: e.reciprocal(sc1[:, 7:8], sc1[:, 6:7]), serial=True)
            P.emit(V, lambda e: e.tensor_mul(cols[:], cols[:], lg[:]), serial=True)
            P.emit(V, lambda e: e.tensor_scalar(cbt[:], cols[:], sc1[:, 7:8], None,
                                                ALU.mult), P.done["vector"],
                   serial=True)
            P.wait("tensor", P.done["vector"])
            P.emit("tensor", lambda e: e.transpose(ps_t[:], cbt[:], idt[:]))
            P.mark_mm()
            P.wait("vector", P.done["tensor"])
            P.emit(V, lambda e, tt=tt: e.tensor_copy(
                cb_sb[:, tt * 128:(tt + 1) * 128], ps_t[:]), P.done["vector"],
                serial=True)
    P.fence()

    with contextlib.ExitStack() as sc:
        w1s = sc.enter_context(nc.sbuf_tensor("w1s", [128, 8 * 2048], BF16))
        w2s = sc.enter_context(nc.sbuf_tensor("w2s", [128, 16 * 1024], BF16))
        h1s = sc.enter_context(nc.sbuf_tensor("h1s", [128, 16 * 512], BF16))
        x1tb = sc.enter_context(nc.sbuf_tensor("x1tb", [128, 8 * 512], F32))
        x1ln = com.ln_xb
        ytb = sc.enter_context(nc.sbuf_tensor("ytb", [128, 8 * 512], F32))
        h2s = sc.enter_context(nc.sbuf_tensor("h2s", [128, 512], BF16))
        rowb = sc.enter_context(nc.sbuf_tensor("rowb", [1, 512], BF16))
        cbb = sc.enter_context(nc.sbuf_tensor("cbb", [128, 512], BF16))
        ps_r = sc.enter_context(nc.psum_tensor("ps_r", [1, 512], F32))
        ps_b = sc.enter_context(nc.psum_tensor("ps_b", [128, 512], F32))
        ldx = P.psem("ldx")
        ldw = P.psem("ldw")
        stx = P.psem("stx")
        for tb in range(4):
            t0 = tb * 512
            P.wait("sync", P.done["vector"])
            P.wait("sync", stx)
            for c in range(8):
                P.dma("sync", x1tb[:, c * 512:(c + 1) * 512],
                      xin[c * 128:(c + 1) * 128, t0:t0 + 512], ldx)
            w = [(e, ldx, ldx.n) for e in ("vector", "scalar", "tensor")]
            emit_ln(P, com, x1tb[:], 512, "bf16", out=x1ln,
                    tag="moeln%d" % tb, waits=w)
            P.wait("vector", P.done["vector"])
            P.emit("vector", lambda e: e.tensor_copy(ytb[:], x1ln[:]),
                   P.done["vector"], serial=True)
            for ex in range(8):
                P.wait("gpsimd", P.done["tensor"])
                for kc in range(8):
                    P.dma("gpsimd", w1s[:, kc * 2048:(kc + 1) * 2048],
                          w1[ex, kc * 128:(kc + 1) * 128, :], ldw)
                for kc in range(16):
                    P.dma("gpsimd", w2s[:, kc * 1024:(kc + 1) * 1024],
                          w2[ex, kc * 128:(kc + 1) * 128, :], ldw)
                P.wait("tensor", P.done["vector"])
                P.emit("tensor", lambda e, ex=ex, t0=t0: e.matmul(
                    ps_r[:], eye8[:, ex:ex + 1], cb_sb[:, t0:t0 + 512],
                    start=True, stop=True))
                P.mark_mm()
                P.wait("vector", P.done["tensor"])
                P.emit("vector", lambda e: e.tensor_copy(rowb[:], ps_r[:]),
                       P.done["vector"], serial=True)
                P.wait("tensor", P.done["vector"])
                P.emit("tensor", lambda e: e.matmul(
                    ps_b[:], com.ones1[:], rowb[:], start=True, stop=True))
                P.mark_mm()
                P.wait("vector", P.done["tensor"])
                P.emit("vector", lambda e: e.tensor_copy(cbb[:], ps_b[:]),
                       P.done["vector"], serial=True)
                P.wait("tensor", ldw)

                def h1evac(hc, g, ps, ex=ex):
                    P.emit("scalar", lambda e, hc=hc: e.activation(
                        out=h1s[:, hc * 512:(hc + 1) * 512], in_=ps,
                        func=AF.Gelu, bias=b1col(ex, hc)),
                        P.done["scalar"], serial=True)

                with contextlib.ExitStack() as sh:
                    psh = [sh.enter_context(nc.psum_tensor(
                        "h1ps%d_%d_%d" % (tb, ex, i), [128, 512], F32))
                        for i in range(2)]
                    evc = []
                    for hc in range(16):
                        b = hc % 2
                        if hc >= 2:
                            P.wait("tensor", P.done["scalar"], evc[hc - 2])
                        for kc in range(8):
                            P.emit("tensor", lambda e, hc=hc, kc=kc, b=b, t0=t0: e.matmul(
                                psh[b][:],
                                w1s[:, kc * 2048 + hc * 128:kc * 2048 + (hc + 1) * 128],
                                mla[:, kc * T + t0:kc * T + t0 + 512],
                                start=(kc == 0), stop=(kc == 7)))
                        P.mark_mm()
                        P.wait("scalar", P.done["tensor"])
                        h1evac(hc, 0, psh[b][:])
                        evc.append(P.done["scalar"].n)
                    P.wait("tensor", P.done["scalar"])
                with contextlib.ExitStack() as sh:
                    psh = [sh.enter_context(nc.psum_tensor(
                        "h2ps%d_%d_%d" % (tb, ex, i), [128, 512], F32))
                        for i in range(2)]
                    evc = []
                    for oc in range(8):
                        b = oc % 2
                        if oc >= 2:
                            P.wait("tensor", P.done["vector"], evc[oc - 2])
                        for kc in range(16):
                            P.emit("tensor", lambda e, oc=oc, kc=kc, b=b: e.matmul(
                                psh[b][:],
                                w2s[:, kc * 1024 + oc * 128:kc * 1024 + (oc + 1) * 128],
                                h1s[:, kc * 512:(kc + 1) * 512],
                                start=(kc == 0), stop=(kc == 15)))
                        P.mark_mm()
                        P.wait("scalar", P.done["tensor"])
                        P.wait("vector", P.done["tensor"])
                        P.emit("vector", lambda e, oc=oc, ex=ex, b=b: e.tensor_scalar(
                            h2s[:], psh[b][:], b2col(ex, oc), None, ALU.add),
                            serial=True)
                        P.emit("vector", lambda e: e.tensor_mul(
                            h2s[:], h2s[:], cbb[:]), serial=True)
                        P.emit("vector", lambda e, oc=oc: e.scalar_tensor_tensor(
                            ytb[:, oc * 512:(oc + 1) * 512], h2s[:], 0.3,
                            ytb[:, oc * 512:(oc + 1) * 512], ALU.mult, ALU.add),
                            P.done["vector"], serial=True)
                        evc.append(P.done["vector"].n)
                    P.wait("tensor", P.done["vector"])
            emit_ln(P, com, ytb[:], 512, "residual", xold=x1tb, xnew=x1tb,
                    tag="fln%d" % tb)
            P.wait("sync", P.done["vector"])
            for c in range(8):
                P.dma("sync", x2[c * 128:(c + 1) * 128, t0:t0 + 512],
                      x1tb[:, c * 512:(c + 1) * 512], stx)
        P.wait("sync", stx)
    if ctx is None:
        P.build()
        return nc
    return None


def build_l1a(mode, ctx=None):
    if ctx is None:
        nc = bass.Bass("TRN2", target_bir_lowering=False, debug=False)
        xin = nc.dram_tensor("xin", [D, T], F32, kind="ExternalInput").ap()
        wq = nc.dram_tensor("wq", [D, D], F32, kind="ExternalInput").ap()
        wk = nc.dram_tensor("wk", [D, D], F32, kind="ExternalInput").ap()
        wv = nc.dram_tensor("wv", [D, D], F32, kind="ExternalInput").ap()
        cfm = nc.dram_tensor("cfm", [D, T], F32, kind="ExternalInput").ap()
        sfm = nc.dram_tensor("sfm", [D, T], F32, kind="ExternalInput").ap()
        ctm = nc.dram_tensor("ctm", [16, T, 64], F32, kind="ExternalInput").ap()
        stm = nc.dram_tensor("stm", [16, T, 64], F32, kind="ExternalInput").ap()
        utri = nc.dram_tensor("utri", [128, 128], F32, kind="ExternalInput").ap()
        if mode == "full":
            wg1 = nc.dram_tensor("wg1", [D, 64], F32, kind="ExternalInput").ap()
            wg2 = nc.dram_tensor("wg2", [64, D], F32, kind="ExternalInput").ap()
            wo = nc.dram_tensor("wo", [D, D], F32, kind="ExternalInput").ap()
            bpk = nc.dram_tensor("bpk", [128, 8], F32, kind="ExternalInput").ap()
            sig_p = nc.dram_tensor("sig_p", [4, 1024], F32, kind="ExternalInput").ap()
            S_p = nc.dram_tensor("S_p", [4, 16 * 128, 64], F32, kind="ExternalInput").ap()
            wselp = nc.dram_tensor("wselp", [128, 4], F32, kind="ExternalInput").ap()
            hsel = nc.dram_tensor("hsel", [128, 8 * 16], F32, kind="ExternalInput").ap()
            ident = nc.dram_tensor("ident", [128, 128], F32, kind="ExternalInput").ap()
            x3 = nc.dram_tensor("x3", [D, T], F32, kind="ExternalOutput").ap()
        else:
            sig_o = nc.dram_tensor("sig_o", [1, 1024], F32, kind="ExternalOutput").ap()
            S_o = nc.dram_tensor("S_o", [16 * 128, 64], F32, kind="ExternalOutput").ap()
        P = Prog(nc)
        com = Common(P, Tn=512, with_ln_xb=False)
        sc0 = P.stack
    else:
        nc, P, com, sc0 = ctx["nc"], ctx["P"], ctx["com"], ctx["sc"]
        io = ctx["io"]
        xin, wq, wk, wv, cfm, sfm, ctm, stm, utri = (
            io["xin"], io["wq"], io["wk"], io["wv"], io["cfm"], io["sfm"],
            io["ctm"], io["stm"], io["utri"])
        if mode == "full":
            (wg1, wg2, wo, bpk, sig_p, S_p, wselp, hsel, ident, x3) = (
                io["wg1"], io["wg2"], io["wo"], io["bpk"], io["sig_p"],
                io["S_p"], io["wselp"], io["hsel"], io["ident"], io["x3"])
        else:
            sig_o, S_o = io["sig_o"], io["S_o"]
    com.col(1e-6)
    sb0 = lambda n_, sh_, dt_: sc0.enter_context(nc.sbuf_tensor(n_, sh_, dt_))
    tg = "l1a" + mode[0]
    xln = sb0(tg + "xln", [128, 8 * T], BF16)
    ut = sb0(tg + "ut", [128, 128], BF16)
    carry = sb0(tg + "carry", [1, 16 * 64], F32)
    ld = P.psem("ld")
    P.dma("gpsimd", ut[:], utri[:, :], ld)
    if mode == "full":
        o_fm = sb0("o_l1", [128, 8 * T], BF16)
        hs = sb0("hsel_sb", [128, 8 * 16], BF16)
        idt = sb0("l1a_idt", [128, 128], BF16)
        bp = sb0("bp_l1", [128, 8], F32)
        sigc = sb0("sigc", [1, 1024], F32)
        scb_p = sb0("scb_p", [128, 16 * 64], BF16)
        P.dma("gpsimd", hs[:], hsel[:, :], ld)
        P.dma("gpsimd", idt[:], ident[:, :], ld)
        P.dma("gpsimd", bp[:], bpk[:, :], ld)
        nw_col = [bp[:, c:c + 1] for c in range(8)]
        with contextlib.ExitStack() as scc:
            wsl = scc.enter_context(nc.sbuf_tensor("wsl", [128, 4], F32))
            sgp = [scc.enter_context(nc.sbuf_tensor("sgp%d" % r, [1, 1024], F32))
                   for r in range(4)]
            Spt = [scc.enter_context(nc.sbuf_tensor("Spt%d" % r,
                                                    [128, 1024], F32))
                   for r in range(4)]
            Sac = scc.enter_context(nc.sbuf_tensor("Sac", [128, 1024], F32))
            ldc = P.psem("l1a_ldc")
            P.dma("gpsimd", wsl[:], wselp[:, :], ldc)
            for r in range(4):
                P.dma("gpsimd", sgp[r][:], sig_p[r:r + 1, :], ldc)
                P.dma("gpsimd", Spt[r][:].rearrange("p (hh d) -> p hh d", d=64),
                      S_p[r].rearrange("(hh p) d -> p hh d", p=128), ldc)
            P.wait("vector", ldc)
            V0 = "vector"
            P.emit(V0, lambda e: e.tensor_scalar(
                sigc[:], sgp[0][:], wsl[0:1, 0:1], None, ALU.mult), serial=True)
            for r in range(1, 4):
                P.emit(V0, lambda e, r=r: e.scalar_tensor_tensor(
                    sigc[:], sgp[r][:], wsl[0:1, r:r + 1], sigc[:],
                    ALU.mult, ALU.add), serial=True)
            P.emit(V0, lambda e: e.tensor_scalar(
                Sac[:], Spt[0][:], wsl[:, 0:1], None, ALU.mult), serial=True)
            for r in range(1, 4):
                P.emit(V0, lambda e, r=r: e.scalar_tensor_tensor(
                    Sac[:], Spt[r][:], wsl[:, r:r + 1], Sac[:],
                    ALU.mult, ALU.add), serial=True)
            P.emit(V0, lambda e: e.tensor_copy(scb_p[:], Sac[:]),
                   P.done["vector"], serial=True)
    L0_scb = scb_p if mode == "full" else None
    with contextlib.ExitStack() as sx:
        xw = sx.enter_context(nc.sbuf_tensor(tg + "xw", [128, 8 * T], BF16))
        P.wait("gpsimd", P.done["vector"])
        P.dma("gpsimd", xw[:].rearrange("p (c t) -> p c t", c=8),
              xin[:, :].rearrange("(c p) t -> p c t", p=128), ld)
        for eng in ("scalar", "tensor", "vector", "gpsimd"):
            P.wait(eng, ld)
        w = [(e, ld, ld.n) for e in ("vector", "scalar", "tensor")]
        emit_ln_big(P, com, xw[:], T, "bf16", out=xln, tag=tg + "ln", waits=w,
                    x_bf16=True)
        P.fence()

    if mode == "full":
        P.wait("vector", P.done["vector"])
        P.emit("vector", lambda e: e.tensor_scalar(
            carry[:], sigc[:], 1.0, None, ALU.add), P.done["vector"],
            serial=True)
    else:
        P.emit("vector", lambda e: e.memset(carry[:], 1.0), P.done["vector"],
               serial=True)

    with contextlib.ExitStack() as sw:
        wsem = P.psem("l1aw")
        qslab = load_wslab_sc(P, sw, tg + "wq_sb", wq, D, D, wsem)
        kslab = load_wslab_sc(P, sw, tg + "wk_sb", wk, D, D, wsem)
        vslab = load_wslab_sc(P, sw, tg + "wv_sb", wv, D, D, wsem)
        P.wait("tensor", wsem)

        with contextlib.ExitStack() as sh:
            kdd = sh.enter_context(nc.sbuf_tensor(tg + "kdd", [128, T], BF16))
            Etm = sh.enter_context(nc.sbuf_tensor(tg + "Etm", [128, 16 * 64], BF16))
            kddtm = sh.enter_context(nc.sbuf_tensor(tg + "kddtm", [128, 16 * 128], BF16))
            vtm = sh.enter_context(nc.sbuf_tensor(tg + "vtm", [128, 16 * 64], BF16))
            trig = sh.enter_context(nc.sbuf_tensor(tg + "trig", [128, T + 2048], BF16))
            qsb = sh.enter_context(nc.sbuf_tensor(tg + "qsb", [128, T], BF16))
            qp = sh.enter_context(nc.sbuf_tensor(tg + "qp", [128, T], BF16))
            rcpb = sh.enter_context(nc.sbuf_tensor(tg + "rcpb", [128, 16 * 64], BF16))
            rcp = rcpb
            rcpf = sh.enter_context(nc.sbuf_tensor(tg + "rcpf", [128, T], BF16))
            t1 = sh.enter_context(nc.sbuf_tensor(tg + "t1f", [128, T], BF16))
            pt = sh.enter_context(nc.sbuf_tensor(tg + "ptl1", [128, 128], BF16))
            S_snap = sh.enter_context(nc.sbuf_tensor(tg + "S_snap", [128, 16 * 64], BF16))
            S_run = sh.enter_context(nc.sbuf_tensor(tg + "S_run", [128, 64], F32))
            if mode == "full":
                scb = L0_scb
            ps_p = sh.enter_context(nc.psum_tensor(tg + "ps_proj", [128, 512], F32))
            ps_c = sh.enter_context(nc.psum_tensor(tg + "ps_cum", [128, 64], F32))
            ps_sg = sh.enter_context(nc.psum_tensor(tg + "ps_sig", [1, 64], F32))
            ps_S = sh.enter_context(nc.psum_tensor(tg + "ps_S", [128, 64], F32))
            ps_pt = sh.enter_context(nc.psum_tensor(tg + "ps_pt", [128, 128], F32))
            ps_o = sh.enter_context(nc.psum_tensor(tg + "ps_o", [64, 128], F32))
            ps_tr = sh.enter_context(nc.psum_tensor(tg + "ps_tr", [64, 128], BF16))
            for h in range(16):
                emit_l1a_head(P, com, locals() | {"P": P}, h, mode)
        if mode == "partial":
            sto = P.psem("sto")
            P.wait("vector", P.done["tensor"])
            P.emit("vector", lambda e: e.tensor_scalar(
                carry[:], carry[:], -1.0, None, ALU.add), P.done["vector"],
                serial=True)
            P.wait("gpsimd", P.done["vector"])
            P.dma("gpsimd", sig_o[:, :], carry[:], sto)
            P.wait("gpsimd", sto)
            P.wait("sync", sto)
    if mode == "full":
        emit_l1a_post(P, com, nc, locals())
    if ctx is None:
        P.build()
        return nc
    return None


def emit_l1a_head(P, com, L, h, mode):
    nc = P.nc
    full = (mode == "full")
    xln, qslab, kslab, vslab = L["xln"], L["qslab"], L["kslab"], L["vslab"]
    kdd, kddtm, vtm, Etm = L["kdd"], L["kddtm"], L["vtm"], L["Etm"]
    trig, qsb, qp, rcp, rcpb, rcpf, t1 = (L["trig"], L["qsb"], L["qp"],
                                          L["rcp"], L["rcpb"], L["rcpf"],
                                          L["t1"])
    ps_p, ps_c, ps_sg, ps_S = L["ps_p"], L["ps_c"], L["ps_sg"], L["ps_S"]
    carry, ut = L["carry"], L["ut"]
    cfm, sfm, ctm, stm = L["cfm"], L["sfm"], L["ctm"], L["stm"]
    ld = P.psem("l1a_trig_ld")
    V = "vector"

    P.wait("gpsimd", P.done["tensor"])
    P.wait("gpsimd", P.done["vector"])
    P.dma("gpsimd", trig[0:64, 0:T], cfm[h * 64:(h + 1) * 64, :], ld)
    P.dma("gpsimd", trig[64:128, 0:T], sfm[h * 64:(h + 1) * 64, :], ld)
    ctm_r = ctm[h].rearrange("(tc p) d -> p tc d", p=128)
    stm_r = stm[h].rearrange("(tc p) d -> p tc d", p=128)
    ctm_d = trig[:, T:T + 1024].rearrange("p (tc d) -> p tc d", d=64)
    stm_d = trig[:, T + 1024:T + 2048].rearrange("p (tc d) -> p tc d", d=64)
    for i in range(2):
        P.dma("gpsimd", ctm_d[:, i * 8:(i + 1) * 8, :],
              ctm_r[:, i * 8:(i + 1) * 8, :], ld)
        P.dma("gpsimd", stm_d[:, i * 8:(i + 1) * 8, :],
              stm_r[:, i * 8:(i + 1) * 8, :], ld)
    for eng in ("tensor", "scalar", "vector"):
        P.wait(eng, ld)

    def fm_proj(slab, func, out_sb):
        for g in range(4):
            for half in range(2):
                for kc in range(8):
                    P.emit("tensor", lambda e, g=g, half=half, kc=kc: e.matmul(
                        ps_p[half * 64:(half + 1) * 64, 0:512],
                        slab[:, kc * D + h * 64:kc * D + (h + 1) * 64],
                        xln[:, kc * T + g * 512:kc * T + (g + 1) * 512],
                        start=(kc == 0), stop=(kc == 7)))
            P.mark_mm()
            P.wait("scalar", P.done["tensor"])
            P.emit("scalar", lambda e, g=g: e.activation(
                out=out_sb[:, g * 512:(g + 1) * 512], in_=ps_p[:, 0:512],
                func=func), P.done["scalar"], serial=True)
            P.wait("tensor", P.done["scalar"])

    P.wait("tensor", P.done["vector"])
    fm_proj(kslab, AF.Exp, t1)
    P.wait(V, P.done["scalar"])
    P.emit(V, lambda e: e.tensor_mul(kdd[:, 0:T], t1[:, 0:T], trig[:, 0:T]),
           P.done["vector"], serial=True)
    if full:
        fm_proj(qslab, AF.Silu, qsb)

    for tc in range(16):
        for kc in range(8):
            P.emit("tensor", lambda e, tc=tc, kc=kc: e.matmul(
                ps_c[:], xln[:, kc * T + tc * 128:kc * T + (tc + 1) * 128],
                kslab[:, kc * D + h * 64:kc * D + (h + 1) * 64],
                start=(kc == 0), stop=(kc == 7)))
        P.mark_mm()
        P.wait("scalar", P.done["tensor"])
        P.emit("scalar", lambda e, tc=tc: e.activation(
            out=Etm[:, tc * 64:(tc + 1) * 64], in_=ps_c[:], func=AF.Exp),
            P.done["scalar"], serial=True)
        P.wait("tensor", P.done["scalar"])
        for kc in range(8):
            P.emit("tensor", lambda e, tc=tc, kc=kc: e.matmul(
                ps_c[:], xln[:, kc * T + tc * 128:kc * T + (tc + 1) * 128],
                vslab[:, kc * D + h * 64:kc * D + (h + 1) * 64],
                start=(kc == 0), stop=(kc == 7)))
        P.mark_mm()
        P.wait(V, P.done["tensor"])
        P.emit(V, lambda e, tc=tc: e.tensor_copy(
            vtm[:, tc * 64:(tc + 1) * 64], ps_c[:]), P.done["vector"],
            serial=True)
        P.wait("tensor", P.done["vector"])
    P.wait(V, P.done["scalar"])
    P.emit(V, lambda e: e.tensor_mul(
        kddtm[:].rearrange("p (tc e) -> p tc e", e=128)[:, :, 0:64],
        Etm[:].rearrange("p (tc d) -> p tc d", d=64),
        ctm_d), serial=True)
    P.emit(V, lambda e: e.tensor_mul(
        kddtm[:].rearrange("p (tc e) -> p tc e", e=128)[:, :, 64:128],
        Etm[:].rearrange("p (tc d) -> p tc d", d=64),
        stm_d), P.done["vector"], serial=True)

    scb, S_snap, idt = L.get("scb"), L.get("S_snap"), L.get("idt")
    S_run = L["S_run"]
    for tc in range(16):
        P.emit(V, lambda e, tc=tc: e.tensor_add(
            Etm[0:1, tc * 64:(tc + 1) * 64], Etm[0:1, tc * 64:(tc + 1) * 64],
            carry[0:1, h * 64:(h + 1) * 64]), P.done["vector"], serial=True)
        P.wait("tensor", P.done["vector"])
        P.emit("tensor", lambda e, tc=tc: e.matmul(
            ps_c[:], ut[:], Etm[:, tc * 64:(tc + 1) * 64],
            start=True, stop=True))
        P.emit("tensor", lambda e, tc=tc: e.matmul(
            ps_sg[:], com.ones128[:], Etm[:, tc * 64:(tc + 1) * 64],
            start=True, stop=True))
        P.mark_mm()
        P.wait(V, P.done["tensor"])
        P.emit(V, lambda e: e.tensor_copy(carry[0:1, h * 64:(h + 1) * 64],
                                          ps_sg[:]), serial=True)
        if full:
            def _recip(e, tc=tc):
                with nc.allow_low_precision(reason="recip feeds bf16 matmul"):
                    return e.reciprocal(rcpb[:, tc * 64:(tc + 1) * 64], ps_c[:])
            P.emit(V, _recip, P.done["vector"], serial=True)
            P.wait("tensor", P.done["vector"])
            P.emit("tensor", lambda e, tc=tc: e.transpose(
                L["ps_tr"][0:64, :], rcpb[:, tc * 64:(tc + 1) * 64], idt[:, :]))
            P.mark_mm()
            P.wait(V, P.done["tensor"])
            P.emit(V, lambda e, tc=tc: e.tensor_copy(
                rcpf[0:64, tc * 128:(tc + 1) * 128], L["ps_tr"][0:64, :]),
                P.done["vector"], serial=True)
            P.wait("tensor", P.done["vector"])
            P.emit("tensor", lambda e, tc=tc: e.matmul(
                L["ps_pt"][64:128, :], idt[0:64, 0:64],
                rcpf[0:64, tc * 128:(tc + 1) * 128], start=True, stop=True))
            P.mark_mm()
            P.wait(V, P.done["tensor"])
            P.emit(V, lambda e, tc=tc: e.tensor_copy(
                rcpf[64:128, tc * 128:(tc + 1) * 128], L["ps_pt"][64:128, :]),
                P.done["vector"], serial=True)
        if tc == 0:
            if full:
                P.emit(V, lambda e: e.tensor_copy(
                    S_run[:], scb[:, h * 64:(h + 1) * 64]),
                    P.done["vector"], serial=True)
            else:
                P.emit(V, lambda e: e.memset(S_run[:], 0.0),
                       P.done["vector"], serial=True)
        P.wait("tensor", P.done["vector"])
        P.emit("tensor", lambda e, tc=tc: e.matmul(
            ps_S[:], kddtm[:, tc * 128:(tc + 1) * 128],
            vtm[:, tc * 64:(tc + 1) * 64], start=True, stop=True))
        P.mark_mm()
        P.wait(V, P.done["tensor"])
        P.emit(V, lambda e: e.tensor_add(S_run[:], S_run[:], ps_S[:]),
               serial=True)
        if full:
            P.emit(V, lambda e, tc=tc: e.tensor_copy(
                S_snap[:, tc * 64:(tc + 1) * 64], S_run[:]),
                P.done["vector"], serial=True)
        else:
            P.emit(V, lambda e: e.wait_ge(P.bar.h, 0), P.done["vector"],
                   serial=True)
        P.wait("tensor", P.done["vector"])

    if not full:
        S_o, sto = L["S_o"], P.psem("l1a_sto")
        P.wait("gpsimd", P.done["vector"])
        P.dma("gpsimd", S_o[h * 128:(h + 1) * 128, :], S_run[:], sto)
        P.wait(V, sto, sto.n)
        return

    P.wait(V, P.done["scalar"])
    P.emit(V, lambda e: e.tensor_mul(t1[:, 0:T], qsb[:, 0:T], trig[:, 0:T]),
           serial=True)
    P.emit(V, lambda e: e.tensor_mul(qp[:, 0:T], t1[:, 0:T], rcpf[:, 0:T]),
           P.done["vector"], serial=True)

    ps_pt, ps_o, pt, o_fm = L["ps_pt"], L["ps_o"], L["pt"], L["o_fm"]
    qch, qrow = h // 2, (h % 2) * 64
    for tc in range(16):
        P.wait("tensor", P.done["vector"])
        P.emit("tensor", lambda e, tc=tc: e.matmul(
            ps_pt[:], kdd[:, tc * 128:(tc + 1) * 128],
            qp[:, tc * 128:(tc + 1) * 128], start=True, stop=True))
        P.mark_mm()
        P.wait(V, P.done["tensor"])
        P.emit(V, lambda e: e.tensor_mul(pt[:], ps_pt[:], ut[:]),
               P.done["vector"], serial=True)
        P.wait("tensor", P.done["vector"])
        S_prev = (scb[:, h * 64:(h + 1) * 64] if tc == 0
                  else S_snap[:, (tc - 1) * 64:tc * 64])
        P.emit("tensor", lambda e, tc=tc, S_prev=S_prev: e.matmul(
            ps_o[:], S_prev, qp[:, tc * 128:(tc + 1) * 128],
            start=True, stop=False))
        P.emit("tensor", lambda e, tc=tc: e.matmul(
            ps_o[:], vtm[:, tc * 64:(tc + 1) * 64], pt[:],
            start=False, stop=True))
        P.mark_mm()
        P.wait(V, P.done["tensor"])
        P.emit(V, lambda e, tc=tc: e.tensor_copy(
            o_fm[qrow:qrow + 64, qch * T + tc * 128:qch * T + (tc + 1) * 128],
            ps_o[:]), P.done["vector"], serial=True)
        P.wait("tensor", P.done["vector"])


def emit_l1a_post(P, com, nc, L):
    V = "vector"
    xln, o_fm, idt = L["xln"], L["o_fm"], L["idt"]
    wg1, wg2, wo, xin, x3, hs = (L["wg1"], L["wg2"], L["wo"], L["xin"],
                                 L["x3"], L["hs"])
    nw_col = L["nw_col"]
    with contextlib.ExitStack() as sc:
        g1b = sc.enter_context(nc.sbuf_tensor(L["tg"] + "g1b", [64, T], BF16))
        sg = sc.enter_context(nc.sbuf_tensor(L["tg"] + "sg", [128, 512], BF16))
        g2 = sc.enter_context(nc.sbuf_tensor(L["tg"] + "g2", [128, T], BF16))
        rstd = sc.enter_context(nc.sbuf_tensor(L["tg"] + "rstd16", [16, T], F32))
        rstdb = sc.enter_context(nc.sbuf_tensor(L["tg"] + "rstdb16", [16, T], BF16))
        rowt = sc.enter_context(nc.sbuf_tensor(L["tg"] + "rowt", [1, 512], BF16))
        wg1s = sc.enter_context(nc.sbuf_tensor(L["tg"] + "wg1s", [128, 8 * 64], BF16))
        wg2s = sc.enter_context(nc.sbuf_tensor(L["tg"] + "wg2s", [64, D], BF16))
        ps_g = sc.enter_context(nc.psum_tensor(L["tg"] + "ps_g1", [64, 512], F32))
        ps_f = sc.enter_context(nc.psum_tensor(L["tg"] + "ps_gf", [128, 512], F32))
        ps_ss = sc.enter_context(nc.psum_tensor(L["tg"] + "ps_ss", [16, T], F32))
        ps_r = sc.enter_context(nc.psum_tensor(L["tg"] + "ps_row", [1, 512], F32))
        ps_bc = sc.enter_context(nc.psum_tensor(L["tg"] + "ps_bc2", [128, 512], F32))
        wl = P.psem("l1a_wg")
        P.wait("gpsimd", P.done["tensor"])
        P.wait("gpsimd", P.done["vector"])
        P.wait("gpsimd", P.done["scalar"])
        for kc in range(8):
            P.dma("gpsimd", wg1s[:, kc * 64:(kc + 1) * 64],
                  wg1[kc * 128:(kc + 1) * 128, :], wl)
        P.dma("gpsimd", wg2s[:, :], wg2[:, :], wl)
        P.wait("tensor", wl)
        for g in range(4):
            P.wait("tensor", P.done["vector"])
            for kc in range(8):
                P.emit("tensor", lambda e, g=g, kc=kc: e.matmul(
                    ps_g[:], wg1s[:, kc * 64:(kc + 1) * 64],
                    xln[:, kc * T + g * 512:kc * T + (g + 1) * 512],
                    start=(kc == 0), stop=(kc == 7)))
            P.mark_mm()
            P.wait(V, P.done["tensor"])
            P.emit(V, lambda e, g=g: e.tensor_copy(
                g1b[:, g * 512:(g + 1) * 512], ps_g[:]), P.done["vector"],
                serial=True)
        for c in range(8):
            for g in range(4):
                P.wait("tensor", P.done["vector"])
                P.emit("tensor", lambda e, c=c, g=g: e.matmul(
                    ps_f[:], wg2s[:, c * 128:(c + 1) * 128],
                    g1b[:, g * 512:(g + 1) * 512], start=True, stop=True))
                P.mark_mm()
                P.wait("scalar", P.done["tensor"])
                P.emit("scalar", lambda e: e.activation(
                    out=sg[:], in_=ps_f[:], func=AF.Sigmoid),
                    P.done["scalar"], serial=True)
                P.wait(V, P.done["scalar"])
                P.emit(V, lambda e, c=c, g=g: e.tensor_mul(
                    o_fm[:, c * T + g * 512:c * T + (g + 1) * 512],
                    o_fm[:, c * T + g * 512:c * T + (g + 1) * 512], sg[:]),
                    P.done["vector"], serial=True)
        for c in range(8):
            P.wait("scalar", P.done["vector"])
            P.wait("scalar", P.done["tensor"])
            P.emit("scalar", lambda e, c=c: e.activation(
                out=g2[:, 0:T], in_=o_fm[:, c * T:(c + 1) * T],
                func=AF.Square), P.done["scalar"], serial=True)
            P.wait("tensor", P.done["scalar"])
            for g in range(4):
                P.emit("tensor", lambda e, c=c, g=g: e.matmul(
                    ps_ss[:, g * 512:(g + 1) * 512], hs[:, c * 16:(c + 1) * 16],
                    g2[:, g * 512:(g + 1) * 512],
                    start=(c == 0), stop=(c == 7)))
            P.mark_mm()
        P.wait("scalar", P.done["tensor"])
        P.emit("scalar", lambda e: e.activation(
            out=rstd[:], in_=ps_ss[:], func=AF.Ln,
            bias=com.col(1e-6)[0:16, 0:1], scale=1.0 / 64), serial=True)
        P.emit("scalar", lambda e: e.activation(
            out=rstd[:], in_=rstd[:], func=AF.Exp, scale=-0.5),
            P.done["scalar"], serial=True)
        P.wait(V, P.done["scalar"])
        P.emit(V, lambda e: e.tensor_copy(rstdb[:], rstd[:]),
               P.done["vector"], serial=True)
        for c in range(8):
            for g in range(4):
                for j in range(2):
                    h = 2 * c + j
                    P.wait("tensor", P.done["vector"])
                    P.emit("tensor", lambda e, h=h, g=g: e.matmul(
                        ps_r[:], idt[0:16, h:h + 1],
                        rstdb[:, g * 512:(g + 1) * 512], start=True, stop=True))
                    P.mark_mm()
                    P.wait(V, P.done["tensor"])
                    P.emit(V, lambda e: e.tensor_copy(rowt[:], ps_r[:]),
                           P.done["vector"], serial=True)
                    P.wait("tensor", P.done["vector"])
                    P.emit("tensor", lambda e, j=j: e.matmul(
                        ps_bc[j * 64:(j + 1) * 64, :],
                        com.onesP_bf[0:1, 0:64], rowt[:],
                        start=True, stop=True))
                    P.mark_mm()
                P.wait(V, P.done["tensor"])
                P.emit(V, lambda e, c=c, g=g: e.scalar_tensor_tensor(
                    o_fm[:, c * T + g * 512:c * T + (g + 1) * 512],
                    o_fm[:, c * T + g * 512:c * T + (g + 1) * 512],
                    nw_col[c], ps_bc[:], ALU.mult, ALU.mult),
                    P.done["vector"], serial=True)
    with contextlib.ExitStack() as sc:
        wl = P.psem("l1a_wo")
        P.wait("gpsimd", P.done["tensor"])
        P.wait("gpsimd", P.done["vector"])
        woslab = load_wslab_sc(P, sc, L["tg"] + "wo_l1", wo, D, D, wl)
        P.wait("tensor", wl)

        def oevac(oc, g, ps):
            P.emit(V, lambda e, oc=oc, g=g: e.tensor_copy(
                xln[:, oc * T + g * 512:oc * T + (g + 1) * 512], ps),
                P.done["vector"], serial=True)

        emit_proj_fm(P, com, o_fm[:], woslab, D, 0, 8, T, oevac, "vector",
                     tag="l1a_wo")
    with contextlib.ExitStack() as sc:
        x2f = sc.enter_context(nc.sbuf_tensor(L["tg"] + "x2f", [128, 8 * T], F32))
        ldf = P.psem("l1a_ldf")
        stf = P.psem("l1a_stf")
        P.wait("sync", P.done["vector"])
        P.wait("sync", P.done["tensor"])
        P.dma("sync", x2f[:].rearrange("p (c t) -> p c t", c=8),
              xin[:, :].rearrange("(c p) t -> p c t", p=128), ldf)
        w = [(e, ldf, ldf.n) for e in ("vector", "scalar", "tensor")]
        emit_ln_big(P, com, xln[:], T, "residual", xold=x2f, xnew=x2f,
                    tag=L["tg"] + "fln", waits=w, x_bf16=True)
        P.wait("sync", P.done["vector"])
        P.dma("sync", x3[:, :].rearrange("(c p) t -> p c t", p=128),
              x2f[:].rearrange("p (c t) -> p c t", c=8), stf)
        P.wait("sync", stf)


TW_A = 2560


def build_l1f_attn(ctx=None, tag=""):
    fused_halo = None
    if ctx is None:
        nc = bass.Bass("TRN2", target_bir_lowering=False, debug=False)
        x_own = nc.dram_tensor("x_own", [D, T], F32, kind="ExternalInput").ap()
        x_halo = nc.dram_tensor("x_halo", [D, 512], F32, kind="ExternalInput").ap()
        wqkv = nc.dram_tensor("wqkv", [D, 16 * 384], F32, kind="ExternalInput").ap()
        wo = nc.dram_tensor("wo", [D, D], F32, kind="ExternalInput").ap()
        tabs = nc.dram_tensor("tabs", [4 * 128, TW_A], F32, kind="ExternalInput").ap()
        masks = nc.dram_tensor("masks", [2 * 128, 1024], F32, kind="ExternalInput").ap()
        consts = nc.dram_tensor("consts", [128, 134], F32, kind="ExternalInput").ap()
        pairbc = nc.dram_tensor("pairbc", [2, 128], F32, kind="ExternalInput").ap()
        y_out = nc.dram_tensor("y", [D, T], F32, kind="ExternalOutput").ap()
        P = Prog(nc)
        com = Common(P, Tn=512, with_ln_xb=False)
        sc0 = P.stack
    else:
        nc, P, com, sc0 = ctx["nc"], ctx["P"], ctx["com"], ctx["sc"]
        io = ctx["io"]
        (x_own, wqkv, wo, tabs, masks, consts, pairbc, y_out) = (
            io["x_own"], io["wqkv"], io["wo"], io["tabs"], io["masks"],
            io["consts"], io["pairbc"], io["y_out"])
        fused_halo = ctx["halo"]
    com.col(1e-38)
    Tq = 2048
    Tk = 2304
    sb0 = lambda n_, sh_, dt_: sc0.enter_context(nc.sbuf_tensor(n_, sh_, dt_))
    xw = sb0(tag + "xw_a", [128, 8 * TW_A], BF16)
    o_fm = sb0(tag + "o_fa", [128, 8 * Tq], BF16)
    tb4 = sb0(tag + "tabs_sb", [128, 4 * TW_A], BF16)
    msk = sb0(tag + "msk_sb", [128, 2 * 1024], BF16)
    cst = sb0(tag + "cst_sb", [128, 134], F32)
    psw = sb0(tag + "psw_sb", [128, 128], BF16)
    ld = P.psem("lfa_ld")
    if fused_halo is None:
        P.dma("gpsimd", xw[:].rearrange("p (c t) -> p c t", c=8)[:, :, 0:512],
              x_halo[:, :].rearrange("(c p) t -> p c t", p=128), ld)
    else:
        halo_g, wslc = fused_halo
        xwr = xw[:].rearrange("p (c t) -> p c t", c=8)
        P.emit("vector", lambda e: e.memset(xwr[:, :, 0:255], 0.0),
               P.done["vector"], serial=True)
        htmp = [sb0(tag + "htmp%d" % r, [128, 8 * 257], BF16)
                for r in range(4)]
        for r in range(4):
            P.dma("gpsimd",
                  htmp[r][:].rearrange("p (c t) -> p c t", c=8),
                  halo_g[r * 1024:(r + 1) * 1024, :].rearrange(
                      "(c p) t -> p c t", p=128), ld)
        P.wait("vector", ld)
        hr = [h[:].rearrange("p (c t) -> p c t", c=8) for h in htmp]
        P.emit("vector", lambda e: e.tensor_scalar(
            xwr[:, :, 255:512], hr[0], wslc[:, 0:1], None, ALU.mult),
            serial=True)
        for r in range(1, 4):
            P.emit("vector", lambda e, r=r: e.scalar_tensor_tensor(
                xwr[:, :, 255:512], hr[r], wslc[:, r:r + 1],
                xwr[:, :, 255:512], ALU.mult, ALU.add), serial=True)
        P.emit("vector", lambda e: e.wait_ge(P.bar.h, 0), P.done["vector"],
               serial=True)
    P.dma("gpsimd", xw[:].rearrange("p (c t) -> p c t", c=8)[:, :, 512:TW_A],
          x_own[:, :].rearrange("(c p) t -> p c t", p=128), ld)
    for i in range(4):
        P.dma("gpsimd", tb4[:, i * TW_A:(i + 1) * TW_A],
              tabs[i * 128:(i + 1) * 128, :], ld)
    for i in range(2):
        P.dma("gpsimd", msk[:, i * 1024:(i + 1) * 1024],
              masks[i * 128:(i + 1) * 128, :], ld)
    P.dma("gpsimd", cst[:], consts[:, :], ld)
    P.dma("gpsimd", psw[:], consts[:, 6:134], ld)
    csq = tb4[:, 0 * TW_A:1 * TW_A]
    snq = tb4[:, 1 * TW_A:2 * TW_A]
    csk = tb4[:, 2 * TW_A:3 * TW_A]
    snk = tb4[:, 3 * TW_A:4 * TW_A]
    qs_col, ks_col = cst[:, 0:1], cst[:, 1:2]
    for eng in ("scalar", "tensor", "vector", "gpsimd"):
        P.wait(eng, ld)
    if fused_halo is not None:
        P.wait("tensor", P.done["vector"])
        P.wait("scalar", P.done["vector"])
    psel = sb0(tag + "psel", [128, 4], BF16)
    pbc = sb0(tag + "pbc", [2, 128], BF16)
    P.dma("gpsimd", pbc[:], pairbc[:, :], ld)
    P.wait("vector", ld)
    P.emit("vector", lambda e: e.tensor_copy(psel[:, 0:2], cst[:, 2:4]),
           P.done["vector"], serial=True)

    with contextlib.ExitStack() as sc:
        slab = sc.enter_context(nc.sbuf_tensor(tag + "qkv_hp", [128, 8 * 384], BF16))
        qraw = sc.enter_context(nc.sbuf_tensor(tag + "qraw", [128, Tq], BF16))
        sq = sc.enter_context(nc.sbuf_tensor(tag + "sq_a", [128, Tk], BF16))
        qrot = sc.enter_context(nc.sbuf_tensor(tag + "qrot", [128, Tq], BF16))
        kraw = sc.enter_context(nc.sbuf_tensor(tag + "kraw_a", [128, Tk], BF16))
        krot = sc.enter_context(nc.sbuf_tensor(tag + "krot", [128, Tk], BF16))
        rnb = sc.enter_context(nc.sbuf_tensor(tag + "rnb", [2, Tk], BF16))
        tq1 = sc.enter_context(nc.sbuf_tensor(tag + "tq1", [128, 512], F32))
        tq2 = sc.enter_context(nc.sbuf_tensor(tag + "tq2", [128, 512], F32))
        vaug = sc.enter_context(nc.sbuf_tensor(tag + "vaug", [128, 18 * 130], BF16))
        pt = sc.enter_context(nc.sbuf_tensor(tag + "pt_a", [128, 1024], BF16))
        rc = sc.enter_context(nc.sbuf_tensor(tag + "rc_a", [128, 256], F32))
        rcb = sc.enter_context(nc.sbuf_tensor(tag + "rcb_a", [128, 2 * 256], BF16))
        ps_a = sc.enter_context(nc.psum_tensor(tag + "psA", [128, 512], F32))
        ps_b = sc.enter_context(nc.psum_tensor(tag + "psB", [128, 512], F32))
        ps_bc = sc.enter_context(nc.psum_tensor(tag + "psBC", [128, 512], F32))
        ps_ss = sc.enter_context(nc.psum_tensor(tag + "psSS", [2, 512], F32))
        ps_st = sc.enter_context(nc.psum_tensor(tag + "psST", [128, 1024], F32))
        ps_o = sc.enter_context(nc.psum_tensor(tag + "psO", [65, 256], F32))
        wsl = P.psem("lfa_w")
        for hp in range(8):
            emit_l1f_attn_hp(P, com, locals(), hp)
    with contextlib.ExitStack() as sc:
        P.wait("gpsimd", P.done["tensor"])
        P.wait("gpsimd", P.done["vector"])
        wl = P.psem("lfa_wo")
        woslab = load_wslab_sc(P, sc, tag + "wo_lfa", wo, D, D, wl)
        xblk = sc.enter_context(nc.sbuf_tensor(tag + "xblk", [128, 8 * 512], F32))
        ytmp = sc.enter_context(nc.sbuf_tensor(tag + "ytmp", [128, 2 * 512], F32))
        ps_w = [sc.enter_context(nc.psum_tensor(tag + "psW%d" % i, [128, 512], F32))
                for i in range(2)]
        ldx = P.psem("lfa_ldx")
        sty = [P.psem("lfa_sty0"), P.psem("lfa_sty1")]
        P.wait("tensor", wl)
        idx = 0
        evcnt = []
        for g in range(4):
            P.wait("sync", P.done["vector"])
            for c in range(8):
                P.dma("sync", xblk[:, c * 512:(c + 1) * 512],
                      x_own[c * 128:(c + 1) * 128, g * 512:(g + 1) * 512], ldx)
            for oc in range(8):
                b = idx % 2
                if idx >= 2:
                    P.wait("tensor", P.done["vector"], evcnt[idx - 2])
                for kc in range(8):
                    P.emit("tensor", lambda e, oc=oc, g=g, kc=kc, b=b: e.matmul(
                        ps_w[b][:], woslab[:, kc * D + oc * 128:kc * D + (oc + 1) * 128],
                        o_fm[:, kc * Tq + g * 512:kc * Tq + (g + 1) * 512],
                        start=(kc == 0), stop=(kc == 7)))
                P.mark_mm()
                P.wait("vector", P.done["tensor"])
                P.wait("vector", ldx, ldx.n)
                P.wait("vector", sty[b], sty[b].n)
                P.emit("vector", lambda e, oc=oc, b=b: e.tensor_add(
                    ytmp[:, b * 512:(b + 1) * 512], ps_w[b][:],
                    xblk[:, oc * 512:(oc + 1) * 512]), P.done["vector"],
                    serial=True)
                evcnt.append(P.done["vector"].n)
                P.wait("gpsimd", P.done["vector"])
                P.dma("gpsimd", y_out[oc * 128:(oc + 1) * 128,
                                      g * 512:(g + 1) * 512],
                      ytmp[:, b * 512:(b + 1) * 512], sty[b])
                idx += 1
        for sm in sty:
            P.wait("gpsimd", sm)
        P.wait("sync", sty[0])
    if ctx is None:
        P.build()
        return nc
    return None


def emit_l1f_attn_hp(P, com, L, hp):
    nc = P.nc
    V = "vector"
    (xw, slab, qraw, sq, qrot, kraw, krot, rnb, tq1, tq2, vaug, pt, rc, rcb,
     ps_a, ps_b, ps_bc, ps_ss, ps_st, ps_o, psel, psw, o_fm, msk,
     qs_col, ks_col, csq, snq, csk, snk, wsl, wqkv) = (
        L["xw"], L["slab"], L["qraw"], L["sq"], L["qrot"], L["kraw"],
        L["krot"], L["rnb"], L["tq1"], L["tq2"], L["vaug"], L["pt"], L["rc"],
        L["rcb"], L["ps_a"], L["ps_b"], L["ps_bc"], L["ps_ss"], L["ps_st"],
        L["ps_o"], L["psel"], L["psw"], L["o_fm"], L["msk"], L["qs_col"],
        L["ks_col"], L["csq"], L["snq"], L["csk"], L["snk"], L["wsl"],
        L["wqkv"])
    Tq, Tk = 2048, 2304
    pbc = L["pbc"]

    P.wait("gpsimd", P.done["tensor"])
    P.wait("gpsimd", P.done["vector"])
    P.wait("gpsimd", P.done["scalar"])
    for kc in range(8):
        P.dma("gpsimd", slab[:, kc * 384:(kc + 1) * 384],
              wqkv[kc * 128:(kc + 1) * 128, hp * 384:(hp + 1) * 384], wsl)
    for eng in ("tensor", "scalar", "vector"):
        P.wait(eng, wsl)
    if hp == 0:
        P.emit(V, lambda e: e.memset(vaug[:], 1.0), P.done["vector"],
               serial=True)

    def rot_path(raw, rot, scol, cst_, snt, t0, Tn, wq_off):
        nblk = (Tn + 511) // 512
        for g in range(nblk):
            N = min(512, Tn - g * 512)
            for kc in range(8):
                sh = 1 if kc >= 4 else 0
                P.emit("tensor", lambda e, g=g, kc=kc, N=N, sh=sh: e.matmul(
                    ps_a[:, 0:N],
                    slab[:, kc * 384 + wq_off:kc * 384 + wq_off + 128],
                    xw[:, kc * TW_A + t0 + g * 512 - sh:
                       kc * TW_A + t0 + g * 512 - sh + N],
                    start=(kc == 0), stop=(kc == 7)))
            P.mark_mm()
            P.wait(V, P.done["tensor"])
            P.emit(V, lambda e, g=g, N=N: e.tensor_copy(
                raw[:, g * 512:g * 512 + N], ps_a[:, 0:N]), P.done["vector"],
                serial=True)
            P.wait("scalar", P.done["vector"])
            P.emit("scalar", lambda e, g=g, N=N: e.activation(
                out=sq[:, g * 512:g * 512 + N], in_=raw[:, g * 512:g * 512 + N],
                func=AF.Square), P.done["scalar"], serial=True)
            P.wait("tensor", P.done["scalar"])
            P.emit("tensor", lambda e, g=g, N=N: e.matmul(
                ps_ss[:, 0:N], psel[:, 0:2], sq[:, g * 512:g * 512 + N],
                start=True, stop=True))
            P.mark_mm()
            P.wait("scalar", P.done["tensor"])
            P.emit("scalar", lambda e, N=N: e.activation(
                out=tq1[0:2, 0:N], in_=ps_ss[:, 0:N], func=AF.Ln,
                bias=com.col(1e-38)[0:2, 0:1]), serial=True)
            P.emit("scalar", lambda e, N=N: e.activation(
                out=tq1[0:2, 0:N], in_=tq1[0:2, 0:N], func=AF.Exp, scale=0.5),
                P.done["scalar"], serial=True)
            P.wait(V, P.done["scalar"])
            P.emit(V, lambda e, N=N: e.tensor_scalar(
                tq1[0:2, 0:N], tq1[0:2, 0:N], 1e-12, None, ALU.max),
                serial=True)
            P.emit(V, lambda e, N=N: e.reciprocal(
                tq1[0:2, 0:N], tq1[0:2, 0:N]), serial=True)
            P.emit(V, lambda e, N=N: e.tensor_copy(
                rnb[0:2, 0:N], tq1[0:2, 0:N]), P.done["vector"], serial=True)
            P.wait("tensor", P.done["vector"])
            P.emit("tensor", lambda e, N=N: e.matmul(
                ps_bc[:, 0:N], pbc[0:2, :], rnb[0:2, 0:N],
                start=True, stop=True))
            P.mark_mm()
            P.wait(V, P.done["tensor"])
            P.emit(V, lambda e, g=g, N=N, scol=scol: e.scalar_tensor_tensor(
                raw[:, g * 512:g * 512 + N], raw[:, g * 512:g * 512 + N],
                scol, ps_bc[:, 0:N], ALU.mult, ALU.mult), P.done["vector"],
                serial=True)
            P.wait("tensor", P.done["vector"])
            P.emit("tensor", lambda e, g=g, N=N: e.matmul(
                ps_b[:, 0:N], psw[:], raw[:, g * 512:g * 512 + N],
                start=True, stop=True))
            P.mark_mm()
            P.wait(V, P.done["tensor"])
            P.emit(V, lambda e, g=g, N=N, cst_=cst_: e.tensor_mul(
                tq1[:, 0:N], raw[:, g * 512:g * 512 + N],
                cst_[:, t0 + g * 512:t0 + g * 512 + N]), serial=True)
            P.emit(V, lambda e, g=g, N=N, snt=snt: e.tensor_mul(
                tq2[:, 0:N], ps_b[:, 0:N],
                snt[:, t0 + g * 512:t0 + g * 512 + N]), serial=True)
            P.emit(V, lambda e, g=g, N=N: e.tensor_add(
                rot[:, g * 512:g * 512 + N], tq1[:, 0:N], tq2[:, 0:N]),
                P.done["vector"], serial=True)
            P.wait("tensor", P.done["vector"])

    rot_path(qraw, qrot, qs_col, csq, snq, 512, Tq, 0)
    rot_path(kraw, krot, ks_col, csk, snk, 256, Tk, 128)

    for kt in range(18):
        for kc in range(8):
            sh = 1 if kc >= 4 else 0
            P.emit("tensor", lambda e, kt=kt, kc=kc, sh=sh: e.matmul(
                ps_a[:, 0:128],
                xw[:, kc * TW_A + 256 + kt * 128 - sh:
                   kc * TW_A + 256 + (kt + 1) * 128 - sh],
                slab[:, kc * 384 + 256:kc * 384 + 384],
                start=(kc == 0), stop=(kc == 7)))
        P.mark_mm()
        P.wait(V, P.done["tensor"])
        P.emit(V, lambda e, kt=kt: e.tensor_copy(
            vaug[:, kt * 130:(kt + 1) * 130].rearrange(
                "p (j w) -> p j w", j=2)[:, :, 0:64],
            ps_a[:, 0:128].rearrange("p (j d) -> p j d", j=2)),
            P.done["vector"], serial=True)
        P.wait("tensor", P.done["vector"])

    for j in range(2):
        h = 2 * hp + j
        jr = j * 64
        for B in range(1, 9):
            for m in range(4):
                kcol = 256 * (B - 1) + 128 * m
                for a in range(2):
                    qcol = 256 * (B - 1) + 128 * a
                    P.emit("tensor", lambda e, m=m, a=a, kcol=kcol, qcol=qcol,
                           jr=jr: e.matmul(
                        ps_st[:, m * 256 + a * 128:m * 256 + (a + 1) * 128],
                        krot[jr:jr + 64, kcol:kcol + 128],
                        qrot[jr:jr + 64, qcol:qcol + 128],
                        start=True, stop=True))
            P.mark_mm()
            P.wait(V, P.done["tensor"])
            moff = 1024 if B == 1 else 0
            P.emit(V, lambda e, moff=moff: e.tensor_add(
                ps_st[:], ps_st[:], msk[:, moff:moff + 1024]),
                P.done["vector"], serial=True)
            P.wait("scalar", P.done["vector"])
            for half in range(2):
                P.emit("scalar", lambda e, half=half: e.activation(
                    out=pt[:, half * 512:(half + 1) * 512],
                    in_=ps_st[:, half * 512:(half + 1) * 512],
                    func=AF.Exp, scale=8.0),
                    P.done["scalar"] if half == 1 else None)
            P.wait("tensor", P.done["scalar"])
            for m in range(4):
                kti = 2 * (B - 1) + m
                P.emit("tensor", lambda e, m=m, kti=kti, j=j: e.matmul(
                    ps_o[:], vaug[:, kti * 130 + j * 65:kti * 130 + (j + 1) * 65],
                    pt[:, m * 256:(m + 1) * 256],
                    start=(m == 0), stop=(m == 3)))
            P.mark_mm()
            P.wait(V, P.done["tensor"])
            P.emit(V, lambda e: e.reciprocal(rc[64:65, 0:256],
                                             ps_o[64:65, :]), serial=True)
            P.emit(V, lambda e: e.tensor_copy(rcb[64:65, 0:256],
                                              rc[64:65, 0:256]), serial=True)
            P.emit(V, lambda e: e.tensor_sub(rc[64:65, 0:256],
                                             rc[64:65, 0:256],
                                             rcb[64:65, 0:256]), serial=True)
            P.emit(V, lambda e: e.tensor_copy(rcb[64:65, 256:512],
                                              rc[64:65, 0:256]),
                   P.done["vector"], serial=True)
            P.wait("tensor", P.done["vector"])
            for i2 in range(2):
                P.emit("tensor", lambda e, i2=i2: e.matmul(
                    ps_bc[0:64, 0:256], com.onesP_bf[64:65, 0:64],
                    rcb[64:65, i2 * 256:(i2 + 1) * 256],
                    start=(i2 == 0), stop=(i2 == 1)))
            P.mark_mm()
            P.wait(V, P.done["tensor"])
            P.emit(V, lambda e: e.tensor_copy(rcb[0:64, 0:256],
                                              ps_bc[0:64, 0:256]),
                   serial=True)
            P.emit(V, lambda e, jr=jr, hp=hp, B=B: e.tensor_mul(
                o_fm[jr:jr + 64, hp * Tq + (B - 1) * 256:
                     hp * Tq + B * 256],
                ps_o[0:64, :], rcb[0:64, 0:256]), P.done["vector"],
                serial=True)
            P.wait("tensor", P.done["vector"])


TW_F = 2176


def build_l1f_ffn(ctx=None, tag=""):
    fused_halo = None
    if ctx is None:
        nc = bass.Bass("TRN2", target_bir_lowering=False, debug=False)
        x_own = nc.dram_tensor("x_own", [D, T], F32, kind="ExternalInput").ap()
        x_h1 = nc.dram_tensor("x_h1", [D, 128], F32, kind="ExternalInput").ap()
        w1 = nc.dram_tensor("w1p", [D, 6144], F32, kind="ExternalInput").ap()
        w2 = nc.dram_tensor("w2p", [3072, D], F32, kind="ExternalInput").ap()
        bpk = nc.dram_tensor("bpk", [128, 72], F32, kind="ExternalInput").ap()
        y_out = nc.dram_tensor("y", [D, T], F32, kind="ExternalOutput").ap()
        P = Prog(nc)
        com = Common(P, Tn=512, with_ln_xb=True)
        sc0 = P.stack
    else:
        nc, P, com, sc0 = ctx["nc"], ctx["P"], ctx["com"], ctx["sc"]
        io = ctx["io"]
        x_own, w1, w2, bpk, y_out = (io["x_own"], io["w1"], io["w2"],
                                     io["bpk"], io["y_out"])
        fused_halo = ctx["halo"]
    V = "vector"
    sb0 = lambda n_, sh_, dt_: sc0.enter_context(nc.sbuf_tensor(n_, sh_, dt_))
    xw = sb0(tag + "xw_f", [128, 8 * TW_F], BF16)
    bp = sb0(tag + "bp_f", [128, 72], F32)
    ld = P.psem("lff_ld")
    xwr = xw[:].rearrange("p (c t) -> p c t", c=8)
    if fused_halo is None:
        P.dma("gpsimd", xwr[:, :, 0:128],
              x_h1[:, :].rearrange("(c p) t -> p c t", p=128), ld)
    else:
        halo_g, wslc = fused_halo
        P.emit("vector", lambda e: e.memset(xwr[:, :, 0:127], 0.0),
               P.done["vector"], serial=True)
        htmp = [sb0(tag + "h1t%d" % r, [128, 8], BF16) for r in range(4)]
        for r in range(4):
            P.dma("gpsimd", htmp[r][:].rearrange("p (c o) -> p c o", o=1),
                  halo_g[r * 1024:(r + 1) * 1024, :].rearrange(
                      "(c p) o -> p c o", p=128), ld,
                  allow_slow_non_contiguous=True)
        P.wait("vector", ld)
        hr = [h[:].rearrange("p (c o) -> p c o", o=1) for h in htmp]
        P.emit("vector", lambda e: e.tensor_scalar(
            xwr[:, :, 127:128], hr[0], wslc[:, 0:1], None, ALU.mult),
            serial=True)
        for r in range(1, 4):
            P.emit("vector", lambda e, r=r: e.scalar_tensor_tensor(
                xwr[:, :, 127:128], hr[r], wslc[:, r:r + 1],
                xwr[:, :, 127:128], ALU.mult, ALU.add), serial=True)
        P.emit("vector", lambda e: e.wait_ge(P.bar.h, 0), P.done["vector"],
               serial=True)
    P.dma("gpsimd", xwr[:, :, 128:TW_F],
          x_own[:, :].rearrange("(c p) t -> p c t", p=128), ld)
    P.dma("gpsimd", bp[:], bpk[:, :], ld)
    for eng in ("scalar", "tensor", "vector", "gpsimd"):
        P.wait(eng, ld)
    if fused_halo is not None:
        P.wait("tensor", P.done["vector"])
        P.wait("scalar", P.done["vector"])
    g_col = [bp[:, c:c + 1] for c in range(8)]
    b_col = [bp[:, 8 + c:8 + c + 1] for c in range(8)]
    b1a = [bp[:, 16 + i:16 + i + 1] for i in range(24)]
    b1g = [bp[:, 40 + i:40 + i + 1] for i in range(24)]
    b2c = [bp[:, 64 + i:64 + i + 1] for i in range(8)]

    with contextlib.ExitStack() as sc:
        xn = sc.enter_context(nc.sbuf_tensor(tag + "xn_f", [128, 8 * 512], BF16))
        prod = sc.enter_context(nc.sbuf_tensor(tag + "prod", [128, 24 * 512], BF16))
        sla = sc.enter_context(nc.sbuf_tensor(tag + "sla", [128, 8 * 512], BF16))
        slg = sc.enter_context(nc.sbuf_tensor(tag + "slg", [128, 8 * 512], BF16))
        sl2 = sc.enter_context(nc.sbuf_tensor(tag + "sl2", [128, 2 * 24 * 128], BF16))
        ta = sc.enter_context(nc.sbuf_tensor(tag + "ta_f", [128, 512], F32))
        ga = sc.enter_context(nc.sbuf_tensor(tag + "ga_f", [128, 512], BF16))
        xblk = sc.enter_context(nc.sbuf_tensor(tag + "xblk_f", [128, 8 * 512], F32))
        ytmp = sc.enter_context(nc.sbuf_tensor(tag + "ytmp_f", [128, 2 * 512], F32))
        ps_a = sc.enter_context(nc.psum_tensor(tag + "psa_f", [128, 512], F32))
        ps_g = sc.enter_context(nc.psum_tensor(tag + "psg_f", [128, 512], F32))
        ps_y = [sc.enter_context(nc.psum_tensor(tag + "psy%d" % i, [128, 512], F32))
                for i in range(2)]
        wl = P.psem("lff_w")
        w2l = [P.psem("lff_w2_0"), P.psem("lff_w2_1")]
        ldx = P.psem("lff_ldx")
        sty = [P.psem("lff_sty0"), P.psem("lff_sty1")]
        for tb in range(4):
            t0 = tb * 512
            xsl = [xw[:, c * TW_F + 128 + t0 - (1 if c >= 4 else 0):
                      c * TW_F + 128 + t0 + 512 - (1 if c >= 4 else 0)]
                   for c in range(8)]
            w = [("vector", P.done["tensor"], P.done["tensor"].n)]
            emit_ln(P, com, None, 512, "bf16", out=xn,
                    tag=tag + "ffln%d" % tb, waits=w, xsl=xsl)
            P.wait(V, P.done["vector"])
            for c in range(8):
                P.emit(V, lambda e, c=c: e.tensor_scalar(
                    xn[:, c * 512:(c + 1) * 512], xn[:, c * 512:(c + 1) * 512],
                    g_col[c], b_col[c], ALU.mult, ALU.add),
                    P.done["vector"] if c == 7 else None, serial=True)
            for jb in range(6):
                P.wait("gpsimd", P.done["tensor"])
                for kc in range(8):
                    P.dma("gpsimd", sla[:, kc * 512:(kc + 1) * 512],
                          w1[kc * 128:(kc + 1) * 128,
                             jb * 512:(jb + 1) * 512], wl)
                    P.dma("gpsimd", slg[:, kc * 512:(kc + 1) * 512],
                          w1[kc * 128:(kc + 1) * 128,
                             3072 + jb * 512:3072 + (jb + 1) * 512], wl)
                P.wait("tensor", wl)
                P.wait("tensor", P.done["vector"])
                for k4 in range(4):
                    kc2 = jb * 4 + k4
                    for kc in range(8):
                        P.emit("tensor", lambda e, k4=k4, kc=kc: e.matmul(
                            ps_a[:], sla[:, kc * 512 + k4 * 128:
                                          kc * 512 + (k4 + 1) * 128],
                            xn[:, kc * 512:(kc + 1) * 512],
                            start=(kc == 0), stop=(kc == 7)))
                    for kc in range(8):
                        P.emit("tensor", lambda e, k4=k4, kc=kc: e.matmul(
                            ps_g[:], slg[:, kc * 512 + k4 * 128:
                                          kc * 512 + (k4 + 1) * 128],
                            xn[:, kc * 512:(kc + 1) * 512],
                            start=(kc == 0), stop=(kc == 7)))
                    P.mark_mm()
                    P.wait("scalar", P.done["tensor"])
                    P.emit("scalar", lambda e, kc2=kc2: e.activation(
                        out=ga[:], in_=ps_g[:], func=AF.Gelu, bias=b1g[kc2]),
                        P.done["scalar"], serial=True)
                    P.wait(V, P.done["scalar"])
                    P.emit(V, lambda e, kc2=kc2: e.tensor_scalar(
                        ta[:], ps_a[:], b1a[kc2], None, ALU.add), serial=True)
                    P.emit(V, lambda e, kc2=kc2: e.tensor_mul(
                        prod[:, kc2 * 512:(kc2 + 1) * 512], ta[:], ga[:]),
                        P.done["vector"], serial=True)
                    P.wait("tensor", P.done["vector"])
            P.wait("sync", P.done["vector"])
            for sm in sty:
                P.wait("sync", sm, sm.n)
            for c in range(8):
                P.dma("sync", xblk[:, c * 512:(c + 1) * 512],
                      x_own[c * 128:(c + 1) * 128, t0:t0 + 512], ldx)
            for oc in range(8):
                b = oc % 2
                P.wait("gpsimd", P.done["tensor"])
                for kc2 in range(24):
                    P.dma("gpsimd", sl2[:, b * 3072 + kc2 * 128:
                                        b * 3072 + (kc2 + 1) * 128],
                          w2[kc2 * 128:(kc2 + 1) * 128,
                             oc * 128:(oc + 1) * 128], w2l[b])
                P.wait("tensor", w2l[b])
                P.wait("tensor", P.done["vector"])
                for kc2 in range(24):
                    P.emit("tensor", lambda e, kc2=kc2, b=b: e.matmul(
                        ps_y[b][:], sl2[:, b * 3072 + kc2 * 128:
                                        b * 3072 + (kc2 + 1) * 128],
                        prod[:, kc2 * 512:(kc2 + 1) * 512],
                        start=(kc2 == 0), stop=(kc2 == 23)))
                P.mark_mm()
                P.wait(V, P.done["tensor"])
                P.wait(V, ldx, ldx.n)
                P.wait(V, sty[b], sty[b].n)
                P.emit(V, lambda e, oc=oc, b=b: e.scalar_tensor_tensor(
                    ytmp[:, b * 512:(b + 1) * 512], ps_y[b][:], b2c[oc],
                    xblk[:, oc * 512:(oc + 1) * 512], ALU.add, ALU.add),
                    P.done["vector"], serial=True)
                P.wait("gpsimd", P.done["vector"])
                P.dma("gpsimd", y_out[oc * 128:(oc + 1) * 128, t0:t0 + 512],
                      ytmp[:, b * 512:(b + 1) * 512], sty[b])
        for sm in sty:
            P.wait("gpsimd", sm)
        P.wait("sync", sty[0])
    if ctx is None:
        P.build()
        return nc
    return None


def prep_l0a(W_qkv):
    Wt = np.ascontiguousarray(W_qkv.T)
    idx = np.zeros(3072, np.int64)
    for ty in range(3):
        for h in range(16):
            for d_ in range(64):
                idx[ty * 1024 + h * 64 + d_] = h * 192 + ty * 64 + d_
    return np.ascontiguousarray(Wt[:, idx])


def prep_l0b(z):
    H, KPH = 16, 32
    qw, kw, vw = z["L0_ffn_q_w"], z["L0_ffn_k_w"], z["L0_ffn_v_w"]
    qb, kb, ow = z["L0_ffn_q_b"], z["L0_ffn_k_b"], z["L0_ffn_o_w"]
    wq_pad = np.zeros((1024, 1024), np.float32)
    wk_pad = np.zeros((1024, 1024), np.float32)
    qb_pad = np.zeros(1024, np.float32)
    kb_pad = np.zeros(1024, np.float32)
    wo_pad = np.zeros((1024, 1024), np.float32)
    for h in range(H):
        sl = slice(h * 64, h * 64 + KPH)
        wq_pad[:, sl] = qw[:, h * 64:h * 64 + KPH]
        qb_pad[sl] = qb[h * 64:h * 64 + KPH]
        wk_pad[:, sl] = kw[:, h * KPH:(h + 1) * KPH]
        kb_pad[sl] = kb[h * KPH:(h + 1) * KPH]
        wo_pad[sl, :] = ow[h * KPH:(h + 1) * KPH, :]
    bcols = ([qb_pad[i * 128:(i + 1) * 128] for i in range(8)]
             + [kb_pad[i * 128:(i + 1) * 128] for i in range(8)]
             + [z["L0_ffn_o_b"][i * 128:(i + 1) * 128] for i in range(8)]
             + [z["L0_ffn_b1"][e, hc * 128:(hc + 1) * 128]
                for e in range(8) for hc in range(16)]
             + [z["L0_ffn_b2"][e, oc * 128:(oc + 1) * 128]
                for e in range(8) for oc in range(8)])
    return {
        "wqkvb": np.ascontiguousarray(
            np.concatenate([wq_pad, wk_pad, vw], axis=1)),
        "wob": np.ascontiguousarray(wo_pad),
        "wrt": np.ascontiguousarray(z["L0_ffn_router_w"]),
        "w1": z["L0_ffn_w1"], "w2": z["L0_ffn_w2"],
        "bpack": np.ascontiguousarray(np.stack(bcols, 1).astype(np.float32)),
        "vbr": np.ascontiguousarray(z["L0_ffn_v_b"][None, :]),
        "rbr": np.ascontiguousarray(z["L0_ffn_router_b"][None, :]),
        "ident": np.eye(128, dtype=np.float32),
    }


def prep_l1a(z, core):
    H, hd, BASE = 16, 64, 10000.0
    b, q = core // 4, core % 4
    theta = (BASE ** (-2.0 / 1024 * np.arange(1024, dtype=np.float64))
             ).reshape(H, hd)
    pos = np.arange(q * T, (q + 1) * T, dtype=np.float64)
    ang = pos[None, :, None] * theta[:, None, :]
    c = np.cos(ang).astype(np.float32)
    s = np.sin(ang).astype(np.float32)
    hsel = np.zeros((128, 128), np.float32)
    for cch in range(8):
        for p in range(128):
            hsel[p, cch * 16 + 2 * cch + p // 64] = 1.0
    return {
        "wq": z["L1_attn_wq"], "wk": z["L1_attn_wk"], "wv": z["L1_attn_wv"],
        "cfm": np.ascontiguousarray(c.transpose(0, 2, 1).reshape(1024, T)),
        "sfm": np.ascontiguousarray(s.transpose(0, 2, 1).reshape(1024, T)),
        "ctm": np.ascontiguousarray(c), "stm": np.ascontiguousarray(s),
        "utri": np.triu(np.ones((128, 128), np.float32)),
        "wg1": z["L1_attn_wg1"], "wg2": z["L1_attn_wg2"],
        "wo": z["L1_attn_wo"],
        "bpk": np.ascontiguousarray(
            z["L1_attn_norm_w"].reshape(8, 128).T.astype(np.float32)),
        "wselp": np.tile((np.arange(4) < q).astype(np.float32), (128, 1)),
        "hsel": hsel, "ident": np.eye(128, dtype=np.float32),
    }


def _l1f_rotab(gpos):
    DIM_HEAD, WINDOW = 64, 256
    half = np.arange(0, DIM_HEAD, 2, dtype=np.float64)
    inv_freq = 1.0 / 10000.0 ** (half / DIM_HEAD)
    fr = gpos[:, None] * inv_freq[None, :]
    fr = np.concatenate([fr, fr], -1)
    base_scale = (half + 0.4 * DIM_HEAD) / (1.4 * DIM_HEAD)
    power = (gpos - 8192 // 2) / (WINDOW // 2)
    sc = base_scale[None, :] ** power[:, None]
    sc = np.concatenate([sc, sc], -1)
    return fr, sc


def prep_l1f_attn(z, core, li):
    b, q = core // 4, core % 4
    Wq = z["L1_ffn_Wqkv"][li]
    qs, ks = z["L1_ffn_q_scale"][li], z["L1_ffn_k_scale"][li]
    pair = np.zeros((1024, 16 * 384), np.float32)
    for hp in range(8):
        for j in range(2):
            h = 2 * hp + j
            pair[:, hp * 384 + j * 64:hp * 384 + (j + 1) * 64] = \
                Wq[:, h * 64:(h + 1) * 64]
            pair[:, hp * 384 + 128 + j * 64:hp * 384 + 128 + (j + 1) * 64] = \
                Wq[:, 1024 + h * 64:1024 + (h + 1) * 64]
            pair[:, hp * 384 + 256 + j * 64:hp * 384 + 256 + (j + 1) * 64] = \
                Wq[:, 2048 + h * 64:2048 + (h + 1) * 64]
    gpos = q * T + np.arange(TW_A, dtype=np.float64) - 512.0
    fr, sc = _l1f_rotab(gpos)
    sgn = np.ones(64)
    sgn[:32] = -1.0
    csq = (np.cos(fr) * sc).T.astype(np.float32)
    snq = (np.sin(fr) * sc * sgn[:, None].T).T.astype(np.float32)
    csk = (np.cos(fr) / sc).T.astype(np.float32)
    snk = (np.sin(fr) / sc * sgn[None, :]).T.astype(np.float32)
    tabs = np.concatenate([np.tile(t, (2, 1))
                           for t in (csq, snq, csk, snk)], 0)
    mv = np.full((128, 1024), -1e30, np.float32)
    for m in range(4):
        for a in range(2):
            for jj in range(128):
                for ii in range(128):
                    r = 256 + a * 128 + ii - m * 128 - jj
                    if 0 <= r <= 256:
                        mv[jj, m * 256 + a * 128 + ii] = 0.0
    mf = mv.copy()
    if q == 0:
        mf[:, 0:512] = -1e30
    masks = np.concatenate([mv, mf], 0)
    consts = np.zeros((128, 134), np.float32)
    consts[:, 0] = np.tile(qs, 2)
    consts[:, 1] = np.tile(ks, 2)
    for p in range(128):
        consts[p, 2 + p // 64] = 1.0
    psw = np.zeros((128, 128), np.float32)
    for d_ in range(128):
        e_ = (d_ // 64) * 64 + (d_ % 64 + 32) % 64
        psw[e_, d_] = 1.0
    consts[:, 6:134] = psw
    pairbc = np.zeros((2, 128), np.float32)
    pairbc[0, 0:64] = 1.0
    pairbc[1, 64:128] = 1.0
    return {"wqkv": np.ascontiguousarray(pair),
            "wo": z["L1_ffn_Wo"][li],
            "tabs": np.ascontiguousarray(tabs.astype(np.float32)),
            "masks": np.ascontiguousarray(masks),
            "consts": consts, "pairbc": pairbc}


def prep_l1f_ffn(z, li):
    W1, W2 = z["L1_ffn_W1"][li], z["L1_ffn_W2"][li]
    b1 = z["L1_ffn_b1"][li]
    w1p = np.zeros((1024, 6144), np.float32)
    w1p[:, 0:2730] = W1[:, 0:2730]
    w1p[:, 3072:3072 + 2730] = W1[:, 2730:5460]
    w2p = np.zeros((3072, 1024), np.float32)
    w2p[0:2730, :] = W2
    b1a = np.zeros(3072, np.float32)
    b1g = np.zeros(3072, np.float32)
    b1a[0:2730] = b1[0:2730]
    b1g[0:2730] = b1[2730:5460]
    bcols = ([z["L1_ffn_ln_g"][li][i * 128:(i + 1) * 128] for i in range(8)]
             + [z["L1_ffn_ln_b"][li][i * 128:(i + 1) * 128] for i in range(8)]
             + [b1a[i * 128:(i + 1) * 128] for i in range(24)]
             + [b1g[i * 128:(i + 1) * 128] for i in range(24)]
             + [z["L1_ffn_b2"][li][i * 128:(i + 1) * 128] for i in range(8)])
    return {"w1p": w1p, "w2p": w2p,
            "bpk": np.ascontiguousarray(np.stack(bcols, 1).astype(np.float32))}


def build_sandwich_ln(mode, ctx=None, tag=""):
    if ctx is None:
        nc = bass.Bass("TRN2", target_bir_lowering=False, debug=False)
        xin = nc.dram_tensor("xin", [D, T], F32, kind="ExternalInput").ap()
        xold = None
        if mode == "post":
            xold = nc.dram_tensor("xold", [D, T], F32, kind="ExternalInput").ap()
        out = nc.dram_tensor("out", [D, T], F32, kind="ExternalOutput").ap()
        P = Prog(nc)
        com = Common(P, Tn=512, with_ln_xb=False)
        sc0 = P.stack
    else:
        nc, P, com, sc0 = ctx["nc"], ctx["P"], ctx["com"], ctx["sc"]
        io = ctx["io"]
        xin, xold, out = io["xin"], io.get("xold"), io["out"]
    sb0 = lambda n_, sh_, dt_: sc0.enter_context(nc.sbuf_tensor(n_, sh_, dt_))
    xt = sb0(tag + "xt_s", [128, 8 * T], BF16)
    ot = sb0(tag + "ot_s", [128, 8 * T], F32)
    ld = P.psem("sln_ld")
    st = P.psem("sln_st")
    P.wait("gpsimd", P.done["vector"])
    P.dma("gpsimd", xt[:].rearrange("p (c t) -> p c t", c=8),
          xin[:, :].rearrange("(c p) t -> p c t", p=128), ld)
    if mode == "post":
        P.wait("sync", P.done["vector"])
        P.dma("sync", ot[:].rearrange("p (c t) -> p c t", c=8),
              xold[:, :].rearrange("(c p) t -> p c t", p=128), ld)
    w = [(e, ld, ld.n) for e in ("vector", "scalar", "tensor")]
    if mode == "pre":
        P.emit("vector", lambda e: e.memset(ot[:], 0.0), P.done["vector"])
    emit_ln_big(P, com, xt[:], T, "residual", xold=ot, xnew=ot,
                tag=tag + "sln", waits=w, x_bf16=True)
    P.wait("sync", P.done["vector"])
    P.dma("sync", out[:, :].rearrange("(c p) t -> p c t", p=128),
          ot[:].rearrange("p (c t) -> p c t", c=8), st)
    P.wait("sync", st)
    if ctx is None:
        P.build()
        return nc
    return None


def l0a_kv_loader_fused(P, ktsg, vsg, HDv=65):
    W = 4 * HDv

    def loader(hg, K_sb, V_sb, sem):
        for cc in range(2):
            for r in range(4):
                P.dma("gpsimd",
                      K_sb[:, cc * SEQ + r * T:cc * SEQ + (r + 1) * T],
                      ktsg[r * 1024 + hg * 256 + cc * 128:
                           r * 1024 + hg * 256 + (cc + 1) * 128, :], sem)
        vsr = vsg.rearrange("(kb p) w -> p kb w", p=128)
        vdr = V_sb[:].rearrange("p (kb w) -> p kb w", w=W)
        for i in range(8):
            P.dma("gpsimd", vdr[:, i * 8:(i + 1) * 8, :],
                  vsr[:, i * 8:(i + 1) * 8, hg * W:(hg + 1) * W], sem)
    return loader


def build_fused():
    import types
    nc = bass.Bass("TRN2", target_bir_lowering=False, debug=False)
    EI = lambda n_, sh_: nc.dram_tensor(n_, sh_, F32, kind="ExternalInput").ap()
    x0own = EI("x0own", [D, T])
    wqkv_a = EI("wqkv_a", [D, 3072])
    wqkvb = EI("wqkvb", [D, 2560])
    wob = EI("wob", [D, D])
    wrt = EI("wrt", [D, 8])
    w1 = EI("w1", [8, D, 2048])
    w2 = EI("w2", [8, 2048, D])
    bpack = EI("bpack", [128, 216])
    vbr = EI("vbr", [1, 512])
    rbr = EI("rbr", [1, 8])
    ident = EI("ident", [128, 128])
    wq = EI("wq", [D, D]); wk = EI("wk", [D, D]); wv = EI("wv", [D, D])
    cfm = EI("cfm", [D, T]); sfm = EI("sfm", [D, T])
    ctm = EI("ctm", [16, T, 64]); stm = EI("stm", [16, T, 64])
    utri = EI("utri", [128, 128])
    wg1 = EI("wg1", [D, 64]); wg2 = EI("wg2", [64, D]); wo_l1 = EI("wo_l1", [D, D])
    bpk_l1 = EI("bpk_l1", [128, 8])
    wselp = EI("wselp", [128, 4])
    whalo = EI("whalo", [128, 4])
    hsel = EI("hsel", [128, 128])
    wqkv6 = EI("wqkv6", [6, D, 16 * 384])
    wo6 = EI("wo6", [6, D, D])
    tabs = EI("tabs", [4 * 128, TW_A])
    masks = EI("masks", [2 * 128, 1024])
    consts6 = EI("consts6", [6, 128, 134])
    pairbc = EI("pairbc", [2, 128])
    w1p6 = EI("w1p6", [6, D, 6144])
    w2p6 = EI("w2p6", [6, 3072, D])
    bpk6 = EI("bpk6", [6, 128, 72])
    out = nc.dram_tensor("out", [D, T], F32, kind="ExternalOutput").ap()
    IN = lambda n_, sh_, dt_=BF16: nc.dram_tensor(n_, sh_, dt_).ap()
    kts_l = IN("kts_l", [D, T]); vs_l = IN("vs_l", [T, 16 * 65])
    ktsg = IN("ktsg", [4 * D, T]); vsg = IN("vsg", [SEQ, 16 * 65])
    kts2_l = IN("kts2_l", [D, T]); vs2_l = IN("vs2_l", [T, 16 * 33])
    kts2g = IN("kts2g", [4 * D, T]); vs2g = IN("vs2g", [SEQ, 16 * 33])
    sig_l = IN("sig_l", [1, 1024], F32); S_l = IN("S_l", [16 * 128, 64], F32)
    sig_g = IN("sig_g", [4, 1024], F32); S_g = IN("S_g", [4 * 16 * 128, 64], F32)
    x1_d = IN("x1_d", [D, T], F32); x2_d = IN("x2_d", [D, T], F32)
    x3_d = IN("x3_d", [D, T], F32)
    xsd = [IN("xs_d%d" % i, [D, T], F32) for i in range(2)]
    halo_l = IN("halo_l", [D, 257], F32); halo_g = IN("halo_g", [4 * D, 257], F32)
    h1_l = IN("h1_l", [D, 1], F32); h1_g = IN("h1_g", [4 * D, 1], F32)

    P = Prog(nc)
    com = Common(P, Tn=512, with_ln_xb=True)
    wsl = P.sb("wsl_f", [128, 4], F32)
    wsl_h = P.sb("wsl_h", [128, 4], F32)
    ldw = P.psem("ldw")
    P.dma("sync", wsl[:], wselp[:, :], ldw)
    P.dma("sync", wsl_h[:], whalo[:, :], ldw)
    for eng in ("vector", "gpsimd", "tensor", "scalar"):
        P.wait(eng, ldw)

    with contextlib.ExitStack() as sc:
        A = types.SimpleNamespace(
            tag="l0a", wqkv=wqkv_a, kts=kts_l, vs=vs_l, ktsg=ktsg, vsg=vsg,
            ag_groups=AG_GROUPS, stage="full",
            kv_slices=[(x0own, 0), (x0own, 1024)],
            q_slices=[(x0own, 0, 0), (x0own, 1024, 1024)],
            Q_fm=sc.enter_context(nc.sbuf_tensor("l0aQ", [128, 8 * T], BF16)),
            o_fm=sc.enter_context(nc.sbuf_tensor("l0aO", [128, 8 * T], BF16)),
            kv_loader=l0a_kv_loader_fused(P, ktsg, vsg),
            x_resid=x0own, x_out=x1_d)
        emit_l0a(P, com, A)
    P.fence()

    with contextlib.ExitStack() as sc:
        io = dict(xin=x1_d, wqkvb=wqkvb, wob=wob, wrt=wrt, w1=w1, w2=w2,
                  bpack=bpack, vbr=vbr, rbr=rbr, ident=ident, x2=x2_d,
                  kts=kts2_l, vs=vs2_l)
        build_l0b_ckpt(ctx=dict(
            nc=nc, P=P, com=com, sc=sc, io=io,
            kv_slices=[(x1_d, 0), (x1_d, 1024)],
            kv_loader=l0a_kv_loader_fused(P, kts2g, vs2g, HDv=33),
            ag=[(kts2_l, kts2g), (vs2_l, vs2g)]))
    P.fence()

    l1a_io = dict(xin=x2_d, wq=wq, wk=wk, wv=wv, cfm=cfm, sfm=sfm, ctm=ctm,
                  stm=stm, utri=utri)
    with contextlib.ExitStack() as sc:
        io = dict(l1a_io)
        io.update(sig_o=sig_l, S_o=S_l)
        build_l1a("partial", ctx=dict(nc=nc, P=P, com=com, sc=sc, io=io))
    emit_ag(P, [(sig_l, sig_g), (S_l, S_g)],
            wait_sems=[P.psem("sto"), P.psem("l1a_sto")])
    P.fence()
    with contextlib.ExitStack() as sc:
        io = dict(l1a_io)
        io.update(wg1=wg1, wg2=wg2, wo=wo_l1, bpk=bpk_l1, sig_p=sig_g,
                  S_p=S_g.rearrange("(r x) d_ -> r x d_", r=4), wselp=wselp,
                  hsel=hsel[:, 0:128], ident=ident, x3=x3_d)
        build_l1a("full", ctx=dict(nc=nc, P=P, com=com, sc=sc, io=io))
    P.fence()

    with contextlib.ExitStack() as sc:
        build_sandwich_ln("pre", ctx=dict(
            nc=nc, P=P, com=com, sc=sc, io=dict(xin=x3_d, out=xsd[0])),
            tag="pre")
    P.fence()
    cur, nxt = xsd[0], xsd[1]
    hst = P.psem("halo_st")
    for li in range(6):
        P.wait("gpsimd", P.done["vector"])
        for nm in ("lff_sty0", "lff_sty1", "sln_st"):
            P.wait("gpsimd", P.psem(nm))
        P.dma("gpsimd", halo_l[:, :], cur[:, T - 257:T], hst)
        emit_ag(P, [(halo_l, halo_g)], wait_sems=[hst])
        with contextlib.ExitStack() as sc:
            io = dict(x_own=cur, wqkv=wqkv6[li], wo=wo6[li], tabs=tabs,
                      masks=masks, consts=consts6[li], pairbc=pairbc,
                      y_out=nxt)
            build_l1f_attn(ctx=dict(nc=nc, P=P, com=com, sc=sc, io=io,
                                    halo=(halo_g, wsl_h)), tag="a%d_" % li)
        P.fence()
        P.wait("gpsimd", P.done["vector"])
        for nm in ("lfa_sty0", "lfa_sty1"):
            P.wait("gpsimd", P.psem(nm))
        P.dma("gpsimd", h1_l[:, :], nxt[:, T - 1:T], hst,
              allow_slow_non_contiguous=True)
        emit_ag(P, [(h1_l, h1_g)], wait_sems=[hst])
        with contextlib.ExitStack() as sc:
            io = dict(x_own=nxt, w1=w1p6[li], w2=w2p6[li], bpk=bpk6[li],
                      y_out=cur)
            build_l1f_ffn(ctx=dict(nc=nc, P=P, com=com, sc=sc, io=io,
                                   halo=(h1_g, wsl_h)), tag="f%d_" % li)
        P.fence()
    with contextlib.ExitStack() as sc:
        build_sandwich_ln("post", ctx=dict(
            nc=nc, P=P, com=com, sc=sc,
            io=dict(xin=cur, xold=x3_d, out=out)), tag="post")
    P.build()
    return nc


FUSED = True


def _prep_core_inputs(z, core):
    b, q = core // 4, core % 4
    m = {}
    X = z["L0_attn_z"].astype(np.float32)
    m["x0own"] = np.ascontiguousarray(X[b, q * T:(q + 1) * T].T)
    m["wqkv_a"] = prep_l0a(z["L0_attn_W_qkv"])
    m.update(prep_l0b(z))
    p1 = prep_l1a(z, core)
    for k_, v_ in p1.items():
        if k_ == "wo":
            m["wo_l1"] = v_
        elif k_ == "bpk":
            m["bpk_l1"] = v_
        else:
            m[k_] = v_
    pa0 = prep_l1f_attn(z, core, 0)
    m["tabs"] = pa0["tabs"]
    m["masks"] = pa0["masks"]
    m["pairbc"] = pa0["pairbc"]
    cons, wq6, wo6l = [], [], []
    for li in range(6):
        pa = prep_l1f_attn(z, core, li)
        cons.append(pa["consts"])
        wq6.append(pa["wqkv"])
        wo6l.append(pa["wo"])
    m["consts6"] = np.ascontiguousarray(np.stack(cons, 0))
    m["wqkv6"] = np.ascontiguousarray(np.stack(wq6, 0))
    m["wo6"] = np.ascontiguousarray(np.stack(wo6l, 0))
    w1l, w2l, bpl = [], [], []
    for li in range(6):
        pf = prep_l1f_ffn(z, li)
        w1l.append(pf["w1p"])
        w2l.append(pf["w2p"])
        bpl.append(pf["bpk"])
    m["w1p6"] = np.ascontiguousarray(np.stack(w1l, 0))
    m["w2p6"] = np.ascontiguousarray(np.stack(w2l, 0))
    m["bpk6"] = np.ascontiguousarray(np.stack(bpl, 0))
    q = core % 4
    m["whalo"] = np.tile((np.arange(4) == q - 1).astype(np.float32), (128, 1))
    return m


def kernel(**z):
    z = {k: np.asarray(v) for k, v in z.items()}
    if FUSED:
        nc = build_fused()
        in_maps = [_prep_core_inputs(z, core) for core in range(NC)]
        res = run_bass_kernel_spmd(nc, in_maps, core_ids=list(range(NC)))
        out = np.zeros((2, SEQ, D), np.float32)
        for core in range(NC):
            b, q = core // 4, core % 4
            out[b, q * T:(q + 1) * T, :] = res.results[core]["out"].T
        return out
    return kernel_ckpt(**z)


def kernel_ckpt(**z):


    z = {k: np.asarray(v) for k, v in z.items()}

    def fm(a):
        return np.ascontiguousarray(np.asarray(a).T)

    X = z["L0_attn_z"].astype(np.float32)

    wq = prep_l0a(z["L0_attn_W_qkv"])
    nc = build_l0a_ckpt()
    in_maps = []
    for core in range(NC):
        b, q = core // 4, core % 4
        in_maps.append({"xfull": fm(X[b]),
                        "xown": fm(X[b, q * T:(q + 1) * T]),
                        "wqkv": wq})
    res = run_bass_kernel_spmd(nc, in_maps, core_ids=list(range(NC)))
    x1s = [res.results[c]["x1"] for c in range(NC)]
    x1f = [np.ascontiguousarray(np.concatenate(x1s[4 * b:4 * b + 4], axis=1))
           for b in range(2)]

    p0b = prep_l0b(z)
    nc = build_l0b_ckpt()
    in_maps = []
    for core in range(NC):
        b, q = core // 4, core % 4
        m = dict(p0b)
        m["xin"] = np.ascontiguousarray(x1s[core])
        m["xfull1"] = x1f[b]
        in_maps.append(m)
    res = run_bass_kernel_spmd(nc, in_maps, core_ids=list(range(NC)))
    x2s = [res.results[c]["x2"] for c in range(NC)]

    preps = [prep_l1a(z, core) for core in range(NC)]
    nc = build_l1a("partial")
    in_maps = []
    for core in range(NC):
        m = {k: preps[core][k] for k in
             ("wq", "wk", "wv", "cfm", "sfm", "ctm", "stm", "utri")}
        m["xin"] = np.ascontiguousarray(x2s[core])
        in_maps.append(m)
    res = run_bass_kernel_spmd(nc, in_maps, core_ids=list(range(NC)))
    sig = [res.results[c]["sig_o"] for c in range(NC)]
    Sp = [res.results[c]["S_o"] for c in range(NC)]

    nc = build_l1a("full")
    in_maps = []
    for core in range(NC):
        b, q = core // 4, core % 4
        m = dict(preps[core])
        m["xin"] = np.ascontiguousarray(x2s[core])
        m["sig_p"] = np.ascontiguousarray(
            np.concatenate([sig[4 * b + r] for r in range(4)], 0))
        m["S_p"] = np.ascontiguousarray(
            np.stack([Sp[4 * b + r] for r in range(4)], 0))
        in_maps.append(m)
    res = run_bass_kernel_spmd(nc, in_maps, core_ids=list(range(NC)))
    x3s = [res.results[c]["x3"] for c in range(NC)]

    nc = build_sandwich_ln("pre")
    in_maps = [{"xin": np.ascontiguousarray(x3s[c])} for c in range(NC)]
    res = run_bass_kernel_spmd(nc, in_maps, core_ids=list(range(NC)))
    xs = [res.results[c]["out"] for c in range(NC)]

    nca = build_l1f_attn()
    ncf = build_l1f_ffn()
    for li in range(6):
        in_maps = []
        for core in range(NC):
            b, q = core // 4, core % 4
            m = dict(prep_l1f_attn(z, core, li))
            m["x_own"] = np.ascontiguousarray(xs[core])
            xh = np.zeros((D, 512), np.float32)
            if q > 0:
                xh[:, 255:512] = xs[core - 1][:, T - 257:T]
            m["x_halo"] = xh
            in_maps.append(m)
        res = run_bass_kernel_spmd(nca, in_maps, core_ids=list(range(NC)))
        ys = [res.results[c]["y"] for c in range(NC)]

        pf = prep_l1f_ffn(z, li)
        in_maps = []
        for core in range(NC):
            b, q = core // 4, core % 4
            m = dict(pf)
            m["x_own"] = np.ascontiguousarray(ys[core])
            xh = np.zeros((D, 128), np.float32)
            if q > 0:
                xh[:, 127] = ys[core - 1][:, T - 1]
            m["x_h1"] = xh
            in_maps.append(m)
        res = run_bass_kernel_spmd(ncf, in_maps, core_ids=list(range(NC)))
        xs = [res.results[c]["y"] for c in range(NC)]

    nc = build_sandwich_ln("post")
    in_maps = [{"xin": np.ascontiguousarray(xs[c]),
                "xold": np.ascontiguousarray(x3s[c])} for c in range(NC)]
    res = run_bass_kernel_spmd(nc, in_maps, core_ids=list(range(NC)))
    xs = [res.results[c]["out"] for c in range(NC)]

    out = np.zeros((2, SEQ, D), np.float32)
    for core in range(NC):
        b, q = core // 4, core % 4
        out[b, q * T:(q + 1) * T, :] = xs[core].T
    return out


# version 04
# speedup vs baseline: 1.1136x; 1.1136x over previous
import contextlib
import math

import numpy as np
import ml_dtypes
import concourse.bass as bass
import concourse.mybir as mybir
from concourse.bass_utils import run_bass_kernel_spmd

F32 = mybir.dt.float32
BF16 = mybir.dt.bfloat16
AF = mybir.ActivationFunctionType
ALU = mybir.AluOpType
AX = mybir.AxisListType

D = 1024
T = 2048
NC = 8
NQ = 4
SEQ = 8192
LN_EPS = 1e-5


class Sem:
    __slots__ = ("h", "n")

    def __init__(self, h):
        self.h = h
        self.n = 0


class Prog:
    ENGINES = ("sync", "scalar", "tensor", "vector", "gpsimd")

    def __init__(self, nc):
        self.nc = nc
        self.ops = {e: [] for e in self.ENGINES}
        self.stack = contextlib.ExitStack()
        self.bar = self.sem("bar")
        self.done = {e: self.sem("done_" + e) for e in self.ENGINES}

    def sem(self, name):
        return Sem(self.stack.enter_context(self.nc.semaphore(name)))

    def psem(self, name):
        d = getattr(self, "_psems", None)
        if d is None:
            d = self._psems = {}
        if name not in d:
            d[name] = self.sem(name)
        return d[name]

    def sb(self, name, shape, dtype):
        return self.stack.enter_context(self.nc.sbuf_tensor(name, shape, dtype))

    def psum(self, name, shape, dtype=F32):
        return self.stack.enter_context(self.nc.psum_tensor(name, shape, dtype))

    def emit(self, eng, fn, sem=None, k=1, serial=False):
        if serial:
            self.wait(eng, self.done[eng])
            sem = sem or self.done[eng]
        if sem is not None:
            sem.n += k
            self.ops[eng].append(lambda e, fn=fn, s=sem.h, k=k: fn(e).then_inc(s, k))
        else:
            self.ops[eng].append(fn)

    def wait(self, eng, sem, val=None):
        v = sem.n if val is None else val
        if v <= 0:
            return
        self.ops[eng].append(lambda e, s=sem.h, v=v: e.wait_ge(s, v))

    def dma(self, eng, out, in_, sem, **kw):
        self.emit(eng, lambda e, o=out, i=in_, kw=kw: e.dma_start(out=o, in_=i, **kw),
                  sem, 16)

    def mark(self, eng):
        self.emit(eng, lambda e: e.wait_ge(self.bar.h, 0), self.done[eng], 1)

    def mark_mm(self, tag=""):
        ops = self.ops["tensor"]
        fn = ops.pop()
        self.emit("tensor", fn, self.done["tensor"], 1)

    def fence(self, engines=None):
        engines = tuple(engines or self.ENGINES)
        base = self.bar.n
        for eng in engines:
            self.emit(eng, lambda e: e.wait_ge(self.bar.h, 0), self.bar, 1)
        tgt = base + len(engines)
        for eng in engines:
            self.wait(eng, self.bar, tgt)

    def build(self):
        with self.nc.Block() as block:
            for eng in self.ENGINES:
                ops = self.ops[eng]
                if ops:
                    getattr(block, eng)(lambda e, ops=ops: [f(e) for f in ops])
        return self.nc


def bf16(a):
    return np.asarray(a).astype(ml_dtypes.bfloat16).astype(np.float32)


AG_GROUPS = [[0, 1, 2, 3], [4, 5, 6, 7]]


def emit_ag(P, pairs, wait_sems=()):
    cc = P.psem("cc")
    for sm in wait_sems:
        P.wait("gpsimd", sm)
    for in_ap, out_ap in pairs:
        P.emit("gpsimd", lambda e, i=in_ap, o=out_ap: e.collective_compute(
            "AllGather", ALU.bypass, replica_groups=AG_GROUPS,
            ins=[i.opt()], outs=[o.opt()]), cc)
    P.wait("gpsimd", cc)


class Common:
    def __init__(self, P, Tn=T, with_ln_xb=True):
        self.Tn = Tn
        self.ones128 = P.sb("c_ones128", [128, 1], BF16)
        self.ones1 = P.sb("c_ones1", [1, 128], BF16)
        self.ones1f = P.sb("c_ones1f", [1, 128], F32)
        self.onesP = P.sb("c_onesP", [128, 128], F32)
        self.onesP_bf = P.sb("c_onesPb", [128, 128], BF16)
        P.emit("vector", lambda e, t=self.ones128: e.memset(t[:], 1.0))
        P.emit("vector", lambda e, t=self.ones1: e.memset(t[:], 1.0))
        P.emit("vector", lambda e, t=self.ones1f: e.memset(t[:], 1.0))
        P.emit("vector", lambda e, t=self.onesP: e.memset(t[:], 1.0))
        P.emit("vector", lambda e, t=self.onesP_bf: e.memset(t[:], 1.0),
               P.done["vector"], 1)
        self._P = P
        self._cols = {}
        self.xsq = P.sb("ln_xsq", [128, 2 * Tn], BF16)
        self.m_row = P.sb("ln_m", [1, Tn], F32)
        self.w_row = P.sb("ln_w", [1, Tn], F32)
        self.rows_bf = P.sb("ln_rows_bf", [1, 2 * Tn], BF16)
        self.tmp = P.sb("ln_tmp", [128, Tn], F32)
        self.ln_xb = P.sb("ln_xb", [128, 8 * 512], BF16) if with_ln_xb else None
        self.col(LN_EPS)

    def col(self, val):
        if val not in self._cols:
            t = self._P.sb("c_col_%d" % len(self._cols), [128, 1], F32)
            self._P.emit("vector", lambda e, t=t, v=val: e.memset(t[:], v),
                         self._P.done["vector"], 1)
            self._cols[val] = t
        return self._cols[val][:, 0:1]


def emit_ln(P, com, x, Tn, mode, out=None, xold=None, xnew=None, tag="ln",
            waits=(), x_bf16=False, xsl=None, out_sl=None, xold_sl=None,
            xnew_sl=None):
    nc = P.nc
    NG = Tn // 512
    assert Tn <= com.Tn
    if xsl is None:
        xsl = [x[:, c * Tn:(c + 1) * Tn] for c in range(8)]
    if out_sl is None and out is not None:
        out_sl = [out[:, c * Tn:(c + 1) * Tn] for c in range(8)]
    if xold_sl is None and xold is not None:
        xold_sl = [xold[:, c * Tn:(c + 1) * Tn] for c in range(8)]
    if xnew_sl is None and xnew is not None:
        xnew_sl = [xnew[:, c * Tn:(c + 1) * Tn] for c in range(8)]
    xb = x if x_bf16 else (out if mode == "bf16" else com.ln_xb)
    if mode == "residual" and not x_bf16:
        assert Tn <= 512, "ln_xb scratch sized for Tn<=512"
    xsq, tmp = com.xsq, com.tmp
    m_row, w_row = com.m_row[:, 0:Tn], com.w_row[:, 0:Tn]
    rstd_bf = com.rows_bf[:, 0:Tn]
    mr_bf = com.rows_bf[:, Tn:2 * Tn]
    epsb = com.col(LN_EPS)[0:1, :]

    for eng, s, v in waits:
        P.wait(eng, s, v)

    if not x_bf16:
        xbsl = (out_sl if mode == "bf16" else
                [xb[:, c * Tn:(c + 1) * Tn] for c in range(8)])
        for c in range(8):
            P.emit("vector", lambda e, c=c: e.tensor_copy(xbsl[c], xsl[c]),
                   P.done["vector"] if c == 7 else None)

    with contextlib.ExitStack() as sp:
        ps_s = sp.enter_context(nc.psum_tensor(tag + "_ps_s", [1, Tn], F32))
        ps_q = sp.enter_context(nc.psum_tensor(tag + "_ps_q", [1, Tn], F32))
        P.wait("scalar", P.done["vector"])
        P.wait("tensor", P.done["vector"])
        pe_base = P.done["tensor"].n
        for c in range(8):
            P.wait("scalar", P.done["tensor"],
                   pe_base if c < 2 else pe_base + c - 1)
            xbc = (xsl[c] if x_bf16 else
                   (out_sl[c] if mode == "bf16"
                    else xb[:, c * Tn:(c + 1) * Tn]))
            P.emit("scalar", lambda e, c=c, xbc=xbc: e.activation(
                out=xsq[:, (c % 2) * Tn:(c % 2) * Tn + Tn],
                in_=xbc, func=AF.Square),
                P.done["scalar"])
            P.wait("tensor", P.done["scalar"])
            for g in range(NG):
                xbg = (xsl[c][:, g * 512:(g + 1) * 512] if x_bf16
                       else (out_sl[c][:, g * 512:(g + 1) * 512]
                             if mode == "bf16" else
                             xb[:, c * Tn + g * 512:c * Tn + (g + 1) * 512]))
                P.emit("tensor", lambda e, c=c, g=g, xbg=xbg: e.matmul(
                    ps_s[:, g * 512:(g + 1) * 512], com.ones128[:],
                    xbg, start=(c == 0), stop=(c == 7)))
                P.emit("tensor", lambda e, c=c, g=g: e.matmul(
                    ps_q[:, g * 512:(g + 1) * 512], com.ones128[:],
                    xsq[:, (c % 2) * Tn + g * 512:(c % 2) * Tn + (g + 1) * 512],
                    start=(c == 0), stop=(c == 7)),
                    P.done["tensor"] if g == NG - 1 else None)

        P.wait("scalar", P.done["tensor"])
        P.emit("scalar", lambda e: e.activation(out=m_row, in_=ps_s[:],
                                                func=AF.Copy, scale=1.0 / D),
               serial=True)
        P.emit("scalar", lambda e: e.activation(out=w_row, in_=m_row,
                                                func=AF.Square), P.done["scalar"],
               serial=True)
        P.wait("vector", P.done["scalar"])
        P.emit("vector", lambda e: e.scalar_tensor_tensor(
            w_row, ps_q[:], 1.0 / D, w_row, ALU.mult, ALU.subtract),
            P.done["vector"])
        P.wait("scalar", P.done["vector"])
        P.emit("scalar", lambda e: e.activation(out=w_row, in_=w_row,
                                                func=AF.Ln, bias=epsb),
               serial=True)
        P.emit("scalar", lambda e: e.activation(out=w_row, in_=w_row,
                                                func=AF.Exp, scale=-0.5),
               P.done["scalar"], serial=True)
        P.wait("vector", P.done["scalar"])
        P.emit("vector", lambda e: e.tensor_copy(rstd_bf, w_row), serial=True)
        P.emit("vector", lambda e: e.tensor_mul(mr_bf, m_row, w_row),
               P.done["vector"], serial=True)

    with contextlib.ExitStack() as sp:
        rb = sp.enter_context(nc.psum_tensor(tag + "_rb", [128, Tn], F32))
        mb = sp.enter_context(nc.psum_tensor(tag + "_mb", [128, Tn], F32))
        P.wait("tensor", P.done["vector"])
        for g in range(NG):
            P.emit("tensor", lambda e, g=g: e.matmul(
                rb[:, g * 512:(g + 1) * 512], com.ones1[:],
                rstd_bf[:, g * 512:(g + 1) * 512], start=True, stop=True))
            P.emit("tensor", lambda e, g=g: e.matmul(
                mb[:, g * 512:(g + 1) * 512], com.ones1[:],
                mr_bf[:, g * 512:(g + 1) * 512], start=True, stop=True),
                P.done["tensor"] if g == NG - 1 else None)

        P.wait("vector", P.done["tensor"])
        tp = tmp[:, 0:Tn]
        for c in range(8):
            P.emit("vector", lambda e, c=c: e.tensor_mul(
                tp, xsl[c], rb[:]), serial=True)
            if mode == "bf16":
                P.emit("vector", lambda e, c=c: e.tensor_sub(
                    out_sl[c], tp, mb[:]), serial=True)
            else:
                P.emit("vector", lambda e, c=c: e.tensor_sub(tp, tp, mb[:]),
                       serial=True)
                P.emit("vector", lambda e, c=c: e.tensor_add(
                    xnew_sl[c], xold_sl[c], tp), serial=True)


def emit_ln_big(P, com, x, Tfull, mode, out=None, xold=None, xnew=None,
                tag="lnb", waits=(), x_bf16=False):
    for g in range(Tfull // 512):
        def sl(t):
            return (None if t is None else
                    [t[:, c * Tfull + g * 512:c * Tfull + (g + 1) * 512]
                     for c in range(8)])
        emit_ln(P, com, None, 512, mode, out=out, xold=xold, xnew=xnew,
                tag="%s%d" % (tag, g), waits=(waits if g == 0 else ()),
                x_bf16=x_bf16, xsl=sl(x), out_sl=sl(out), xold_sl=sl(xold),
                xnew_sl=sl(xnew))


def load_wslab(P, name, w_dram, IN, OUT, sem):
    slab = P.sb(name, [128, (IN // 128) * OUT], BF16)
    for kc in range(IN // 128):
        P.dma("gpsimd", slab[:, kc * OUT:(kc + 1) * OUT],
              w_dram[kc * 128:(kc + 1) * 128, 0:OUT], sem)
    return slab


def load_wslab_sc(P, sc, name, w_dram, IN, OUT, sem, col0=0):
    slab = sc.enter_context(P.nc.sbuf_tensor(name, [128, (IN // 128) * OUT], BF16))
    for kc in range(IN // 128):
        P.dma("gpsimd", slab[:, kc * OUT:(kc + 1) * OUT],
              w_dram[kc * 128:(kc + 1) * 128, col0:col0 + OUT], sem)
    return slab


def emit_proj_fm(P, com, xb, slab, OUT, oc0, n_oc, Tn, evac, evac_eng,
                 nkc=8, tag="proj", extra_mm=None):
    nc = P.nc
    NG = Tn // 512
    with contextlib.ExitStack() as sp:
        ps = [sp.enter_context(nc.psum_tensor(tag + "_ps%d" % i, [128, 512], F32))
              for i in range(2)]
        idx = 0
        ev = P.done[evac_eng]
        ev_counts = []
        P.wait("tensor", P.done["vector"])
        for oc in range(oc0, oc0 + n_oc):
            for g in range(NG):
                b = idx % 2
                if idx >= 2:
                    P.wait("tensor", ev, ev_counts[idx - 2])
                for kc in range(nkc):
                    P.emit("tensor", lambda e, oc=oc, g=g, kc=kc, b=b: e.matmul(
                        ps[b][:], slab[:, kc * OUT + oc * 128:kc * OUT + (oc + 1) * 128],
                        xb[:, kc * Tn + g * 512:kc * Tn + (g + 1) * 512],
                        start=(kc == 0), stop=(kc == nkc - 1 and extra_mm is None)))
                if extra_mm is not None:
                    P.emit("tensor", lambda e, oc=oc, g=g, b=b: extra_mm(e, oc, g, ps[b]))
                P.mark_mm(tag)
                P.wait(evac_eng, P.done["tensor"])
                evac(oc, g, ps[b][:])
                ev_counts.append(ev.n)
                idx += 1
        P.wait("tensor", ev)


def emit_proj_tm(P, com, xb, slab, OUT, ob0, n_ob, Tn, evac, evac_eng,
                 nkc=8, tag="ptm", extra_mm=None):
    nc = P.nc
    with contextlib.ExitStack() as sp:
        ps = [sp.enter_context(nc.psum_tensor(tag + "_ps%d" % i, [128, 512], F32))
              for i in range(2)]
        idx = 0
        ev = P.done[evac_eng]
        ev_counts = []
        P.wait("tensor", P.done["vector"])
        for tt in range(Tn // 128):
            for ob in range(ob0, ob0 + n_ob):
                b = idx % 2
                if idx >= 2:
                    P.wait("tensor", ev, ev_counts[idx - 2])
                for kc in range(nkc):
                    P.emit("tensor", lambda e, tt=tt, ob=ob, kc=kc, b=b: e.matmul(
                        ps[b][:], xb[:, kc * Tn + tt * 128:kc * Tn + (tt + 1) * 128],
                        slab[:, kc * OUT + ob * 512:kc * OUT + (ob + 1) * 512],
                        start=(kc == 0),
                        stop=(kc == nkc - 1 and extra_mm is None)))
                if extra_mm is not None:
                    P.emit("tensor", lambda e, tt=tt, ob=ob, b=b: extra_mm(e, tt, ob, ps[b]))
                P.mark_mm(tag)
                P.wait(evac_eng, P.done["tensor"])
                evac(tt, ob, ps[b][:])
                ev_counts.append(ev.n)
                idx += 1
        P.wait("tensor", ev)


def emit_flash(P, com, *, Q_fm, o_fm, kv_loader, HDv, scale, tag):
    nc = P.nc
    real = HDv - 1
    n_kb = 64
    ngr = n_kb // 2
    with contextlib.ExitStack() as sc:
        K_sb = sc.enter_context(nc.sbuf_tensor(tag + "_K", [128, 2 * SEQ], BF16))
        V_sb = sc.enter_context(nc.sbuf_tensor(tag + "_V", [128, n_kb * 4 * HDv], BF16))
        pt = sc.enter_context(nc.sbuf_tensor(tag + "_pt", [128, 2 * 1024], BF16))
        rc = sc.enter_context(nc.sbuf_tensor(tag + "_rc", [128, 512], F32))
        rcb = sc.enter_context(nc.sbuf_tensor(tag + "_rcb", [128, 2 * 512], BF16))
        bcb = sc.enter_context(nc.sbuf_tensor(tag + "_bcb", [64, 512], BF16))
        ps_s = sc.enter_context(nc.psum_tensor(tag + "_s", [128, 2 * 1024], F32))
        ps_o = [sc.enter_context(nc.psum_tensor(tag + "_o%d" % i, [HDv, 512], F32))
                for i in range(2)]
        ps_bc = sc.enter_context(nc.psum_tensor(tag + "_bc", [real, 512], F32))
        ld = P.sem(tag + "_ld")
        oi = 0
        epi_v = {}
        for hg in range(4):
            if hg > 0:
                P.wait("gpsimd", P.done["tensor"])
            kv_loader(hg, K_sb, V_sb, ld)
            for eng in ("tensor", "scalar"):
                P.wait(eng, ld)
            for h4 in range(4):
                h = hg * 4 + h4
                kch, krow = h4 // 2, (h4 % 2) * 64
                qch, qrow = h // 2, (h % 2) * 64
                for qb in range(4):
                    b_o = oi % 2
                    if oi >= 2:
                        P.wait("tensor", P.done["vector"], epi_v[oi - 2])
                    s_cnt = [0] * ngr
                    e_cnt = [0] * ngr
                    a_cnt = [0] * ngr

                    def s_group(g, krow=krow, kch=kch, qrow=qrow, qch=qch, qb=qb):
                        bs = g % 2
                        if g >= 2:
                            P.wait("tensor", P.done["scalar"], e_cnt[g - 2])
                        for j in range(2):
                            kb = 2 * g + j
                            P.emit("tensor", lambda e, kb=kb, bs=bs, j=j, krow=krow,
                                   kch=kch, qrow=qrow, qch=qch, qb=qb: e.matmul(
                                ps_s[:, bs * 1024 + j * 512: bs * 1024 + (j + 1) * 512],
                                K_sb[krow:krow + 64,
                                     kch * SEQ + kb * 128: kch * SEQ + (kb + 1) * 128],
                                Q_fm[qrow:qrow + 64,
                                     qch * T + qb * 512: qch * T + (qb + 1) * 512],
                                start=True, stop=True))
                        P.mark_mm()
                        s_cnt[g] = P.done["tensor"].n

                    def av_group(g, h4=h4, b_o=b_o):
                        bs = g % 2
                        P.wait("tensor", P.done["scalar"], e_cnt[g])
                        for j in range(2):
                            kb = 2 * g + j
                            P.emit("tensor", lambda e, kb=kb, bs=bs, j=j, b_o=b_o,
                                   h4=h4: e.matmul(
                                ps_o[b_o][:],
                                V_sb[:, kb * 4 * HDv + h4 * HDv: kb * 4 * HDv + (h4 + 1) * HDv],
                                pt[:, bs * 1024 + j * 512: bs * 1024 + (j + 1) * 512],
                                start=(kb == 0), stop=(kb == n_kb - 1)))
                        P.mark_mm()
                        a_cnt[g] = P.done["tensor"].n

                    P.wait("tensor", P.done["scalar"])
                    for g in range(ngr):
                        s_group(g)
                        if g >= 1:
                            av_group(g - 1)
                        P.wait("scalar", P.done["tensor"], s_cnt[g])
                        if g >= 2:
                            P.wait("scalar", P.done["tensor"], a_cnt[g - 2])
                        for j in range(2):
                            P.emit("scalar", lambda e, bs=g % 2, j=j: e.activation(
                                out=pt[:, bs * 1024 + j * 512:bs * 1024 + (j + 1) * 512],
                                in_=ps_s[:, bs * 1024 + j * 512:bs * 1024 + (j + 1) * 512],
                                func=AF.Exp, scale=scale),
                                P.done["scalar"] if j == 1 else None)
                        e_cnt[g] = P.done["scalar"].n
                    av_group(ngr - 1)

                    P.wait("vector", P.done["tensor"], a_cnt[ngr - 1])
                    P.emit("vector", lambda e, b_o=b_o: e.reciprocal(
                        rc[real:real + 1, :], ps_o[b_o][real:real + 1, :]),
                        serial=True)
                    P.emit("vector", lambda e: e.tensor_copy(
                        rcb[real:real + 1, 0:512], rc[real:real + 1, :]),
                        serial=True)
                    P.emit("vector", lambda e: e.tensor_sub(
                        rc[real:real + 1, :], rc[real:real + 1, :],
                        rcb[real:real + 1, 0:512]), serial=True)
                    P.emit("vector", lambda e: e.tensor_copy(
                        rcb[real:real + 1, 512:1024], rc[real:real + 1, :]),
                        P.done["vector"], serial=True)
                    P.wait("tensor", P.done["vector"])
                    for j in range(2):
                        P.emit("tensor", lambda e, j=j: e.matmul(
                            ps_bc[:], com.onesP_bf[real:real + 1, 0:real],
                            rcb[real:real + 1, j * 512:(j + 1) * 512],
                            start=(j == 0), stop=(j == 1)))
                    P.mark_mm()
                    P.wait("vector", P.done["tensor"])
                    P.emit("vector", lambda e: e.tensor_copy(bcb[0:real, :], ps_bc[:]),
                           serial=True)
                    P.emit("vector", lambda e, b_o=b_o, qrow=qrow, qch=qch,
                           qb=qb: e.tensor_mul(
                        o_fm[qrow:qrow + real,
                             qch * T + qb * 512: qch * T + (qb + 1) * 512],
                        ps_o[b_o][0:real, :], bcb[0:real, :]),
                        P.done["vector"], serial=True)
                    epi_v[oi] = P.done["vector"].n
                    oi += 1
        P.wait("gpsimd", P.done["vector"])



def emit_l0a_slice(P, com, slab, xsrc, t0g, kts, vs, Q_fm, q_tloc, st, tag,
                   OUT=3072, HDv=65, nvh=2, qn_oc=8, kn_oc=8, do_kv=True,
                   qbias=None, kbias=None, vb_row=None):
    nc = P.nc
    Tn = 1024
    with contextlib.ExitStack() as sc:
        xh = sc.enter_context(nc.sbuf_tensor(tag + "_x", [128, 8 * Tn], BF16))
        xln = sc.enter_context(nc.sbuf_tensor(tag + "_xln", [128, 8 * Tn], BF16))
        ktmp = sc.enter_context(nc.sbuf_tensor(tag + "_kt", [128, 2 * 512], BF16))
        vt = [sc.enter_context(nc.sbuf_tensor(tag + "_vt%d" % i, [128, 16 * HDv], BF16))
              for i in range(2)]
        ld = P.psem("l0a_ld")
        kst = [P.psem("l0a_kst0"), P.psem("l0a_kst1")]
        vst = [P.psem("l0a_vst0"), P.psem("l0a_vst1")]
        P.wait("gpsimd", P.done["vector"])
        for c in range(8):
            P.dma("gpsimd", xh[:, c * Tn:(c + 1) * Tn],
                  xsrc[c * 128:(c + 1) * 128, t0g:t0g + Tn], ld)
        for sm in kst + vst:
            P.wait("vector", sm, sm.n)
        for i in range(2):
            P.emit("vector", lambda e, i=i: e.memset(vt[i][:], 1.0),
                   P.done["vector"], serial=True)
        w = [(e, ld, ld.n) for e in ("vector", "scalar", "tensor")]
        emit_ln_big(P, com, xh[:], Tn, "bf16", out=xln, tag=tag + "_ln",
                    waits=w, x_bf16=True)

        kd = {"i": 0}

        def kevac(oc, g, ps):
            i = kd["i"]
            b = i % 2
            keng = "vector"
            P.wait(keng, kst[b], kst[b].n)
            if kbias is None:
                P.emit("vector", lambda e, b=b: e.tensor_copy(
                    ktmp[:, b * 512:(b + 1) * 512], ps), P.done["vector"],
                    serial=True)
            else:
                P.emit("vector", lambda e, b=b, oc=oc: e.tensor_scalar(
                    ktmp[:, b * 512:(b + 1) * 512], ps, kbias[oc - qn_oc],
                    None, ALU.add), P.done["vector"], serial=True)
            P.wait("gpsimd", P.done[keng])
            P.dma("gpsimd", kts[(oc - qn_oc) * 128:(oc - qn_oc + 1) * 128,
                                t0g + g * 512:t0g + (g + 1) * 512],
                  ktmp[:, b * 512:(b + 1) * 512], kst[b])
            kd["i"] += 1

        if do_kv:
            emit_proj_fm(P, com, xln[:], slab, OUT, qn_oc, kn_oc, Tn, kevac,
                         "vector", tag=tag + "_kp")

        vb0 = (qn_oc + kn_oc) * 128 // 512
        vd = {"i": 0}
        nvt = Tn // 128

        def vevac(tt, ob, ps):
            i = vd["i"]
            bt = tt % 2
            half = ob - vb0
            if half == 0:
                P.wait("vector", vst[bt], vst[bt].n)
            nh = 512 // (HDv - 1)
            P.emit("vector", lambda e, bt=bt, half=half, nh=nh: e.tensor_copy(
                vt[bt][:].rearrange("p (h w) -> p h w", h=16)
                [:, half * nh:(half + 1) * nh, 0:HDv - 1],
                ps.rearrange("p (h c) -> p h c", h=nh)), P.done["vector"],
                serial=True)
            if half == nvh - 1:
                P.wait("gpsimd", P.done["vector"])
                P.dma("gpsimd", vs[t0g + tt * 128:t0g + (tt + 1) * 128, :],
                      vt[bt][:], vst[bt])
            vd["i"] += 1

        if do_kv:
            vxm = None
            if vb_row is not None:
                def vxm(e, tt, ob, ps, vb_row=vb_row, vb0=vb0):
                    return e.matmul(ps[:], com.ones1[:],
                                    vb_row[:, (ob - vb0) * 512:(ob - vb0 + 1) * 512],
                                    start=False, stop=True)
            emit_proj_tm(P, com, xln[:], slab, OUT, vb0, nvh, Tn, vevac,
                         "vector", tag=tag + "_vp", extra_mm=vxm)

        if q_tloc is not None:
            def qevac(oc, g, ps):
                if qbias is None:
                    P.emit("vector", lambda e, oc=oc, g=g: e.tensor_copy(
                        Q_fm[:, oc * T + q_tloc + g * 512:
                             oc * T + q_tloc + (g + 1) * 512],
                        ps), P.done["vector"], serial=True)
                else:
                    P.emit("vector", lambda e, oc=oc, g=g: e.tensor_scalar(
                        Q_fm[:, oc * T + q_tloc + g * 512:
                             oc * T + q_tloc + (g + 1) * 512],
                        ps, qbias[oc], None, ALU.add),
                        P.done["vector"], serial=True)

            emit_proj_fm(P, com, xln[:], slab, OUT, 0, qn_oc, Tn, qevac,
                         "vector", tag=tag + "_qp")
        for sm in kst + vst:
            P.wait("gpsimd", sm)
        st.append((kst, vst))


def l0a_kv_loader(P, kts, vs, HDv=65):
    W = 4 * HDv

    def loader(hg, K_sb, V_sb, sem):
        for cc in range(2):
            P.dma("gpsimd", K_sb[:, cc * SEQ:(cc + 1) * SEQ],
                  kts[hg * 256 + cc * 128:hg * 256 + (cc + 1) * 128, :], sem)
        vsr = vs.rearrange("(kb p) w -> p kb w", p=128)
        vdr = V_sb[:].rearrange("p (kb w) -> p kb w", w=W)
        for i in range(8):
            P.dma("gpsimd", vdr[:, i * 8:(i + 1) * 8, :],
                  vsr[:, i * 8:(i + 1) * 8, hg * W:(hg + 1) * W], sem)
    return loader


def emit_l0a(P, com, A):
    nc = P.nc
    st = []
    with contextlib.ExitStack() as sa:
        wsem = P.psem("l0a_w")
        slab = load_wslab_sc(P, sa, A.tag + "wqkv_sb", A.wqkv, D, 3072, wsem)
        P.wait("tensor", wsem)
        for si, (xsrc, t0g) in enumerate(A.kv_slices):
            emit_l0a_slice(P, com, slab, xsrc, t0g, A.kts, A.vs, None, None,
                           st, "%sA_%d" % (A.tag, si))
        for si, (xsrc, t0s, t0d) in enumerate(A.q_slices):
            emit_l0a_slice(P, com, slab, xsrc, t0s, A.kts, A.vs, A.Q_fm,
                           t0d, st, "%sQ_%d" % (A.tag, si), do_kv=False)
    P.fence()
    if A.ag_groups is not None:
        cc = P.psem("cc")
        P.wait("gpsimd", P.psem("l0a_kst0"))
        P.wait("gpsimd", P.psem("l0a_kst1"))
        P.wait("gpsimd", P.psem("l0a_vst0"))
        P.wait("gpsimd", P.psem("l0a_vst1"))
        P.emit("gpsimd", lambda e: e.collective_compute(
            "AllGather", ALU.bypass, replica_groups=A.ag_groups,
            ins=[A.kts.opt()], outs=[A.ktsg.opt()]), cc)
        P.emit("gpsimd", lambda e: e.collective_compute(
            "AllGather", ALU.bypass, replica_groups=A.ag_groups,
            ins=[A.vs.opt()], outs=[A.vsg.opt()]), cc)
        P.wait("gpsimd", cc)
    if A.stage == "A":
        return
    emit_flash(P, com, Q_fm=A.Q_fm, o_fm=A.o_fm,
               kv_loader=A.kv_loader, HDv=65, scale=0.125, tag=A.tag + "F")
    P.fence()
    with contextlib.ExitStack() as sc:
        xo = sc.enter_context(nc.sbuf_tensor(A.tag + "xo", [128, 8 * T], F32))
        ld = P.psem("c_ld")
        sto = P.psem("c_st")
        P.wait("sync", P.done["vector"])
        for c in range(8):
            P.dma("sync", xo[:, c * T:(c + 1) * T],
                  A.x_resid[c * 128:(c + 1) * 128, :], ld)
        w = [(e, ld, ld.n) for e in ("vector", "scalar", "tensor")]
        emit_ln_big(P, com, A.o_fm[:], T, "residual", xold=xo, xnew=xo,
                    tag=A.tag + "postln", waits=w, x_bf16=True)
        P.wait("sync", P.done["vector"])
        for c in range(8):
            P.dma("sync", A.x_out[c * 128:(c + 1) * 128, :],
                  xo[:, c * T:(c + 1) * T], sto)
        P.wait("sync", sto)


def build_l0a_ckpt(stage="full"):
    import types
    nc = bass.Bass("TRN2", target_bir_lowering=False, debug=False)
    xfull = nc.dram_tensor("xfull", [D, SEQ], F32, kind="ExternalInput").ap()
    xown = nc.dram_tensor("xown", [D, T], F32, kind="ExternalInput").ap()
    wqkv = nc.dram_tensor("wqkv", [D, 3072], F32, kind="ExternalInput").ap()
    kvkind = "ExternalOutput" if stage == "A" else "Internal"
    x1 = nc.dram_tensor("x1", [D, T], F32, kind="ExternalOutput").ap()
    kts = nc.dram_tensor("kts", [D, SEQ], BF16, kind=kvkind).ap()
    vs = nc.dram_tensor("vs", [SEQ, 16 * 65], BF16, kind=kvkind).ap()
    P = Prog(nc)
    com = Common(P, Tn=512, with_ln_xb=False)
    A = types.SimpleNamespace(
        tag="l0a", wqkv=wqkv, kts=kts, vs=vs, ktsg=None, vsg=None,
        ag_groups=None, stage=stage,
        kv_slices=[(xfull, s * T + h * 1024) for s in range(4)
                   for h in range(2)],
        q_slices=[(xown, h * 1024, h * 1024) for h in range(2)],
        Q_fm=P.sb("Q_fm", [128, 8 * T], BF16),
        o_fm=P.sb("o_fm", [128, 8 * T], BF16),
        kv_loader=l0a_kv_loader(P, kts, vs),
        x_resid=xown, x_out=x1)
    emit_l0a(P, com, A)
    P.build()
    return nc


def build_l0b_ckpt(ctx=None):
    if ctx is None:
        nc = bass.Bass("TRN2", target_bir_lowering=False, debug=False)
        xin = nc.dram_tensor("xin", [D, T], F32, kind="ExternalInput").ap()
        xfull1 = nc.dram_tensor("xfull1", [D, SEQ], F32, kind="ExternalInput").ap()
        wqkvb = nc.dram_tensor("wqkvb", [D, 2560], F32, kind="ExternalInput").ap()
        wob = nc.dram_tensor("wob", [D, D], F32, kind="ExternalInput").ap()
        wrt = nc.dram_tensor("wrt", [D, 8], F32, kind="ExternalInput").ap()
        w1 = nc.dram_tensor("w1", [8, D, 2048], F32, kind="ExternalInput").ap()
        w2 = nc.dram_tensor("w2", [8, 2048, D], F32, kind="ExternalInput").ap()
        bpack = nc.dram_tensor("bpack", [128, 216], F32, kind="ExternalInput").ap()
        vbr = nc.dram_tensor("vbr", [1, 512], F32, kind="ExternalInput").ap()
        rbr = nc.dram_tensor("rbr", [1, 8], F32, kind="ExternalInput").ap()
        ident = nc.dram_tensor("ident", [128, 128], F32, kind="ExternalInput").ap()
        x2 = nc.dram_tensor("x2", [D, T], F32, kind="ExternalOutput").ap()
        kts = nc.dram_tensor("kts2", [D, SEQ], BF16).ap()
        vs = nc.dram_tensor("vs2", [SEQ, 16 * 33], BF16).ap()
        P = Prog(nc)
        com = Common(P, Tn=512, with_ln_xb=False)
        com.ln_xb = P.sb("lnxb512", [128, 8 * 512], BF16)
        sc0 = P.stack
        kv_slices = [(xfull1, s_ * T + h_ * 1024) for s_ in range(4)
                     for h_ in range(2)]
        kvld = l0a_kv_loader(P, kts, vs, HDv=33)
        ag = None
    else:
        nc, P, com, sc0 = ctx["nc"], ctx["P"], ctx["com"], ctx["sc"]
        io = ctx["io"]
        (xin, wqkvb, wob, wrt, w1, w2, bpack, vbr, rbr, ident, x2, kts,
         vs) = (io["xin"], io["wqkvb"], io["wob"], io["wrt"], io["w1"],
                io["w2"], io["bpack"], io["vbr"], io["rbr"], io["ident"],
                io["x2"], io["kts"], io["vs"])
        kv_slices = ctx["kv_slices"]
        kvld = ctx["kv_loader"]
        ag = ctx.get("ag")
    sb0 = lambda n_, sh_, dt_: sc0.enter_context(nc.sbuf_tensor(n_, sh_, dt_))
    mla = sb0("l0b_mla", [128, 8 * T], BF16)
    bp = sb0("l0b_bp", [128, 216], F32)
    vb_row = sb0("vb_row", [1, 512], BF16)
    rb_row = sb0("rb_row", [1, 8], BF16)
    idt = sb0("l0b_idt", [128, 128], BF16)
    eye8 = sb0("eye8", [8, 8], BF16)
    ld0 = P.psem("ld0")
    P.dma("gpsimd", bp[:], bpack[:, :], ld0)
    P.dma("gpsimd", vb_row[:], vbr[:, :], ld0)
    P.dma("gpsimd", rb_row[:], rbr[:, :], ld0)
    P.dma("gpsimd", idt[:], ident[:, :], ld0)
    P.dma("gpsimd", eye8[:], ident[0:8, 0:8], ld0)
    qbias = [bp[:, i:i + 1] for i in range(8)]
    kbias = [bp[:, 8 + i:8 + i + 1] for i in range(8)]
    obias = [bp[:, 16 + i:16 + i + 1] for i in range(8)]
    b1col = lambda e_, hc: bp[:, 24 + e_ * 16 + hc:24 + e_ * 16 + hc + 1]
    b2col = lambda e_, oc: bp[:, 152 + e_ * 8 + oc:152 + e_ * 8 + oc + 1]
    for eng in ("scalar", "tensor", "vector", "gpsimd"):
        P.wait(eng, ld0)
    st = []

    with contextlib.ExitStack() as sa:
        wsem = P.psem("l0b_w")
        slab = load_wslab_sc(P, sa, "wqkvb_sb", wqkvb, D, 2560, wsem)
        P.wait("tensor", wsem)
        for si, (xsrc_, t0g) in enumerate(kv_slices):
            emit_l0a_slice(P, com, slab, xsrc_, t0g, kts, vs, None, None,
                           st, "l0bA_%d" % si, OUT=2560, HDv=33,
                           nvh=1, qn_oc=8, kn_oc=8, kbias=kbias,
                           vb_row=vb_row)
    P.fence()
    if ag is not None:
        emit_ag(P, ag, wait_sems=[P.psem(nm) for nm in
                                  ("l0a_kst0", "l0a_kst1", "l0a_vst0",
                                   "l0a_vst1")])
    with contextlib.ExitStack() as sb:
        Q_fm = sb.enter_context(nc.sbuf_tensor("Q_fm", [128, 8 * T], BF16))
        o_fm = sb.enter_context(nc.sbuf_tensor("o_fm", [128, 8 * T], BF16))
        P.emit("vector", lambda e: e.memset(o_fm[:], 0.0), P.done["vector"])
        with contextlib.ExitStack() as sq:
            wsem = P.psem("l0b_wq")
            qslab = load_wslab_sc(P, sq, "wq_sb", wqkvb, D, 1024, wsem, col0=0)
            P.wait("tensor", wsem)
            for half in range(2):
                emit_l0a_slice(P, com, qslab, xin, half * 1024, kts, vs, Q_fm,
                               half * 1024, st, "l0bQ_%d" % half, OUT=1024,
                               HDv=33, nvh=1, do_kv=False, qbias=qbias)
        P.fence()
        emit_flash(P, com, Q_fm=Q_fm, o_fm=o_fm,
                   kv_loader=kvld,
                   HDv=33, scale=1.0 / math.sqrt(32.0), tag="l0bF")
        P.fence()

        with contextlib.ExitStack() as sc:
            wsem = P.psem("l0b_wo")
            woslab = load_wslab_sc(P, sc, "wo_sb", wob, D, D, wsem)
            P.wait("tensor", wsem)

            def oevac(oc, g, ps):
                P.emit("vector", lambda e, oc=oc, g=g: e.tensor_scalar(
                    mla[:, oc * T + g * 512:oc * T + (g + 1) * 512],
                    ps, obias[oc], None, ALU.add),
                    P.done["vector"], serial=True)

            emit_proj_fm(P, com, o_fm[:], woslab, D, 0, 8, T, oevac, "vector",
                         tag="l0b_wo")
        P.fence()

    cb_sb = sb0("cb_sb", [8, T], BF16)
    with contextlib.ExitStack() as sc:
        wsem = P.psem("l0b_wr")
        rslab = load_wslab_sc(P, sc, "wr_sb", wrt, D, 8, wsem)
        lg = sc.enter_context(nc.sbuf_tensor("rt_lg", [128, 8], F32))
        sc1 = sc.enter_context(nc.sbuf_tensor("rt_s", [128, 8], F32))
        cols = sc.enter_context(nc.sbuf_tensor("rt_c", [128, 8], F32))
        cbt = sc.enter_context(nc.sbuf_tensor("rt_cb", [128, 8], BF16))
        rbb = sc.enter_context(nc.sbuf_tensor("rt_rbb", [128, 8], F32))
        ps_l = sc.enter_context(nc.psum_tensor("rt_ps", [128, 8], F32))
        ps_t = sc.enter_context(nc.psum_tensor("rt_pt", [8, 128], BF16))
        P.wait("tensor", wsem)
        P.wait("tensor", P.done["vector"])
        P.emit("tensor", lambda e: e.matmul(ps_l[:], com.ones1[:], rb_row[:],
                                            start=True, stop=True))
        P.mark_mm()
        P.wait("vector", P.done["tensor"])
        P.emit("vector", lambda e: e.tensor_copy(rbb[:], ps_l[:]),
               P.done["vector"], serial=True)
        for tt in range(T // 128):
            P.wait("tensor", P.done["vector"])
            for kc in range(8):
                P.emit("tensor", lambda e, tt=tt, kc=kc: e.matmul(
                    ps_l[:], mla[:, kc * T + tt * 128:kc * T + (tt + 1) * 128],
                    rslab[:, kc * 8:(kc + 1) * 8],
                    start=(kc == 0), stop=(kc == 7)))
            P.mark_mm()
            P.wait("vector", P.done["tensor"])
            V = "vector"
            P.emit(V, lambda e: e.tensor_add(lg[:], ps_l[:], rbb[:]), serial=True)
            P.emit(V, lambda e: e.reduce_max(sc1[:, 0:1], lg[:], AX.X), serial=True)
            P.emit(V, lambda e: e.tensor_scalar(lg[:], lg[:], sc1[:, 0:1], None,
                                                ALU.subtract), serial=True)
            P.wait("scalar", P.done["vector"])
            P.emit("scalar", lambda e: e.activation(out=lg[:], in_=lg[:],
                                                    func=AF.Exp),
                   P.done["scalar"], serial=True)
            P.wait(V, P.done["scalar"])
            P.emit(V, lambda e: e.reduce_sum(sc1[:, 1:2], lg[:], AX.X), serial=True)
            P.emit(V, lambda e: e.reciprocal(sc1[:, 2:3], sc1[:, 1:2]), serial=True)
            P.emit(V, lambda e: e.tensor_scalar(lg[:], lg[:], sc1[:, 2:3], None,
                                                ALU.mult), serial=True)
            P.emit(V, lambda e: e.reduce_max(sc1[:, 3:4], lg[:], AX.X), serial=True)
            P.emit(V, lambda e: e.tensor_scalar(cols[:], lg[:], sc1[:, 3:4], None,
                                                ALU.is_ge), serial=True)
            P.emit(V, lambda e: e.scalar_tensor_tensor(
                cols[:], cols[:], -1e30, lg[:], ALU.mult, ALU.add), serial=True)
            P.emit(V, lambda e: e.reduce_max(sc1[:, 4:5], cols[:], AX.X),
                   serial=True)
            P.emit(V, lambda e: e.tensor_scalar(cols[:], lg[:], sc1[:, 4:5], None,
                                                ALU.is_ge), serial=True)
            P.emit(V, lambda e: e.tensor_add(sc1[:, 5:6], sc1[:, 3:4],
                                             sc1[:, 4:5]), serial=True)
            P.emit(V, lambda e: e.tensor_scalar(sc1[:, 6:7], sc1[:, 5:6], 1e-8,
                                                None, ALU.add), serial=True)
            P.emit(V, lambda e: e.reciprocal(sc1[:, 7:8], sc1[:, 6:7]), serial=True)
            P.emit(V, lambda e: e.tensor_mul(cols[:], cols[:], lg[:]), serial=True)
            P.emit(V, lambda e: e.tensor_scalar(cbt[:], cols[:], sc1[:, 7:8], None,
                                                ALU.mult), P.done["vector"],
                   serial=True)
            P.wait("tensor", P.done["vector"])
            P.emit("tensor", lambda e: e.transpose(ps_t[:], cbt[:], idt[:]))
            P.mark_mm()
            P.wait("vector", P.done["tensor"])
            P.emit(V, lambda e, tt=tt: e.tensor_copy(
                cb_sb[:, tt * 128:(tt + 1) * 128], ps_t[:]), P.done["vector"],
                serial=True)
    P.fence()

    with contextlib.ExitStack() as sc:
        w1s = sc.enter_context(nc.sbuf_tensor("w1s", [128, 8 * 2048], BF16))
        w2s = sc.enter_context(nc.sbuf_tensor("w2s", [128, 16 * 1024], BF16))
        h1s = sc.enter_context(nc.sbuf_tensor("h1s", [128, 16 * 512], BF16))
        x1tb = sc.enter_context(nc.sbuf_tensor("x1tb", [128, 8 * 512], F32))
        x1ln = com.ln_xb
        ytb = sc.enter_context(nc.sbuf_tensor("ytb", [128, 8 * 512], F32))
        h2s = sc.enter_context(nc.sbuf_tensor("h2s", [128, 512], BF16))
        rowb = sc.enter_context(nc.sbuf_tensor("rowb", [1, 512], BF16))
        cbb = sc.enter_context(nc.sbuf_tensor("cbb", [128, 512], BF16))
        ps_r = sc.enter_context(nc.psum_tensor("ps_r", [1, 512], F32))
        ps_b = sc.enter_context(nc.psum_tensor("ps_b", [128, 512], F32))
        ldx = P.psem("ldx")
        ldw = P.psem("ldw")
        stx = P.psem("stx")
        for tb in range(4):
            t0 = tb * 512
            P.wait("sync", P.done["vector"])
            P.wait("sync", stx)
            for c in range(8):
                P.dma("sync", x1tb[:, c * 512:(c + 1) * 512],
                      xin[c * 128:(c + 1) * 128, t0:t0 + 512], ldx)
            w = [(e, ldx, ldx.n) for e in ("vector", "scalar", "tensor")]
            emit_ln(P, com, x1tb[:], 512, "bf16", out=x1ln,
                    tag="moeln%d" % tb, waits=w)
            P.wait("vector", P.done["vector"])
            P.emit("vector", lambda e: e.tensor_copy(ytb[:], x1ln[:]),
                   P.done["vector"], serial=True)
            for ex in range(8):
                P.wait("gpsimd", P.done["tensor"])
                for kc in range(8):
                    P.dma("gpsimd", w1s[:, kc * 2048:(kc + 1) * 2048],
                          w1[ex, kc * 128:(kc + 1) * 128, :], ldw)
                for kc in range(16):
                    P.dma("gpsimd", w2s[:, kc * 1024:(kc + 1) * 1024],
                          w2[ex, kc * 128:(kc + 1) * 128, :], ldw)
                P.wait("tensor", P.done["vector"])
                P.emit("tensor", lambda e, ex=ex, t0=t0: e.matmul(
                    ps_r[:], eye8[:, ex:ex + 1], cb_sb[:, t0:t0 + 512],
                    start=True, stop=True))
                P.mark_mm()
                P.wait("vector", P.done["tensor"])
                P.emit("vector", lambda e: e.tensor_copy(rowb[:], ps_r[:]),
                       P.done["vector"], serial=True)
                P.wait("tensor", P.done["vector"])
                P.emit("tensor", lambda e: e.matmul(
                    ps_b[:], com.ones1[:], rowb[:], start=True, stop=True))
                P.mark_mm()
                P.wait("vector", P.done["tensor"])
                P.emit("vector", lambda e: e.tensor_copy(cbb[:], ps_b[:]),
                       P.done["vector"], serial=True)
                P.wait("tensor", ldw)

                def h1evac(hc, g, ps, ex=ex):
                    P.emit("scalar", lambda e, hc=hc: e.activation(
                        out=h1s[:, hc * 512:(hc + 1) * 512], in_=ps,
                        func=AF.Gelu, bias=b1col(ex, hc)),
                        P.done["scalar"], serial=True)

                with contextlib.ExitStack() as sh:
                    psh = [sh.enter_context(nc.psum_tensor(
                        "h1ps%d_%d_%d" % (tb, ex, i), [128, 512], F32))
                        for i in range(2)]
                    evc = []
                    for hc in range(16):
                        b = hc % 2
                        if hc >= 2:
                            P.wait("tensor", P.done["scalar"], evc[hc - 2])
                        for kc in range(8):
                            P.emit("tensor", lambda e, hc=hc, kc=kc, b=b, t0=t0: e.matmul(
                                psh[b][:],
                                w1s[:, kc * 2048 + hc * 128:kc * 2048 + (hc + 1) * 128],
                                mla[:, kc * T + t0:kc * T + t0 + 512],
                                start=(kc == 0), stop=(kc == 7)))
                        P.mark_mm()
                        P.wait("scalar", P.done["tensor"])
                        h1evac(hc, 0, psh[b][:])
                        evc.append(P.done["scalar"].n)
                    P.wait("tensor", P.done["scalar"])
                with contextlib.ExitStack() as sh:
                    psh = [sh.enter_context(nc.psum_tensor(
                        "h2ps%d_%d_%d" % (tb, ex, i), [128, 512], F32))
                        for i in range(2)]
                    evc = []
                    for oc in range(8):
                        b = oc % 2
                        if oc >= 2:
                            P.wait("tensor", P.done["vector"], evc[oc - 2])
                        for kc in range(16):
                            P.emit("tensor", lambda e, oc=oc, kc=kc, b=b: e.matmul(
                                psh[b][:],
                                w2s[:, kc * 1024 + oc * 128:kc * 1024 + (oc + 1) * 128],
                                h1s[:, kc * 512:(kc + 1) * 512],
                                start=(kc == 0), stop=(kc == 15)))
                        P.mark_mm()
                        P.wait("scalar", P.done["tensor"])
                        P.wait("vector", P.done["tensor"])
                        P.emit("vector", lambda e, oc=oc, ex=ex, b=b: e.tensor_scalar(
                            h2s[:], psh[b][:], b2col(ex, oc), None, ALU.add),
                            serial=True)
                        P.emit("vector", lambda e: e.tensor_mul(
                            h2s[:], h2s[:], cbb[:]), serial=True)
                        P.emit("vector", lambda e, oc=oc: e.scalar_tensor_tensor(
                            ytb[:, oc * 512:(oc + 1) * 512], h2s[:], 0.3,
                            ytb[:, oc * 512:(oc + 1) * 512], ALU.mult, ALU.add),
                            P.done["vector"], serial=True)
                        evc.append(P.done["vector"].n)
                    P.wait("tensor", P.done["vector"])
            emit_ln(P, com, ytb[:], 512, "residual", xold=x1tb, xnew=x1tb,
                    tag="fln%d" % tb)
            P.wait("sync", P.done["vector"])
            for c in range(8):
                P.dma("sync", x2[c * 128:(c + 1) * 128, t0:t0 + 512],
                      x1tb[:, c * 512:(c + 1) * 512], stx)
        P.wait("sync", stx)
    if ctx is None:
        P.build()
        return nc
    return None


def build_l1a(mode, ctx=None):
    if ctx is None:
        nc = bass.Bass("TRN2", target_bir_lowering=False, debug=False)
        xin = nc.dram_tensor("xin", [D, T], F32, kind="ExternalInput").ap()
        wq = nc.dram_tensor("wq", [D, D], F32, kind="ExternalInput").ap()
        wk = nc.dram_tensor("wk", [D, D], F32, kind="ExternalInput").ap()
        wv = nc.dram_tensor("wv", [D, D], F32, kind="ExternalInput").ap()
        cfm = nc.dram_tensor("cfm", [D, T], F32, kind="ExternalInput").ap()
        sfm = nc.dram_tensor("sfm", [D, T], F32, kind="ExternalInput").ap()
        ctm = nc.dram_tensor("ctm", [16, T, 64], F32, kind="ExternalInput").ap()
        stm = nc.dram_tensor("stm", [16, T, 64], F32, kind="ExternalInput").ap()
        utri = nc.dram_tensor("utri", [128, 128], F32, kind="ExternalInput").ap()
        if mode == "full":
            wg1 = nc.dram_tensor("wg1", [D, 64], F32, kind="ExternalInput").ap()
            wg2 = nc.dram_tensor("wg2", [64, D], F32, kind="ExternalInput").ap()
            wo = nc.dram_tensor("wo", [D, D], F32, kind="ExternalInput").ap()
            bpk = nc.dram_tensor("bpk", [128, 8], F32, kind="ExternalInput").ap()
            sig_p = nc.dram_tensor("sig_p", [4, 1024], F32, kind="ExternalInput").ap()
            S_p = nc.dram_tensor("S_p", [4, 16 * 128, 64], F32, kind="ExternalInput").ap()
            wselp = nc.dram_tensor("wselp", [128, 4], F32, kind="ExternalInput").ap()
            hsel = nc.dram_tensor("hsel", [128, 8 * 16], F32, kind="ExternalInput").ap()
            ident = nc.dram_tensor("ident", [128, 128], F32, kind="ExternalInput").ap()
            x3 = nc.dram_tensor("x3", [D, T], F32, kind="ExternalOutput").ap()
        else:
            sig_o = nc.dram_tensor("sig_o", [1, 1024], F32, kind="ExternalOutput").ap()
            S_o = nc.dram_tensor("S_o", [16 * 128, 64], F32, kind="ExternalOutput").ap()
        P = Prog(nc)
        com = Common(P, Tn=512, with_ln_xb=False)
        sc0 = P.stack
    else:
        nc, P, com, sc0 = ctx["nc"], ctx["P"], ctx["com"], ctx["sc"]
        io = ctx["io"]
        xin, wq, wk, wv, cfm, sfm, ctm, stm, utri = (
            io["xin"], io["wq"], io["wk"], io["wv"], io["cfm"], io["sfm"],
            io["ctm"], io["stm"], io["utri"])
        if mode == "full":
            (wg1, wg2, wo, bpk, sig_p, S_p, wselp, hsel, ident, x3) = (
                io["wg1"], io["wg2"], io["wo"], io["bpk"], io["sig_p"],
                io["S_p"], io["wselp"], io["hsel"], io["ident"], io["x3"])
        else:
            sig_o, S_o = io["sig_o"], io["S_o"]
    com.col(1e-6)
    sb0 = lambda n_, sh_, dt_: sc0.enter_context(nc.sbuf_tensor(n_, sh_, dt_))
    tg = "l1a" + mode[0]
    xln = sb0(tg + "xln", [128, 8 * T], BF16)
    ut = sb0(tg + "ut", [128, 128], BF16)
    carry = sb0(tg + "carry", [1, 16 * 64], F32)
    ld = P.psem("ld")
    P.dma("gpsimd", ut[:], utri[:, :], ld)
    if mode == "full":
        o_fm = sb0("o_l1", [128, 8 * T], BF16)
        hs = sb0("hsel_sb", [128, 8 * 16], BF16)
        idt = sb0("l1a_idt", [128, 128], BF16)
        bp = sb0("bp_l1", [128, 8], F32)
        sigc = sb0("sigc", [1, 1024], F32)
        scb_p = sb0("scb_p", [128, 16 * 64], BF16)
        P.dma("gpsimd", hs[:], hsel[:, :], ld)
        P.dma("gpsimd", idt[:], ident[:, :], ld)
        P.dma("gpsimd", bp[:], bpk[:, :], ld)
        nw_col = [bp[:, c:c + 1] for c in range(8)]
        with contextlib.ExitStack() as scc:
            wsl = scc.enter_context(nc.sbuf_tensor("wsl", [128, 4], F32))
            sgp = [scc.enter_context(nc.sbuf_tensor("sgp%d" % r, [1, 1024], F32))
                   for r in range(4)]
            Spt = [scc.enter_context(nc.sbuf_tensor("Spt%d" % r,
                                                    [128, 1024], F32))
                   for r in range(4)]
            Sac = scc.enter_context(nc.sbuf_tensor("Sac", [128, 1024], F32))
            ldc = P.psem("l1a_ldc")
            P.dma("gpsimd", wsl[:], wselp[:, :], ldc)
            for r in range(4):
                P.dma("gpsimd", sgp[r][:], sig_p[r:r + 1, :], ldc)
                P.dma("gpsimd", Spt[r][:].rearrange("p (hh d) -> p hh d", d=64),
                      S_p[r].rearrange("(hh p) d -> p hh d", p=128), ldc)
            P.wait("vector", ldc)
            V0 = "vector"
            P.emit(V0, lambda e: e.tensor_scalar(
                sigc[:], sgp[0][:], wsl[0:1, 0:1], None, ALU.mult), serial=True)
            for r in range(1, 4):
                P.emit(V0, lambda e, r=r: e.scalar_tensor_tensor(
                    sigc[:], sgp[r][:], wsl[0:1, r:r + 1], sigc[:],
                    ALU.mult, ALU.add), serial=True)
            P.emit(V0, lambda e: e.tensor_scalar(
                Sac[:], Spt[0][:], wsl[:, 0:1], None, ALU.mult), serial=True)
            for r in range(1, 4):
                P.emit(V0, lambda e, r=r: e.scalar_tensor_tensor(
                    Sac[:], Spt[r][:], wsl[:, r:r + 1], Sac[:],
                    ALU.mult, ALU.add), serial=True)
            P.emit(V0, lambda e: e.tensor_copy(scb_p[:], Sac[:]),
                   P.done["vector"], serial=True)
    L0_scb = scb_p if mode == "full" else None
    with contextlib.ExitStack() as sx:
        xw = sx.enter_context(nc.sbuf_tensor(tg + "xw", [128, 8 * T], BF16))
        P.wait("gpsimd", P.done["vector"])
        P.dma("gpsimd", xw[:].rearrange("p (c t) -> p c t", c=8),
              xin[:, :].rearrange("(c p) t -> p c t", p=128), ld)
        for eng in ("scalar", "tensor", "vector", "gpsimd"):
            P.wait(eng, ld)
        w = [(e, ld, ld.n) for e in ("vector", "scalar", "tensor")]
        emit_ln_big(P, com, xw[:], T, "bf16", out=xln, tag=tg + "ln", waits=w,
                    x_bf16=True)
        P.fence()

    if mode == "full":
        P.wait("vector", P.done["vector"])
        P.emit("vector", lambda e: e.tensor_scalar(
            carry[:], sigc[:], 1.0, None, ALU.add), P.done["vector"],
            serial=True)
    else:
        P.emit("vector", lambda e: e.memset(carry[:], 1.0), P.done["vector"],
               serial=True)

    with contextlib.ExitStack() as sw:
        wsem = P.psem("l1aw")
        qslab = load_wslab_sc(P, sw, tg + "wq_sb", wq, D, D, wsem)
        kslab = load_wslab_sc(P, sw, tg + "wk_sb", wk, D, D, wsem)
        vslab = load_wslab_sc(P, sw, tg + "wv_sb", wv, D, D, wsem)
        P.wait("tensor", wsem)

        with contextlib.ExitStack() as sh:
            kdd = sh.enter_context(nc.sbuf_tensor(tg + "kdd", [128, T], BF16))
            Etm = sh.enter_context(nc.sbuf_tensor(tg + "Etm", [128, 16 * 64], BF16))
            kddtm = sh.enter_context(nc.sbuf_tensor(tg + "kddtm", [128, 16 * 128], BF16))
            vtm = sh.enter_context(nc.sbuf_tensor(tg + "vtm", [128, 16 * 64], BF16))
            trig = sh.enter_context(nc.sbuf_tensor(tg + "trig", [128, T + 2048], BF16))
            qsb = sh.enter_context(nc.sbuf_tensor(tg + "qsb", [128, T], BF16))
            qp = sh.enter_context(nc.sbuf_tensor(tg + "qp", [128, T], BF16))
            rcpb = sh.enter_context(nc.sbuf_tensor(tg + "rcpb", [128, 16 * 64], BF16))
            rcp = rcpb
            rcpf = sh.enter_context(nc.sbuf_tensor(tg + "rcpf", [128, T], BF16))
            t1 = sh.enter_context(nc.sbuf_tensor(tg + "t1f", [128, T], BF16))
            pt = sh.enter_context(nc.sbuf_tensor(tg + "ptl1", [128, 128], BF16))
            S_snap = sh.enter_context(nc.sbuf_tensor(tg + "S_snap", [128, 16 * 64], BF16))
            S_run = sh.enter_context(nc.sbuf_tensor(tg + "S_run", [128, 64], F32))
            if mode == "full":
                scb = L0_scb
            ps_p = sh.enter_context(nc.psum_tensor(tg + "ps_proj", [128, 512], F32))
            ps_c = sh.enter_context(nc.psum_tensor(tg + "ps_cum", [128, 64], F32))
            ps_sg = sh.enter_context(nc.psum_tensor(tg + "ps_sig", [1, 64], F32))
            ps_S = sh.enter_context(nc.psum_tensor(tg + "ps_S", [128, 64], F32))
            ps_pt = sh.enter_context(nc.psum_tensor(tg + "ps_pt", [128, 128], F32))
            ps_o = sh.enter_context(nc.psum_tensor(tg + "ps_o", [64, 128], F32))
            ps_tr = sh.enter_context(nc.psum_tensor(tg + "ps_tr", [64, 128], BF16))
            for h in range(16):
                emit_l1a_head(P, com, locals() | {"P": P}, h, mode)
        if mode == "partial":
            sto = P.psem("sto")
            P.wait("vector", P.done["tensor"])
            P.emit("vector", lambda e: e.tensor_scalar(
                carry[:], carry[:], -1.0, None, ALU.add), P.done["vector"],
                serial=True)
            P.wait("gpsimd", P.done["vector"])
            P.dma("gpsimd", sig_o[:, :], carry[:], sto)
            P.wait("gpsimd", sto)
            P.wait("sync", sto)
    if mode == "full":
        emit_l1a_post(P, com, nc, locals())
    if ctx is None:
        P.build()
        return nc
    return None


def emit_l1a_head(P, com, L, h, mode):
    nc = P.nc
    full = (mode == "full")
    xln, qslab, kslab, vslab = L["xln"], L["qslab"], L["kslab"], L["vslab"]
    kdd, kddtm, vtm, Etm = L["kdd"], L["kddtm"], L["vtm"], L["Etm"]
    trig, qsb, qp, rcp, rcpb, rcpf, t1 = (L["trig"], L["qsb"], L["qp"],
                                          L["rcp"], L["rcpb"], L["rcpf"],
                                          L["t1"])
    ps_p, ps_c, ps_sg, ps_S = L["ps_p"], L["ps_c"], L["ps_sg"], L["ps_S"]
    carry, ut = L["carry"], L["ut"]
    cfm, sfm, ctm, stm = L["cfm"], L["sfm"], L["ctm"], L["stm"]
    ld = P.psem("l1a_trig_ld")
    V = "vector"

    P.wait("gpsimd", P.done["tensor"])
    P.wait("gpsimd", P.done["vector"])
    P.dma("gpsimd", trig[0:64, 0:T], cfm[h * 64:(h + 1) * 64, :], ld)
    P.dma("gpsimd", trig[64:128, 0:T], sfm[h * 64:(h + 1) * 64, :], ld)
    ctm_r = ctm[h].rearrange("(tc p) d -> p tc d", p=128)
    stm_r = stm[h].rearrange("(tc p) d -> p tc d", p=128)
    ctm_d = trig[:, T:T + 1024].rearrange("p (tc d) -> p tc d", d=64)
    stm_d = trig[:, T + 1024:T + 2048].rearrange("p (tc d) -> p tc d", d=64)
    for i in range(2):
        P.dma("gpsimd", ctm_d[:, i * 8:(i + 1) * 8, :],
              ctm_r[:, i * 8:(i + 1) * 8, :], ld)
        P.dma("gpsimd", stm_d[:, i * 8:(i + 1) * 8, :],
              stm_r[:, i * 8:(i + 1) * 8, :], ld)
    for eng in ("tensor", "scalar", "vector"):
        P.wait(eng, ld)

    def fm_proj(slab, func, out_sb):
        for g in range(4):
            for half in range(2):
                for kc in range(8):
                    P.emit("tensor", lambda e, g=g, half=half, kc=kc: e.matmul(
                        ps_p[half * 64:(half + 1) * 64, 0:512],
                        slab[:, kc * D + h * 64:kc * D + (h + 1) * 64],
                        xln[:, kc * T + g * 512:kc * T + (g + 1) * 512],
                        start=(kc == 0), stop=(kc == 7)))
            P.mark_mm()
            P.wait("scalar", P.done["tensor"])
            P.emit("scalar", lambda e, g=g: e.activation(
                out=out_sb[:, g * 512:(g + 1) * 512], in_=ps_p[:, 0:512],
                func=func), P.done["scalar"], serial=True)
            P.wait("tensor", P.done["scalar"])

    P.wait("tensor", P.done["vector"])
    fm_proj(kslab, AF.Exp, t1)
    P.wait(V, P.done["scalar"])
    P.emit(V, lambda e: e.tensor_mul(kdd[:, 0:T], t1[:, 0:T], trig[:, 0:T]),
           P.done["vector"], serial=True)
    if full:
        fm_proj(qslab, AF.Silu, qsb)

    for tc in range(16):
        for kc in range(8):
            P.emit("tensor", lambda e, tc=tc, kc=kc: e.matmul(
                ps_c[:], xln[:, kc * T + tc * 128:kc * T + (tc + 1) * 128],
                kslab[:, kc * D + h * 64:kc * D + (h + 1) * 64],
                start=(kc == 0), stop=(kc == 7)))
        P.mark_mm()
        P.wait("scalar", P.done["tensor"])
        P.emit("scalar", lambda e, tc=tc: e.activation(
            out=Etm[:, tc * 64:(tc + 1) * 64], in_=ps_c[:], func=AF.Exp),
            P.done["scalar"], serial=True)
        P.wait("tensor", P.done["scalar"])
        for kc in range(8):
            P.emit("tensor", lambda e, tc=tc, kc=kc: e.matmul(
                ps_c[:], xln[:, kc * T + tc * 128:kc * T + (tc + 1) * 128],
                vslab[:, kc * D + h * 64:kc * D + (h + 1) * 64],
                start=(kc == 0), stop=(kc == 7)))
        P.mark_mm()
        P.wait(V, P.done["tensor"])
        P.emit(V, lambda e, tc=tc: e.tensor_copy(
            vtm[:, tc * 64:(tc + 1) * 64], ps_c[:]), P.done["vector"],
            serial=True)
        P.wait("tensor", P.done["vector"])
    P.wait(V, P.done["scalar"])
    P.emit(V, lambda e: e.tensor_mul(
        kddtm[:].rearrange("p (tc e) -> p tc e", e=128)[:, :, 0:64],
        Etm[:].rearrange("p (tc d) -> p tc d", d=64),
        ctm_d), serial=True)
    P.emit(V, lambda e: e.tensor_mul(
        kddtm[:].rearrange("p (tc e) -> p tc e", e=128)[:, :, 64:128],
        Etm[:].rearrange("p (tc d) -> p tc d", d=64),
        stm_d), P.done["vector"], serial=True)

    scb, S_snap, idt = L.get("scb"), L.get("S_snap"), L.get("idt")
    S_run = L["S_run"]
    for tc in range(16):
        P.emit(V, lambda e, tc=tc: e.tensor_add(
            Etm[0:1, tc * 64:(tc + 1) * 64], Etm[0:1, tc * 64:(tc + 1) * 64],
            carry[0:1, h * 64:(h + 1) * 64]), P.done["vector"], serial=True)
        P.wait("tensor", P.done["vector"])
        P.emit("tensor", lambda e, tc=tc: e.matmul(
            ps_c[:], ut[:], Etm[:, tc * 64:(tc + 1) * 64],
            start=True, stop=True))
        P.emit("tensor", lambda e, tc=tc: e.matmul(
            ps_sg[:], com.ones128[:], Etm[:, tc * 64:(tc + 1) * 64],
            start=True, stop=True))
        P.mark_mm()
        P.wait(V, P.done["tensor"])
        P.emit(V, lambda e: e.tensor_copy(carry[0:1, h * 64:(h + 1) * 64],
                                          ps_sg[:]), serial=True)
        if full:
            def _recip(e, tc=tc):
                with nc.allow_low_precision(reason="recip feeds bf16 matmul"):
                    return e.reciprocal(rcpb[:, tc * 64:(tc + 1) * 64], ps_c[:])
            P.emit(V, _recip, P.done["vector"], serial=True)
            P.wait("tensor", P.done["vector"])
            P.emit("tensor", lambda e, tc=tc: e.transpose(
                L["ps_tr"][0:64, :], rcpb[:, tc * 64:(tc + 1) * 64], idt[:, :]))
            P.mark_mm()
            P.wait(V, P.done["tensor"])
            P.emit(V, lambda e, tc=tc: e.tensor_copy(
                rcpf[0:64, tc * 128:(tc + 1) * 128], L["ps_tr"][0:64, :]),
                P.done["vector"], serial=True)
            P.wait("tensor", P.done["vector"])
            P.emit("tensor", lambda e, tc=tc: e.matmul(
                L["ps_pt"][64:128, :], idt[0:64, 0:64],
                rcpf[0:64, tc * 128:(tc + 1) * 128], start=True, stop=True))
            P.mark_mm()
            P.wait(V, P.done["tensor"])
            P.emit(V, lambda e, tc=tc: e.tensor_copy(
                rcpf[64:128, tc * 128:(tc + 1) * 128], L["ps_pt"][64:128, :]),
                P.done["vector"], serial=True)
        if tc == 0:
            if full:
                P.emit(V, lambda e: e.tensor_copy(
                    S_run[:], scb[:, h * 64:(h + 1) * 64]),
                    P.done["vector"], serial=True)
            else:
                P.emit(V, lambda e: e.memset(S_run[:], 0.0),
                       P.done["vector"], serial=True)
        P.wait("tensor", P.done["vector"])
        P.emit("tensor", lambda e, tc=tc: e.matmul(
            ps_S[:], kddtm[:, tc * 128:(tc + 1) * 128],
            vtm[:, tc * 64:(tc + 1) * 64], start=True, stop=True))
        P.mark_mm()
        P.wait(V, P.done["tensor"])
        P.emit(V, lambda e: e.tensor_add(S_run[:], S_run[:], ps_S[:]),
               serial=True)
        if full:
            P.emit(V, lambda e, tc=tc: e.tensor_copy(
                S_snap[:, tc * 64:(tc + 1) * 64], S_run[:]),
                P.done["vector"], serial=True)
        else:
            P.emit(V, lambda e: e.wait_ge(P.bar.h, 0), P.done["vector"],
                   serial=True)
        P.wait("tensor", P.done["vector"])

    if not full:
        S_o, sto = L["S_o"], P.psem("l1a_sto")
        P.wait("gpsimd", P.done["vector"])
        P.dma("gpsimd", S_o[h * 128:(h + 1) * 128, :], S_run[:], sto)
        P.wait(V, sto, sto.n)
        return

    P.wait(V, P.done["scalar"])
    P.emit(V, lambda e: e.tensor_mul(t1[:, 0:T], qsb[:, 0:T], trig[:, 0:T]),
           serial=True)
    P.emit(V, lambda e: e.tensor_mul(qp[:, 0:T], t1[:, 0:T], rcpf[:, 0:T]),
           P.done["vector"], serial=True)

    ps_pt, ps_o, pt, o_fm = L["ps_pt"], L["ps_o"], L["pt"], L["o_fm"]
    qch, qrow = h // 2, (h % 2) * 64
    for tc in range(16):
        P.wait("tensor", P.done["vector"])
        P.emit("tensor", lambda e, tc=tc: e.matmul(
            ps_pt[:], kdd[:, tc * 128:(tc + 1) * 128],
            qp[:, tc * 128:(tc + 1) * 128], start=True, stop=True))
        P.mark_mm()
        P.wait(V, P.done["tensor"])
        P.emit(V, lambda e: e.tensor_mul(pt[:], ps_pt[:], ut[:]),
               P.done["vector"], serial=True)
        P.wait("tensor", P.done["vector"])
        S_prev = (scb[:, h * 64:(h + 1) * 64] if tc == 0
                  else S_snap[:, (tc - 1) * 64:tc * 64])
        P.emit("tensor", lambda e, tc=tc, S_prev=S_prev: e.matmul(
            ps_o[:], S_prev, qp[:, tc * 128:(tc + 1) * 128],
            start=True, stop=False))
        P.emit("tensor", lambda e, tc=tc: e.matmul(
            ps_o[:], vtm[:, tc * 64:(tc + 1) * 64], pt[:],
            start=False, stop=True))
        P.mark_mm()
        P.wait(V, P.done["tensor"])
        P.emit(V, lambda e, tc=tc: e.tensor_copy(
            o_fm[qrow:qrow + 64, qch * T + tc * 128:qch * T + (tc + 1) * 128],
            ps_o[:]), P.done["vector"], serial=True)
        P.wait("tensor", P.done["vector"])


def emit_l1a_post(P, com, nc, L):
    V = "vector"
    xln, o_fm, idt = L["xln"], L["o_fm"], L["idt"]
    wg1, wg2, wo, xin, x3, hs = (L["wg1"], L["wg2"], L["wo"], L["xin"],
                                 L["x3"], L["hs"])
    nw_col = L["nw_col"]
    with contextlib.ExitStack() as sc:
        g1b = sc.enter_context(nc.sbuf_tensor(L["tg"] + "g1b", [64, T], BF16))
        sg = sc.enter_context(nc.sbuf_tensor(L["tg"] + "sg", [128, 512], BF16))
        g2 = sc.enter_context(nc.sbuf_tensor(L["tg"] + "g2", [128, T], BF16))
        rstd = sc.enter_context(nc.sbuf_tensor(L["tg"] + "rstd16", [16, T], F32))
        rstdb = sc.enter_context(nc.sbuf_tensor(L["tg"] + "rstdb16", [16, T], BF16))
        rowt = sc.enter_context(nc.sbuf_tensor(L["tg"] + "rowt", [1, 512], BF16))
        wg1s = sc.enter_context(nc.sbuf_tensor(L["tg"] + "wg1s", [128, 8 * 64], BF16))
        wg2s = sc.enter_context(nc.sbuf_tensor(L["tg"] + "wg2s", [64, D], BF16))
        ps_g = sc.enter_context(nc.psum_tensor(L["tg"] + "ps_g1", [64, 512], F32))
        ps_f = sc.enter_context(nc.psum_tensor(L["tg"] + "ps_gf", [128, 512], F32))
        ps_ss = sc.enter_context(nc.psum_tensor(L["tg"] + "ps_ss", [16, T], F32))
        ps_r = sc.enter_context(nc.psum_tensor(L["tg"] + "ps_row", [1, 512], F32))
        ps_bc = sc.enter_context(nc.psum_tensor(L["tg"] + "ps_bc2", [128, 512], F32))
        wl = P.psem("l1a_wg")
        P.wait("gpsimd", P.done["tensor"])
        P.wait("gpsimd", P.done["vector"])
        P.wait("gpsimd", P.done["scalar"])
        for kc in range(8):
            P.dma("gpsimd", wg1s[:, kc * 64:(kc + 1) * 64],
                  wg1[kc * 128:(kc + 1) * 128, :], wl)
        P.dma("gpsimd", wg2s[:, :], wg2[:, :], wl)
        P.wait("tensor", wl)
        for g in range(4):
            P.wait("tensor", P.done["vector"])
            for kc in range(8):
                P.emit("tensor", lambda e, g=g, kc=kc: e.matmul(
                    ps_g[:], wg1s[:, kc * 64:(kc + 1) * 64],
                    xln[:, kc * T + g * 512:kc * T + (g + 1) * 512],
                    start=(kc == 0), stop=(kc == 7)))
            P.mark_mm()
            P.wait(V, P.done["tensor"])
            P.emit(V, lambda e, g=g: e.tensor_copy(
                g1b[:, g * 512:(g + 1) * 512], ps_g[:]), P.done["vector"],
                serial=True)
        for c in range(8):
            for g in range(4):
                P.wait("tensor", P.done["vector"])
                P.emit("tensor", lambda e, c=c, g=g: e.matmul(
                    ps_f[:], wg2s[:, c * 128:(c + 1) * 128],
                    g1b[:, g * 512:(g + 1) * 512], start=True, stop=True))
                P.mark_mm()
                P.wait("scalar", P.done["tensor"])
                P.emit("scalar", lambda e: e.activation(
                    out=sg[:], in_=ps_f[:], func=AF.Sigmoid),
                    P.done["scalar"], serial=True)
                P.wait(V, P.done["scalar"])
                P.emit(V, lambda e, c=c, g=g: e.tensor_mul(
                    o_fm[:, c * T + g * 512:c * T + (g + 1) * 512],
                    o_fm[:, c * T + g * 512:c * T + (g + 1) * 512], sg[:]),
                    P.done["vector"], serial=True)
        for c in range(8):
            P.wait("scalar", P.done["vector"])
            P.wait("scalar", P.done["tensor"])
            P.emit("scalar", lambda e, c=c: e.activation(
                out=g2[:, 0:T], in_=o_fm[:, c * T:(c + 1) * T],
                func=AF.Square), P.done["scalar"], serial=True)
            P.wait("tensor", P.done["scalar"])
            for g in range(4):
                P.emit("tensor", lambda e, c=c, g=g: e.matmul(
                    ps_ss[:, g * 512:(g + 1) * 512], hs[:, c * 16:(c + 1) * 16],
                    g2[:, g * 512:(g + 1) * 512],
                    start=(c == 0), stop=(c == 7)))
            P.mark_mm()
        P.wait("scalar", P.done["tensor"])
        P.emit("scalar", lambda e: e.activation(
            out=rstd[:], in_=ps_ss[:], func=AF.Ln,
            bias=com.col(1e-6)[0:16, 0:1], scale=1.0 / 64), serial=True)
        P.emit("scalar", lambda e: e.activation(
            out=rstd[:], in_=rstd[:], func=AF.Exp, scale=-0.5),
            P.done["scalar"], serial=True)
        P.wait(V, P.done["scalar"])
        P.emit(V, lambda e: e.tensor_copy(rstdb[:], rstd[:]),
               P.done["vector"], serial=True)
        for c in range(8):
            for g in range(4):
                for j in range(2):
                    h = 2 * c + j
                    P.wait("tensor", P.done["vector"])
                    P.emit("tensor", lambda e, h=h, g=g: e.matmul(
                        ps_r[:], idt[0:16, h:h + 1],
                        rstdb[:, g * 512:(g + 1) * 512], start=True, stop=True))
                    P.mark_mm()
                    P.wait(V, P.done["tensor"])
                    P.emit(V, lambda e: e.tensor_copy(rowt[:], ps_r[:]),
                           P.done["vector"], serial=True)
                    P.wait("tensor", P.done["vector"])
                    P.emit("tensor", lambda e, j=j: e.matmul(
                        ps_bc[j * 64:(j + 1) * 64, :],
                        com.onesP_bf[0:1, 0:64], rowt[:],
                        start=True, stop=True))
                    P.mark_mm()
                P.wait(V, P.done["tensor"])
                P.emit(V, lambda e, c=c, g=g: e.scalar_tensor_tensor(
                    o_fm[:, c * T + g * 512:c * T + (g + 1) * 512],
                    o_fm[:, c * T + g * 512:c * T + (g + 1) * 512],
                    nw_col[c], ps_bc[:], ALU.mult, ALU.mult),
                    P.done["vector"], serial=True)
    with contextlib.ExitStack() as sc:
        wl = P.psem("l1a_wo")
        P.wait("gpsimd", P.done["tensor"])
        P.wait("gpsimd", P.done["vector"])
        woslab = load_wslab_sc(P, sc, L["tg"] + "wo_l1", wo, D, D, wl)
        P.wait("tensor", wl)

        def oevac(oc, g, ps):
            P.emit(V, lambda e, oc=oc, g=g: e.tensor_copy(
                xln[:, oc * T + g * 512:oc * T + (g + 1) * 512], ps),
                P.done["vector"], serial=True)

        emit_proj_fm(P, com, o_fm[:], woslab, D, 0, 8, T, oevac, "vector",
                     tag="l1a_wo")
    with contextlib.ExitStack() as sc:
        x2f = sc.enter_context(nc.sbuf_tensor(L["tg"] + "x2f", [128, 8 * T], F32))
        ldf = P.psem("l1a_ldf")
        stf = P.psem("l1a_stf")
        P.wait("sync", P.done["vector"])
        P.wait("sync", P.done["tensor"])
        P.dma("sync", x2f[:].rearrange("p (c t) -> p c t", c=8),
              xin[:, :].rearrange("(c p) t -> p c t", p=128), ldf)
        w = [(e, ldf, ldf.n) for e in ("vector", "scalar", "tensor")]
        emit_ln_big(P, com, xln[:], T, "residual", xold=x2f, xnew=x2f,
                    tag=L["tg"] + "fln", waits=w, x_bf16=True)
        P.wait("sync", P.done["vector"])
        P.dma("sync", x3[:, :].rearrange("(c p) t -> p c t", p=128),
              x2f[:].rearrange("p (c t) -> p c t", c=8), stf)
        P.wait("sync", stf)


TW_A = 2560


def build_l1f_attn(ctx=None, tag=""):
    fused_halo = None
    if ctx is None:
        nc = bass.Bass("TRN2", target_bir_lowering=False, debug=False)
        x_own = nc.dram_tensor("x_own", [D, T], F32, kind="ExternalInput").ap()
        x_halo = nc.dram_tensor("x_halo", [D, 512], F32, kind="ExternalInput").ap()
        wqkv = nc.dram_tensor("wqkv", [D, 16 * 384], F32, kind="ExternalInput").ap()
        wo = nc.dram_tensor("wo", [D, D], F32, kind="ExternalInput").ap()
        tabs = nc.dram_tensor("tabs", [4 * 128, TW_A], F32, kind="ExternalInput").ap()
        masks = nc.dram_tensor("masks", [2 * 128, 1024], F32, kind="ExternalInput").ap()
        consts = nc.dram_tensor("consts", [128, 134], F32, kind="ExternalInput").ap()
        pairbc = nc.dram_tensor("pairbc", [2, 128], F32, kind="ExternalInput").ap()
        y_out = nc.dram_tensor("y", [D, T], F32, kind="ExternalOutput").ap()
        P = Prog(nc)
        com = Common(P, Tn=512, with_ln_xb=False)
        sc0 = P.stack
    else:
        nc, P, com, sc0 = ctx["nc"], ctx["P"], ctx["com"], ctx["sc"]
        io = ctx["io"]
        (x_own, wqkv, wo, tabs, masks, consts, pairbc, y_out) = (
            io["x_own"], io["wqkv"], io["wo"], io["tabs"], io["masks"],
            io["consts"], io["pairbc"], io["y_out"])
        fused_halo = ctx["halo"]
    com.col(1e-38)
    Tq = 2048
    Tk = 2304
    sb0 = lambda n_, sh_, dt_: sc0.enter_context(nc.sbuf_tensor(n_, sh_, dt_))
    xw = sb0(tag + "xw_a", [128, 8 * TW_A], BF16)
    o_fm = sb0(tag + "o_fa", [128, 8 * Tq], BF16)
    tb4 = sb0(tag + "tabs_sb", [128, 4 * TW_A], BF16)
    msk = sb0(tag + "msk_sb", [128, 2 * 1024], BF16)
    cst = sb0(tag + "cst_sb", [128, 134], F32)
    psw = sb0(tag + "psw_sb", [128, 128], BF16)
    ld = P.psem("lfa_ld")
    if fused_halo is None:
        P.dma("gpsimd", xw[:].rearrange("p (c t) -> p c t", c=8)[:, :, 0:512],
              x_halo[:, :].rearrange("(c p) t -> p c t", p=128), ld)
    else:
        halo_g, wslc = fused_halo
        xwr = xw[:].rearrange("p (c t) -> p c t", c=8)
        P.emit("vector", lambda e: e.memset(xwr[:, :, 0:255], 0.0),
               P.done["vector"], serial=True)
        htmp = [sb0(tag + "htmp%d" % r, [128, 8 * 257], BF16)
                for r in range(4)]
        for r in range(4):
            P.dma("gpsimd",
                  htmp[r][:].rearrange("p (c t) -> p c t", c=8),
                  halo_g[r * 1024:(r + 1) * 1024, :].rearrange(
                      "(c p) t -> p c t", p=128), ld)
        P.wait("vector", ld)
        hr = [h[:].rearrange("p (c t) -> p c t", c=8) for h in htmp]
        P.emit("vector", lambda e: e.tensor_scalar(
            xwr[:, :, 255:512], hr[0], wslc[:, 0:1], None, ALU.mult),
            serial=True)
        for r in range(1, 4):
            P.emit("vector", lambda e, r=r: e.scalar_tensor_tensor(
                xwr[:, :, 255:512], hr[r], wslc[:, r:r + 1],
                xwr[:, :, 255:512], ALU.mult, ALU.add), serial=True)
        P.emit("vector", lambda e: e.wait_ge(P.bar.h, 0), P.done["vector"],
               serial=True)
    P.dma("gpsimd", xw[:].rearrange("p (c t) -> p c t", c=8)[:, :, 512:TW_A],
          x_own[:, :].rearrange("(c p) t -> p c t", p=128), ld)
    for i in range(4):
        P.dma("gpsimd", tb4[:, i * TW_A:(i + 1) * TW_A],
              tabs[i * 128:(i + 1) * 128, :], ld)
    for i in range(2):
        P.dma("gpsimd", msk[:, i * 1024:(i + 1) * 1024],
              masks[i * 128:(i + 1) * 128, :], ld)
    P.dma("gpsimd", cst[:], consts[:, :], ld)
    P.dma("gpsimd", psw[:], consts[:, 6:134], ld)
    csq = tb4[:, 0 * TW_A:1 * TW_A]
    snq = tb4[:, 1 * TW_A:2 * TW_A]
    csk = tb4[:, 2 * TW_A:3 * TW_A]
    snk = tb4[:, 3 * TW_A:4 * TW_A]
    qs_col, ks_col = cst[:, 0:1], cst[:, 1:2]
    for eng in ("scalar", "tensor", "vector", "gpsimd"):
        P.wait(eng, ld)
    if fused_halo is not None:
        P.wait("tensor", P.done["vector"])
        P.wait("scalar", P.done["vector"])
    psel = sb0(tag + "psel", [128, 4], BF16)
    pbc = sb0(tag + "pbc", [2, 128], BF16)
    P.dma("gpsimd", pbc[:], pairbc[:, :], ld)
    P.wait("vector", ld)
    P.emit("vector", lambda e: e.tensor_copy(psel[:, 0:2], cst[:, 2:4]),
           P.done["vector"], serial=True)

    with contextlib.ExitStack() as sc:
        slab = sc.enter_context(nc.sbuf_tensor(tag + "qkv_hp", [128, 8 * 384], BF16))
        qraw = sc.enter_context(nc.sbuf_tensor(tag + "qraw", [128, Tq], BF16))
        sq = sc.enter_context(nc.sbuf_tensor(tag + "sq_a", [128, Tk], BF16))
        qrot = sc.enter_context(nc.sbuf_tensor(tag + "qrot", [128, Tq], BF16))
        kraw = sc.enter_context(nc.sbuf_tensor(tag + "kraw_a", [128, Tk], BF16))
        krot = sc.enter_context(nc.sbuf_tensor(tag + "krot", [128, Tk], BF16))
        rnb = sc.enter_context(nc.sbuf_tensor(tag + "rnb", [2, Tk], BF16))
        tq1 = sc.enter_context(nc.sbuf_tensor(tag + "tq1", [128, 512], F32))
        tq2 = sc.enter_context(nc.sbuf_tensor(tag + "tq2", [128, 512], F32))
        vaug = sc.enter_context(nc.sbuf_tensor(tag + "vaug", [128, 18 * 130], BF16))
        pt = sc.enter_context(nc.sbuf_tensor(tag + "pt_a", [128, 2 * 1024], BF16))
        rc = sc.enter_context(nc.sbuf_tensor(tag + "rc_a", [128, 2048], F32))
        rcb = sc.enter_context(nc.sbuf_tensor(tag + "rcb_a", [128, 2 * 2048], BF16))
        ps_a = sc.enter_context(nc.psum_tensor(tag + "psA", [128, 512], F32))
        ps_b = sc.enter_context(nc.psum_tensor(tag + "psB", [128, 512], F32))
        ps_bc = ps_a
        ps_ss = sc.enter_context(nc.psum_tensor(tag + "psSS", [2, 512], F32))
        ps_st = sc.enter_context(nc.psum_tensor(tag + "psST", [128, 2048], F32))
        ps_o = sc.enter_context(nc.psum_tensor(tag + "psO", [65, 512], F32))
        wsl = P.psem("lfa_w")
        for hp in range(8):
            emit_l1f_attn_hp(P, com, locals(), hp)
    with contextlib.ExitStack() as sc:
        P.wait("gpsimd", P.done["tensor"])
        P.wait("gpsimd", P.done["vector"])
        wl = P.psem("lfa_wo")
        woslab = load_wslab_sc(P, sc, tag + "wo_lfa", wo, D, D, wl)
        xblk = sc.enter_context(nc.sbuf_tensor(tag + "xblk", [128, 8 * 512], F32))
        ytmp = sc.enter_context(nc.sbuf_tensor(tag + "ytmp", [128, 2 * 512], F32))
        ps_w = [sc.enter_context(nc.psum_tensor(tag + "psW%d" % i, [128, 512], F32))
                for i in range(2)]
        ldx = P.psem("lfa_ldx")
        sty = [P.psem("lfa_sty0"), P.psem("lfa_sty1")]
        P.wait("tensor", wl)
        idx = 0
        evcnt = []
        for g in range(4):
            P.wait("sync", P.done["vector"])
            for c in range(8):
                P.dma("sync", xblk[:, c * 512:(c + 1) * 512],
                      x_own[c * 128:(c + 1) * 128, g * 512:(g + 1) * 512], ldx)
            for oc in range(8):
                b = idx % 2
                if idx >= 2:
                    P.wait("tensor", P.done["vector"], evcnt[idx - 2])
                for kc in range(8):
                    P.emit("tensor", lambda e, oc=oc, g=g, kc=kc, b=b: e.matmul(
                        ps_w[b][:], woslab[:, kc * D + oc * 128:kc * D + (oc + 1) * 128],
                        o_fm[:, kc * Tq + g * 512:kc * Tq + (g + 1) * 512],
                        start=(kc == 0), stop=(kc == 7)))
                P.mark_mm()
                P.wait("vector", P.done["tensor"])
                P.wait("vector", ldx, ldx.n)
                P.wait("vector", sty[b], sty[b].n)
                P.emit("vector", lambda e, oc=oc, b=b: e.tensor_add(
                    ytmp[:, b * 512:(b + 1) * 512], ps_w[b][:],
                    xblk[:, oc * 512:(oc + 1) * 512]), P.done["vector"],
                    serial=True)
                evcnt.append(P.done["vector"].n)
                P.wait("gpsimd", P.done["vector"])
                P.dma("gpsimd", y_out[oc * 128:(oc + 1) * 128,
                                      g * 512:(g + 1) * 512],
                      ytmp[:, b * 512:(b + 1) * 512], sty[b])
                idx += 1
        for sm in sty:
            P.wait("gpsimd", sm)
        P.wait("sync", sty[0])
    if ctx is None:
        P.build()
        return nc
    return None


def emit_l1f_attn_hp(P, com, L, hp):
    nc = P.nc
    V = "vector"
    (xw, slab, qraw, sq, qrot, kraw, krot, rnb, tq1, tq2, vaug, pt, rc, rcb,
     ps_a, ps_b, ps_bc, ps_ss, ps_st, ps_o, psel, psw, o_fm, msk,
     qs_col, ks_col, csq, snq, csk, snk, wsl, wqkv) = (
        L["xw"], L["slab"], L["qraw"], L["sq"], L["qrot"], L["kraw"],
        L["krot"], L["rnb"], L["tq1"], L["tq2"], L["vaug"], L["pt"], L["rc"],
        L["rcb"], L["ps_a"], L["ps_b"], L["ps_bc"], L["ps_ss"], L["ps_st"],
        L["ps_o"], L["psel"], L["psw"], L["o_fm"], L["msk"], L["qs_col"],
        L["ks_col"], L["csq"], L["snq"], L["csk"], L["snk"], L["wsl"],
        L["wqkv"])
    Tq, Tk = 2048, 2304
    pbc = L["pbc"]

    P.wait("gpsimd", P.done["tensor"])
    P.wait("gpsimd", P.done["vector"])
    P.wait("gpsimd", P.done["scalar"])
    for kc in range(8):
        P.dma("gpsimd", slab[:, kc * 384:(kc + 1) * 384],
              wqkv[kc * 128:(kc + 1) * 128, hp * 384:(hp + 1) * 384], wsl)
    for eng in ("tensor", "scalar", "vector"):
        P.wait(eng, wsl)
    if hp == 0:
        P.emit(V, lambda e: e.memset(vaug[:], 1.0), P.done["vector"],
               serial=True)

    def rot_path(raw, rot, scol, cst_, snt, t0, Tn, wq_off):
        nblk = (Tn + 511) // 512
        for g in range(nblk):
            N = min(512, Tn - g * 512)
            for kc in range(8):
                sh = 1 if kc >= 4 else 0
                P.emit("tensor", lambda e, g=g, kc=kc, N=N, sh=sh: e.matmul(
                    ps_a[:, 0:N],
                    slab[:, kc * 384 + wq_off:kc * 384 + wq_off + 128],
                    xw[:, kc * TW_A + t0 + g * 512 - sh:
                       kc * TW_A + t0 + g * 512 - sh + N],
                    start=(kc == 0), stop=(kc == 7)))
            P.mark_mm()
            P.wait(V, P.done["tensor"])
            P.emit(V, lambda e, g=g, N=N: e.tensor_copy(
                raw[:, g * 512:g * 512 + N], ps_a[:, 0:N]), P.done["vector"],
                serial=True)
            P.wait("scalar", P.done["vector"])
            P.emit("scalar", lambda e, g=g, N=N: e.activation(
                out=sq[:, g * 512:g * 512 + N], in_=raw[:, g * 512:g * 512 + N],
                func=AF.Square), P.done["scalar"], serial=True)
            P.wait("tensor", P.done["scalar"])
            P.emit("tensor", lambda e, g=g, N=N: e.matmul(
                ps_ss[:, 0:N], psel[:, 0:2], sq[:, g * 512:g * 512 + N],
                start=True, stop=True))
            P.mark_mm()
            P.wait("scalar", P.done["tensor"])
            P.emit("scalar", lambda e, N=N: e.activation(
                out=tq1[0:2, 0:N], in_=ps_ss[:, 0:N], func=AF.Ln,
                bias=com.col(1e-38)[0:2, 0:1]), serial=True)
            P.emit("scalar", lambda e, N=N: e.activation(
                out=tq1[0:2, 0:N], in_=tq1[0:2, 0:N], func=AF.Exp, scale=0.5),
                P.done["scalar"], serial=True)
            P.wait(V, P.done["scalar"])
            P.emit(V, lambda e, N=N: e.tensor_scalar(
                tq1[0:2, 0:N], tq1[0:2, 0:N], 1e-12, None, ALU.max),
                serial=True)
            P.emit(V, lambda e, N=N: e.reciprocal(
                tq1[0:2, 0:N], tq1[0:2, 0:N]), serial=True)
            P.emit(V, lambda e, N=N: e.tensor_copy(
                rnb[0:2, 0:N], tq1[0:2, 0:N]), P.done["vector"], serial=True)
            P.wait("tensor", P.done["vector"])
            P.emit("tensor", lambda e, N=N: e.matmul(
                ps_bc[:, 0:N], pbc[0:2, :], rnb[0:2, 0:N],
                start=True, stop=True))
            P.mark_mm()
            P.wait(V, P.done["tensor"])
            P.emit(V, lambda e, g=g, N=N, scol=scol: e.scalar_tensor_tensor(
                raw[:, g * 512:g * 512 + N], raw[:, g * 512:g * 512 + N],
                scol, ps_bc[:, 0:N], ALU.mult, ALU.mult), P.done["vector"],
                serial=True)
            P.wait("tensor", P.done["vector"])
            P.emit("tensor", lambda e, g=g, N=N: e.matmul(
                ps_b[:, 0:N], psw[:], raw[:, g * 512:g * 512 + N],
                start=True, stop=True))
            P.mark_mm()
            P.wait(V, P.done["tensor"])
            P.emit(V, lambda e, g=g, N=N, cst_=cst_: e.tensor_mul(
                tq1[:, 0:N], raw[:, g * 512:g * 512 + N],
                cst_[:, t0 + g * 512:t0 + g * 512 + N]), serial=True)
            P.emit(V, lambda e, g=g, N=N, snt=snt: e.tensor_mul(
                tq2[:, 0:N], ps_b[:, 0:N],
                snt[:, t0 + g * 512:t0 + g * 512 + N]), serial=True)
            P.emit(V, lambda e, g=g, N=N: e.tensor_add(
                rot[:, g * 512:g * 512 + N], tq1[:, 0:N], tq2[:, 0:N]),
                P.done["vector"], serial=True)
            P.wait("tensor", P.done["vector"])

    rot_path(qraw, qrot, qs_col, csq, snq, 512, Tq, 0)
    rot_path(kraw, krot, ks_col, csk, snk, 256, Tk, 128)

    for kt in range(18):
        for kc in range(8):
            sh = 1 if kc >= 4 else 0
            P.emit("tensor", lambda e, kt=kt, kc=kc, sh=sh: e.matmul(
                ps_a[:, 0:128],
                xw[:, kc * TW_A + 256 + kt * 128 - sh:
                   kc * TW_A + 256 + (kt + 1) * 128 - sh],
                slab[:, kc * 384 + 256:kc * 384 + 384],
                start=(kc == 0), stop=(kc == 7)))
        P.mark_mm()
        P.wait(V, P.done["tensor"])
        P.emit(V, lambda e, kt=kt: e.tensor_copy(
            vaug[:, kt * 130:(kt + 1) * 130].rearrange(
                "p (j w) -> p j w", j=2)[:, :, 0:64],
            ps_a[:, 0:128].rearrange("p (j d) -> p j d", j=2)),
            P.done["vector"], serial=True)
        P.wait("tensor", P.done["vector"])

    for j in range(2):
        h = 2 * hp + j
        jr = j * 64
        s_cnt = {}
        e_cnt = {}
        a_cnt = {}
        v_cnt = {}

        def st_group(B, j=j, jr=jr):
            p = B % 2
            if B >= 3:
                P.wait("tensor", P.done["scalar"], e_cnt[B - 2])
            for m in range(4):
                kcol = 256 * (B - 1) + 128 * m
                for a in range(2):
                    qcol = 256 * (B - 1) + 128 * a
                    P.emit("tensor", lambda e, m=m, a=a, kcol=kcol,
                           qcol=qcol, jr=jr, p=p: e.matmul(
                        ps_st[:, p * 1024 + m * 256 + a * 128:
                              p * 1024 + m * 256 + (a + 1) * 128],
                        krot[jr:jr + 64, kcol:kcol + 128],
                        qrot[jr:jr + 64, qcol:qcol + 128],
                        start=True, stop=True))
            P.mark_mm()
            s_cnt[B] = P.done["tensor"].n

        def tail_group(B, j=j, jr=jr):
            p = B % 2
            P.wait(V, P.done["tensor"], s_cnt[B])
            moff = 1024 if B == 1 else 0
            P.emit(V, lambda e, moff=moff, p=p: e.tensor_add(
                ps_st[:, p * 1024:(p + 1) * 1024],
                ps_st[:, p * 1024:(p + 1) * 1024],
                msk[:, moff:moff + 1024]), P.done["vector"], serial=True)
            m_cnt = P.done["vector"].n
            P.wait("scalar", P.done["vector"], m_cnt)
            if B >= 3:
                P.wait("scalar", P.done["tensor"], a_cnt[B - 2])
            for half in range(2):
                P.emit("scalar", lambda e, half=half, p=p: e.activation(
                    out=pt[:, p * 1024 + half * 512:p * 1024 + (half + 1) * 512],
                    in_=ps_st[:, p * 1024 + half * 512:
                              p * 1024 + (half + 1) * 512],
                    func=AF.Exp, scale=8.0),
                    P.done["scalar"] if half == 1 else None)
            e_cnt[B] = P.done["scalar"].n
            P.wait("tensor", P.done["scalar"], e_cnt[B])
            if B >= 3:
                P.wait("tensor", P.done["vector"], v_cnt[B - 2])
            for m in range(4):
                kti = 2 * (B - 1) + m
                P.emit("tensor", lambda e, m=m, kti=kti, j=j, p=p: e.matmul(
                    ps_o[:, p * 256:(p + 1) * 256],
                    vaug[:, kti * 130 + j * 65:kti * 130 + (j + 1) * 65],
                    pt[:, p * 1024 + m * 256:p * 1024 + (m + 1) * 256],
                    start=(m == 0), stop=(m == 3)))
            P.mark_mm()
            a_cnt[B] = P.done["tensor"].n
            P.wait(V, P.done["tensor"], a_cnt[B])
            P.emit(V, lambda e, B=B, p=p: e.reciprocal(
                rc[64:65, (B - 1) * 256:B * 256], ps_o[64:65, p * 256:
                                                       (p + 1) * 256]),
                serial=True)
            P.emit(V, lambda e, B=B, p=p, jr=jr, hp=hp: e.tensor_copy(
                o_fm[jr:jr + 64, hp * Tq + (B - 1) * 256:hp * Tq + B * 256],
                ps_o[0:64, p * 256:(p + 1) * 256]), P.done["vector"],
                serial=True)
            v_cnt[B] = P.done["vector"].n

        st_group(1)
        for B in range(2, 9):
            st_group(B)
            tail_group(B - 1)
        tail_group(8)
        P.emit(V, lambda e: e.tensor_copy(rcb[64:65, 0:2048],
                                          rc[64:65, 0:2048]), serial=True)
        P.emit(V, lambda e: e.tensor_sub(rc[64:65, 0:2048],
                                         rc[64:65, 0:2048],
                                         rcb[64:65, 0:2048]), serial=True)
        P.emit(V, lambda e: e.tensor_copy(rcb[64:65, 2048:4096],
                                          rc[64:65, 0:2048]),
               P.done["vector"], serial=True)
        P.wait("tensor", P.done["vector"])
        for g in range(4):
            if g > 0:
                P.wait("tensor", P.done["vector"])
            for i2 in range(2):
                P.emit("tensor", lambda e, g=g, i2=i2, jr=jr: e.matmul(
                    ps_bc[jr:jr + 64, :], com.onesP_bf[64:65, 0:64],
                    rcb[64:65, i2 * 2048 + g * 512:i2 * 2048 + (g + 1) * 512],
                    start=(i2 == 0), stop=(i2 == 1)))
            P.mark_mm()
            P.wait(V, P.done["tensor"])
            P.emit(V, lambda e, g=g, jr=jr, hp=hp: e.tensor_mul(
                o_fm[jr:jr + 64, hp * Tq + g * 512:hp * Tq + (g + 1) * 512],
                o_fm[jr:jr + 64, hp * Tq + g * 512:hp * Tq + (g + 1) * 512],
                ps_bc[jr:jr + 64, :]), P.done["vector"], serial=True)
        P.wait("tensor", P.done["vector"])


TW_F = 2176


def build_l1f_ffn(ctx=None, tag=""):
    fused_halo = None
    if ctx is None:
        nc = bass.Bass("TRN2", target_bir_lowering=False, debug=False)
        x_own = nc.dram_tensor("x_own", [D, T], F32, kind="ExternalInput").ap()
        x_h1 = nc.dram_tensor("x_h1", [D, 128], F32, kind="ExternalInput").ap()
        w1 = nc.dram_tensor("w1p", [D, 6144], F32, kind="ExternalInput").ap()
        w2 = nc.dram_tensor("w2p", [3072, D], F32, kind="ExternalInput").ap()
        bpk = nc.dram_tensor("bpk", [128, 72], F32, kind="ExternalInput").ap()
        y_out = nc.dram_tensor("y", [D, T], F32, kind="ExternalOutput").ap()
        P = Prog(nc)
        com = Common(P, Tn=512, with_ln_xb=True)
        sc0 = P.stack
    else:
        nc, P, com, sc0 = ctx["nc"], ctx["P"], ctx["com"], ctx["sc"]
        io = ctx["io"]
        x_own, w1, w2, bpk, y_out = (io["x_own"], io["w1"], io["w2"],
                                     io["bpk"], io["y_out"])
        fused_halo = ctx["halo"]
    V = "vector"
    sb0 = lambda n_, sh_, dt_: sc0.enter_context(nc.sbuf_tensor(n_, sh_, dt_))
    xw = sb0(tag + "xw_f", [128, 8 * TW_F], BF16)
    bp = sb0(tag + "bp_f", [128, 72], F32)
    ld = P.psem("lff_ld")
    xwr = xw[:].rearrange("p (c t) -> p c t", c=8)
    if fused_halo is None:
        P.dma("gpsimd", xwr[:, :, 0:128],
              x_h1[:, :].rearrange("(c p) t -> p c t", p=128), ld)
    else:
        halo_g, wslc = fused_halo
        P.emit("vector", lambda e: e.memset(xwr[:, :, 0:127], 0.0),
               P.done["vector"], serial=True)
        htmp = [sb0(tag + "h1t%d" % r, [128, 8], BF16) for r in range(4)]
        for r in range(4):
            P.dma("gpsimd", htmp[r][:].rearrange("p (c o) -> p c o", o=1),
                  halo_g[r * 1024:(r + 1) * 1024, :].rearrange(
                      "(c p) o -> p c o", p=128), ld,
                  allow_slow_non_contiguous=True)
        P.wait("vector", ld)
        hr = [h[:].rearrange("p (c o) -> p c o", o=1) for h in htmp]
        P.emit("vector", lambda e: e.tensor_scalar(
            xwr[:, :, 127:128], hr[0], wslc[:, 0:1], None, ALU.mult),
            serial=True)
        for r in range(1, 4):
            P.emit("vector", lambda e, r=r: e.scalar_tensor_tensor(
                xwr[:, :, 127:128], hr[r], wslc[:, r:r + 1],
                xwr[:, :, 127:128], ALU.mult, ALU.add), serial=True)
        P.emit("vector", lambda e: e.wait_ge(P.bar.h, 0), P.done["vector"],
               serial=True)
    P.dma("gpsimd", xwr[:, :, 128:TW_F],
          x_own[:, :].rearrange("(c p) t -> p c t", p=128), ld)
    P.dma("gpsimd", bp[:], bpk[:, :], ld)
    for eng in ("scalar", "tensor", "vector", "gpsimd"):
        P.wait(eng, ld)
    if fused_halo is not None:
        P.wait("tensor", P.done["vector"])
        P.wait("scalar", P.done["vector"])
    g_col = [bp[:, c:c + 1] for c in range(8)]
    b_col = [bp[:, 8 + c:8 + c + 1] for c in range(8)]
    b1a = [bp[:, 16 + i:16 + i + 1] for i in range(24)]
    b1g = [bp[:, 40 + i:40 + i + 1] for i in range(24)]
    b2c = [bp[:, 64 + i:64 + i + 1] for i in range(8)]

    with contextlib.ExitStack() as sc:
        xn = sc.enter_context(nc.sbuf_tensor(tag + "xn_f", [128, 8 * 512], BF16))
        prod = sc.enter_context(nc.sbuf_tensor(tag + "prod", [128, 24 * 512], BF16))
        sla = sc.enter_context(nc.sbuf_tensor(tag + "sla", [128, 2 * 8 * 512], BF16))
        slg = sc.enter_context(nc.sbuf_tensor(tag + "slg", [128, 2 * 8 * 512], BF16))
        sl2 = sc.enter_context(nc.sbuf_tensor(tag + "sl2", [128, 2 * 24 * 128], BF16))
        ta = sc.enter_context(nc.sbuf_tensor(tag + "ta_f", [128, 2 * 512], F32))
        ga = sc.enter_context(nc.sbuf_tensor(tag + "ga_f", [128, 2 * 512], BF16))
        xblk = sc.enter_context(nc.sbuf_tensor(tag + "xblk_f", [128, 8 * 512], F32))
        ytmp = sc.enter_context(nc.sbuf_tensor(tag + "ytmp_f", [128, 2 * 512], F32))
        ps_a = sc.enter_context(nc.psum_tensor(tag + "psa_f", [128, 2 * 512], F32))
        ps_g = sc.enter_context(nc.psum_tensor(tag + "psg_f", [128, 2 * 512], F32))
        ps_y = [sc.enter_context(nc.psum_tensor(tag + "psy%d" % i, [128, 512], F32))
                for i in range(2)]
        wl = P.psem("lff_w")
        w2l = [P.psem("lff_w2_0"), P.psem("lff_w2_1")]
        ldx = P.psem("lff_ldx")
        sty = [P.psem("lff_sty0"), P.psem("lff_sty1")]
        for tb in range(4):
            t0 = tb * 512
            xsl = [xw[:, c * TW_F + 128 + t0 - (1 if c >= 4 else 0):
                      c * TW_F + 128 + t0 + 512 - (1 if c >= 4 else 0)]
                   for c in range(8)]
            w = [("vector", P.done["tensor"], P.done["tensor"].n)]
            emit_ln(P, com, None, 512, "bf16", out=xn,
                    tag=tag + "ffln%d" % tb, waits=w, xsl=xsl)
            P.wait(V, P.done["vector"])
            for c in range(8):
                P.emit(V, lambda e, c=c: e.tensor_scalar(
                    xn[:, c * 512:(c + 1) * 512], xn[:, c * 512:(c + 1) * 512],
                    g_col[c], b_col[c], ALU.mult, ALU.add),
                    P.done["vector"] if c == 7 else None, serial=True)
            t_cnt = {}
            v_cnt = {}
            for jb in range(6):
                bj = jb % 2
                if jb >= 2:
                    P.wait("gpsimd", P.done["tensor"], t_cnt[jb - 2])
                for kc in range(8):
                    P.dma("gpsimd", sla[:, bj * 4096 + kc * 512:
                                        bj * 4096 + (kc + 1) * 512],
                          w1[kc * 128:(kc + 1) * 128,
                             jb * 512:(jb + 1) * 512], wl)
                    P.dma("gpsimd", slg[:, bj * 4096 + kc * 512:
                                        bj * 4096 + (kc + 1) * 512],
                          w1[kc * 128:(kc + 1) * 128,
                             3072 + jb * 512:3072 + (jb + 1) * 512], wl)
                P.wait("tensor", wl)
                if jb == 0:
                    P.wait("tensor", P.done["vector"])
                for k4 in range(4):
                    kc2 = jb * 4 + k4
                    bk = kc2 % 2
                    if kc2 >= 2:
                        P.wait("tensor", P.done["vector"], v_cnt[kc2 - 2])
                    for kc in range(8):
                        P.emit("tensor", lambda e, k4=k4, kc=kc, bj=bj, bk=bk: e.matmul(
                            ps_a[:, bk * 512:(bk + 1) * 512],
                            sla[:, bj * 4096 + kc * 512 + k4 * 128:
                                bj * 4096 + kc * 512 + (k4 + 1) * 128],
                            xn[:, kc * 512:(kc + 1) * 512],
                            start=(kc == 0), stop=(kc == 7)))
                    for kc in range(8):
                        P.emit("tensor", lambda e, k4=k4, kc=kc, bj=bj, bk=bk: e.matmul(
                            ps_g[:, bk * 512:(bk + 1) * 512],
                            slg[:, bj * 4096 + kc * 512 + k4 * 128:
                                bj * 4096 + kc * 512 + (k4 + 1) * 128],
                            xn[:, kc * 512:(kc + 1) * 512],
                            start=(kc == 0), stop=(kc == 7)))
                    P.mark_mm()
                    t_cnt[kc2 // 4 if False else jb] = P.done["tensor"].n
                    P.wait("scalar", P.done["tensor"])
                    P.emit("scalar", lambda e, kc2=kc2, bk=bk: e.activation(
                        out=ga[:, bk * 512:(bk + 1) * 512],
                        in_=ps_g[:, bk * 512:(bk + 1) * 512],
                        func=AF.Gelu, bias=b1g[kc2]),
                        P.done["scalar"], serial=True)
                    P.wait(V, P.done["scalar"])
                    P.emit(V, lambda e, kc2=kc2, bk=bk: e.tensor_scalar(
                        ta[:, bk * 512:(bk + 1) * 512],
                        ps_a[:, bk * 512:(bk + 1) * 512],
                        b1a[kc2], None, ALU.add), serial=True)
                    P.emit(V, lambda e, kc2=kc2, bk=bk: e.tensor_mul(
                        prod[:, kc2 * 512:(kc2 + 1) * 512],
                        ta[:, bk * 512:(bk + 1) * 512],
                        ga[:, bk * 512:(bk + 1) * 512]),
                        P.done["vector"], serial=True)
                    v_cnt[kc2] = P.done["vector"].n
            P.wait("sync", P.done["vector"])
            for sm in sty:
                P.wait("sync", sm, sm.n)
            for c in range(8):
                P.dma("sync", xblk[:, c * 512:(c + 1) * 512],
                      x_own[c * 128:(c + 1) * 128, t0:t0 + 512], ldx)
            tw_cnt = {}
            vw_cnt = {}
            for oc in range(8):
                b = oc % 2
                if oc >= 2:
                    P.wait("gpsimd", P.done["tensor"], tw_cnt[oc - 2])
                for kc2 in range(24):
                    P.dma("gpsimd", sl2[:, b * 3072 + kc2 * 128:
                                        b * 3072 + (kc2 + 1) * 128],
                          w2[kc2 * 128:(kc2 + 1) * 128,
                             oc * 128:(oc + 1) * 128], w2l[b])
                P.wait("tensor", w2l[b])
                if oc == 0:
                    P.wait("tensor", P.done["vector"])
                elif oc >= 2:
                    P.wait("tensor", P.done["vector"], vw_cnt[oc - 2])
                for kc2 in range(24):
                    P.emit("tensor", lambda e, kc2=kc2, b=b: e.matmul(
                        ps_y[b][:], sl2[:, b * 3072 + kc2 * 128:
                                        b * 3072 + (kc2 + 1) * 128],
                        prod[:, kc2 * 512:(kc2 + 1) * 512],
                        start=(kc2 == 0), stop=(kc2 == 23)))
                P.mark_mm()
                tw_cnt[oc] = P.done["tensor"].n
                P.wait(V, P.done["tensor"], tw_cnt[oc])
                P.wait(V, ldx, ldx.n)
                P.wait(V, sty[b], sty[b].n)
                P.emit(V, lambda e, oc=oc, b=b: e.scalar_tensor_tensor(
                    ytmp[:, b * 512:(b + 1) * 512], ps_y[b][:], b2c[oc],
                    xblk[:, oc * 512:(oc + 1) * 512], ALU.add, ALU.add),
                    P.done["vector"], serial=True)
                vw_cnt[oc] = P.done["vector"].n
                P.wait("gpsimd", P.done["vector"])
                P.dma("gpsimd", y_out[oc * 128:(oc + 1) * 128, t0:t0 + 512],
                      ytmp[:, b * 512:(b + 1) * 512], sty[b])
        for sm in sty:
            P.wait("gpsimd", sm)
        P.wait("sync", sty[0])
    if ctx is None:
        P.build()
        return nc
    return None


def prep_l0a(W_qkv):
    Wt = np.ascontiguousarray(W_qkv.T)
    idx = np.zeros(3072, np.int64)
    for ty in range(3):
        for h in range(16):
            for d_ in range(64):
                idx[ty * 1024 + h * 64 + d_] = h * 192 + ty * 64 + d_
    return np.ascontiguousarray(Wt[:, idx])


def prep_l0b(z):
    H, KPH = 16, 32
    qw, kw, vw = z["L0_ffn_q_w"], z["L0_ffn_k_w"], z["L0_ffn_v_w"]
    qb, kb, ow = z["L0_ffn_q_b"], z["L0_ffn_k_b"], z["L0_ffn_o_w"]
    wq_pad = np.zeros((1024, 1024), np.float32)
    wk_pad = np.zeros((1024, 1024), np.float32)
    qb_pad = np.zeros(1024, np.float32)
    kb_pad = np.zeros(1024, np.float32)
    wo_pad = np.zeros((1024, 1024), np.float32)
    for h in range(H):
        sl = slice(h * 64, h * 64 + KPH)
        wq_pad[:, sl] = qw[:, h * 64:h * 64 + KPH]
        qb_pad[sl] = qb[h * 64:h * 64 + KPH]
        wk_pad[:, sl] = kw[:, h * KPH:(h + 1) * KPH]
        kb_pad[sl] = kb[h * KPH:(h + 1) * KPH]
        wo_pad[sl, :] = ow[h * KPH:(h + 1) * KPH, :]
    bcols = ([qb_pad[i * 128:(i + 1) * 128] for i in range(8)]
             + [kb_pad[i * 128:(i + 1) * 128] for i in range(8)]
             + [z["L0_ffn_o_b"][i * 128:(i + 1) * 128] for i in range(8)]
             + [z["L0_ffn_b1"][e, hc * 128:(hc + 1) * 128]
                for e in range(8) for hc in range(16)]
             + [z["L0_ffn_b2"][e, oc * 128:(oc + 1) * 128]
                for e in range(8) for oc in range(8)])
    return {
        "wqkvb": np.ascontiguousarray(
            np.concatenate([wq_pad, wk_pad, vw], axis=1)),
        "wob": np.ascontiguousarray(wo_pad),
        "wrt": np.ascontiguousarray(z["L0_ffn_router_w"]),
        "w1": z["L0_ffn_w1"], "w2": z["L0_ffn_w2"],
        "bpack": np.ascontiguousarray(np.stack(bcols, 1).astype(np.float32)),
        "vbr": np.ascontiguousarray(z["L0_ffn_v_b"][None, :]),
        "rbr": np.ascontiguousarray(z["L0_ffn_router_b"][None, :]),
        "ident": np.eye(128, dtype=np.float32),
    }


def prep_l1a(z, core):
    H, hd, BASE = 16, 64, 10000.0
    b, q = core // 4, core % 4
    theta = (BASE ** (-2.0 / 1024 * np.arange(1024, dtype=np.float64))
             ).reshape(H, hd)
    pos = np.arange(q * T, (q + 1) * T, dtype=np.float64)
    ang = pos[None, :, None] * theta[:, None, :]
    c = np.cos(ang).astype(np.float32)
    s = np.sin(ang).astype(np.float32)
    hsel = np.zeros((128, 128), np.float32)
    for cch in range(8):
        for p in range(128):
            hsel[p, cch * 16 + 2 * cch + p // 64] = 1.0
    return {
        "wq": z["L1_attn_wq"], "wk": z["L1_attn_wk"], "wv": z["L1_attn_wv"],
        "cfm": np.ascontiguousarray(c.transpose(0, 2, 1).reshape(1024, T)),
        "sfm": np.ascontiguousarray(s.transpose(0, 2, 1).reshape(1024, T)),
        "ctm": np.ascontiguousarray(c), "stm": np.ascontiguousarray(s),
        "utri": np.triu(np.ones((128, 128), np.float32)),
        "wg1": z["L1_attn_wg1"], "wg2": z["L1_attn_wg2"],
        "wo": z["L1_attn_wo"],
        "bpk": np.ascontiguousarray(
            z["L1_attn_norm_w"].reshape(8, 128).T.astype(np.float32)),
        "wselp": np.tile((np.arange(4) < q).astype(np.float32), (128, 1)),
        "hsel": hsel, "ident": np.eye(128, dtype=np.float32),
    }


def _l1f_rotab(gpos):
    DIM_HEAD, WINDOW = 64, 256
    half = np.arange(0, DIM_HEAD, 2, dtype=np.float64)
    inv_freq = 1.0 / 10000.0 ** (half / DIM_HEAD)
    fr = gpos[:, None] * inv_freq[None, :]
    fr = np.concatenate([fr, fr], -1)
    base_scale = (half + 0.4 * DIM_HEAD) / (1.4 * DIM_HEAD)
    power = (gpos - 8192 // 2) / (WINDOW // 2)
    sc = base_scale[None, :] ** power[:, None]
    sc = np.concatenate([sc, sc], -1)
    return fr, sc


def prep_l1f_attn(z, core, li):
    b, q = core // 4, core % 4
    Wq = z["L1_ffn_Wqkv"][li]
    qs, ks = z["L1_ffn_q_scale"][li], z["L1_ffn_k_scale"][li]
    pair = np.zeros((1024, 16 * 384), np.float32)
    for hp in range(8):
        for j in range(2):
            h = 2 * hp + j
            pair[:, hp * 384 + j * 64:hp * 384 + (j + 1) * 64] = \
                Wq[:, h * 64:(h + 1) * 64]
            pair[:, hp * 384 + 128 + j * 64:hp * 384 + 128 + (j + 1) * 64] = \
                Wq[:, 1024 + h * 64:1024 + (h + 1) * 64]
            pair[:, hp * 384 + 256 + j * 64:hp * 384 + 256 + (j + 1) * 64] = \
                Wq[:, 2048 + h * 64:2048 + (h + 1) * 64]
    gpos = q * T + np.arange(TW_A, dtype=np.float64) - 512.0
    fr, sc = _l1f_rotab(gpos)
    sgn = np.ones(64)
    sgn[:32] = -1.0
    csq = (np.cos(fr) * sc).T.astype(np.float32)
    snq = (np.sin(fr) * sc * sgn[:, None].T).T.astype(np.float32)
    csk = (np.cos(fr) / sc).T.astype(np.float32)
    snk = (np.sin(fr) / sc * sgn[None, :]).T.astype(np.float32)
    tabs = np.concatenate([np.tile(t, (2, 1))
                           for t in (csq, snq, csk, snk)], 0)
    mv = np.full((128, 1024), -1e30, np.float32)
    for m in range(4):
        for a in range(2):
            for jj in range(128):
                for ii in range(128):
                    r = 256 + a * 128 + ii - m * 128 - jj
                    if 0 <= r <= 256:
                        mv[jj, m * 256 + a * 128 + ii] = 0.0
    mf = mv.copy()
    if q == 0:
        mf[:, 0:512] = -1e30
    masks = np.concatenate([mv, mf], 0)
    consts = np.zeros((128, 134), np.float32)
    consts[:, 0] = np.tile(qs, 2)
    consts[:, 1] = np.tile(ks, 2)
    for p in range(128):
        consts[p, 2 + p // 64] = 1.0
    psw = np.zeros((128, 128), np.float32)
    for d_ in range(128):
        e_ = (d_ // 64) * 64 + (d_ % 64 + 32) % 64
        psw[e_, d_] = 1.0
    consts[:, 6:134] = psw
    pairbc = np.zeros((2, 128), np.float32)
    pairbc[0, 0:64] = 1.0
    pairbc[1, 64:128] = 1.0
    return {"wqkv": np.ascontiguousarray(pair),
            "wo": z["L1_ffn_Wo"][li],
            "tabs": np.ascontiguousarray(tabs.astype(np.float32)),
            "masks": np.ascontiguousarray(masks),
            "consts": consts, "pairbc": pairbc}


def prep_l1f_ffn(z, li):
    W1, W2 = z["L1_ffn_W1"][li], z["L1_ffn_W2"][li]
    b1 = z["L1_ffn_b1"][li]
    w1p = np.zeros((1024, 6144), np.float32)
    w1p[:, 0:2730] = W1[:, 0:2730]
    w1p[:, 3072:3072 + 2730] = W1[:, 2730:5460]
    w2p = np.zeros((3072, 1024), np.float32)
    w2p[0:2730, :] = W2
    b1a = np.zeros(3072, np.float32)
    b1g = np.zeros(3072, np.float32)
    b1a[0:2730] = b1[0:2730]
    b1g[0:2730] = b1[2730:5460]
    bcols = ([z["L1_ffn_ln_g"][li][i * 128:(i + 1) * 128] for i in range(8)]
             + [z["L1_ffn_ln_b"][li][i * 128:(i + 1) * 128] for i in range(8)]
             + [b1a[i * 128:(i + 1) * 128] for i in range(24)]
             + [b1g[i * 128:(i + 1) * 128] for i in range(24)]
             + [z["L1_ffn_b2"][li][i * 128:(i + 1) * 128] for i in range(8)])
    return {"w1p": w1p, "w2p": w2p,
            "bpk": np.ascontiguousarray(np.stack(bcols, 1).astype(np.float32))}


def build_sandwich_ln(mode, ctx=None, tag=""):
    if ctx is None:
        nc = bass.Bass("TRN2", target_bir_lowering=False, debug=False)
        xin = nc.dram_tensor("xin", [D, T], F32, kind="ExternalInput").ap()
        xold = None
        if mode == "post":
            xold = nc.dram_tensor("xold", [D, T], F32, kind="ExternalInput").ap()
        out = nc.dram_tensor("out", [D, T], F32, kind="ExternalOutput").ap()
        P = Prog(nc)
        com = Common(P, Tn=512, with_ln_xb=False)
        sc0 = P.stack
    else:
        nc, P, com, sc0 = ctx["nc"], ctx["P"], ctx["com"], ctx["sc"]
        io = ctx["io"]
        xin, xold, out = io["xin"], io.get("xold"), io["out"]
    sb0 = lambda n_, sh_, dt_: sc0.enter_context(nc.sbuf_tensor(n_, sh_, dt_))
    xt = sb0(tag + "xt_s", [128, 8 * T], BF16)
    ot = sb0(tag + "ot_s", [128, 8 * T], F32)
    ld = P.psem("sln_ld")
    st = P.psem("sln_st")
    P.wait("gpsimd", P.done["vector"])
    P.dma("gpsimd", xt[:].rearrange("p (c t) -> p c t", c=8),
          xin[:, :].rearrange("(c p) t -> p c t", p=128), ld)
    if mode == "post":
        P.wait("sync", P.done["vector"])
        P.dma("sync", ot[:].rearrange("p (c t) -> p c t", c=8),
              xold[:, :].rearrange("(c p) t -> p c t", p=128), ld)
    w = [(e, ld, ld.n) for e in ("vector", "scalar", "tensor")]
    if mode == "pre":
        P.emit("vector", lambda e: e.memset(ot[:], 0.0), P.done["vector"])
    emit_ln_big(P, com, xt[:], T, "residual", xold=ot, xnew=ot,
                tag=tag + "sln", waits=w, x_bf16=True)
    P.wait("sync", P.done["vector"])
    P.dma("sync", out[:, :].rearrange("(c p) t -> p c t", p=128),
          ot[:].rearrange("p (c t) -> p c t", c=8), st)
    P.wait("sync", st)
    if ctx is None:
        P.build()
        return nc
    return None


def l0a_kv_loader_fused(P, ktsg, vsg, HDv=65):
    W = 4 * HDv

    def loader(hg, K_sb, V_sb, sem):
        for cc in range(2):
            for r in range(4):
                P.dma("gpsimd",
                      K_sb[:, cc * SEQ + r * T:cc * SEQ + (r + 1) * T],
                      ktsg[r * 1024 + hg * 256 + cc * 128:
                           r * 1024 + hg * 256 + (cc + 1) * 128, :], sem)
        vsr = vsg.rearrange("(kb p) w -> p kb w", p=128)
        vdr = V_sb[:].rearrange("p (kb w) -> p kb w", w=W)
        for i in range(8):
            P.dma("gpsimd", vdr[:, i * 8:(i + 1) * 8, :],
                  vsr[:, i * 8:(i + 1) * 8, hg * W:(hg + 1) * W], sem)
    return loader


def build_fused():
    import types
    nc = bass.Bass("TRN2", target_bir_lowering=False, debug=False)
    EI = lambda n_, sh_: nc.dram_tensor(n_, sh_, F32, kind="ExternalInput").ap()
    x0own = EI("x0own", [D, T])
    wqkv_a = EI("wqkv_a", [D, 3072])
    wqkvb = EI("wqkvb", [D, 2560])
    wob = EI("wob", [D, D])
    wrt = EI("wrt", [D, 8])
    w1 = EI("w1", [8, D, 2048])
    w2 = EI("w2", [8, 2048, D])
    bpack = EI("bpack", [128, 216])
    vbr = EI("vbr", [1, 512])
    rbr = EI("rbr", [1, 8])
    ident = EI("ident", [128, 128])
    wq = EI("wq", [D, D]); wk = EI("wk", [D, D]); wv = EI("wv", [D, D])
    cfm = EI("cfm", [D, T]); sfm = EI("sfm", [D, T])
    ctm = EI("ctm", [16, T, 64]); stm = EI("stm", [16, T, 64])
    utri = EI("utri", [128, 128])
    wg1 = EI("wg1", [D, 64]); wg2 = EI("wg2", [64, D]); wo_l1 = EI("wo_l1", [D, D])
    bpk_l1 = EI("bpk_l1", [128, 8])
    wselp = EI("wselp", [128, 4])
    whalo = EI("whalo", [128, 4])
    hsel = EI("hsel", [128, 128])
    wqkv6 = EI("wqkv6", [6, D, 16 * 384])
    wo6 = EI("wo6", [6, D, D])
    tabs = EI("tabs", [4 * 128, TW_A])
    masks = EI("masks", [2 * 128, 1024])
    consts6 = EI("consts6", [6, 128, 134])
    pairbc = EI("pairbc", [2, 128])
    w1p6 = EI("w1p6", [6, D, 6144])
    w2p6 = EI("w2p6", [6, 3072, D])
    bpk6 = EI("bpk6", [6, 128, 72])
    out = nc.dram_tensor("out", [D, T], F32, kind="ExternalOutput").ap()
    IN = lambda n_, sh_, dt_=BF16: nc.dram_tensor(n_, sh_, dt_).ap()
    kts_l = IN("kts_l", [D, T]); vs_l = IN("vs_l", [T, 16 * 65])
    ktsg = IN("ktsg", [4 * D, T]); vsg = IN("vsg", [SEQ, 16 * 65])
    kts2_l = IN("kts2_l", [D, T]); vs2_l = IN("vs2_l", [T, 16 * 33])
    kts2g = IN("kts2g", [4 * D, T]); vs2g = IN("vs2g", [SEQ, 16 * 33])
    sig_l = IN("sig_l", [1, 1024], F32); S_l = IN("S_l", [16 * 128, 64], F32)
    sig_g = IN("sig_g", [4, 1024], F32); S_g = IN("S_g", [4 * 16 * 128, 64], F32)
    x1_d = IN("x1_d", [D, T], F32); x2_d = IN("x2_d", [D, T], F32)
    x3_d = IN("x3_d", [D, T], F32)
    xsd = [IN("xs_d%d" % i, [D, T], F32) for i in range(2)]
    halo_l = IN("halo_l", [D, 257], F32); halo_g = IN("halo_g", [4 * D, 257], F32)
    h1_l = IN("h1_l", [D, 1], F32); h1_g = IN("h1_g", [4 * D, 1], F32)

    P = Prog(nc)
    com = Common(P, Tn=512, with_ln_xb=True)
    wsl = P.sb("wsl_f", [128, 4], F32)
    wsl_h = P.sb("wsl_h", [128, 4], F32)
    ldw = P.psem("ldw")
    P.dma("sync", wsl[:], wselp[:, :], ldw)
    P.dma("sync", wsl_h[:], whalo[:, :], ldw)
    for eng in ("vector", "gpsimd", "tensor", "scalar"):
        P.wait(eng, ldw)

    with contextlib.ExitStack() as sc:
        A = types.SimpleNamespace(
            tag="l0a", wqkv=wqkv_a, kts=kts_l, vs=vs_l, ktsg=ktsg, vsg=vsg,
            ag_groups=AG_GROUPS, stage="full",
            kv_slices=[(x0own, 0), (x0own, 1024)],
            q_slices=[(x0own, 0, 0), (x0own, 1024, 1024)],
            Q_fm=sc.enter_context(nc.sbuf_tensor("l0aQ", [128, 8 * T], BF16)),
            o_fm=sc.enter_context(nc.sbuf_tensor("l0aO", [128, 8 * T], BF16)),
            kv_loader=l0a_kv_loader_fused(P, ktsg, vsg),
            x_resid=x0own, x_out=x1_d)
        emit_l0a(P, com, A)
    P.fence()

    with contextlib.ExitStack() as sc:
        io = dict(xin=x1_d, wqkvb=wqkvb, wob=wob, wrt=wrt, w1=w1, w2=w2,
                  bpack=bpack, vbr=vbr, rbr=rbr, ident=ident, x2=x2_d,
                  kts=kts2_l, vs=vs2_l)
        build_l0b_ckpt(ctx=dict(
            nc=nc, P=P, com=com, sc=sc, io=io,
            kv_slices=[(x1_d, 0), (x1_d, 1024)],
            kv_loader=l0a_kv_loader_fused(P, kts2g, vs2g, HDv=33),
            ag=[(kts2_l, kts2g), (vs2_l, vs2g)]))
    P.fence()

    l1a_io = dict(xin=x2_d, wq=wq, wk=wk, wv=wv, cfm=cfm, sfm=sfm, ctm=ctm,
                  stm=stm, utri=utri)
    with contextlib.ExitStack() as sc:
        io = dict(l1a_io)
        io.update(sig_o=sig_l, S_o=S_l)
        build_l1a("partial", ctx=dict(nc=nc, P=P, com=com, sc=sc, io=io))
    emit_ag(P, [(sig_l, sig_g), (S_l, S_g)],
            wait_sems=[P.psem("sto"), P.psem("l1a_sto")])
    P.fence()
    with contextlib.ExitStack() as sc:
        io = dict(l1a_io)
        io.update(wg1=wg1, wg2=wg2, wo=wo_l1, bpk=bpk_l1, sig_p=sig_g,
                  S_p=S_g.rearrange("(r x) d_ -> r x d_", r=4), wselp=wselp,
                  hsel=hsel[:, 0:128], ident=ident, x3=x3_d)
        build_l1a("full", ctx=dict(nc=nc, P=P, com=com, sc=sc, io=io))
    P.fence()

    with contextlib.ExitStack() as sc:
        build_sandwich_ln("pre", ctx=dict(
            nc=nc, P=P, com=com, sc=sc, io=dict(xin=x3_d, out=xsd[0])),
            tag="pre")
    P.fence()
    cur, nxt = xsd[0], xsd[1]
    hst = P.psem("halo_st")
    for li in range(6):
        P.wait("gpsimd", P.done["vector"])
        for nm in ("lff_sty0", "lff_sty1", "sln_st"):
            P.wait("gpsimd", P.psem(nm))
        P.dma("gpsimd", halo_l[:, :], cur[:, T - 257:T], hst)
        emit_ag(P, [(halo_l, halo_g)], wait_sems=[hst])
        with contextlib.ExitStack() as sc:
            io = dict(x_own=cur, wqkv=wqkv6[li], wo=wo6[li], tabs=tabs,
                      masks=masks, consts=consts6[li], pairbc=pairbc,
                      y_out=nxt)
            build_l1f_attn(ctx=dict(nc=nc, P=P, com=com, sc=sc, io=io,
                                    halo=(halo_g, wsl_h)), tag="a%d_" % li)
        P.fence()
        P.wait("gpsimd", P.done["vector"])
        for nm in ("lfa_sty0", "lfa_sty1"):
            P.wait("gpsimd", P.psem(nm))
        P.dma("gpsimd", h1_l[:, :], nxt[:, T - 1:T], hst,
              allow_slow_non_contiguous=True)
        emit_ag(P, [(h1_l, h1_g)], wait_sems=[hst])
        with contextlib.ExitStack() as sc:
            io = dict(x_own=nxt, w1=w1p6[li], w2=w2p6[li], bpk=bpk6[li],
                      y_out=cur)
            build_l1f_ffn(ctx=dict(nc=nc, P=P, com=com, sc=sc, io=io,
                                   halo=(h1_g, wsl_h)), tag="f%d_" % li)
        P.fence()
    with contextlib.ExitStack() as sc:
        build_sandwich_ln("post", ctx=dict(
            nc=nc, P=P, com=com, sc=sc,
            io=dict(xin=cur, xold=x3_d, out=out)), tag="post")
    P.build()
    return nc


FUSED = True


def _prep_core_inputs(z, core):
    b, q = core // 4, core % 4
    m = {}
    X = z["L0_attn_z"].astype(np.float32)
    m["x0own"] = np.ascontiguousarray(X[b, q * T:(q + 1) * T].T)
    m["wqkv_a"] = prep_l0a(z["L0_attn_W_qkv"])
    m.update(prep_l0b(z))
    p1 = prep_l1a(z, core)
    for k_, v_ in p1.items():
        if k_ == "wo":
            m["wo_l1"] = v_
        elif k_ == "bpk":
            m["bpk_l1"] = v_
        else:
            m[k_] = v_
    pa0 = prep_l1f_attn(z, core, 0)
    m["tabs"] = pa0["tabs"]
    m["masks"] = pa0["masks"]
    m["pairbc"] = pa0["pairbc"]
    cons, wq6, wo6l = [], [], []
    for li in range(6):
        pa = prep_l1f_attn(z, core, li)
        cons.append(pa["consts"])
        wq6.append(pa["wqkv"])
        wo6l.append(pa["wo"])
    m["consts6"] = np.ascontiguousarray(np.stack(cons, 0))
    m["wqkv6"] = np.ascontiguousarray(np.stack(wq6, 0))
    m["wo6"] = np.ascontiguousarray(np.stack(wo6l, 0))
    w1l, w2l, bpl = [], [], []
    for li in range(6):
        pf = prep_l1f_ffn(z, li)
        w1l.append(pf["w1p"])
        w2l.append(pf["w2p"])
        bpl.append(pf["bpk"])
    m["w1p6"] = np.ascontiguousarray(np.stack(w1l, 0))
    m["w2p6"] = np.ascontiguousarray(np.stack(w2l, 0))
    m["bpk6"] = np.ascontiguousarray(np.stack(bpl, 0))
    q = core % 4
    m["whalo"] = np.tile((np.arange(4) == q - 1).astype(np.float32), (128, 1))
    return m


def kernel(**z):
    z = {k: np.asarray(v) for k, v in z.items()}
    if FUSED:
        nc = build_fused()
        in_maps = [_prep_core_inputs(z, core) for core in range(NC)]
        res = run_bass_kernel_spmd(nc, in_maps, core_ids=list(range(NC)))
        out = np.zeros((2, SEQ, D), np.float32)
        for core in range(NC):
            b, q = core // 4, core % 4
            out[b, q * T:(q + 1) * T, :] = res.results[core]["out"].T
        return out
    return kernel_ckpt(**z)


def kernel_ckpt(**z):



    z = {k: np.asarray(v) for k, v in z.items()}

    def fm(a):
        return np.ascontiguousarray(np.asarray(a).T)

    X = z["L0_attn_z"].astype(np.float32)

    wq = prep_l0a(z["L0_attn_W_qkv"])
    nc = build_l0a_ckpt()
    in_maps = []
    for core in range(NC):
        b, q = core // 4, core % 4
        in_maps.append({"xfull": fm(X[b]),
                        "xown": fm(X[b, q * T:(q + 1) * T]),
                        "wqkv": wq})
    res = run_bass_kernel_spmd(nc, in_maps, core_ids=list(range(NC)))
    x1s = [res.results[c]["x1"] for c in range(NC)]
    x1f = [np.ascontiguousarray(np.concatenate(x1s[4 * b:4 * b + 4], axis=1))
           for b in range(2)]

    p0b = prep_l0b(z)
    nc = build_l0b_ckpt()
    in_maps = []
    for core in range(NC):
        b, q = core // 4, core % 4
        m = dict(p0b)
        m["xin"] = np.ascontiguousarray(x1s[core])
        m["xfull1"] = x1f[b]
        in_maps.append(m)
    res = run_bass_kernel_spmd(nc, in_maps, core_ids=list(range(NC)))
    x2s = [res.results[c]["x2"] for c in range(NC)]

    preps = [prep_l1a(z, core) for core in range(NC)]
    nc = build_l1a("partial")
    in_maps = []
    for core in range(NC):
        m = {k: preps[core][k] for k in
             ("wq", "wk", "wv", "cfm", "sfm", "ctm", "stm", "utri")}
        m["xin"] = np.ascontiguousarray(x2s[core])
        in_maps.append(m)
    res = run_bass_kernel_spmd(nc, in_maps, core_ids=list(range(NC)))
    sig = [res.results[c]["sig_o"] for c in range(NC)]
    Sp = [res.results[c]["S_o"] for c in range(NC)]

    nc = build_l1a("full")
    in_maps = []
    for core in range(NC):
        b, q = core // 4, core % 4
        m = dict(preps[core])
        m["xin"] = np.ascontiguousarray(x2s[core])
        m["sig_p"] = np.ascontiguousarray(
            np.concatenate([sig[4 * b + r] for r in range(4)], 0))
        m["S_p"] = np.ascontiguousarray(
            np.stack([Sp[4 * b + r] for r in range(4)], 0))
        in_maps.append(m)
    res = run_bass_kernel_spmd(nc, in_maps, core_ids=list(range(NC)))
    x3s = [res.results[c]["x3"] for c in range(NC)]

    nc = build_sandwich_ln("pre")
    in_maps = [{"xin": np.ascontiguousarray(x3s[c])} for c in range(NC)]
    res = run_bass_kernel_spmd(nc, in_maps, core_ids=list(range(NC)))
    xs = [res.results[c]["out"] for c in range(NC)]

    nca = build_l1f_attn()
    ncf = build_l1f_ffn()
    for li in range(6):
        in_maps = []
        for core in range(NC):
            b, q = core // 4, core % 4
            m = dict(prep_l1f_attn(z, core, li))
            m["x_own"] = np.ascontiguousarray(xs[core])
            xh = np.zeros((D, 512), np.float32)
            if q > 0:
                xh[:, 255:512] = xs[core - 1][:, T - 257:T]
            m["x_halo"] = xh
            in_maps.append(m)
        res = run_bass_kernel_spmd(nca, in_maps, core_ids=list(range(NC)))
        ys = [res.results[c]["y"] for c in range(NC)]

        pf = prep_l1f_ffn(z, li)
        in_maps = []
        for core in range(NC):
            b, q = core // 4, core % 4
            m = dict(pf)
            m["x_own"] = np.ascontiguousarray(ys[core])
            xh = np.zeros((D, 128), np.float32)
            if q > 0:
                xh[:, 127] = ys[core - 1][:, T - 1]
            m["x_h1"] = xh
            in_maps.append(m)
        res = run_bass_kernel_spmd(ncf, in_maps, core_ids=list(range(NC)))
        xs = [res.results[c]["y"] for c in range(NC)]

    nc = build_sandwich_ln("post")
    in_maps = [{"xin": np.ascontiguousarray(xs[c]),
                "xold": np.ascontiguousarray(x3s[c])} for c in range(NC)]
    res = run_bass_kernel_spmd(nc, in_maps, core_ids=list(range(NC)))
    xs = [res.results[c]["out"] for c in range(NC)]

    out = np.zeros((2, SEQ, D), np.float32)
    for core in range(NC):
        b, q = core // 4, core % 4
        out[b, q * T:(q + 1) * T, :] = xs[core].T
    return out


# version 06
# speedup vs baseline: 1.1953x; 1.0734x over previous
import contextlib
import math

import numpy as np
import ml_dtypes
import concourse.bass as bass
import concourse.mybir as mybir
from concourse.bass_utils import run_bass_kernel_spmd

F32 = mybir.dt.float32
BF16 = mybir.dt.bfloat16
AF = mybir.ActivationFunctionType
ALU = mybir.AluOpType
AX = mybir.AxisListType

D = 1024
T = 2048
NC = 8
NQ = 4
SEQ = 8192
LN_EPS = 1e-5


class Sem:
    __slots__ = ("h", "n")

    def __init__(self, h):
        self.h = h
        self.n = 0


class Prog:
    ENGINES = ("sync", "scalar", "tensor", "vector", "gpsimd")

    def __init__(self, nc):
        self.nc = nc
        self.ops = {e: [] for e in self.ENGINES}
        self.stack = contextlib.ExitStack()
        self.bar = self.sem("bar")
        self.done = {e: self.sem("done_" + e) for e in self.ENGINES}

    def sem(self, name):
        return Sem(self.stack.enter_context(self.nc.semaphore(name)))

    def psem(self, name):
        d = getattr(self, "_psems", None)
        if d is None:
            d = self._psems = {}
        if name not in d:
            d[name] = self.sem(name)
        return d[name]

    def sb(self, name, shape, dtype):
        return self.stack.enter_context(self.nc.sbuf_tensor(name, shape, dtype))

    def psum(self, name, shape, dtype=F32):
        return self.stack.enter_context(self.nc.psum_tensor(name, shape, dtype))

    def emit(self, eng, fn, sem=None, k=1, serial=False):
        if serial:
            self.wait(eng, self.done[eng])
            sem = sem or self.done[eng]
        if sem is not None:
            sem.n += k
            self.ops[eng].append(lambda e, fn=fn, s=sem.h, k=k: fn(e).then_inc(s, k))
        else:
            self.ops[eng].append(fn)

    def wait(self, eng, sem, val=None):
        v = sem.n if val is None else val
        if v <= 0:
            return
        self.ops[eng].append(lambda e, s=sem.h, v=v: e.wait_ge(s, v))

    def dma(self, eng, out, in_, sem, **kw):
        self.emit(eng, lambda e, o=out, i=in_, kw=kw: e.dma_start(out=o, in_=i, **kw),
                  sem, 16)

    def mark(self, eng):
        self.emit(eng, lambda e: e.wait_ge(self.bar.h, 0), self.done[eng], 1)

    def mark_mm(self, tag=""):
        ops = self.ops["tensor"]
        fn = ops.pop()
        self.emit("tensor", fn, self.done["tensor"], 1)

    def fence(self, engines=None):
        engines = tuple(engines or self.ENGINES)
        base = self.bar.n
        for eng in engines:
            self.emit(eng, lambda e: e.wait_ge(self.bar.h, 0), self.bar, 1)
        tgt = base + len(engines)
        for eng in engines:
            self.wait(eng, self.bar, tgt)

    def build(self):
        with self.nc.Block() as block:
            for eng in self.ENGINES:
                ops = self.ops[eng]
                if ops:
                    getattr(block, eng)(lambda e, ops=ops: [f(e) for f in ops])
        return self.nc


def bf16(a):
    return np.asarray(a).astype(ml_dtypes.bfloat16).astype(np.float32)


AG_GROUPS = [[0, 1, 2, 3], [4, 5, 6, 7]]


def emit_ag(P, pairs, wait_sems=()):
    cc = P.psem("cc")
    for sm in wait_sems:
        P.wait("gpsimd", sm)
    for in_ap, out_ap in pairs:
        P.emit("gpsimd", lambda e, i=in_ap, o=out_ap: e.collective_compute(
            "AllGather", ALU.bypass, replica_groups=AG_GROUPS,
            ins=[i.opt()], outs=[o.opt()]), cc)
    P.wait("gpsimd", cc)


class Common:
    def __init__(self, P, Tn=T, with_ln_xb=True):
        self.Tn = Tn
        self.ones128 = P.sb("c_ones128", [128, 1], BF16)
        self.ones1 = P.sb("c_ones1", [1, 128], BF16)
        self.ones1f = P.sb("c_ones1f", [1, 128], F32)
        self.onesP = P.sb("c_onesP", [128, 128], F32)
        self.onesP_bf = P.sb("c_onesPb", [128, 128], BF16)
        P.emit("vector", lambda e, t=self.ones128: e.memset(t[:], 1.0))
        P.emit("vector", lambda e, t=self.ones1: e.memset(t[:], 1.0))
        P.emit("vector", lambda e, t=self.ones1f: e.memset(t[:], 1.0))
        P.emit("vector", lambda e, t=self.onesP: e.memset(t[:], 1.0))
        P.emit("vector", lambda e, t=self.onesP_bf: e.memset(t[:], 1.0),
               P.done["vector"], 1)
        self._P = P
        self._cols = {}
        self.xsq = P.sb("ln_xsq", [128, 2 * Tn], BF16)
        self.m_row = P.sb("ln_m", [1, Tn], F32)
        self.w_row = P.sb("ln_w", [1, Tn], F32)
        self.rows_bf = P.sb("ln_rows_bf", [1, 2 * Tn], BF16)
        self.tmp = P.sb("ln_tmp", [128, Tn], F32)
        self.ln_xb = P.sb("ln_xb", [128, 8 * 512], BF16) if with_ln_xb else None
        self.col(LN_EPS)

    def col(self, val):
        if val not in self._cols:
            t = self._P.sb("c_col_%d" % len(self._cols), [128, 1], F32)
            self._P.emit("vector", lambda e, t=t, v=val: e.memset(t[:], v),
                         self._P.done["vector"], 1)
            self._cols[val] = t
        return self._cols[val][:, 0:1]


def emit_ln(P, com, x, Tn, mode, out=None, xold=None, xnew=None, tag="ln",
            waits=(), x_bf16=False, xsl=None, out_sl=None, xold_sl=None,
            xnew_sl=None):
    nc = P.nc
    NG = Tn // 512
    assert Tn <= com.Tn
    if xsl is None:
        xsl = [x[:, c * Tn:(c + 1) * Tn] for c in range(8)]
    if out_sl is None and out is not None:
        out_sl = [out[:, c * Tn:(c + 1) * Tn] for c in range(8)]
    if xold_sl is None and xold is not None:
        xold_sl = [xold[:, c * Tn:(c + 1) * Tn] for c in range(8)]
    if xnew_sl is None and xnew is not None:
        xnew_sl = [xnew[:, c * Tn:(c + 1) * Tn] for c in range(8)]
    xb = x if x_bf16 else (out if mode == "bf16" else com.ln_xb)
    if mode == "residual" and not x_bf16:
        assert Tn <= 512, "ln_xb scratch sized for Tn<=512"
    xsq, tmp = com.xsq, com.tmp
    m_row, w_row = com.m_row[:, 0:Tn], com.w_row[:, 0:Tn]
    rstd_bf = com.rows_bf[:, 0:Tn]
    mr_bf = com.rows_bf[:, Tn:2 * Tn]
    epsb = com.col(LN_EPS)[0:1, :]

    for eng, s, v in waits:
        P.wait(eng, s, v)

    if not x_bf16:
        xbsl = (out_sl if mode == "bf16" else
                [xb[:, c * Tn:(c + 1) * Tn] for c in range(8)])
        for c in range(8):
            P.emit("vector", lambda e, c=c: e.tensor_copy(xbsl[c], xsl[c]),
                   P.done["vector"] if c == 7 else None)

    with contextlib.ExitStack() as sp:
        ps_s = sp.enter_context(nc.psum_tensor(tag + "_ps_s", [1, Tn], F32))
        ps_q = sp.enter_context(nc.psum_tensor(tag + "_ps_q", [1, Tn], F32))
        P.wait("scalar", P.done["vector"])
        P.wait("tensor", P.done["vector"])
        pe_base = P.done["tensor"].n
        for c in range(8):
            P.wait("scalar", P.done["tensor"],
                   pe_base if c < 2 else pe_base + c - 1)
            xbc = (xsl[c] if x_bf16 else
                   (out_sl[c] if mode == "bf16"
                    else xb[:, c * Tn:(c + 1) * Tn]))
            P.emit("scalar", lambda e, c=c, xbc=xbc: e.activation(
                out=xsq[:, (c % 2) * Tn:(c % 2) * Tn + Tn],
                in_=xbc, func=AF.Square),
                P.done["scalar"])
            P.wait("tensor", P.done["scalar"])
            for g in range(NG):
                xbg = (xsl[c][:, g * 512:(g + 1) * 512] if x_bf16
                       else (out_sl[c][:, g * 512:(g + 1) * 512]
                             if mode == "bf16" else
                             xb[:, c * Tn + g * 512:c * Tn + (g + 1) * 512]))
                P.emit("tensor", lambda e, c=c, g=g, xbg=xbg: e.matmul(
                    ps_s[:, g * 512:(g + 1) * 512], com.ones128[:],
                    xbg, start=(c == 0), stop=(c == 7)))
                P.emit("tensor", lambda e, c=c, g=g: e.matmul(
                    ps_q[:, g * 512:(g + 1) * 512], com.ones128[:],
                    xsq[:, (c % 2) * Tn + g * 512:(c % 2) * Tn + (g + 1) * 512],
                    start=(c == 0), stop=(c == 7)),
                    P.done["tensor"] if g == NG - 1 else None)

        P.wait("scalar", P.done["tensor"])
        P.emit("scalar", lambda e: e.activation(out=m_row, in_=ps_s[:],
                                                func=AF.Copy, scale=1.0 / D),
               serial=True)
        P.emit("scalar", lambda e: e.activation(out=w_row, in_=m_row,
                                                func=AF.Square), P.done["scalar"],
               serial=True)
        P.wait("vector", P.done["scalar"])
        P.emit("vector", lambda e: e.scalar_tensor_tensor(
            w_row, ps_q[:], 1.0 / D, w_row, ALU.mult, ALU.subtract),
            P.done["vector"])
        P.wait("scalar", P.done["vector"])
        P.emit("scalar", lambda e: e.activation(out=w_row, in_=w_row,
                                                func=AF.Ln, bias=epsb),
               serial=True)
        P.emit("scalar", lambda e: e.activation(out=w_row, in_=w_row,
                                                func=AF.Exp, scale=-0.5),
               P.done["scalar"], serial=True)
        P.wait("vector", P.done["scalar"])
        P.emit("vector", lambda e: e.tensor_copy(rstd_bf, w_row), serial=True)
        P.emit("vector", lambda e: e.tensor_mul(mr_bf, m_row, w_row),
               P.done["vector"], serial=True)

    with contextlib.ExitStack() as sp:
        rb = sp.enter_context(nc.psum_tensor(tag + "_rb", [128, Tn], F32))
        mb = sp.enter_context(nc.psum_tensor(tag + "_mb", [128, Tn], F32))
        P.wait("tensor", P.done["vector"])
        for g in range(NG):
            P.emit("tensor", lambda e, g=g: e.matmul(
                rb[:, g * 512:(g + 1) * 512], com.ones1[:],
                rstd_bf[:, g * 512:(g + 1) * 512], start=True, stop=True))
            P.emit("tensor", lambda e, g=g: e.matmul(
                mb[:, g * 512:(g + 1) * 512], com.ones1[:],
                mr_bf[:, g * 512:(g + 1) * 512], start=True, stop=True),
                P.done["tensor"] if g == NG - 1 else None)

        P.wait("vector", P.done["tensor"])
        tp = tmp[:, 0:Tn]
        for c in range(8):
            P.emit("vector", lambda e, c=c: e.tensor_mul(
                tp, xsl[c], rb[:]), serial=True)
            if mode == "bf16":
                P.emit("vector", lambda e, c=c: e.tensor_sub(
                    out_sl[c], tp, mb[:]), serial=True)
            else:
                P.emit("vector", lambda e, c=c: e.tensor_sub(tp, tp, mb[:]),
                       serial=True)
                P.emit("vector", lambda e, c=c: e.tensor_add(
                    xnew_sl[c], xold_sl[c], tp), serial=True)


def emit_ln_big(P, com, x, Tfull, mode, out=None, xold=None, xnew=None,
                tag="lnb", waits=(), x_bf16=False):
    for g in range(Tfull // 512):
        def sl(t):
            return (None if t is None else
                    [t[:, c * Tfull + g * 512:c * Tfull + (g + 1) * 512]
                     for c in range(8)])
        emit_ln(P, com, None, 512, mode, out=out, xold=xold, xnew=xnew,
                tag="%s%d" % (tag, g), waits=(waits if g == 0 else ()),
                x_bf16=x_bf16, xsl=sl(x), out_sl=sl(out), xold_sl=sl(xold),
                xnew_sl=sl(xnew))


def load_wslab(P, name, w_dram, IN, OUT, sem):
    slab = P.sb(name, [128, (IN // 128) * OUT], BF16)
    for kc in range(IN // 128):
        P.dma("gpsimd", slab[:, kc * OUT:(kc + 1) * OUT],
              w_dram[kc * 128:(kc + 1) * 128, 0:OUT], sem)
    return slab


def load_wslab_sc(P, sc, name, w_dram, IN, OUT, sem, col0=0):
    slab = sc.enter_context(P.nc.sbuf_tensor(name, [128, (IN // 128) * OUT], BF16))
    for kc in range(IN // 128):
        P.dma("gpsimd", slab[:, kc * OUT:(kc + 1) * OUT],
              w_dram[kc * 128:(kc + 1) * 128, col0:col0 + OUT], sem)
    return slab


def emit_proj_fm(P, com, xb, slab, OUT, oc0, n_oc, Tn, evac, evac_eng,
                 nkc=8, tag="proj", extra_mm=None):
    nc = P.nc
    NG = Tn // 512
    with contextlib.ExitStack() as sp:
        ps = [sp.enter_context(nc.psum_tensor(tag + "_ps%d" % i, [128, 512], F32))
              for i in range(2)]
        idx = 0
        ev = P.done[evac_eng]
        ev_counts = []
        P.wait("tensor", P.done["vector"])
        for oc in range(oc0, oc0 + n_oc):
            for g in range(NG):
                b = idx % 2
                if idx >= 2:
                    P.wait("tensor", ev, ev_counts[idx - 2])
                for kc in range(nkc):
                    P.emit("tensor", lambda e, oc=oc, g=g, kc=kc, b=b: e.matmul(
                        ps[b][:], slab[:, kc * OUT + oc * 128:kc * OUT + (oc + 1) * 128],
                        xb[:, kc * Tn + g * 512:kc * Tn + (g + 1) * 512],
                        start=(kc == 0), stop=(kc == nkc - 1 and extra_mm is None)))
                if extra_mm is not None:
                    P.emit("tensor", lambda e, oc=oc, g=g, b=b: extra_mm(e, oc, g, ps[b]))
                P.mark_mm(tag)
                P.wait(evac_eng, P.done["tensor"])
                evac(oc, g, ps[b][:])
                ev_counts.append(ev.n)
                idx += 1
        P.wait("tensor", ev)


def emit_proj_tm(P, com, xb, slab, OUT, ob0, n_ob, Tn, evac, evac_eng,
                 nkc=8, tag="ptm", extra_mm=None):
    nc = P.nc
    with contextlib.ExitStack() as sp:
        ps = [sp.enter_context(nc.psum_tensor(tag + "_ps%d" % i, [128, 512], F32))
              for i in range(2)]
        idx = 0
        ev = P.done[evac_eng]
        ev_counts = []
        P.wait("tensor", P.done["vector"])
        for tt in range(Tn // 128):
            for ob in range(ob0, ob0 + n_ob):
                b = idx % 2
                if idx >= 2:
                    P.wait("tensor", ev, ev_counts[idx - 2])
                for kc in range(nkc):
                    P.emit("tensor", lambda e, tt=tt, ob=ob, kc=kc, b=b: e.matmul(
                        ps[b][:], xb[:, kc * Tn + tt * 128:kc * Tn + (tt + 1) * 128],
                        slab[:, kc * OUT + ob * 512:kc * OUT + (ob + 1) * 512],
                        start=(kc == 0),
                        stop=(kc == nkc - 1 and extra_mm is None)))
                if extra_mm is not None:
                    P.emit("tensor", lambda e, tt=tt, ob=ob, b=b: extra_mm(e, tt, ob, ps[b]))
                P.mark_mm(tag)
                P.wait(evac_eng, P.done["tensor"])
                evac(tt, ob, ps[b][:])
                ev_counts.append(ev.n)
                idx += 1
        P.wait("tensor", ev)


def emit_flash(P, com, *, Q_fm, o_fm, kv_loader, HDv, scale, tag):
    nc = P.nc
    real = HDv - 1
    n_kb = 64
    ngr = n_kb // 2
    with contextlib.ExitStack() as sc:
        K_sb = sc.enter_context(nc.sbuf_tensor(tag + "_K", [128, 2 * SEQ], BF16))
        V_sb = sc.enter_context(nc.sbuf_tensor(tag + "_V", [128, n_kb * 4 * HDv], BF16))
        pt = sc.enter_context(nc.sbuf_tensor(tag + "_pt", [128, 2 * 1024], BF16))
        rc = sc.enter_context(nc.sbuf_tensor(tag + "_rc", [128, 512], F32))
        rcb = sc.enter_context(nc.sbuf_tensor(tag + "_rcb", [128, 2 * 512], BF16))
        bcb = sc.enter_context(nc.sbuf_tensor(tag + "_bcb", [64, 512], BF16))
        ps_s = sc.enter_context(nc.psum_tensor(tag + "_s", [128, 2 * 1024], F32))
        ps_o = [sc.enter_context(nc.psum_tensor(tag + "_o%d" % i, [HDv, 512], F32))
                for i in range(2)]
        ps_bc = sc.enter_context(nc.psum_tensor(tag + "_bc", [real, 512], F32))
        ld = P.sem(tag + "_ld")
        oi = 0
        epi_v = {}
        for hg in range(4):
            if hg > 0:
                P.wait("gpsimd", P.done["tensor"])
            kv_loader(hg, K_sb, V_sb, ld)
            for eng in ("tensor", "scalar"):
                P.wait(eng, ld)
            for h4 in range(4):
                h = hg * 4 + h4
                kch, krow = h4 // 2, (h4 % 2) * 64
                qch, qrow = h // 2, (h % 2) * 64
                for qb in range(4):
                    b_o = oi % 2
                    if oi >= 2:
                        P.wait("tensor", P.done["vector"], epi_v[oi - 2])
                    s_cnt = [0] * ngr
                    e_cnt = [0] * ngr
                    a_cnt = [0] * ngr

                    def s_group(g, krow=krow, kch=kch, qrow=qrow, qch=qch, qb=qb):
                        bs = g % 2
                        if g >= 2:
                            P.wait("tensor", P.done["scalar"], e_cnt[g - 2])
                        for j in range(2):
                            kb = 2 * g + j
                            P.emit("tensor", lambda e, kb=kb, bs=bs, j=j, krow=krow,
                                   kch=kch, qrow=qrow, qch=qch, qb=qb: e.matmul(
                                ps_s[:, bs * 1024 + j * 512: bs * 1024 + (j + 1) * 512],
                                K_sb[krow:krow + 64,
                                     kch * SEQ + kb * 128: kch * SEQ + (kb + 1) * 128],
                                Q_fm[qrow:qrow + 64,
                                     qch * T + qb * 512: qch * T + (qb + 1) * 512],
                                start=True, stop=True))
                        P.mark_mm()
                        s_cnt[g] = P.done["tensor"].n

                    def av_group(g, h4=h4, b_o=b_o):
                        bs = g % 2
                        P.wait("tensor", P.done["scalar"], e_cnt[g])
                        for j in range(2):
                            kb = 2 * g + j
                            P.emit("tensor", lambda e, kb=kb, bs=bs, j=j, b_o=b_o,
                                   h4=h4: e.matmul(
                                ps_o[b_o][:],
                                V_sb[:, kb * 4 * HDv + h4 * HDv: kb * 4 * HDv + (h4 + 1) * HDv],
                                pt[:, bs * 1024 + j * 512: bs * 1024 + (j + 1) * 512],
                                start=(kb == 0), stop=(kb == n_kb - 1)))
                        P.mark_mm()
                        a_cnt[g] = P.done["tensor"].n

                    P.wait("tensor", P.done["scalar"])
                    for g in range(ngr):
                        s_group(g)
                        if g >= 1:
                            av_group(g - 1)
                        P.wait("scalar", P.done["tensor"], s_cnt[g])
                        if g >= 2:
                            P.wait("scalar", P.done["tensor"], a_cnt[g - 2])
                        for j in range(2):
                            P.emit("scalar", lambda e, bs=g % 2, j=j: e.activation(
                                out=pt[:, bs * 1024 + j * 512:bs * 1024 + (j + 1) * 512],
                                in_=ps_s[:, bs * 1024 + j * 512:bs * 1024 + (j + 1) * 512],
                                func=AF.Exp, scale=scale),
                                P.done["scalar"] if j == 1 else None)
                        e_cnt[g] = P.done["scalar"].n
                    av_group(ngr - 1)

                    P.wait("vector", P.done["tensor"], a_cnt[ngr - 1])
                    P.emit("vector", lambda e, b_o=b_o: e.reciprocal(
                        rc[real:real + 1, :], ps_o[b_o][real:real + 1, :]),
                        serial=True)
                    P.emit("vector", lambda e: e.tensor_copy(
                        rcb[real:real + 1, 0:512], rc[real:real + 1, :]),
                        serial=True)
                    P.emit("vector", lambda e: e.tensor_sub(
                        rc[real:real + 1, :], rc[real:real + 1, :],
                        rcb[real:real + 1, 0:512]), serial=True)
                    P.emit("vector", lambda e: e.tensor_copy(
                        rcb[real:real + 1, 512:1024], rc[real:real + 1, :]),
                        P.done["vector"], serial=True)
                    P.wait("tensor", P.done["vector"])
                    for j in range(2):
                        P.emit("tensor", lambda e, j=j: e.matmul(
                            ps_bc[:], com.onesP_bf[real:real + 1, 0:real],
                            rcb[real:real + 1, j * 512:(j + 1) * 512],
                            start=(j == 0), stop=(j == 1)))
                    P.mark_mm()
                    P.wait("vector", P.done["tensor"])
                    P.emit("vector", lambda e: e.tensor_copy(bcb[0:real, :], ps_bc[:]),
                           serial=True)
                    P.emit("vector", lambda e, b_o=b_o, qrow=qrow, qch=qch,
                           qb=qb: e.tensor_mul(
                        o_fm[qrow:qrow + real,
                             qch * T + qb * 512: qch * T + (qb + 1) * 512],
                        ps_o[b_o][0:real, :], bcb[0:real, :]),
                        P.done["vector"], serial=True)
                    epi_v[oi] = P.done["vector"].n
                    oi += 1
        P.wait("gpsimd", P.done["vector"])



def emit_l0a_slice(P, com, slab, xsrc, t0g, kts, vs, Q_fm, q_tloc, st, tag,
                   OUT=3072, HDv=65, nvh=2, qn_oc=8, kn_oc=8, do_kv=True,
                   qbias=None, kbias=None, vb_row=None):
    nc = P.nc
    Tn = 1024
    with contextlib.ExitStack() as sc:
        xh = sc.enter_context(nc.sbuf_tensor(tag + "_x", [128, 8 * Tn], BF16))
        xln = sc.enter_context(nc.sbuf_tensor(tag + "_xln", [128, 8 * Tn], BF16))
        ktmp = sc.enter_context(nc.sbuf_tensor(tag + "_kt", [128, 2 * 512], BF16))
        vt = [sc.enter_context(nc.sbuf_tensor(tag + "_vt%d" % i, [128, 16 * HDv], BF16))
              for i in range(2)]
        ld = P.psem("l0a_ld")
        kst = [P.psem("l0a_kst0"), P.psem("l0a_kst1")]
        vst = [P.psem("l0a_vst0"), P.psem("l0a_vst1")]
        P.wait("gpsimd", P.done["vector"])
        for c in range(8):
            P.dma("gpsimd", xh[:, c * Tn:(c + 1) * Tn],
                  xsrc[c * 128:(c + 1) * 128, t0g:t0g + Tn], ld)
        for sm in kst + vst:
            P.wait("vector", sm, sm.n)
        for i in range(2):
            P.emit("vector", lambda e, i=i: e.memset(vt[i][:], 1.0),
                   P.done["vector"], serial=True)
        w = [(e, ld, ld.n) for e in ("vector", "scalar", "tensor")]
        emit_ln_big(P, com, xh[:], Tn, "bf16", out=xln, tag=tag + "_ln",
                    waits=w, x_bf16=True)

        kd = {"i": 0}

        def kevac(oc, g, ps):
            i = kd["i"]
            b = i % 2
            keng = "vector"
            P.wait(keng, kst[b], kst[b].n)
            if kbias is None:
                P.emit("vector", lambda e, b=b: e.tensor_copy(
                    ktmp[:, b * 512:(b + 1) * 512], ps), P.done["vector"],
                    serial=True)
            else:
                P.emit("vector", lambda e, b=b, oc=oc: e.tensor_scalar(
                    ktmp[:, b * 512:(b + 1) * 512], ps, kbias[oc - qn_oc],
                    None, ALU.add), P.done["vector"], serial=True)
            P.wait("gpsimd", P.done[keng])
            P.dma("gpsimd", kts[(oc - qn_oc) * 128:(oc - qn_oc + 1) * 128,
                                t0g + g * 512:t0g + (g + 1) * 512],
                  ktmp[:, b * 512:(b + 1) * 512], kst[b])
            kd["i"] += 1

        if do_kv:
            emit_proj_fm(P, com, xln[:], slab, OUT, qn_oc, kn_oc, Tn, kevac,
                         "vector", tag=tag + "_kp")

        vb0 = (qn_oc + kn_oc) * 128 // 512
        vd = {"i": 0}
        nvt = Tn // 128

        def vevac(tt, ob, ps):
            i = vd["i"]
            bt = tt % 2
            half = ob - vb0
            if half == 0:
                P.wait("vector", vst[bt], vst[bt].n)
            nh = 512 // (HDv - 1)
            P.emit("vector", lambda e, bt=bt, half=half, nh=nh: e.tensor_copy(
                vt[bt][:].rearrange("p (h w) -> p h w", h=16)
                [:, half * nh:(half + 1) * nh, 0:HDv - 1],
                ps.rearrange("p (h c) -> p h c", h=nh)), P.done["vector"],
                serial=True)
            if half == nvh - 1:
                P.wait("gpsimd", P.done["vector"])
                P.dma("gpsimd", vs[t0g + tt * 128:t0g + (tt + 1) * 128, :],
                      vt[bt][:], vst[bt])
            vd["i"] += 1

        if do_kv:
            vxm = None
            if vb_row is not None:
                def vxm(e, tt, ob, ps, vb_row=vb_row, vb0=vb0):
                    return e.matmul(ps[:], com.ones1[:],
                                    vb_row[:, (ob - vb0) * 512:(ob - vb0 + 1) * 512],
                                    start=False, stop=True)
            emit_proj_tm(P, com, xln[:], slab, OUT, vb0, nvh, Tn, vevac,
                         "vector", tag=tag + "_vp", extra_mm=vxm)

        if q_tloc is not None:
            def qevac(oc, g, ps):
                if qbias is None:
                    P.emit("vector", lambda e, oc=oc, g=g: e.tensor_copy(
                        Q_fm[:, oc * T + q_tloc + g * 512:
                             oc * T + q_tloc + (g + 1) * 512],
                        ps), P.done["vector"], serial=True)
                else:
                    P.emit("vector", lambda e, oc=oc, g=g: e.tensor_scalar(
                        Q_fm[:, oc * T + q_tloc + g * 512:
                             oc * T + q_tloc + (g + 1) * 512],
                        ps, qbias[oc], None, ALU.add),
                        P.done["vector"], serial=True)

            emit_proj_fm(P, com, xln[:], slab, OUT, 0, qn_oc, Tn, qevac,
                         "vector", tag=tag + "_qp")
        for sm in kst + vst:
            P.wait("gpsimd", sm)
        st.append((kst, vst))


def l0a_kv_loader(P, kts, vs, HDv=65):
    W = 4 * HDv

    def loader(hg, K_sb, V_sb, sem):
        for cc in range(2):
            P.dma("gpsimd", K_sb[:, cc * SEQ:(cc + 1) * SEQ],
                  kts[hg * 256 + cc * 128:hg * 256 + (cc + 1) * 128, :], sem)
        vsr = vs.rearrange("(kb p) w -> p kb w", p=128)
        vdr = V_sb[:].rearrange("p (kb w) -> p kb w", w=W)
        for i in range(8):
            P.dma("gpsimd", vdr[:, i * 8:(i + 1) * 8, :],
                  vsr[:, i * 8:(i + 1) * 8, hg * W:(hg + 1) * W], sem)
    return loader


def emit_l0a(P, com, A):
    nc = P.nc
    st = []
    with contextlib.ExitStack() as sa:
        wsem = P.psem("l0a_w")
        slab = load_wslab_sc(P, sa, A.tag + "wqkv_sb", A.wqkv, D, 3072, wsem)
        P.wait("tensor", wsem)
        for si, (xsrc, t0g) in enumerate(A.kv_slices):
            emit_l0a_slice(P, com, slab, xsrc, t0g, A.kts, A.vs, None, None,
                           st, "%sA_%d" % (A.tag, si))
        for si, (xsrc, t0s, t0d) in enumerate(A.q_slices):
            emit_l0a_slice(P, com, slab, xsrc, t0s, A.kts, A.vs, A.Q_fm,
                           t0d, st, "%sQ_%d" % (A.tag, si), do_kv=False)
    P.fence()
    if A.ag_groups is not None:
        cc = P.psem("cc")
        P.wait("gpsimd", P.psem("l0a_kst0"))
        P.wait("gpsimd", P.psem("l0a_kst1"))
        P.wait("gpsimd", P.psem("l0a_vst0"))
        P.wait("gpsimd", P.psem("l0a_vst1"))
        P.emit("gpsimd", lambda e: e.collective_compute(
            "AllGather", ALU.bypass, replica_groups=A.ag_groups,
            ins=[A.kts.opt()], outs=[A.ktsg.opt()]), cc)
        P.emit("gpsimd", lambda e: e.collective_compute(
            "AllGather", ALU.bypass, replica_groups=A.ag_groups,
            ins=[A.vs.opt()], outs=[A.vsg.opt()]), cc)
        P.wait("gpsimd", cc)
    if A.stage == "A":
        return
    emit_flash(P, com, Q_fm=A.Q_fm, o_fm=A.o_fm,
               kv_loader=A.kv_loader, HDv=65, scale=0.125, tag=A.tag + "F")
    P.fence()
    with contextlib.ExitStack() as sc:
        xo = sc.enter_context(nc.sbuf_tensor(A.tag + "xo", [128, 8 * T], F32))
        ld = P.psem("c_ld")
        sto = P.psem("c_st")
        P.wait("sync", P.done["vector"])
        for c in range(8):
            P.dma("sync", xo[:, c * T:(c + 1) * T],
                  A.x_resid[c * 128:(c + 1) * 128, :], ld)
        w = [(e, ld, ld.n) for e in ("vector", "scalar", "tensor")]
        emit_ln_big(P, com, A.o_fm[:], T, "residual", xold=xo, xnew=xo,
                    tag=A.tag + "postln", waits=w, x_bf16=True)
        P.wait("sync", P.done["vector"])
        for c in range(8):
            P.dma("sync", A.x_out[c * 128:(c + 1) * 128, :],
                  xo[:, c * T:(c + 1) * T], sto)
        P.wait("sync", sto)


def build_l0a_ckpt(stage="full"):
    import types
    nc = bass.Bass("TRN2", target_bir_lowering=False, debug=False)
    xfull = nc.dram_tensor("xfull", [D, SEQ], F32, kind="ExternalInput").ap()
    xown = nc.dram_tensor("xown", [D, T], F32, kind="ExternalInput").ap()
    wqkv = nc.dram_tensor("wqkv", [D, 3072], F32, kind="ExternalInput").ap()
    kvkind = "ExternalOutput" if stage == "A" else "Internal"
    x1 = nc.dram_tensor("x1", [D, T], F32, kind="ExternalOutput").ap()
    kts = nc.dram_tensor("kts", [D, SEQ], BF16, kind=kvkind).ap()
    vs = nc.dram_tensor("vs", [SEQ, 16 * 65], BF16, kind=kvkind).ap()
    P = Prog(nc)
    com = Common(P, Tn=512, with_ln_xb=False)
    A = types.SimpleNamespace(
        tag="l0a", wqkv=wqkv, kts=kts, vs=vs, ktsg=None, vsg=None,
        ag_groups=None, stage=stage,
        kv_slices=[(xfull, s * T + h * 1024) for s in range(4)
                   for h in range(2)],
        q_slices=[(xown, h * 1024, h * 1024) for h in range(2)],
        Q_fm=P.sb("Q_fm", [128, 8 * T], BF16),
        o_fm=P.sb("o_fm", [128, 8 * T], BF16),
        kv_loader=l0a_kv_loader(P, kts, vs),
        x_resid=xown, x_out=x1)
    emit_l0a(P, com, A)
    P.build()
    return nc


def build_l0b_ckpt(ctx=None):
    if ctx is None:
        nc = bass.Bass("TRN2", target_bir_lowering=False, debug=False)
        xin = nc.dram_tensor("xin", [D, T], F32, kind="ExternalInput").ap()
        xfull1 = nc.dram_tensor("xfull1", [D, SEQ], F32, kind="ExternalInput").ap()
        wqkvb = nc.dram_tensor("wqkvb", [D, 2560], F32, kind="ExternalInput").ap()
        wob = nc.dram_tensor("wob", [D, D], F32, kind="ExternalInput").ap()
        wrt = nc.dram_tensor("wrt", [D, 8], F32, kind="ExternalInput").ap()
        w1 = nc.dram_tensor("w1", [8, D, 2048], F32, kind="ExternalInput").ap()
        w2 = nc.dram_tensor("w2", [8, 2048, D], F32, kind="ExternalInput").ap()
        bpack = nc.dram_tensor("bpack", [128, 216], F32, kind="ExternalInput").ap()
        vbr = nc.dram_tensor("vbr", [1, 512], F32, kind="ExternalInput").ap()
        rbr = nc.dram_tensor("rbr", [1, 8], F32, kind="ExternalInput").ap()
        ident = nc.dram_tensor("ident", [128, 128], F32, kind="ExternalInput").ap()
        x2 = nc.dram_tensor("x2", [D, T], F32, kind="ExternalOutput").ap()
        kts = nc.dram_tensor("kts2", [D, SEQ], BF16).ap()
        vs = nc.dram_tensor("vs2", [SEQ, 16 * 33], BF16).ap()
        P = Prog(nc)
        com = Common(P, Tn=512, with_ln_xb=False)
        com.ln_xb = P.sb("lnxb512", [128, 8 * 512], BF16)
        sc0 = P.stack
        kv_slices = [(xfull1, s_ * T + h_ * 1024) for s_ in range(4)
                     for h_ in range(2)]
        kvld = l0a_kv_loader(P, kts, vs, HDv=33)
        ag = None
    else:
        nc, P, com, sc0 = ctx["nc"], ctx["P"], ctx["com"], ctx["sc"]
        io = ctx["io"]
        (xin, wqkvb, wob, wrt, w1, w2, bpack, vbr, rbr, ident, x2, kts,
         vs) = (io["xin"], io["wqkvb"], io["wob"], io["wrt"], io["w1"],
                io["w2"], io["bpack"], io["vbr"], io["rbr"], io["ident"],
                io["x2"], io["kts"], io["vs"])
        kv_slices = ctx["kv_slices"]
        kvld = ctx["kv_loader"]
        ag = ctx.get("ag")
    sb0 = lambda n_, sh_, dt_: sc0.enter_context(nc.sbuf_tensor(n_, sh_, dt_))
    mla = sb0("l0b_mla", [128, 8 * T], BF16)
    bp = sb0("l0b_bp", [128, 216], F32)
    vb_row = sb0("vb_row", [1, 512], BF16)
    rb_row = sb0("rb_row", [1, 8], BF16)
    idt = sb0("l0b_idt", [128, 128], BF16)
    eye8 = sb0("eye8", [8, 8], BF16)
    ld0 = P.psem("ld0")
    P.dma("gpsimd", bp[:], bpack[:, :], ld0)
    P.dma("gpsimd", vb_row[:], vbr[:, :], ld0)
    P.dma("gpsimd", rb_row[:], rbr[:, :], ld0)
    P.dma("gpsimd", idt[:], ident[:, :], ld0)
    P.dma("gpsimd", eye8[:], ident[0:8, 0:8], ld0)
    qbias = [bp[:, i:i + 1] for i in range(8)]
    kbias = [bp[:, 8 + i:8 + i + 1] for i in range(8)]
    obias = [bp[:, 16 + i:16 + i + 1] for i in range(8)]
    b1col = lambda e_, hc: bp[:, 24 + e_ * 16 + hc:24 + e_ * 16 + hc + 1]
    b2col = lambda e_, oc: bp[:, 152 + e_ * 8 + oc:152 + e_ * 8 + oc + 1]
    for eng in ("scalar", "tensor", "vector", "gpsimd"):
        P.wait(eng, ld0)
    st = []

    with contextlib.ExitStack() as sa:
        wsem = P.psem("l0b_w")
        slab = load_wslab_sc(P, sa, "wqkvb_sb", wqkvb, D, 2560, wsem)
        P.wait("tensor", wsem)
        for si, (xsrc_, t0g) in enumerate(kv_slices):
            emit_l0a_slice(P, com, slab, xsrc_, t0g, kts, vs, None, None,
                           st, "l0bA_%d" % si, OUT=2560, HDv=33,
                           nvh=1, qn_oc=8, kn_oc=8, kbias=kbias,
                           vb_row=vb_row)
    P.fence()
    if ag is not None:
        emit_ag(P, ag, wait_sems=[P.psem(nm) for nm in
                                  ("l0a_kst0", "l0a_kst1", "l0a_vst0",
                                   "l0a_vst1")])
    with contextlib.ExitStack() as sb:
        Q_fm = sb.enter_context(nc.sbuf_tensor("Q_fm", [128, 8 * T], BF16))
        o_fm = sb.enter_context(nc.sbuf_tensor("o_fm", [128, 8 * T], BF16))
        P.emit("vector", lambda e: e.memset(o_fm[:], 0.0), P.done["vector"])
        with contextlib.ExitStack() as sq:
            wsem = P.psem("l0b_wq")
            qslab = load_wslab_sc(P, sq, "wq_sb", wqkvb, D, 1024, wsem, col0=0)
            P.wait("tensor", wsem)
            for half in range(2):
                emit_l0a_slice(P, com, qslab, xin, half * 1024, kts, vs, Q_fm,
                               half * 1024, st, "l0bQ_%d" % half, OUT=1024,
                               HDv=33, nvh=1, do_kv=False, qbias=qbias)
        P.fence()
        emit_flash(P, com, Q_fm=Q_fm, o_fm=o_fm,
                   kv_loader=kvld,
                   HDv=33, scale=1.0 / math.sqrt(32.0), tag="l0bF")
        P.fence()

        with contextlib.ExitStack() as sc:
            wsem = P.psem("l0b_wo")
            woslab = load_wslab_sc(P, sc, "wo_sb", wob, D, D, wsem)
            P.wait("tensor", wsem)

            def oevac(oc, g, ps):
                P.emit("vector", lambda e, oc=oc, g=g: e.tensor_scalar(
                    mla[:, oc * T + g * 512:oc * T + (g + 1) * 512],
                    ps, obias[oc], None, ALU.add),
                    P.done["vector"], serial=True)

            emit_proj_fm(P, com, o_fm[:], woslab, D, 0, 8, T, oevac, "vector",
                         tag="l0b_wo")
        P.fence()

    cb_sb = sb0("cb_sb", [8, T], BF16)
    with contextlib.ExitStack() as sc:
        wsem = P.psem("l0b_wr")
        rslab = load_wslab_sc(P, sc, "wr_sb", wrt, D, 8, wsem)
        lg = sc.enter_context(nc.sbuf_tensor("rt_lg", [128, 8], F32))
        sc1 = sc.enter_context(nc.sbuf_tensor("rt_s", [128, 8], F32))
        cols = sc.enter_context(nc.sbuf_tensor("rt_c", [128, 8], F32))
        cbt = sc.enter_context(nc.sbuf_tensor("rt_cb", [128, 8], BF16))
        rbb = sc.enter_context(nc.sbuf_tensor("rt_rbb", [128, 8], F32))
        ps_l = sc.enter_context(nc.psum_tensor("rt_ps", [128, 8], F32))
        ps_t = sc.enter_context(nc.psum_tensor("rt_pt", [8, 128], BF16))
        P.wait("tensor", wsem)
        P.wait("tensor", P.done["vector"])
        P.emit("tensor", lambda e: e.matmul(ps_l[:], com.ones1[:], rb_row[:],
                                            start=True, stop=True))
        P.mark_mm()
        P.wait("vector", P.done["tensor"])
        P.emit("vector", lambda e: e.tensor_copy(rbb[:], ps_l[:]),
               P.done["vector"], serial=True)
        for tt in range(T // 128):
            P.wait("tensor", P.done["vector"])
            for kc in range(8):
                P.emit("tensor", lambda e, tt=tt, kc=kc: e.matmul(
                    ps_l[:], mla[:, kc * T + tt * 128:kc * T + (tt + 1) * 128],
                    rslab[:, kc * 8:(kc + 1) * 8],
                    start=(kc == 0), stop=(kc == 7)))
            P.mark_mm()
            P.wait("vector", P.done["tensor"])
            V = "vector"
            P.emit(V, lambda e: e.tensor_add(lg[:], ps_l[:], rbb[:]), serial=True)
            P.emit(V, lambda e: e.reduce_max(sc1[:, 0:1], lg[:], AX.X), serial=True)
            P.emit(V, lambda e: e.tensor_scalar(lg[:], lg[:], sc1[:, 0:1], None,
                                                ALU.subtract), serial=True)
            P.wait("scalar", P.done["vector"])
            P.emit("scalar", lambda e: e.activation(out=lg[:], in_=lg[:],
                                                    func=AF.Exp),
                   P.done["scalar"], serial=True)
            P.wait(V, P.done["scalar"])
            P.emit(V, lambda e: e.reduce_sum(sc1[:, 1:2], lg[:], AX.X), serial=True)
            P.emit(V, lambda e: e.reciprocal(sc1[:, 2:3], sc1[:, 1:2]), serial=True)
            P.emit(V, lambda e: e.tensor_scalar(lg[:], lg[:], sc1[:, 2:3], None,
                                                ALU.mult), serial=True)
            P.emit(V, lambda e: e.reduce_max(sc1[:, 3:4], lg[:], AX.X), serial=True)
            P.emit(V, lambda e: e.tensor_scalar(cols[:], lg[:], sc1[:, 3:4], None,
                                                ALU.is_ge), serial=True)
            P.emit(V, lambda e: e.scalar_tensor_tensor(
                cols[:], cols[:], -1e30, lg[:], ALU.mult, ALU.add), serial=True)
            P.emit(V, lambda e: e.reduce_max(sc1[:, 4:5], cols[:], AX.X),
                   serial=True)
            P.emit(V, lambda e: e.tensor_scalar(cols[:], lg[:], sc1[:, 4:5], None,
                                                ALU.is_ge), serial=True)
            P.emit(V, lambda e: e.tensor_add(sc1[:, 5:6], sc1[:, 3:4],
                                             sc1[:, 4:5]), serial=True)
            P.emit(V, lambda e: e.tensor_scalar(sc1[:, 6:7], sc1[:, 5:6], 1e-8,
                                                None, ALU.add), serial=True)
            P.emit(V, lambda e: e.reciprocal(sc1[:, 7:8], sc1[:, 6:7]), serial=True)
            P.emit(V, lambda e: e.tensor_mul(cols[:], cols[:], lg[:]), serial=True)
            P.emit(V, lambda e: e.tensor_scalar(cbt[:], cols[:], sc1[:, 7:8], None,
                                                ALU.mult), P.done["vector"],
                   serial=True)
            P.wait("tensor", P.done["vector"])
            P.emit("tensor", lambda e: e.transpose(ps_t[:], cbt[:], idt[:]))
            P.mark_mm()
            P.wait("vector", P.done["tensor"])
            P.emit(V, lambda e, tt=tt: e.tensor_copy(
                cb_sb[:, tt * 128:(tt + 1) * 128], ps_t[:]), P.done["vector"],
                serial=True)
    P.fence()

    with contextlib.ExitStack() as sc:
        w1s = sc.enter_context(nc.sbuf_tensor("w1s", [128, 8 * 2048], BF16))
        w2s = sc.enter_context(nc.sbuf_tensor("w2s", [128, 16 * 1024], BF16))
        h1s = sc.enter_context(nc.sbuf_tensor("h1s", [128, 16 * 512], BF16))
        x1tb = sc.enter_context(nc.sbuf_tensor("x1tb", [128, 8 * 512], F32))
        x1ln = com.ln_xb
        ytb = sc.enter_context(nc.sbuf_tensor("ytb", [128, 8 * 512], F32))
        h2s = sc.enter_context(nc.sbuf_tensor("h2s", [128, 512], BF16))
        rowb = sc.enter_context(nc.sbuf_tensor("rowb", [1, 512], BF16))
        cbb = sc.enter_context(nc.sbuf_tensor("cbb", [128, 512], BF16))
        ps_r = sc.enter_context(nc.psum_tensor("ps_r", [1, 512], F32))
        ps_b = sc.enter_context(nc.psum_tensor("ps_b", [128, 512], F32))
        ldx = P.psem("ldx")
        ldw = P.psem("ldw")
        stx = P.psem("stx")
        for tb in range(4):
            t0 = tb * 512
            P.wait("sync", P.done["vector"])
            P.wait("sync", stx)
            for c in range(8):
                P.dma("sync", x1tb[:, c * 512:(c + 1) * 512],
                      xin[c * 128:(c + 1) * 128, t0:t0 + 512], ldx)
            w = [(e, ldx, ldx.n) for e in ("vector", "scalar", "tensor")]
            emit_ln(P, com, x1tb[:], 512, "bf16", out=x1ln,
                    tag="moeln%d" % tb, waits=w)
            P.wait("vector", P.done["vector"])
            P.emit("vector", lambda e: e.tensor_copy(ytb[:], x1ln[:]),
                   P.done["vector"], serial=True)
            for ex in range(8):
                P.wait("gpsimd", P.done["tensor"])
                for kc in range(8):
                    P.dma("gpsimd", w1s[:, kc * 2048:(kc + 1) * 2048],
                          w1[ex, kc * 128:(kc + 1) * 128, :], ldw)
                for kc in range(16):
                    P.dma("gpsimd", w2s[:, kc * 1024:(kc + 1) * 1024],
                          w2[ex, kc * 128:(kc + 1) * 128, :], ldw)
                P.wait("tensor", P.done["vector"])
                P.emit("tensor", lambda e, ex=ex, t0=t0: e.matmul(
                    ps_r[:], eye8[:, ex:ex + 1], cb_sb[:, t0:t0 + 512],
                    start=True, stop=True))
                P.mark_mm()
                P.wait("vector", P.done["tensor"])
                P.emit("vector", lambda e: e.tensor_copy(rowb[:], ps_r[:]),
                       P.done["vector"], serial=True)
                P.wait("tensor", P.done["vector"])
                P.emit("tensor", lambda e: e.matmul(
                    ps_b[:], com.ones1[:], rowb[:], start=True, stop=True))
                P.mark_mm()
                P.wait("vector", P.done["tensor"])
                P.emit("vector", lambda e: e.tensor_copy(cbb[:], ps_b[:]),
                       P.done["vector"], serial=True)
                P.wait("tensor", ldw)

                def h1evac(hc, g, ps, ex=ex):
                    P.emit("scalar", lambda e, hc=hc: e.activation(
                        out=h1s[:, hc * 512:(hc + 1) * 512], in_=ps,
                        func=AF.Gelu, bias=b1col(ex, hc)),
                        P.done["scalar"], serial=True)

                with contextlib.ExitStack() as sh:
                    psh = [sh.enter_context(nc.psum_tensor(
                        "h1ps%d_%d_%d" % (tb, ex, i), [128, 512], F32))
                        for i in range(2)]
                    evc = []
                    for hc in range(16):
                        b = hc % 2
                        if hc >= 2:
                            P.wait("tensor", P.done["scalar"], evc[hc - 2])
                        for kc in range(8):
                            P.emit("tensor", lambda e, hc=hc, kc=kc, b=b, t0=t0: e.matmul(
                                psh[b][:],
                                w1s[:, kc * 2048 + hc * 128:kc * 2048 + (hc + 1) * 128],
                                mla[:, kc * T + t0:kc * T + t0 + 512],
                                start=(kc == 0), stop=(kc == 7)))
                        P.mark_mm()
                        P.wait("scalar", P.done["tensor"])
                        h1evac(hc, 0, psh[b][:])
                        evc.append(P.done["scalar"].n)
                    P.wait("tensor", P.done["scalar"])
                with contextlib.ExitStack() as sh:
                    psh = [sh.enter_context(nc.psum_tensor(
                        "h2ps%d_%d_%d" % (tb, ex, i), [128, 512], F32))
                        for i in range(2)]
                    evc = []
                    for oc in range(8):
                        b = oc % 2
                        if oc >= 2:
                            P.wait("tensor", P.done["vector"], evc[oc - 2])
                        for kc in range(16):
                            P.emit("tensor", lambda e, oc=oc, kc=kc, b=b: e.matmul(
                                psh[b][:],
                                w2s[:, kc * 1024 + oc * 128:kc * 1024 + (oc + 1) * 128],
                                h1s[:, kc * 512:(kc + 1) * 512],
                                start=(kc == 0), stop=(kc == 15)))
                        P.mark_mm()
                        P.wait("scalar", P.done["tensor"])
                        P.wait("vector", P.done["tensor"])
                        P.emit("vector", lambda e, oc=oc, ex=ex, b=b: e.tensor_scalar(
                            h2s[:], psh[b][:], b2col(ex, oc), None, ALU.add),
                            serial=True)
                        P.emit("vector", lambda e: e.tensor_mul(
                            h2s[:], h2s[:], cbb[:]), serial=True)
                        P.emit("vector", lambda e, oc=oc: e.scalar_tensor_tensor(
                            ytb[:, oc * 512:(oc + 1) * 512], h2s[:], 0.3,
                            ytb[:, oc * 512:(oc + 1) * 512], ALU.mult, ALU.add),
                            P.done["vector"], serial=True)
                        evc.append(P.done["vector"].n)
                    P.wait("tensor", P.done["vector"])
            emit_ln(P, com, ytb[:], 512, "residual", xold=x1tb, xnew=x1tb,
                    tag="fln%d" % tb)
            P.wait("sync", P.done["vector"])
            for c in range(8):
                P.dma("sync", x2[c * 128:(c + 1) * 128, t0:t0 + 512],
                      x1tb[:, c * 512:(c + 1) * 512], stx)
        P.wait("sync", stx)
    if ctx is None:
        P.build()
        return nc
    return None


def build_l1a(mode, ctx=None):
    if ctx is None:
        nc = bass.Bass("TRN2", target_bir_lowering=False, debug=False)
        xin = nc.dram_tensor("xin", [D, T], F32, kind="ExternalInput").ap()
        wq = nc.dram_tensor("wq", [D, D], F32, kind="ExternalInput").ap()
        wk = nc.dram_tensor("wk", [D, D], F32, kind="ExternalInput").ap()
        wv = nc.dram_tensor("wv", [D, D], F32, kind="ExternalInput").ap()
        cfm = nc.dram_tensor("cfm", [D, T], F32, kind="ExternalInput").ap()
        sfm = nc.dram_tensor("sfm", [D, T], F32, kind="ExternalInput").ap()
        ctm = nc.dram_tensor("ctm", [16, T, 64], F32, kind="ExternalInput").ap()
        stm = nc.dram_tensor("stm", [16, T, 64], F32, kind="ExternalInput").ap()
        utri = nc.dram_tensor("utri", [128, 128], F32, kind="ExternalInput").ap()
        if mode == "full":
            wg1 = nc.dram_tensor("wg1", [D, 64], F32, kind="ExternalInput").ap()
            wg2 = nc.dram_tensor("wg2", [64, D], F32, kind="ExternalInput").ap()
            wo = nc.dram_tensor("wo", [D, D], F32, kind="ExternalInput").ap()
            bpk = nc.dram_tensor("bpk", [128, 8], F32, kind="ExternalInput").ap()
            sig_p = nc.dram_tensor("sig_p", [4, 1024], F32, kind="ExternalInput").ap()
            S_p = nc.dram_tensor("S_p", [4, 16 * 128, 64], F32, kind="ExternalInput").ap()
            wselp = nc.dram_tensor("wselp", [128, 4], F32, kind="ExternalInput").ap()
            hsel = nc.dram_tensor("hsel", [128, 8 * 16], F32, kind="ExternalInput").ap()
            ident = nc.dram_tensor("ident", [128, 128], F32, kind="ExternalInput").ap()
            x3 = nc.dram_tensor("x3", [D, T], F32, kind="ExternalOutput").ap()
        else:
            sig_o = nc.dram_tensor("sig_o", [1, 1024], F32, kind="ExternalOutput").ap()
            S_o = nc.dram_tensor("S_o", [16 * 128, 64], F32, kind="ExternalOutput").ap()
        P = Prog(nc)
        com = Common(P, Tn=512, with_ln_xb=False)
        sc0 = P.stack
    else:
        nc, P, com, sc0 = ctx["nc"], ctx["P"], ctx["com"], ctx["sc"]
        io = ctx["io"]
        xin, wq, wk, wv, cfm, sfm, ctm, stm, utri = (
            io["xin"], io["wq"], io["wk"], io["wv"], io["cfm"], io["sfm"],
            io["ctm"], io["stm"], io["utri"])
        if mode == "full":
            (wg1, wg2, wo, bpk, sig_p, S_p, wselp, hsel, ident, x3) = (
                io["wg1"], io["wg2"], io["wo"], io["bpk"], io["sig_p"],
                io["S_p"], io["wselp"], io["hsel"], io["ident"], io["x3"])
        else:
            sig_o, S_o = io["sig_o"], io["S_o"]
    com.col(1e-6)
    sb0 = lambda n_, sh_, dt_: sc0.enter_context(nc.sbuf_tensor(n_, sh_, dt_))
    tg = "l1a" + mode[0]
    xln = sb0(tg + "xln", [128, 8 * T], BF16)
    ut = sb0(tg + "ut", [128, 128], BF16)
    carry = sb0(tg + "carry", [1, 16 * 64], F32)
    ld = P.psem("ld")
    P.dma("gpsimd", ut[:], utri[:, :], ld)
    if mode == "full":
        o_fm = sb0("o_l1", [128, 8 * T], BF16)
        hs = sb0("hsel_sb", [128, 8 * 16], BF16)
        idt = sb0("l1a_idt", [128, 128], BF16)
        bp = sb0("bp_l1", [128, 8], F32)
        sigc = sb0("sigc", [1, 1024], F32)
        scb_p = sb0("scb_p", [128, 16 * 64], BF16)
        P.dma("gpsimd", hs[:], hsel[:, :], ld)
        P.dma("gpsimd", idt[:], ident[:, :], ld)
        P.dma("gpsimd", bp[:], bpk[:, :], ld)
        nw_col = [bp[:, c:c + 1] for c in range(8)]
        with contextlib.ExitStack() as scc:
            wsl = scc.enter_context(nc.sbuf_tensor("wsl", [128, 4], F32))
            sgp = [scc.enter_context(nc.sbuf_tensor("sgp%d" % r, [1, 1024], F32))
                   for r in range(4)]
            Spt = [scc.enter_context(nc.sbuf_tensor("Spt%d" % r,
                                                    [128, 1024], F32))
                   for r in range(4)]
            Sac = scc.enter_context(nc.sbuf_tensor("Sac", [128, 1024], F32))
            ldc = P.psem("l1a_ldc")
            P.dma("gpsimd", wsl[:], wselp[:, :], ldc)
            for r in range(4):
                P.dma("gpsimd", sgp[r][:], sig_p[r:r + 1, :], ldc)
                P.dma("gpsimd", Spt[r][:].rearrange("p (hh d) -> p hh d", d=64),
                      S_p[r].rearrange("(hh p) d -> p hh d", p=128), ldc)
            P.wait("vector", ldc)
            V0 = "vector"
            P.emit(V0, lambda e: e.tensor_scalar(
                sigc[:], sgp[0][:], wsl[0:1, 0:1], None, ALU.mult), serial=True)
            for r in range(1, 4):
                P.emit(V0, lambda e, r=r: e.scalar_tensor_tensor(
                    sigc[:], sgp[r][:], wsl[0:1, r:r + 1], sigc[:],
                    ALU.mult, ALU.add), serial=True)
            P.emit(V0, lambda e: e.tensor_scalar(
                Sac[:], Spt[0][:], wsl[:, 0:1], None, ALU.mult), serial=True)
            for r in range(1, 4):
                P.emit(V0, lambda e, r=r: e.scalar_tensor_tensor(
                    Sac[:], Spt[r][:], wsl[:, r:r + 1], Sac[:],
                    ALU.mult, ALU.add), serial=True)
            P.emit(V0, lambda e: e.tensor_copy(scb_p[:], Sac[:]),
                   P.done["vector"], serial=True)
    L0_scb = scb_p if mode == "full" else None
    with contextlib.ExitStack() as sx:
        xw = sx.enter_context(nc.sbuf_tensor(tg + "xw", [128, 8 * T], BF16))
        P.wait("gpsimd", P.done["vector"])
        P.dma("gpsimd", xw[:].rearrange("p (c t) -> p c t", c=8),
              xin[:, :].rearrange("(c p) t -> p c t", p=128), ld)
        for eng in ("scalar", "tensor", "vector", "gpsimd"):
            P.wait(eng, ld)
        w = [(e, ld, ld.n) for e in ("vector", "scalar", "tensor")]
        emit_ln_big(P, com, xw[:], T, "bf16", out=xln, tag=tg + "ln", waits=w,
                    x_bf16=True)
        P.fence()

    if mode == "full":
        P.wait("vector", P.done["vector"])
        P.emit("vector", lambda e: e.tensor_scalar(
            carry[:], sigc[:], 1.0, None, ALU.add), P.done["vector"],
            serial=True)
    else:
        P.emit("vector", lambda e: e.memset(carry[:], 1.0), P.done["vector"],
               serial=True)

    with contextlib.ExitStack() as sw:
        wsem = P.psem("l1aw")
        qslab = load_wslab_sc(P, sw, tg + "wq_sb", wq, D, D, wsem)
        kslab = load_wslab_sc(P, sw, tg + "wk_sb", wk, D, D, wsem)
        vslab = load_wslab_sc(P, sw, tg + "wv_sb", wv, D, D, wsem)
        P.wait("tensor", wsem)

        with contextlib.ExitStack() as sh:
            kdd = sh.enter_context(nc.sbuf_tensor(tg + "kdd", [128, T], BF16))
            Etm = sh.enter_context(nc.sbuf_tensor(tg + "Etm", [128, 16 * 64], BF16))
            kddtm = sh.enter_context(nc.sbuf_tensor(tg + "kddtm", [128, 16 * 128], BF16))
            vtm = sh.enter_context(nc.sbuf_tensor(tg + "vtm", [128, 16 * 64], BF16))
            trig = sh.enter_context(nc.sbuf_tensor(tg + "trig", [128, T + 2048], BF16))
            qsb = sh.enter_context(nc.sbuf_tensor(tg + "qsb", [128, T], BF16))
            qp = sh.enter_context(nc.sbuf_tensor(tg + "qp", [128, T], BF16))
            rcpb = sh.enter_context(nc.sbuf_tensor(tg + "rcpb", [128, 16 * 64], BF16))
            rcp = rcpb
            rcpf = sh.enter_context(nc.sbuf_tensor(tg + "rcpf", [128, T], BF16))
            t1 = sh.enter_context(nc.sbuf_tensor(tg + "t1f", [128, T], BF16))
            pt = sh.enter_context(nc.sbuf_tensor(tg + "ptl1", [128, 128], BF16))
            S_snap = sh.enter_context(nc.sbuf_tensor(tg + "S_snap", [128, 16 * 64], BF16))
            S_run = sh.enter_context(nc.sbuf_tensor(tg + "S_run", [128, 64], F32))
            if mode == "full":
                scb = L0_scb
            ps_p = sh.enter_context(nc.psum_tensor(tg + "ps_proj", [128, 512], F32))
            ps_c = sh.enter_context(nc.psum_tensor(tg + "ps_cum", [128, 64], F32))
            ps_sg = sh.enter_context(nc.psum_tensor(tg + "ps_sig", [1, 64], F32))
            ps_S = sh.enter_context(nc.psum_tensor(tg + "ps_S", [128, 64], F32))
            ps_pt = sh.enter_context(nc.psum_tensor(tg + "ps_pt", [128, 128], F32))
            ps_o = sh.enter_context(nc.psum_tensor(tg + "ps_o", [64, 128], F32))
            ps_tr = sh.enter_context(nc.psum_tensor(tg + "ps_tr", [64, 128], BF16))
            for h in range(16):
                emit_l1a_head(P, com, locals() | {"P": P}, h, mode)
        if mode == "partial":
            sto = P.psem("sto")
            P.wait("vector", P.done["tensor"])
            P.emit("vector", lambda e: e.tensor_scalar(
                carry[:], carry[:], -1.0, None, ALU.add), P.done["vector"],
                serial=True)
            P.wait("gpsimd", P.done["vector"])
            P.dma("gpsimd", sig_o[:, :], carry[:], sto)
            P.wait("gpsimd", sto)
            P.wait("sync", sto)
    if mode == "full":
        emit_l1a_post(P, com, nc, locals())
    if ctx is None:
        P.build()
        return nc
    return None


def emit_l1a_head(P, com, L, h, mode):
    nc = P.nc
    full = (mode == "full")
    xln, qslab, kslab, vslab = L["xln"], L["qslab"], L["kslab"], L["vslab"]
    kdd, kddtm, vtm, Etm = L["kdd"], L["kddtm"], L["vtm"], L["Etm"]
    trig, qsb, qp, rcp, rcpb, rcpf, t1 = (L["trig"], L["qsb"], L["qp"],
                                          L["rcp"], L["rcpb"], L["rcpf"],
                                          L["t1"])
    ps_p, ps_c, ps_sg, ps_S = L["ps_p"], L["ps_c"], L["ps_sg"], L["ps_S"]
    carry, ut = L["carry"], L["ut"]
    cfm, sfm, ctm, stm = L["cfm"], L["sfm"], L["ctm"], L["stm"]
    ld = P.psem("l1a_trig_ld")
    V = "vector"

    P.wait("gpsimd", P.done["tensor"])
    P.wait("gpsimd", P.done["vector"])
    P.dma("gpsimd", trig[0:64, 0:T], cfm[h * 64:(h + 1) * 64, :], ld)
    P.dma("gpsimd", trig[64:128, 0:T], sfm[h * 64:(h + 1) * 64, :], ld)
    ctm_r = ctm[h].rearrange("(tc p) d -> p tc d", p=128)
    stm_r = stm[h].rearrange("(tc p) d -> p tc d", p=128)
    ctm_d = trig[:, T:T + 1024].rearrange("p (tc d) -> p tc d", d=64)
    stm_d = trig[:, T + 1024:T + 2048].rearrange("p (tc d) -> p tc d", d=64)
    for i in range(2):
        P.dma("gpsimd", ctm_d[:, i * 8:(i + 1) * 8, :],
              ctm_r[:, i * 8:(i + 1) * 8, :], ld)
        P.dma("gpsimd", stm_d[:, i * 8:(i + 1) * 8, :],
              stm_r[:, i * 8:(i + 1) * 8, :], ld)
    for eng in ("tensor", "scalar", "vector"):
        P.wait(eng, ld)

    def fm_proj(slab, func, out_sb):
        for g in range(4):
            for half in range(2):
                for kc in range(8):
                    P.emit("tensor", lambda e, g=g, half=half, kc=kc: e.matmul(
                        ps_p[half * 64:(half + 1) * 64, 0:512],
                        slab[:, kc * D + h * 64:kc * D + (h + 1) * 64],
                        xln[:, kc * T + g * 512:kc * T + (g + 1) * 512],
                        start=(kc == 0), stop=(kc == 7)))
            P.mark_mm()
            P.wait("scalar", P.done["tensor"])
            P.emit("scalar", lambda e, g=g: e.activation(
                out=out_sb[:, g * 512:(g + 1) * 512], in_=ps_p[:, 0:512],
                func=func), P.done["scalar"], serial=True)
            P.wait("tensor", P.done["scalar"])

    P.wait("tensor", P.done["vector"])
    fm_proj(kslab, AF.Exp, t1)
    P.wait(V, P.done["scalar"])
    P.emit(V, lambda e: e.tensor_mul(kdd[:, 0:T], t1[:, 0:T], trig[:, 0:T]),
           P.done["vector"], serial=True)
    if full:
        fm_proj(qslab, AF.Silu, qsb)

    for tc in range(16):
        for kc in range(8):
            P.emit("tensor", lambda e, tc=tc, kc=kc: e.matmul(
                ps_c[:], xln[:, kc * T + tc * 128:kc * T + (tc + 1) * 128],
                kslab[:, kc * D + h * 64:kc * D + (h + 1) * 64],
                start=(kc == 0), stop=(kc == 7)))
        P.mark_mm()
        P.wait("scalar", P.done["tensor"])
        P.emit("scalar", lambda e, tc=tc: e.activation(
            out=Etm[:, tc * 64:(tc + 1) * 64], in_=ps_c[:], func=AF.Exp),
            P.done["scalar"], serial=True)
        P.wait("tensor", P.done["scalar"])
        for kc in range(8):
            P.emit("tensor", lambda e, tc=tc, kc=kc: e.matmul(
                ps_c[:], xln[:, kc * T + tc * 128:kc * T + (tc + 1) * 128],
                vslab[:, kc * D + h * 64:kc * D + (h + 1) * 64],
                start=(kc == 0), stop=(kc == 7)))
        P.mark_mm()
        P.wait(V, P.done["tensor"])
        P.emit(V, lambda e, tc=tc: e.tensor_copy(
            vtm[:, tc * 64:(tc + 1) * 64], ps_c[:]), P.done["vector"],
            serial=True)
        P.wait("tensor", P.done["vector"])
    P.wait(V, P.done["scalar"])
    P.emit(V, lambda e: e.tensor_mul(
        kddtm[:].rearrange("p (tc e) -> p tc e", e=128)[:, :, 0:64],
        Etm[:].rearrange("p (tc d) -> p tc d", d=64),
        ctm_d), serial=True)
    P.emit(V, lambda e: e.tensor_mul(
        kddtm[:].rearrange("p (tc e) -> p tc e", e=128)[:, :, 64:128],
        Etm[:].rearrange("p (tc d) -> p tc d", d=64),
        stm_d), P.done["vector"], serial=True)

    scb, S_snap, idt = L.get("scb"), L.get("S_snap"), L.get("idt")
    S_run = L["S_run"]
    for tc in range(16):
        P.emit(V, lambda e, tc=tc: e.tensor_add(
            Etm[0:1, tc * 64:(tc + 1) * 64], Etm[0:1, tc * 64:(tc + 1) * 64],
            carry[0:1, h * 64:(h + 1) * 64]), P.done["vector"], serial=True)
        P.wait("tensor", P.done["vector"])
        P.emit("tensor", lambda e, tc=tc: e.matmul(
            ps_c[:], ut[:], Etm[:, tc * 64:(tc + 1) * 64],
            start=True, stop=True))
        P.emit("tensor", lambda e, tc=tc: e.matmul(
            ps_sg[:], com.ones128[:], Etm[:, tc * 64:(tc + 1) * 64],
            start=True, stop=True))
        P.mark_mm()
        P.wait(V, P.done["tensor"])
        P.emit(V, lambda e: e.tensor_copy(carry[0:1, h * 64:(h + 1) * 64],
                                          ps_sg[:]), serial=True)
        if full:
            def _recip(e, tc=tc):
                with nc.allow_low_precision(reason="recip feeds bf16 matmul"):
                    return e.reciprocal(rcpb[:, tc * 64:(tc + 1) * 64], ps_c[:])
            P.emit(V, _recip, P.done["vector"], serial=True)
            P.wait("tensor", P.done["vector"])
            P.emit("tensor", lambda e, tc=tc: e.transpose(
                L["ps_tr"][0:64, :], rcpb[:, tc * 64:(tc + 1) * 64], idt[:, :]))
            P.mark_mm()
            P.wait(V, P.done["tensor"])
            P.emit(V, lambda e, tc=tc: e.tensor_copy(
                rcpf[0:64, tc * 128:(tc + 1) * 128], L["ps_tr"][0:64, :]),
                P.done["vector"], serial=True)
            P.wait("tensor", P.done["vector"])
            P.emit("tensor", lambda e, tc=tc: e.matmul(
                L["ps_pt"][64:128, :], idt[0:64, 0:64],
                rcpf[0:64, tc * 128:(tc + 1) * 128], start=True, stop=True))
            P.mark_mm()
            P.wait(V, P.done["tensor"])
            P.emit(V, lambda e, tc=tc: e.tensor_copy(
                rcpf[64:128, tc * 128:(tc + 1) * 128], L["ps_pt"][64:128, :]),
                P.done["vector"], serial=True)
        if tc == 0:
            if full:
                P.emit(V, lambda e: e.tensor_copy(
                    S_run[:], scb[:, h * 64:(h + 1) * 64]),
                    P.done["vector"], serial=True)
            else:
                P.emit(V, lambda e: e.memset(S_run[:], 0.0),
                       P.done["vector"], serial=True)
        P.wait("tensor", P.done["vector"])
        P.emit("tensor", lambda e, tc=tc: e.matmul(
            ps_S[:], kddtm[:, tc * 128:(tc + 1) * 128],
            vtm[:, tc * 64:(tc + 1) * 64], start=True, stop=True))
        P.mark_mm()
        P.wait(V, P.done["tensor"])
        P.emit(V, lambda e: e.tensor_add(S_run[:], S_run[:], ps_S[:]),
               serial=True)
        if full:
            P.emit(V, lambda e, tc=tc: e.tensor_copy(
                S_snap[:, tc * 64:(tc + 1) * 64], S_run[:]),
                P.done["vector"], serial=True)
        else:
            P.emit(V, lambda e: e.wait_ge(P.bar.h, 0), P.done["vector"],
                   serial=True)
        P.wait("tensor", P.done["vector"])

    if not full:
        S_o, sto = L["S_o"], P.psem("l1a_sto")
        P.wait("gpsimd", P.done["vector"])
        P.dma("gpsimd", S_o[h * 128:(h + 1) * 128, :], S_run[:], sto)
        P.wait(V, sto, sto.n)
        return

    P.wait(V, P.done["scalar"])
    P.emit(V, lambda e: e.tensor_mul(t1[:, 0:T], qsb[:, 0:T], trig[:, 0:T]),
           serial=True)
    P.emit(V, lambda e: e.tensor_mul(qp[:, 0:T], t1[:, 0:T], rcpf[:, 0:T]),
           P.done["vector"], serial=True)

    ps_pt, ps_o, pt, o_fm = L["ps_pt"], L["ps_o"], L["pt"], L["o_fm"]
    qch, qrow = h // 2, (h % 2) * 64
    for tc in range(16):
        P.wait("tensor", P.done["vector"])
        P.emit("tensor", lambda e, tc=tc: e.matmul(
            ps_pt[:], kdd[:, tc * 128:(tc + 1) * 128],
            qp[:, tc * 128:(tc + 1) * 128], start=True, stop=True))
        P.mark_mm()
        P.wait(V, P.done["tensor"])
        P.emit(V, lambda e: e.tensor_mul(pt[:], ps_pt[:], ut[:]),
               P.done["vector"], serial=True)
        P.wait("tensor", P.done["vector"])
        S_prev = (scb[:, h * 64:(h + 1) * 64] if tc == 0
                  else S_snap[:, (tc - 1) * 64:tc * 64])
        P.emit("tensor", lambda e, tc=tc, S_prev=S_prev: e.matmul(
            ps_o[:], S_prev, qp[:, tc * 128:(tc + 1) * 128],
            start=True, stop=False))
        P.emit("tensor", lambda e, tc=tc: e.matmul(
            ps_o[:], vtm[:, tc * 64:(tc + 1) * 64], pt[:],
            start=False, stop=True))
        P.mark_mm()
        P.wait(V, P.done["tensor"])
        P.emit(V, lambda e, tc=tc: e.tensor_copy(
            o_fm[qrow:qrow + 64, qch * T + tc * 128:qch * T + (tc + 1) * 128],
            ps_o[:]), P.done["vector"], serial=True)
        P.wait("tensor", P.done["vector"])


def emit_l1a_post(P, com, nc, L):
    V = "vector"
    xln, o_fm, idt = L["xln"], L["o_fm"], L["idt"]
    wg1, wg2, wo, xin, x3, hs = (L["wg1"], L["wg2"], L["wo"], L["xin"],
                                 L["x3"], L["hs"])
    nw_col = L["nw_col"]
    with contextlib.ExitStack() as sc:
        g1b = sc.enter_context(nc.sbuf_tensor(L["tg"] + "g1b", [64, T], BF16))
        sg = sc.enter_context(nc.sbuf_tensor(L["tg"] + "sg", [128, 512], BF16))
        g2 = sc.enter_context(nc.sbuf_tensor(L["tg"] + "g2", [128, T], BF16))
        rstd = sc.enter_context(nc.sbuf_tensor(L["tg"] + "rstd16", [16, T], F32))
        rstdb = sc.enter_context(nc.sbuf_tensor(L["tg"] + "rstdb16", [16, T], BF16))
        rowt = sc.enter_context(nc.sbuf_tensor(L["tg"] + "rowt", [1, 512], BF16))
        wg1s = sc.enter_context(nc.sbuf_tensor(L["tg"] + "wg1s", [128, 8 * 64], BF16))
        wg2s = sc.enter_context(nc.sbuf_tensor(L["tg"] + "wg2s", [64, D], BF16))
        ps_g = sc.enter_context(nc.psum_tensor(L["tg"] + "ps_g1", [64, 512], F32))
        ps_f = sc.enter_context(nc.psum_tensor(L["tg"] + "ps_gf", [128, 512], F32))
        ps_ss = sc.enter_context(nc.psum_tensor(L["tg"] + "ps_ss", [16, T], F32))
        ps_r = sc.enter_context(nc.psum_tensor(L["tg"] + "ps_row", [1, 512], F32))
        ps_bc = sc.enter_context(nc.psum_tensor(L["tg"] + "ps_bc2", [128, 512], F32))
        wl = P.psem("l1a_wg")
        P.wait("gpsimd", P.done["tensor"])
        P.wait("gpsimd", P.done["vector"])
        P.wait("gpsimd", P.done["scalar"])
        for kc in range(8):
            P.dma("gpsimd", wg1s[:, kc * 64:(kc + 1) * 64],
                  wg1[kc * 128:(kc + 1) * 128, :], wl)
        P.dma("gpsimd", wg2s[:, :], wg2[:, :], wl)
        P.wait("tensor", wl)
        for g in range(4):
            P.wait("tensor", P.done["vector"])
            for kc in range(8):
                P.emit("tensor", lambda e, g=g, kc=kc: e.matmul(
                    ps_g[:], wg1s[:, kc * 64:(kc + 1) * 64],
                    xln[:, kc * T + g * 512:kc * T + (g + 1) * 512],
                    start=(kc == 0), stop=(kc == 7)))
            P.mark_mm()
            P.wait(V, P.done["tensor"])
            P.emit(V, lambda e, g=g: e.tensor_copy(
                g1b[:, g * 512:(g + 1) * 512], ps_g[:]), P.done["vector"],
                serial=True)
        for c in range(8):
            for g in range(4):
                P.wait("tensor", P.done["vector"])
                P.emit("tensor", lambda e, c=c, g=g: e.matmul(
                    ps_f[:], wg2s[:, c * 128:(c + 1) * 128],
                    g1b[:, g * 512:(g + 1) * 512], start=True, stop=True))
                P.mark_mm()
                P.wait("scalar", P.done["tensor"])
                P.emit("scalar", lambda e: e.activation(
                    out=sg[:], in_=ps_f[:], func=AF.Sigmoid),
                    P.done["scalar"], serial=True)
                P.wait(V, P.done["scalar"])
                P.emit(V, lambda e, c=c, g=g: e.tensor_mul(
                    o_fm[:, c * T + g * 512:c * T + (g + 1) * 512],
                    o_fm[:, c * T + g * 512:c * T + (g + 1) * 512], sg[:]),
                    P.done["vector"], serial=True)
        for c in range(8):
            P.wait("scalar", P.done["vector"])
            P.wait("scalar", P.done["tensor"])
            P.emit("scalar", lambda e, c=c: e.activation(
                out=g2[:, 0:T], in_=o_fm[:, c * T:(c + 1) * T],
                func=AF.Square), P.done["scalar"], serial=True)
            P.wait("tensor", P.done["scalar"])
            for g in range(4):
                P.emit("tensor", lambda e, c=c, g=g: e.matmul(
                    ps_ss[:, g * 512:(g + 1) * 512], hs[:, c * 16:(c + 1) * 16],
                    g2[:, g * 512:(g + 1) * 512],
                    start=(c == 0), stop=(c == 7)))
            P.mark_mm()
        P.wait("scalar", P.done["tensor"])
        P.emit("scalar", lambda e: e.activation(
            out=rstd[:], in_=ps_ss[:], func=AF.Ln,
            bias=com.col(1e-6)[0:16, 0:1], scale=1.0 / 64), serial=True)
        P.emit("scalar", lambda e: e.activation(
            out=rstd[:], in_=rstd[:], func=AF.Exp, scale=-0.5),
            P.done["scalar"], serial=True)
        P.wait(V, P.done["scalar"])
        P.emit(V, lambda e: e.tensor_copy(rstdb[:], rstd[:]),
               P.done["vector"], serial=True)
        for c in range(8):
            for g in range(4):
                for j in range(2):
                    h = 2 * c + j
                    P.wait("tensor", P.done["vector"])
                    P.emit("tensor", lambda e, h=h, g=g: e.matmul(
                        ps_r[:], idt[0:16, h:h + 1],
                        rstdb[:, g * 512:(g + 1) * 512], start=True, stop=True))
                    P.mark_mm()
                    P.wait(V, P.done["tensor"])
                    P.emit(V, lambda e: e.tensor_copy(rowt[:], ps_r[:]),
                           P.done["vector"], serial=True)
                    P.wait("tensor", P.done["vector"])
                    P.emit("tensor", lambda e, j=j: e.matmul(
                        ps_bc[j * 64:(j + 1) * 64, :],
                        com.onesP_bf[0:1, 0:64], rowt[:],
                        start=True, stop=True))
                    P.mark_mm()
                P.wait(V, P.done["tensor"])
                P.emit(V, lambda e, c=c, g=g: e.scalar_tensor_tensor(
                    o_fm[:, c * T + g * 512:c * T + (g + 1) * 512],
                    o_fm[:, c * T + g * 512:c * T + (g + 1) * 512],
                    nw_col[c], ps_bc[:], ALU.mult, ALU.mult),
                    P.done["vector"], serial=True)
    with contextlib.ExitStack() as sc:
        wl = P.psem("l1a_wo")
        P.wait("gpsimd", P.done["tensor"])
        P.wait("gpsimd", P.done["vector"])
        woslab = load_wslab_sc(P, sc, L["tg"] + "wo_l1", wo, D, D, wl)
        P.wait("tensor", wl)

        def oevac(oc, g, ps):
            P.emit(V, lambda e, oc=oc, g=g: e.tensor_copy(
                xln[:, oc * T + g * 512:oc * T + (g + 1) * 512], ps),
                P.done["vector"], serial=True)

        emit_proj_fm(P, com, o_fm[:], woslab, D, 0, 8, T, oevac, "vector",
                     tag="l1a_wo")
    with contextlib.ExitStack() as sc:
        x2f = sc.enter_context(nc.sbuf_tensor(L["tg"] + "x2f", [128, 8 * T], F32))
        ldf = P.psem("l1a_ldf")
        stf = P.psem("l1a_stf")
        P.wait("sync", P.done["vector"])
        P.wait("sync", P.done["tensor"])
        P.dma("sync", x2f[:].rearrange("p (c t) -> p c t", c=8),
              xin[:, :].rearrange("(c p) t -> p c t", p=128), ldf)
        w = [(e, ldf, ldf.n) for e in ("vector", "scalar", "tensor")]
        emit_ln_big(P, com, xln[:], T, "residual", xold=x2f, xnew=x2f,
                    tag=L["tg"] + "fln", waits=w, x_bf16=True)
        P.wait("sync", P.done["vector"])
        P.dma("sync", x3[:, :].rearrange("(c p) t -> p c t", p=128),
              x2f[:].rearrange("p (c t) -> p c t", c=8), stf)
        P.wait("sync", stf)


TW_A = 2560


def build_l1f_attn(ctx=None, tag=""):
    fused_halo = None
    if ctx is None:
        nc = bass.Bass("TRN2", target_bir_lowering=False, debug=False)
        x_own = nc.dram_tensor("x_own", [D, T], F32, kind="ExternalInput").ap()
        x_halo = nc.dram_tensor("x_halo", [D, 512], F32, kind="ExternalInput").ap()
        wqkv = nc.dram_tensor("wqkv", [D, 16 * 384], F32, kind="ExternalInput").ap()
        wo = nc.dram_tensor("wo", [D, D], F32, kind="ExternalInput").ap()
        tabs = nc.dram_tensor("tabs", [4 * 128, TW_A], F32, kind="ExternalInput").ap()
        masks = nc.dram_tensor("masks", [2 * 128, 1024], F32, kind="ExternalInput").ap()
        consts = nc.dram_tensor("consts", [128, 134], F32, kind="ExternalInput").ap()
        pairbc = nc.dram_tensor("pairbc", [2, 128], F32, kind="ExternalInput").ap()
        y_out = nc.dram_tensor("y", [D, T], F32, kind="ExternalOutput").ap()
        P = Prog(nc)
        com = Common(P, Tn=512, with_ln_xb=False)
        sc0 = P.stack
    else:
        nc, P, com, sc0 = ctx["nc"], ctx["P"], ctx["com"], ctx["sc"]
        io = ctx["io"]
        (x_own, wqkv, wo, tabs, masks, consts, pairbc, y_out) = (
            io["x_own"], io["wqkv"], io["wo"], io["tabs"], io["masks"],
            io["consts"], io["pairbc"], io["y_out"])
        fused_halo = ctx["halo"]
    com.col(1e-38)
    Tq = 2048
    Tk = 2304
    sb0 = lambda n_, sh_, dt_: sc0.enter_context(nc.sbuf_tensor(n_, sh_, dt_))
    xw = sb0(tag + "xw_a", [128, 8 * TW_A], BF16)
    o_fm = sb0(tag + "o_fa", [128, 8 * Tq], BF16)
    tb4 = sb0(tag + "tabs_sb", [128, 4 * TW_A], BF16)
    msk = sb0(tag + "msk_sb", [128, 2 * 1024], BF16)
    cst = sb0(tag + "cst_sb", [128, 134], F32)
    psw = sb0(tag + "psw_sb", [128, 128], BF16)
    ld = P.psem("lfa_ld")
    if fused_halo is None:
        P.dma("gpsimd", xw[:].rearrange("p (c t) -> p c t", c=8)[:, :, 0:512],
              x_halo[:, :].rearrange("(c p) t -> p c t", p=128), ld)
    else:
        halo_g, wslc = fused_halo
        xwr = xw[:].rearrange("p (c t) -> p c t", c=8)
        P.emit("vector", lambda e: e.memset(xwr[:, :, 0:255], 0.0),
               P.done["vector"], serial=True)
        htmp = [sb0(tag + "htmp%d" % r, [128, 8 * 257], BF16)
                for r in range(4)]
        for r in range(4):
            P.dma("gpsimd",
                  htmp[r][:].rearrange("p (c t) -> p c t", c=8),
                  halo_g[r * 1024:(r + 1) * 1024, :].rearrange(
                      "(c p) t -> p c t", p=128), ld)
        P.wait("vector", ld)
        hr = [h[:].rearrange("p (c t) -> p c t", c=8) for h in htmp]
        P.emit("vector", lambda e: e.tensor_scalar(
            xwr[:, :, 255:512], hr[0], wslc[:, 0:1], None, ALU.mult),
            serial=True)
        for r in range(1, 4):
            P.emit("vector", lambda e, r=r: e.scalar_tensor_tensor(
                xwr[:, :, 255:512], hr[r], wslc[:, r:r + 1],
                xwr[:, :, 255:512], ALU.mult, ALU.add), serial=True)
        P.emit("vector", lambda e: e.wait_ge(P.bar.h, 0), P.done["vector"],
               serial=True)
    P.dma("gpsimd", xw[:].rearrange("p (c t) -> p c t", c=8)[:, :, 512:TW_A],
          x_own[:, :].rearrange("(c p) t -> p c t", p=128), ld)
    for i in range(4):
        P.dma("gpsimd", tb4[:, i * TW_A:(i + 1) * TW_A],
              tabs[i * 128:(i + 1) * 128, :], ld)
    for i in range(2):
        P.dma("gpsimd", msk[:, i * 1024:(i + 1) * 1024],
              masks[i * 128:(i + 1) * 128, :], ld)
    P.dma("gpsimd", cst[:], consts[:, :], ld)
    P.dma("gpsimd", psw[:], consts[:, 6:134], ld)
    csq = tb4[:, 0 * TW_A:1 * TW_A]
    snq = tb4[:, 1 * TW_A:2 * TW_A]
    csk = tb4[:, 2 * TW_A:3 * TW_A]
    snk = tb4[:, 3 * TW_A:4 * TW_A]
    qs_col, ks_col = cst[:, 0:1], cst[:, 1:2]
    for eng in ("scalar", "tensor", "vector", "gpsimd"):
        P.wait(eng, ld)
    if fused_halo is not None:
        P.wait("tensor", P.done["vector"])
        P.wait("scalar", P.done["vector"])
    psel = sb0(tag + "psel", [128, 4], BF16)
    pbc = sb0(tag + "pbc", [2, 128], BF16)
    P.dma("gpsimd", pbc[:], pairbc[:, :], ld)
    P.wait("vector", ld)
    P.emit("vector", lambda e: e.tensor_copy(psel[:, 0:2], cst[:, 2:4]),
           P.done["vector"], serial=True)

    with contextlib.ExitStack() as sc:
        slab = sc.enter_context(nc.sbuf_tensor(tag + "qkv_hp", [128, 8 * 384], BF16))
        qraw = sc.enter_context(nc.sbuf_tensor(tag + "qraw", [128, Tq], BF16))
        sq = sc.enter_context(nc.sbuf_tensor(tag + "sq_a", [128, Tk], BF16))
        qrot = sc.enter_context(nc.sbuf_tensor(tag + "qrot", [128, Tq], BF16))
        kraw = sc.enter_context(nc.sbuf_tensor(tag + "kraw_a", [128, Tk], BF16))
        krot = sc.enter_context(nc.sbuf_tensor(tag + "krot", [128, Tk], BF16))
        rnb = sc.enter_context(nc.sbuf_tensor(tag + "rnb", [2, Tk], BF16))
        tq1 = sc.enter_context(nc.sbuf_tensor(tag + "tq1", [128, 512], F32))
        tq2 = sc.enter_context(nc.sbuf_tensor(tag + "tq2", [128, 512], F32))
        vaug = sc.enter_context(nc.sbuf_tensor(tag + "vaug", [128, 18 * 130], BF16))
        pt = sc.enter_context(nc.sbuf_tensor(tag + "pt_a", [128, 2 * 1024], BF16))
        rc = sc.enter_context(nc.sbuf_tensor(tag + "rc_a", [128, 2048], F32))
        rcb = sc.enter_context(nc.sbuf_tensor(tag + "rcb_a", [128, 2 * 2048], BF16))
        ps_a = sc.enter_context(nc.psum_tensor(tag + "psA", [128, 512], F32))
        ps_b = sc.enter_context(nc.psum_tensor(tag + "psB", [128, 512], F32))
        ps_bc = ps_a
        ps_ss = sc.enter_context(nc.psum_tensor(tag + "psSS", [2, 512], F32))
        ps_st = sc.enter_context(nc.psum_tensor(tag + "psST", [128, 2048], F32))
        ps_o = sc.enter_context(nc.psum_tensor(tag + "psO", [65, 512], F32))
        wsl = P.psem("lfa_w")
        for hp in range(8):
            emit_l1f_attn_hp(P, com, locals(), hp)
    with contextlib.ExitStack() as sc:
        P.wait("gpsimd", P.done["tensor"])
        P.wait("gpsimd", P.done["vector"])
        wl = P.psem("lfa_wo")
        woslab = load_wslab_sc(P, sc, tag + "wo_lfa", wo, D, D, wl)
        xblk = sc.enter_context(nc.sbuf_tensor(tag + "xblk", [128, 8 * 512], F32))
        ytmp = sc.enter_context(nc.sbuf_tensor(tag + "ytmp", [128, 2 * 512], F32))
        ps_w = [sc.enter_context(nc.psum_tensor(tag + "psW%d" % i, [128, 512], F32))
                for i in range(2)]
        ldx = P.psem("lfa_ldx")
        sty = [P.psem("lfa_sty0"), P.psem("lfa_sty1")]
        P.wait("tensor", wl)
        idx = 0
        evcnt = []
        for g in range(4):
            P.wait("sync", P.done["vector"])
            for c in range(8):
                P.dma("sync", xblk[:, c * 512:(c + 1) * 512],
                      x_own[c * 128:(c + 1) * 128, g * 512:(g + 1) * 512], ldx)
            for oc in range(8):
                b = idx % 2
                if idx >= 2:
                    P.wait("tensor", P.done["vector"], evcnt[idx - 2])
                for kc in range(8):
                    P.emit("tensor", lambda e, oc=oc, g=g, kc=kc, b=b: e.matmul(
                        ps_w[b][:], woslab[:, kc * D + oc * 128:kc * D + (oc + 1) * 128],
                        o_fm[:, kc * Tq + g * 512:kc * Tq + (g + 1) * 512],
                        start=(kc == 0), stop=(kc == 7)))
                P.mark_mm()
                P.wait("vector", P.done["tensor"])
                P.wait("vector", ldx, ldx.n)
                P.wait("vector", sty[b], sty[b].n)
                P.emit("vector", lambda e, oc=oc, b=b: e.tensor_add(
                    ytmp[:, b * 512:(b + 1) * 512], ps_w[b][:],
                    xblk[:, oc * 512:(oc + 1) * 512]), P.done["vector"],
                    serial=True)
                evcnt.append(P.done["vector"].n)
                P.wait("gpsimd", P.done["vector"])
                P.dma("gpsimd", y_out[oc * 128:(oc + 1) * 128,
                                      g * 512:(g + 1) * 512],
                      ytmp[:, b * 512:(b + 1) * 512], sty[b])
                idx += 1
        for sm in sty:
            P.wait("gpsimd", sm)
        P.wait("sync", sty[0])
    if ctx is None:
        P.build()
        return nc
    return None


def emit_l1f_attn_hp(P, com, L, hp):
    nc = P.nc
    V = "vector"
    (xw, slab, qraw, sq, qrot, kraw, krot, rnb, tq1, tq2, vaug, pt, rc, rcb,
     ps_a, ps_b, ps_bc, ps_ss, ps_st, ps_o, psel, psw, o_fm, msk,
     qs_col, ks_col, csq, snq, csk, snk, wsl, wqkv) = (
        L["xw"], L["slab"], L["qraw"], L["sq"], L["qrot"], L["kraw"],
        L["krot"], L["rnb"], L["tq1"], L["tq2"], L["vaug"], L["pt"], L["rc"],
        L["rcb"], L["ps_a"], L["ps_b"], L["ps_bc"], L["ps_ss"], L["ps_st"],
        L["ps_o"], L["psel"], L["psw"], L["o_fm"], L["msk"], L["qs_col"],
        L["ks_col"], L["csq"], L["snq"], L["csk"], L["snk"], L["wsl"],
        L["wqkv"])
    Tq, Tk = 2048, 2304
    pbc = L["pbc"]

    P.wait("gpsimd", P.done["tensor"])
    P.wait("gpsimd", P.done["vector"])
    P.wait("gpsimd", P.done["scalar"])
    for kc in range(8):
        P.dma("gpsimd", slab[:, kc * 384:(kc + 1) * 384],
              wqkv[kc * 128:(kc + 1) * 128, hp * 384:(hp + 1) * 384], wsl)
    for eng in ("tensor", "scalar", "vector"):
        P.wait(eng, wsl)
    if hp == 0:
        P.emit(V, lambda e: e.memset(vaug[:], 1.0), P.done["vector"],
               serial=True)

    def rot_path(raw, rot, scol, cst_, snt, t0, Tn, wq_off):
        nblk = (Tn + 511) // 512
        pj_cnt = {}
        pv_cnt = {}
        for g in range(nblk):
            N = min(512, Tn - g * 512)
            pq = g % 2
            if g >= 2:
                P.wait("tensor", P.done["vector"], pv_cnt[g - 2])
            for kc in range(8):
                sh = 1 if kc >= 4 else 0
                P.emit("tensor", lambda e, g=g, kc=kc, N=N, sh=sh, pq=pq: e.matmul(
                    ps_st[:, pq * 1024:pq * 1024 + N],
                    slab[:, kc * 384 + wq_off:kc * 384 + wq_off + 128],
                    xw[:, kc * TW_A + t0 + g * 512 - sh:
                       kc * TW_A + t0 + g * 512 - sh + N],
                    start=(kc == 0), stop=(kc == 7)))
            P.mark_mm()
            pj_cnt[g] = P.done["tensor"].n
            P.wait(V, P.done["tensor"], pj_cnt[g])
            P.emit(V, lambda e, g=g, N=N, pq=pq: e.tensor_copy(
                raw[:, g * 512:g * 512 + N],
                ps_st[:, pq * 1024:pq * 1024 + N]), P.done["vector"],
                serial=True)
            pv_cnt[g] = P.done["vector"].n
            P.wait("scalar", P.done["vector"])
            P.emit("scalar", lambda e, g=g, N=N: e.activation(
                out=sq[:, g * 512:g * 512 + N], in_=raw[:, g * 512:g * 512 + N],
                func=AF.Square), P.done["scalar"], serial=True)
            P.wait("tensor", P.done["scalar"])
            P.emit("tensor", lambda e, g=g, N=N: e.matmul(
                ps_ss[:, 0:N], psel[:, 0:2], sq[:, g * 512:g * 512 + N],
                start=True, stop=True))
            P.mark_mm()
            P.wait(V, P.done["tensor"])
            P.emit(V, lambda e, N=N: e.tensor_scalar(
                tq1[0:2, 0:N], ps_ss[:, 0:N], 1e-24, None, ALU.max),
                P.done["vector"], serial=True)
            P.wait("scalar", P.done["vector"])
            P.emit("scalar", lambda e, N=N: e.activation(
                out=tq1[0:2, 0:N], in_=tq1[0:2, 0:N], func=AF.Ln),
                serial=True)
            P.emit("scalar", lambda e, N=N: e.activation(
                out=rnb[0:2, 0:N], in_=tq1[0:2, 0:N], func=AF.Exp,
                scale=-0.5), P.done["scalar"], serial=True)
            P.wait(V, P.done["scalar"])
            P.wait("tensor", P.done["scalar"])
            P.wait("tensor", P.done["vector"])
            P.emit("tensor", lambda e, N=N: e.matmul(
                ps_bc[:, 0:N], pbc[0:2, :], rnb[0:2, 0:N],
                start=True, stop=True))
            P.mark_mm()
            P.wait(V, P.done["tensor"])
            P.emit(V, lambda e, g=g, N=N, scol=scol: e.scalar_tensor_tensor(
                raw[:, g * 512:g * 512 + N], raw[:, g * 512:g * 512 + N],
                scol, ps_bc[:, 0:N], ALU.mult, ALU.mult), P.done["vector"],
                serial=True)
            P.wait("tensor", P.done["vector"])
            P.emit("tensor", lambda e, g=g, N=N: e.matmul(
                ps_b[:, 0:N], psw[:], raw[:, g * 512:g * 512 + N],
                start=True, stop=True))
            P.mark_mm()
            P.wait(V, P.done["tensor"])
            P.emit(V, lambda e, g=g, N=N, cst_=cst_: e.tensor_mul(
                tq1[:, 0:N], raw[:, g * 512:g * 512 + N],
                cst_[:, t0 + g * 512:t0 + g * 512 + N]), serial=True)
            P.emit(V, lambda e, g=g, N=N, snt=snt: e.tensor_mul(
                tq2[:, 0:N], ps_b[:, 0:N],
                snt[:, t0 + g * 512:t0 + g * 512 + N]), serial=True)
            P.emit(V, lambda e, g=g, N=N: e.tensor_add(
                rot[:, g * 512:g * 512 + N], tq1[:, 0:N], tq2[:, 0:N]),
                P.done["vector"], serial=True)
            P.wait("tensor", P.done["vector"])

    rot_path(qraw, qrot, qs_col, csq, snq, 512, Tq, 0)
    rot_path(kraw, krot, ks_col, csk, snk, 256, Tk, 128)

    vt_cnt = {}
    vv_cnt = {}
    for kt in range(18):
        pq = kt % 2
        if kt >= 2:
            P.wait("tensor", P.done["vector"], vv_cnt[kt - 2])
        for kc in range(8):
            sh = 1 if kc >= 4 else 0
            P.emit("tensor", lambda e, kt=kt, kc=kc, sh=sh, pq=pq: e.matmul(
                ps_st[:, pq * 1024:pq * 1024 + 128],
                xw[:, kc * TW_A + 256 + kt * 128 - sh:
                   kc * TW_A + 256 + (kt + 1) * 128 - sh],
                slab[:, kc * 384 + 256:kc * 384 + 384],
                start=(kc == 0), stop=(kc == 7)))
        P.mark_mm()
        vt_cnt[kt] = P.done["tensor"].n
        P.wait(V, P.done["tensor"], vt_cnt[kt])
        P.emit(V, lambda e, kt=kt, pq=pq: e.tensor_copy(
            vaug[:, kt * 130:(kt + 1) * 130].rearrange(
                "p (j w) -> p j w", j=2)[:, :, 0:64],
            ps_st[:, pq * 1024:pq * 1024 + 128].rearrange(
                "p (j d) -> p j d", j=2)),
            P.done["vector"], serial=True)
        vv_cnt[kt] = P.done["vector"].n
    P.wait("tensor", P.done["vector"])

    for j in range(2):
        h = 2 * hp + j
        jr = j * 64
        s_cnt = {}
        e_cnt = {}
        a_cnt = {}
        v_cnt = {}
        z_cnt = {}

        def st_group(B, j=j, jr=jr):
            p = B % 2
            if B >= 3:
                P.wait("tensor", P.done["scalar"], e_cnt[B - 2])
            for m in range(4):
                kcol = 256 * (B - 1) + 128 * m
                for a in range(2):
                    qcol = 256 * (B - 1) + 128 * a
                    P.emit("tensor", lambda e, m=m, a=a, kcol=kcol,
                           qcol=qcol, jr=jr, p=p: e.matmul(
                        ps_st[:, p * 1024 + m * 256 + a * 128:
                              p * 1024 + m * 256 + (a + 1) * 128],
                        krot[jr:jr + 64, kcol:kcol + 128],
                        qrot[jr:jr + 64, qcol:qcol + 128],
                        start=True, stop=True))
            P.mark_mm()
            s_cnt[B] = P.done["tensor"].n

        def tail_group(B, j=j, jr=jr):
            p = B % 2
            P.wait(V, P.done["tensor"], s_cnt[B])
            moff = 1024 if B == 1 else 0
            P.emit(V, lambda e, moff=moff, p=p: e.tensor_add(
                ps_st[:, p * 1024:(p + 1) * 1024],
                ps_st[:, p * 1024:(p + 1) * 1024],
                msk[:, moff:moff + 1024]), P.done["vector"], serial=True)
            m_cnt = P.done["vector"].n
            P.wait("scalar", P.done["vector"], m_cnt)
            if B >= 3:
                P.wait("scalar", P.done["tensor"], a_cnt[B - 2])
            for half in range(2):
                P.emit("scalar", lambda e, half=half, p=p: e.activation(
                    out=pt[:, p * 1024 + half * 512:p * 1024 + (half + 1) * 512],
                    in_=ps_st[:, p * 1024 + half * 512:
                              p * 1024 + (half + 1) * 512],
                    func=AF.Exp, scale=8.0),
                    P.done["scalar"] if half == 1 else None)
            e_cnt[B] = P.done["scalar"].n
            P.wait("tensor", P.done["scalar"], e_cnt[B])
            if B >= 3:
                P.wait("tensor", P.done["vector"], v_cnt[B - 2])
                P.wait("tensor", P.done["scalar"], z_cnt[B - 2])
            for m in range(4):
                kti = 2 * (B - 1) + m
                P.emit("tensor", lambda e, m=m, kti=kti, j=j, p=p: e.matmul(
                    ps_o[:, p * 256:(p + 1) * 256],
                    vaug[:, kti * 130 + j * 65:kti * 130 + (j + 1) * 65],
                    pt[:, p * 1024 + m * 256:p * 1024 + (m + 1) * 256],
                    start=(m == 0), stop=(m == 3)))
            P.mark_mm()
            a_cnt[B] = P.done["tensor"].n
            P.wait("scalar", P.done["tensor"], a_cnt[B])
            P.emit("scalar", lambda e, B=B, p=p: e.activation(
                out=rc[64:65, (B - 1) * 256:B * 256],
                in_=ps_o[64:65, p * 256:(p + 1) * 256], func=AF.Copy),
                P.done["scalar"], serial=True)
            z_cnt[B] = P.done["scalar"].n
            P.wait(V, P.done["tensor"], a_cnt[B])
            P.emit(V, lambda e, B=B, p=p, jr=jr, hp=hp: e.tensor_copy(
                o_fm[jr:jr + 64, hp * Tq + (B - 1) * 256:hp * Tq + B * 256],
                ps_o[0:64, p * 256:(p + 1) * 256]), P.done["vector"],
                serial=True)
            v_cnt[B] = P.done["vector"].n

        st_group(1)
        for B in range(2, 9):
            st_group(B)
            tail_group(B - 1)
        tail_group(8)
        P.emit("scalar", lambda e: e.activation(
            out=rc[64:65, 0:2048], in_=rc[64:65, 0:2048], func=AF.Ln),
            serial=True)
        P.emit("scalar", lambda e: e.activation(
            out=rc[64:65, 0:2048], in_=rc[64:65, 0:2048], func=AF.Exp,
            scale=-1.0), P.done["scalar"], serial=True)
        P.wait(V, P.done["scalar"])
        P.emit(V, lambda e: e.tensor_copy(rcb[64:65, 0:2048],
                                          rc[64:65, 0:2048]), serial=True)
        P.emit(V, lambda e: e.tensor_sub(rc[64:65, 0:2048],
                                         rc[64:65, 0:2048],
                                         rcb[64:65, 0:2048]), serial=True)
        P.emit(V, lambda e: e.tensor_copy(rcb[64:65, 2048:4096],
                                          rc[64:65, 0:2048]),
               P.done["vector"], serial=True)
        P.wait("tensor", P.done["vector"])
        for g in range(4):
            if g > 0:
                P.wait("tensor", P.done["vector"])
            for i2 in range(2):
                P.emit("tensor", lambda e, g=g, i2=i2, jr=jr: e.matmul(
                    ps_bc[jr:jr + 64, :], com.onesP_bf[64:65, 0:64],
                    rcb[64:65, i2 * 2048 + g * 512:i2 * 2048 + (g + 1) * 512],
                    start=(i2 == 0), stop=(i2 == 1)))
            P.mark_mm()
            P.wait(V, P.done["tensor"])
            P.emit(V, lambda e, g=g, jr=jr, hp=hp: e.tensor_mul(
                o_fm[jr:jr + 64, hp * Tq + g * 512:hp * Tq + (g + 1) * 512],
                o_fm[jr:jr + 64, hp * Tq + g * 512:hp * Tq + (g + 1) * 512],
                ps_bc[jr:jr + 64, :]), P.done["vector"], serial=True)
        P.wait("tensor", P.done["vector"])


TW_F = 2176


def build_l1f_ffn(ctx=None, tag=""):
    fused_halo = None
    if ctx is None:
        nc = bass.Bass("TRN2", target_bir_lowering=False, debug=False)
        x_own = nc.dram_tensor("x_own", [D, T], F32, kind="ExternalInput").ap()
        x_h1 = nc.dram_tensor("x_h1", [D, 128], F32, kind="ExternalInput").ap()
        w1 = nc.dram_tensor("w1p", [D, 6144], F32, kind="ExternalInput").ap()
        w2 = nc.dram_tensor("w2p", [3072, D], F32, kind="ExternalInput").ap()
        bpk = nc.dram_tensor("bpk", [128, 72], F32, kind="ExternalInput").ap()
        y_out = nc.dram_tensor("y", [D, T], F32, kind="ExternalOutput").ap()
        P = Prog(nc)
        com = Common(P, Tn=512, with_ln_xb=True)
        sc0 = P.stack
    else:
        nc, P, com, sc0 = ctx["nc"], ctx["P"], ctx["com"], ctx["sc"]
        io = ctx["io"]
        x_own, w1, w2, bpk, y_out = (io["x_own"], io["w1"], io["w2"],
                                     io["bpk"], io["y_out"])
        fused_halo = ctx["halo"]
    V = "vector"
    sb0 = lambda n_, sh_, dt_: sc0.enter_context(nc.sbuf_tensor(n_, sh_, dt_))
    xw = sb0(tag + "xw_f", [128, 8 * TW_F], BF16)
    bp = sb0(tag + "bp_f", [128, 72], F32)
    ld = P.psem("lff_ld")
    xwr = xw[:].rearrange("p (c t) -> p c t", c=8)
    if fused_halo is None:
        P.dma("gpsimd", xwr[:, :, 0:128],
              x_h1[:, :].rearrange("(c p) t -> p c t", p=128), ld)
    else:
        halo_g, wslc = fused_halo
        P.emit("vector", lambda e: e.memset(xwr[:, :, 0:127], 0.0),
               P.done["vector"], serial=True)
        htmp = [sb0(tag + "h1t%d" % r, [128, 8], BF16) for r in range(4)]
        for r in range(4):
            P.dma("gpsimd", htmp[r][:].rearrange("p (c o) -> p c o", o=1),
                  halo_g[r * 1024:(r + 1) * 1024, :].rearrange(
                      "(c p) o -> p c o", p=128), ld,
                  allow_slow_non_contiguous=True)
        P.wait("vector", ld)
        hr = [h[:].rearrange("p (c o) -> p c o", o=1) for h in htmp]
        P.emit("vector", lambda e: e.tensor_scalar(
            xwr[:, :, 127:128], hr[0], wslc[:, 0:1], None, ALU.mult),
            serial=True)
        for r in range(1, 4):
            P.emit("vector", lambda e, r=r: e.scalar_tensor_tensor(
                xwr[:, :, 127:128], hr[r], wslc[:, r:r + 1],
                xwr[:, :, 127:128], ALU.mult, ALU.add), serial=True)
        P.emit("vector", lambda e: e.wait_ge(P.bar.h, 0), P.done["vector"],
               serial=True)
    P.dma("gpsimd", xwr[:, :, 128:TW_F],
          x_own[:, :].rearrange("(c p) t -> p c t", p=128), ld)
    P.dma("gpsimd", bp[:], bpk[:, :], ld)
    for eng in ("scalar", "tensor", "vector", "gpsimd"):
        P.wait(eng, ld)
    if fused_halo is not None:
        P.wait("tensor", P.done["vector"])
        P.wait("scalar", P.done["vector"])
    g_col = [bp[:, c:c + 1] for c in range(8)]
    b_col = [bp[:, 8 + c:8 + c + 1] for c in range(8)]
    b1a = [bp[:, 16 + i:16 + i + 1] for i in range(24)]
    b1g = [bp[:, 40 + i:40 + i + 1] for i in range(24)]
    b2c = [bp[:, 64 + i:64 + i + 1] for i in range(8)]

    with contextlib.ExitStack() as sc:
        xn = sc.enter_context(nc.sbuf_tensor(tag + "xn_f", [128, 8 * 512], BF16))
        prod = sc.enter_context(nc.sbuf_tensor(tag + "prod", [128, 24 * 512], BF16))
        sla = sc.enter_context(nc.sbuf_tensor(tag + "sla", [128, 2 * 8 * 512], BF16))
        slg = sc.enter_context(nc.sbuf_tensor(tag + "slg", [128, 2 * 8 * 512], BF16))
        sl2 = sc.enter_context(nc.sbuf_tensor(tag + "sl2", [128, 2 * 24 * 128], BF16))
        ta = sc.enter_context(nc.sbuf_tensor(tag + "ta_f", [128, 2 * 512], F32))
        ga = sc.enter_context(nc.sbuf_tensor(tag + "ga_f", [128, 2 * 512], BF16))
        xblk = sc.enter_context(nc.sbuf_tensor(tag + "xblk_f", [128, 8 * 512], F32))
        ytmp = sc.enter_context(nc.sbuf_tensor(tag + "ytmp_f", [128, 2 * 512], F32))
        ps_a = sc.enter_context(nc.psum_tensor(tag + "psa_f", [128, 2 * 512], F32))
        ps_g = sc.enter_context(nc.psum_tensor(tag + "psg_f", [128, 2 * 512], F32))
        ps_y = [sc.enter_context(nc.psum_tensor(tag + "psy%d" % i, [128, 512], F32))
                for i in range(2)]
        wl = P.psem("lff_w")
        w2l = [P.psem("lff_w2_0"), P.psem("lff_w2_1")]
        ldx = P.psem("lff_ldx")
        sty = [P.psem("lff_sty0"), P.psem("lff_sty1")]
        for tb in range(4):
            t0 = tb * 512
            xsl = [xw[:, c * TW_F + 128 + t0 - (1 if c >= 4 else 0):
                      c * TW_F + 128 + t0 + 512 - (1 if c >= 4 else 0)]
                   for c in range(8)]
            w = [("vector", P.done["tensor"], P.done["tensor"].n)]
            emit_ln(P, com, None, 512, "bf16", out=xn,
                    tag=tag + "ffln%d" % tb, waits=w, xsl=xsl)
            P.wait(V, P.done["vector"])
            for c in range(8):
                P.emit(V, lambda e, c=c: e.tensor_scalar(
                    xn[:, c * 512:(c + 1) * 512], xn[:, c * 512:(c + 1) * 512],
                    g_col[c], b_col[c], ALU.mult, ALU.add),
                    P.done["vector"] if c == 7 else None, serial=True)
            t_cnt = {}
            v_cnt = {}
            for jb in range(6):
                bj = jb % 2
                if jb >= 2:
                    P.wait("gpsimd", P.done["tensor"], t_cnt[jb - 2])
                for kc in range(8):
                    P.dma("gpsimd", sla[:, bj * 4096 + kc * 512:
                                        bj * 4096 + (kc + 1) * 512],
                          w1[kc * 128:(kc + 1) * 128,
                             jb * 512:(jb + 1) * 512], wl)
                    P.dma("gpsimd", slg[:, bj * 4096 + kc * 512:
                                        bj * 4096 + (kc + 1) * 512],
                          w1[kc * 128:(kc + 1) * 128,
                             3072 + jb * 512:3072 + (jb + 1) * 512], wl)
                P.wait("tensor", wl)
                if jb == 0:
                    P.wait("tensor", P.done["vector"])
                for k4 in range(4):
                    kc2 = jb * 4 + k4
                    bk = kc2 % 2
                    if kc2 >= 2:
                        P.wait("tensor", P.done["vector"], v_cnt[kc2 - 2])
                    for kc in range(8):
                        P.emit("tensor", lambda e, k4=k4, kc=kc, bj=bj, bk=bk: e.matmul(
                            ps_a[:, bk * 512:(bk + 1) * 512],
                            sla[:, bj * 4096 + kc * 512 + k4 * 128:
                                bj * 4096 + kc * 512 + (k4 + 1) * 128],
                            xn[:, kc * 512:(kc + 1) * 512],
                            start=(kc == 0), stop=(kc == 7)))
                    for kc in range(8):
                        P.emit("tensor", lambda e, k4=k4, kc=kc, bj=bj, bk=bk: e.matmul(
                            ps_g[:, bk * 512:(bk + 1) * 512],
                            slg[:, bj * 4096 + kc * 512 + k4 * 128:
                                bj * 4096 + kc * 512 + (k4 + 1) * 128],
                            xn[:, kc * 512:(kc + 1) * 512],
                            start=(kc == 0), stop=(kc == 7)))
                    P.mark_mm()
                    t_cnt[kc2 // 4 if False else jb] = P.done["tensor"].n
                    P.wait("scalar", P.done["tensor"])
                    P.emit("scalar", lambda e, kc2=kc2, bk=bk: e.activation(
                        out=ga[:, bk * 512:(bk + 1) * 512],
                        in_=ps_g[:, bk * 512:(bk + 1) * 512],
                        func=AF.Gelu, bias=b1g[kc2]),
                        P.done["scalar"], serial=True)
                    P.wait(V, P.done["scalar"])
                    P.emit(V, lambda e, kc2=kc2, bk=bk: e.tensor_scalar(
                        ta[:, bk * 512:(bk + 1) * 512],
                        ps_a[:, bk * 512:(bk + 1) * 512],
                        b1a[kc2], None, ALU.add), serial=True)
                    P.emit(V, lambda e, kc2=kc2, bk=bk: e.tensor_mul(
                        prod[:, kc2 * 512:(kc2 + 1) * 512],
                        ta[:, bk * 512:(bk + 1) * 512],
                        ga[:, bk * 512:(bk + 1) * 512]),
                        P.done["vector"], serial=True)
                    v_cnt[kc2] = P.done["vector"].n
            P.wait("sync", P.done["vector"])
            for sm in sty:
                P.wait("sync", sm, sm.n)
            for c in range(8):
                P.dma("sync", xblk[:, c * 512:(c + 1) * 512],
                      x_own[c * 128:(c + 1) * 128, t0:t0 + 512], ldx)
            tw_cnt = {}
            vw_cnt = {}
            for oc in range(8):
                b = oc % 2
                if oc >= 2:
                    P.wait("gpsimd", P.done["tensor"], tw_cnt[oc - 2])
                for kc2 in range(24):
                    P.dma("gpsimd", sl2[:, b * 3072 + kc2 * 128:
                                        b * 3072 + (kc2 + 1) * 128],
                          w2[kc2 * 128:(kc2 + 1) * 128,
                             oc * 128:(oc + 1) * 128], w2l[b])
                P.wait("tensor", w2l[b])
                if oc == 0:
                    P.wait("tensor", P.done["vector"])
                elif oc >= 2:
                    P.wait("tensor", P.done["vector"], vw_cnt[oc - 2])
                for kc2 in range(24):
                    P.emit("tensor", lambda e, kc2=kc2, b=b: e.matmul(
                        ps_y[b][:], sl2[:, b * 3072 + kc2 * 128:
                                        b * 3072 + (kc2 + 1) * 128],
                        prod[:, kc2 * 512:(kc2 + 1) * 512],
                        start=(kc2 == 0), stop=(kc2 == 23)))
                P.mark_mm()
                tw_cnt[oc] = P.done["tensor"].n
                P.wait(V, P.done["tensor"], tw_cnt[oc])
                P.wait(V, ldx, ldx.n)
                P.wait(V, sty[b], sty[b].n)
                P.emit(V, lambda e, oc=oc, b=b: e.scalar_tensor_tensor(
                    ytmp[:, b * 512:(b + 1) * 512], ps_y[b][:], b2c[oc],
                    xblk[:, oc * 512:(oc + 1) * 512], ALU.add, ALU.add),
                    P.done["vector"], serial=True)
                vw_cnt[oc] = P.done["vector"].n
                P.wait("gpsimd", P.done["vector"])
                P.dma("gpsimd", y_out[oc * 128:(oc + 1) * 128, t0:t0 + 512],
                      ytmp[:, b * 512:(b + 1) * 512], sty[b])
        for sm in sty:
            P.wait("gpsimd", sm)
        P.wait("sync", sty[0])
    if ctx is None:
        P.build()
        return nc
    return None


def prep_l0a(W_qkv):
    Wt = np.ascontiguousarray(W_qkv.T)
    idx = np.zeros(3072, np.int64)
    for ty in range(3):
        for h in range(16):
            for d_ in range(64):
                idx[ty * 1024 + h * 64 + d_] = h * 192 + ty * 64 + d_
    return np.ascontiguousarray(Wt[:, idx])


def prep_l0b(z):
    H, KPH = 16, 32
    qw, kw, vw = z["L0_ffn_q_w"], z["L0_ffn_k_w"], z["L0_ffn_v_w"]
    qb, kb, ow = z["L0_ffn_q_b"], z["L0_ffn_k_b"], z["L0_ffn_o_w"]
    wq_pad = np.zeros((1024, 1024), np.float32)
    wk_pad = np.zeros((1024, 1024), np.float32)
    qb_pad = np.zeros(1024, np.float32)
    kb_pad = np.zeros(1024, np.float32)
    wo_pad = np.zeros((1024, 1024), np.float32)
    for h in range(H):
        sl = slice(h * 64, h * 64 + KPH)
        wq_pad[:, sl] = qw[:, h * 64:h * 64 + KPH]
        qb_pad[sl] = qb[h * 64:h * 64 + KPH]
        wk_pad[:, sl] = kw[:, h * KPH:(h + 1) * KPH]
        kb_pad[sl] = kb[h * KPH:(h + 1) * KPH]
        wo_pad[sl, :] = ow[h * KPH:(h + 1) * KPH, :]
    bcols = ([qb_pad[i * 128:(i + 1) * 128] for i in range(8)]
             + [kb_pad[i * 128:(i + 1) * 128] for i in range(8)]
             + [z["L0_ffn_o_b"][i * 128:(i + 1) * 128] for i in range(8)]
             + [z["L0_ffn_b1"][e, hc * 128:(hc + 1) * 128]
                for e in range(8) for hc in range(16)]
             + [z["L0_ffn_b2"][e, oc * 128:(oc + 1) * 128]
                for e in range(8) for oc in range(8)])
    return {
        "wqkvb": np.ascontiguousarray(
            np.concatenate([wq_pad, wk_pad, vw], axis=1)),
        "wob": np.ascontiguousarray(wo_pad),
        "wrt": np.ascontiguousarray(z["L0_ffn_router_w"]),
        "w1": z["L0_ffn_w1"], "w2": z["L0_ffn_w2"],
        "bpack": np.ascontiguousarray(np.stack(bcols, 1).astype(np.float32)),
        "vbr": np.ascontiguousarray(z["L0_ffn_v_b"][None, :]),
        "rbr": np.ascontiguousarray(z["L0_ffn_router_b"][None, :]),
        "ident": np.eye(128, dtype=np.float32),
    }


def prep_l1a(z, core):
    H, hd, BASE = 16, 64, 10000.0
    b, q = core // 4, core % 4
    theta = (BASE ** (-2.0 / 1024 * np.arange(1024, dtype=np.float64))
             ).reshape(H, hd)
    pos = np.arange(q * T, (q + 1) * T, dtype=np.float64)
    ang = pos[None, :, None] * theta[:, None, :]
    c = np.cos(ang).astype(np.float32)
    s = np.sin(ang).astype(np.float32)
    hsel = np.zeros((128, 128), np.float32)
    for cch in range(8):
        for p in range(128):
            hsel[p, cch * 16 + 2 * cch + p // 64] = 1.0
    return {
        "wq": z["L1_attn_wq"], "wk": z["L1_attn_wk"], "wv": z["L1_attn_wv"],
        "cfm": np.ascontiguousarray(c.transpose(0, 2, 1).reshape(1024, T)),
        "sfm": np.ascontiguousarray(s.transpose(0, 2, 1).reshape(1024, T)),
        "ctm": np.ascontiguousarray(c), "stm": np.ascontiguousarray(s),
        "utri": np.triu(np.ones((128, 128), np.float32)),
        "wg1": z["L1_attn_wg1"], "wg2": z["L1_attn_wg2"],
        "wo": z["L1_attn_wo"],
        "bpk": np.ascontiguousarray(
            z["L1_attn_norm_w"].reshape(8, 128).T.astype(np.float32)),
        "wselp": np.tile((np.arange(4) < q).astype(np.float32), (128, 1)),
        "hsel": hsel, "ident": np.eye(128, dtype=np.float32),
    }


def _l1f_rotab(gpos):
    DIM_HEAD, WINDOW = 64, 256
    half = np.arange(0, DIM_HEAD, 2, dtype=np.float64)
    inv_freq = 1.0 / 10000.0 ** (half / DIM_HEAD)
    fr = gpos[:, None] * inv_freq[None, :]
    fr = np.concatenate([fr, fr], -1)
    base_scale = (half + 0.4 * DIM_HEAD) / (1.4 * DIM_HEAD)
    power = (gpos - 8192 // 2) / (WINDOW // 2)
    sc = base_scale[None, :] ** power[:, None]
    sc = np.concatenate([sc, sc], -1)
    return fr, sc


def prep_l1f_attn(z, core, li):
    b, q = core // 4, core % 4
    Wq = z["L1_ffn_Wqkv"][li]
    qs, ks = z["L1_ffn_q_scale"][li], z["L1_ffn_k_scale"][li]
    pair = np.zeros((1024, 16 * 384), np.float32)
    for hp in range(8):
        for j in range(2):
            h = 2 * hp + j
            pair[:, hp * 384 + j * 64:hp * 384 + (j + 1) * 64] = \
                Wq[:, h * 64:(h + 1) * 64]
            pair[:, hp * 384 + 128 + j * 64:hp * 384 + 128 + (j + 1) * 64] = \
                Wq[:, 1024 + h * 64:1024 + (h + 1) * 64]
            pair[:, hp * 384 + 256 + j * 64:hp * 384 + 256 + (j + 1) * 64] = \
                Wq[:, 2048 + h * 64:2048 + (h + 1) * 64]
    gpos = q * T + np.arange(TW_A, dtype=np.float64) - 512.0
    fr, sc = _l1f_rotab(gpos)
    sgn = np.ones(64)
    sgn[:32] = -1.0
    csq = (np.cos(fr) * sc).T.astype(np.float32)
    snq = (np.sin(fr) * sc * sgn[:, None].T).T.astype(np.float32)
    csk = (np.cos(fr) / sc).T.astype(np.float32)
    snk = (np.sin(fr) / sc * sgn[None, :]).T.astype(np.float32)
    tabs = np.concatenate([np.tile(t, (2, 1))
                           for t in (csq, snq, csk, snk)], 0)
    mv = np.full((128, 1024), -1e30, np.float32)
    for m in range(4):
        for a in range(2):
            for jj in range(128):
                for ii in range(128):
                    r = 256 + a * 128 + ii - m * 128 - jj
                    if 0 <= r <= 256:
                        mv[jj, m * 256 + a * 128 + ii] = 0.0
    mf = mv.copy()
    if q == 0:
        mf[:, 0:512] = -1e30
    masks = np.concatenate([mv, mf], 0)
    consts = np.zeros((128, 134), np.float32)
    consts[:, 0] = np.tile(qs, 2)
    consts[:, 1] = np.tile(ks, 2)
    for p in range(128):
        consts[p, 2 + p // 64] = 1.0
    psw = np.zeros((128, 128), np.float32)
    for d_ in range(128):
        e_ = (d_ // 64) * 64 + (d_ % 64 + 32) % 64
        psw[e_, d_] = 1.0
    consts[:, 6:134] = psw
    pairbc = np.zeros((2, 128), np.float32)
    pairbc[0, 0:64] = 1.0
    pairbc[1, 64:128] = 1.0
    return {"wqkv": np.ascontiguousarray(pair),
            "wo": z["L1_ffn_Wo"][li],
            "tabs": np.ascontiguousarray(tabs.astype(np.float32)),
            "masks": np.ascontiguousarray(masks),
            "consts": consts, "pairbc": pairbc}


def prep_l1f_ffn(z, li):
    W1, W2 = z["L1_ffn_W1"][li], z["L1_ffn_W2"][li]
    b1 = z["L1_ffn_b1"][li]
    w1p = np.zeros((1024, 6144), np.float32)
    w1p[:, 0:2730] = W1[:, 0:2730]
    w1p[:, 3072:3072 + 2730] = W1[:, 2730:5460]
    w2p = np.zeros((3072, 1024), np.float32)
    w2p[0:2730, :] = W2
    b1a = np.zeros(3072, np.float32)
    b1g = np.zeros(3072, np.float32)
    b1a[0:2730] = b1[0:2730]
    b1g[0:2730] = b1[2730:5460]
    bcols = ([z["L1_ffn_ln_g"][li][i * 128:(i + 1) * 128] for i in range(8)]
             + [z["L1_ffn_ln_b"][li][i * 128:(i + 1) * 128] for i in range(8)]
             + [b1a[i * 128:(i + 1) * 128] for i in range(24)]
             + [b1g[i * 128:(i + 1) * 128] for i in range(24)]
             + [z["L1_ffn_b2"][li][i * 128:(i + 1) * 128] for i in range(8)])
    return {"w1p": w1p, "w2p": w2p,
            "bpk": np.ascontiguousarray(np.stack(bcols, 1).astype(np.float32))}


def build_sandwich_ln(mode, ctx=None, tag=""):
    if ctx is None:
        nc = bass.Bass("TRN2", target_bir_lowering=False, debug=False)
        xin = nc.dram_tensor("xin", [D, T], F32, kind="ExternalInput").ap()
        xold = None
        if mode == "post":
            xold = nc.dram_tensor("xold", [D, T], F32, kind="ExternalInput").ap()
        out = nc.dram_tensor("out", [D, T], F32, kind="ExternalOutput").ap()
        P = Prog(nc)
        com = Common(P, Tn=512, with_ln_xb=False)
        sc0 = P.stack
    else:
        nc, P, com, sc0 = ctx["nc"], ctx["P"], ctx["com"], ctx["sc"]
        io = ctx["io"]
        xin, xold, out = io["xin"], io.get("xold"), io["out"]
    sb0 = lambda n_, sh_, dt_: sc0.enter_context(nc.sbuf_tensor(n_, sh_, dt_))
    xt = sb0(tag + "xt_s", [128, 8 * T], BF16)
    ot = sb0(tag + "ot_s", [128, 8 * T], F32)
    ld = P.psem("sln_ld")
    st = P.psem("sln_st")
    P.wait("gpsimd", P.done["vector"])
    P.dma("gpsimd", xt[:].rearrange("p (c t) -> p c t", c=8),
          xin[:, :].rearrange("(c p) t -> p c t", p=128), ld)
    if mode == "post":
        P.wait("sync", P.done["vector"])
        P.dma("sync", ot[:].rearrange("p (c t) -> p c t", c=8),
              xold[:, :].rearrange("(c p) t -> p c t", p=128), ld)
    w = [(e, ld, ld.n) for e in ("vector", "scalar", "tensor")]
    if mode == "pre":
        P.emit("vector", lambda e: e.memset(ot[:], 0.0), P.done["vector"])
    emit_ln_big(P, com, xt[:], T, "residual", xold=ot, xnew=ot,
                tag=tag + "sln", waits=w, x_bf16=True)
    P.wait("sync", P.done["vector"])
    P.dma("sync", out[:, :].rearrange("(c p) t -> p c t", p=128),
          ot[:].rearrange("p (c t) -> p c t", c=8), st)
    P.wait("sync", st)
    if ctx is None:
        P.build()
        return nc
    return None


def l0a_kv_loader_fused(P, ktsg, vsg, HDv=65):
    W = 4 * HDv

    def loader(hg, K_sb, V_sb, sem):
        for cc in range(2):
            for r in range(4):
                P.dma("gpsimd",
                      K_sb[:, cc * SEQ + r * T:cc * SEQ + (r + 1) * T],
                      ktsg[r * 1024 + hg * 256 + cc * 128:
                           r * 1024 + hg * 256 + (cc + 1) * 128, :], sem)
        vsr = vsg.rearrange("(kb p) w -> p kb w", p=128)
        vdr = V_sb[:].rearrange("p (kb w) -> p kb w", w=W)
        for i in range(8):
            P.dma("gpsimd", vdr[:, i * 8:(i + 1) * 8, :],
                  vsr[:, i * 8:(i + 1) * 8, hg * W:(hg + 1) * W], sem)
    return loader


def build_fused():
    import types
    nc = bass.Bass("TRN2", target_bir_lowering=False, debug=False)
    EI = lambda n_, sh_: nc.dram_tensor(n_, sh_, F32, kind="ExternalInput").ap()
    x0own = EI("x0own", [D, T])
    wqkv_a = EI("wqkv_a", [D, 3072])
    wqkvb = EI("wqkvb", [D, 2560])
    wob = EI("wob", [D, D])
    wrt = EI("wrt", [D, 8])
    w1 = EI("w1", [8, D, 2048])
    w2 = EI("w2", [8, 2048, D])
    bpack = EI("bpack", [128, 216])
    vbr = EI("vbr", [1, 512])
    rbr = EI("rbr", [1, 8])
    ident = EI("ident", [128, 128])
    wq = EI("wq", [D, D]); wk = EI("wk", [D, D]); wv = EI("wv", [D, D])
    cfm = EI("cfm", [D, T]); sfm = EI("sfm", [D, T])
    ctm = EI("ctm", [16, T, 64]); stm = EI("stm", [16, T, 64])
    utri = EI("utri", [128, 128])
    wg1 = EI("wg1", [D, 64]); wg2 = EI("wg2", [64, D]); wo_l1 = EI("wo_l1", [D, D])
    bpk_l1 = EI("bpk_l1", [128, 8])
    wselp = EI("wselp", [128, 4])
    whalo = EI("whalo", [128, 4])
    hsel = EI("hsel", [128, 128])
    wqkv6 = EI("wqkv6", [6, D, 16 * 384])
    wo6 = EI("wo6", [6, D, D])
    tabs = EI("tabs", [4 * 128, TW_A])
    masks = EI("masks", [2 * 128, 1024])
    consts6 = EI("consts6", [6, 128, 134])
    pairbc = EI("pairbc", [2, 128])
    w1p6 = EI("w1p6", [6, D, 6144])
    w2p6 = EI("w2p6", [6, 3072, D])
    bpk6 = EI("bpk6", [6, 128, 72])
    out = nc.dram_tensor("out", [D, T], F32, kind="ExternalOutput").ap()
    IN = lambda n_, sh_, dt_=BF16: nc.dram_tensor(n_, sh_, dt_).ap()
    kts_l = IN("kts_l", [D, T]); vs_l = IN("vs_l", [T, 16 * 65])
    ktsg = IN("ktsg", [4 * D, T]); vsg = IN("vsg", [SEQ, 16 * 65])
    kts2_l = IN("kts2_l", [D, T]); vs2_l = IN("vs2_l", [T, 16 * 33])
    kts2g = IN("kts2g", [4 * D, T]); vs2g = IN("vs2g", [SEQ, 16 * 33])
    sig_l = IN("sig_l", [1, 1024], F32); S_l = IN("S_l", [16 * 128, 64], F32)
    sig_g = IN("sig_g", [4, 1024], F32); S_g = IN("S_g", [4 * 16 * 128, 64], F32)
    x1_d = IN("x1_d", [D, T], F32); x2_d = IN("x2_d", [D, T], F32)
    x3_d = IN("x3_d", [D, T], F32)
    xsd = [IN("xs_d%d" % i, [D, T], F32) for i in range(2)]
    halo_l = IN("halo_l", [D, 257], F32); halo_g = IN("halo_g", [4 * D, 257], F32)
    h1_l = IN("h1_l", [D, 1], F32); h1_g = IN("h1_g", [4 * D, 1], F32)

    P = Prog(nc)
    com = Common(P, Tn=512, with_ln_xb=True)
    wsl = P.sb("wsl_f", [128, 4], F32)
    wsl_h = P.sb("wsl_h", [128, 4], F32)
    ldw = P.psem("ldw")
    P.dma("sync", wsl[:], wselp[:, :], ldw)
    P.dma("sync", wsl_h[:], whalo[:, :], ldw)
    for eng in ("vector", "gpsimd", "tensor", "scalar"):
        P.wait(eng, ldw)

    with contextlib.ExitStack() as sc:
        A = types.SimpleNamespace(
            tag="l0a", wqkv=wqkv_a, kts=kts_l, vs=vs_l, ktsg=ktsg, vsg=vsg,
            ag_groups=AG_GROUPS, stage="full",
            kv_slices=[(x0own, 0), (x0own, 1024)],
            q_slices=[(x0own, 0, 0), (x0own, 1024, 1024)],
            Q_fm=sc.enter_context(nc.sbuf_tensor("l0aQ", [128, 8 * T], BF16)),
            o_fm=sc.enter_context(nc.sbuf_tensor("l0aO", [128, 8 * T], BF16)),
            kv_loader=l0a_kv_loader_fused(P, ktsg, vsg),
            x_resid=x0own, x_out=x1_d)
        emit_l0a(P, com, A)
    P.fence()

    with contextlib.ExitStack() as sc:
        io = dict(xin=x1_d, wqkvb=wqkvb, wob=wob, wrt=wrt, w1=w1, w2=w2,
                  bpack=bpack, vbr=vbr, rbr=rbr, ident=ident, x2=x2_d,
                  kts=kts2_l, vs=vs2_l)
        build_l0b_ckpt(ctx=dict(
            nc=nc, P=P, com=com, sc=sc, io=io,
            kv_slices=[(x1_d, 0), (x1_d, 1024)],
            kv_loader=l0a_kv_loader_fused(P, kts2g, vs2g, HDv=33),
            ag=[(kts2_l, kts2g), (vs2_l, vs2g)]))
    P.fence()

    l1a_io = dict(xin=x2_d, wq=wq, wk=wk, wv=wv, cfm=cfm, sfm=sfm, ctm=ctm,
                  stm=stm, utri=utri)
    with contextlib.ExitStack() as sc:
        io = dict(l1a_io)
        io.update(sig_o=sig_l, S_o=S_l)
        build_l1a("partial", ctx=dict(nc=nc, P=P, com=com, sc=sc, io=io))
    emit_ag(P, [(sig_l, sig_g), (S_l, S_g)],
            wait_sems=[P.psem("sto"), P.psem("l1a_sto")])
    P.fence()
    with contextlib.ExitStack() as sc:
        io = dict(l1a_io)
        io.update(wg1=wg1, wg2=wg2, wo=wo_l1, bpk=bpk_l1, sig_p=sig_g,
                  S_p=S_g.rearrange("(r x) d_ -> r x d_", r=4), wselp=wselp,
                  hsel=hsel[:, 0:128], ident=ident, x3=x3_d)
        build_l1a("full", ctx=dict(nc=nc, P=P, com=com, sc=sc, io=io))
    P.fence()

    with contextlib.ExitStack() as sc:
        build_sandwich_ln("pre", ctx=dict(
            nc=nc, P=P, com=com, sc=sc, io=dict(xin=x3_d, out=xsd[0])),
            tag="pre")
    P.fence()
    cur, nxt = xsd[0], xsd[1]
    hst = P.psem("halo_st")
    for li in range(6):
        P.wait("gpsimd", P.done["vector"])
        for nm in ("lff_sty0", "lff_sty1", "sln_st"):
            P.wait("gpsimd", P.psem(nm))
        P.dma("gpsimd", halo_l[:, :], cur[:, T - 257:T], hst)
        emit_ag(P, [(halo_l, halo_g)], wait_sems=[hst])
        with contextlib.ExitStack() as sc:
            io = dict(x_own=cur, wqkv=wqkv6[li], wo=wo6[li], tabs=tabs,
                      masks=masks, consts=consts6[li], pairbc=pairbc,
                      y_out=nxt)
            build_l1f_attn(ctx=dict(nc=nc, P=P, com=com, sc=sc, io=io,
                                    halo=(halo_g, wsl_h)), tag="a%d_" % li)
        P.fence()
        P.wait("gpsimd", P.done["vector"])
        for nm in ("lfa_sty0", "lfa_sty1"):
            P.wait("gpsimd", P.psem(nm))
        P.dma("gpsimd", h1_l[:, :], nxt[:, T - 1:T], hst,
              allow_slow_non_contiguous=True)
        emit_ag(P, [(h1_l, h1_g)], wait_sems=[hst])
        with contextlib.ExitStack() as sc:
            io = dict(x_own=nxt, w1=w1p6[li], w2=w2p6[li], bpk=bpk6[li],
                      y_out=cur)
            build_l1f_ffn(ctx=dict(nc=nc, P=P, com=com, sc=sc, io=io,
                                   halo=(h1_g, wsl_h)), tag="f%d_" % li)
        P.fence()
    with contextlib.ExitStack() as sc:
        build_sandwich_ln("post", ctx=dict(
            nc=nc, P=P, com=com, sc=sc,
            io=dict(xin=cur, xold=x3_d, out=out)), tag="post")
    P.build()
    return nc


FUSED = True


def _prep_core_inputs(z, core):
    b, q = core // 4, core % 4
    m = {}
    X = z["L0_attn_z"].astype(np.float32)
    m["x0own"] = np.ascontiguousarray(X[b, q * T:(q + 1) * T].T)
    m["wqkv_a"] = prep_l0a(z["L0_attn_W_qkv"])
    m.update(prep_l0b(z))
    p1 = prep_l1a(z, core)
    for k_, v_ in p1.items():
        if k_ == "wo":
            m["wo_l1"] = v_
        elif k_ == "bpk":
            m["bpk_l1"] = v_
        else:
            m[k_] = v_
    pa0 = prep_l1f_attn(z, core, 0)
    m["tabs"] = pa0["tabs"]
    m["masks"] = pa0["masks"]
    m["pairbc"] = pa0["pairbc"]
    cons, wq6, wo6l = [], [], []
    for li in range(6):
        pa = prep_l1f_attn(z, core, li)
        cons.append(pa["consts"])
        wq6.append(pa["wqkv"])
        wo6l.append(pa["wo"])
    m["consts6"] = np.ascontiguousarray(np.stack(cons, 0))
    m["wqkv6"] = np.ascontiguousarray(np.stack(wq6, 0))
    m["wo6"] = np.ascontiguousarray(np.stack(wo6l, 0))
    w1l, w2l, bpl = [], [], []
    for li in range(6):
        pf = prep_l1f_ffn(z, li)
        w1l.append(pf["w1p"])
        w2l.append(pf["w2p"])
        bpl.append(pf["bpk"])
    m["w1p6"] = np.ascontiguousarray(np.stack(w1l, 0))
    m["w2p6"] = np.ascontiguousarray(np.stack(w2l, 0))
    m["bpk6"] = np.ascontiguousarray(np.stack(bpl, 0))
    q = core % 4
    m["whalo"] = np.tile((np.arange(4) == q - 1).astype(np.float32), (128, 1))
    return m


def kernel(**z):
    z = {k: np.asarray(v) for k, v in z.items()}
    if FUSED:
        nc = build_fused()
        in_maps = [_prep_core_inputs(z, core) for core in range(NC)]
        res = run_bass_kernel_spmd(nc, in_maps, core_ids=list(range(NC)))
        out = np.zeros((2, SEQ, D), np.float32)
        for core in range(NC):
            b, q = core // 4, core % 4
            out[b, q * T:(q + 1) * T, :] = res.results[core]["out"].T
        return out
    return kernel_ckpt(**z)


def kernel_ckpt(**z):





    z = {k: np.asarray(v) for k, v in z.items()}

    def fm(a):
        return np.ascontiguousarray(np.asarray(a).T)

    X = z["L0_attn_z"].astype(np.float32)

    wq = prep_l0a(z["L0_attn_W_qkv"])
    nc = build_l0a_ckpt()
    in_maps = []
    for core in range(NC):
        b, q = core // 4, core % 4
        in_maps.append({"xfull": fm(X[b]),
                        "xown": fm(X[b, q * T:(q + 1) * T]),
                        "wqkv": wq})
    res = run_bass_kernel_spmd(nc, in_maps, core_ids=list(range(NC)))
    x1s = [res.results[c]["x1"] for c in range(NC)]
    x1f = [np.ascontiguousarray(np.concatenate(x1s[4 * b:4 * b + 4], axis=1))
           for b in range(2)]

    p0b = prep_l0b(z)
    nc = build_l0b_ckpt()
    in_maps = []
    for core in range(NC):
        b, q = core // 4, core % 4
        m = dict(p0b)
        m["xin"] = np.ascontiguousarray(x1s[core])
        m["xfull1"] = x1f[b]
        in_maps.append(m)
    res = run_bass_kernel_spmd(nc, in_maps, core_ids=list(range(NC)))
    x2s = [res.results[c]["x2"] for c in range(NC)]

    preps = [prep_l1a(z, core) for core in range(NC)]
    nc = build_l1a("partial")
    in_maps = []
    for core in range(NC):
        m = {k: preps[core][k] for k in
             ("wq", "wk", "wv", "cfm", "sfm", "ctm", "stm", "utri")}
        m["xin"] = np.ascontiguousarray(x2s[core])
        in_maps.append(m)
    res = run_bass_kernel_spmd(nc, in_maps, core_ids=list(range(NC)))
    sig = [res.results[c]["sig_o"] for c in range(NC)]
    Sp = [res.results[c]["S_o"] for c in range(NC)]

    nc = build_l1a("full")
    in_maps = []
    for core in range(NC):
        b, q = core // 4, core % 4
        m = dict(preps[core])
        m["xin"] = np.ascontiguousarray(x2s[core])
        m["sig_p"] = np.ascontiguousarray(
            np.concatenate([sig[4 * b + r] for r in range(4)], 0))
        m["S_p"] = np.ascontiguousarray(
            np.stack([Sp[4 * b + r] for r in range(4)], 0))
        in_maps.append(m)
    res = run_bass_kernel_spmd(nc, in_maps, core_ids=list(range(NC)))
    x3s = [res.results[c]["x3"] for c in range(NC)]

    nc = build_sandwich_ln("pre")
    in_maps = [{"xin": np.ascontiguousarray(x3s[c])} for c in range(NC)]
    res = run_bass_kernel_spmd(nc, in_maps, core_ids=list(range(NC)))
    xs = [res.results[c]["out"] for c in range(NC)]

    nca = build_l1f_attn()
    ncf = build_l1f_ffn()
    for li in range(6):
        in_maps = []
        for core in range(NC):
            b, q = core // 4, core % 4
            m = dict(prep_l1f_attn(z, core, li))
            m["x_own"] = np.ascontiguousarray(xs[core])
            xh = np.zeros((D, 512), np.float32)
            if q > 0:
                xh[:, 255:512] = xs[core - 1][:, T - 257:T]
            m["x_halo"] = xh
            in_maps.append(m)
        res = run_bass_kernel_spmd(nca, in_maps, core_ids=list(range(NC)))
        ys = [res.results[c]["y"] for c in range(NC)]

        pf = prep_l1f_ffn(z, li)
        in_maps = []
        for core in range(NC):
            b, q = core // 4, core % 4
            m = dict(pf)
            m["x_own"] = np.ascontiguousarray(ys[core])
            xh = np.zeros((D, 128), np.float32)
            if q > 0:
                xh[:, 127] = ys[core - 1][:, T - 1]
            m["x_h1"] = xh
            in_maps.append(m)
        res = run_bass_kernel_spmd(ncf, in_maps, core_ids=list(range(NC)))
        xs = [res.results[c]["y"] for c in range(NC)]

    nc = build_sandwich_ln("post")
    in_maps = [{"xin": np.ascontiguousarray(xs[c]),
                "xold": np.ascontiguousarray(x3s[c])} for c in range(NC)]
    res = run_bass_kernel_spmd(nc, in_maps, core_ids=list(range(NC)))
    xs = [res.results[c]["out"] for c in range(NC)]

    out = np.zeros((2, SEQ, D), np.float32)
    for core in range(NC):
        b, q = core // 4, core % 4
        out[b, q * T:(q + 1) * T, :] = xs[core].T
    return out
